# version 10
# baseline (speedup 1.0000x reference)
import numpy as np
import ml_dtypes
from contextlib import ExitStack

import concourse.bass as bass
import concourse.mybir as mybir
from concourse.bass_utils import run_bass_kernel_spmd

F32 = mybir.dt.float32
BF16 = mybir.dt.bfloat16
Alu = mybir.AluOpType
Act = mybir.ActivationFunctionType
AXX = mybir.AxisListType.X

MAGIC = float(np.float32(1.5 * 2 ** 23))
NCORE = 8
B, L, D, NST, RDT, HID = 2, 4096, 2048, 16, 64, 8192
TOK = (B * L) // NCORE
DSL = D // NCORE
CH = 512
NCH = L // CH
NKD = D // 128
NTT = TOK // 128
NKH = HID // 128
EPS = 1e-5

DEBUG_STOP = None
DEBUG_DUMPS = False
DEBUG_BC1 = False
DEBUG_A1 = False
DEBUG_NOBC = False


class _Prog:

    ENGINES = ("sync", "vector", "scalar", "tensor", "gpsimd")

    def __init__(self):
        self.ops = []
        self.cnt = {}

    def sem(self, name):
        self.cnt.setdefault(name, 0)

    def emit(self, eng, fn, inc=None, k=1):
        if inc is not None:
            self.sem(inc)
            self.cnt[inc] += k
        self.ops.append(("op", eng, fn, inc, k))

    def dma(self, eng, out, in_, inc):
        self.emit(eng, lambda E, o=out, i=in_: E.dma_start(out=o, in_=i),
                  inc=inc, k=16)

    def wait(self, eng, semname, val=None):
        v = self.cnt.get(semname, 0) if val is None else val
        if v > 0:
            self.ops.append(("wait", eng, semname, v, None))

    def val(self, semname):
        return self.cnt.get(semname, 0)

    def barrier(self):
        self.sem("PH")
        for e in self.ENGINES:
            if e in ("vector", "scalar", "tensor"):
                self.emit(e, lambda E: E.drain(), inc="PH")
            else:
                self.emit(e, lambda E: E.nop(), inc="PH")
        tgt = self.cnt["PH"]
        for e in self.ENGINES:
            self.wait(e, "PH", tgt)

    def lower(self, nc, block, ctx):
        handles = {name: ctx.enter_context(nc.semaphore(name))
                   for name in self.cnt}
        for eng_name in self.ENGINES:
            my = [op for op in self.ops if op[1] == eng_name]
            if not my:
                continue

            def section(E, my=my, handles=handles):
                for kind, _eng, a, b, c in my:
                    if kind == "wait":
                        E.wait_ge(handles[a], b)
                    else:
                        instr = a(E)
                        if b is not None:
                            instr.then_inc(handles[b], c)
            getattr(block, eng_name)(section)


def build_nc():
    nc = bass.Bass("TRN2", target_bir_lowering=False)

    T = {}

    def din(name, shape, dt=F32):
        T[name] = nc.dram_tensor(name, shape, dt, kind="ExternalInput").ap()

    def dout(name, shape, dt=F32):
        T[name] = nc.dram_tensor(name, shape, dt, kind="ExternalOutput").ap()

    def dint(name, shape, dt=F32):
        T[name] = nc.dram_tensor(name, shape, dt)

    din("x_tok", [TOK, D]); din("x_dsl", [B, L, DSL]); din("wx", [96, D])
    din("wdtT", [RDT, DSL]); din("bdt", [128, 2]); din("alog", [128, 2 * NST])
    din("dvec", [128, 2]); din("wgTd", [D, 1024]); din("wvTd", [D, 1024])
    din("woTd", [1024, D]); din("gbias", [1, HID]); din("vbias", [1, HID])
    din("obias", [1, D])
    dout("out", [TOK, D])
    if DEBUG_STOP == "mixer":
        dout("zdbg", [B * L, DSL])
        dout("dtdbg", [128, 2 * CH * B * NCH])
        dout("xdbg", [128, 2 * CH * B * NCH])
        dout("xtdbg", [128, 2 * CH * B * NCH])
        dout("abdbg", [128, 4 * CH * NST])
        dout("atdbg", [128, 2 * NST])
        dout("xrdbg", [128, 4 * DSL * B * NCH])
        dout("ydbg", [128, 2 * CH * B * NCH])
    if DEBUG_STOP == "ffn":
        dout("z1dbg", [TOK, D])
        dout("gvdbg", [NTT * 16 * 128, 512])
        dout("gvqdbg", [NTT * 128, HID], BF16)
        dout("zmydbg", [TOK, D])
        dout("lndbg", [128, 8 * NTT])
    if DEBUG_STOP == "ph0":
        dout("agdbg", [NCORE * 96, TOK])
        dout("wqdbg", [NCORE * D, 1024], BF16)
        dout("gmdbg", [NCORE, 4])
        dout("wgdbg", [128, 8])
    dint("ag1_in", [96, TOK]); dint("ag1_out", [NCORE * 96, TOK])
    dint("gm_in", [1, 4]); dint("gm_out", [NCORE, 4])
    dint("wgq_in", [D, 1024], BF16); dint("wgq_out", [NCORE * D, 1024], BF16)
    dint("wvq_in", [D, 1024], BF16); dint("wvq_out", [NCORE * D, 1024], BF16)
    dint("woq_in", [1024, D], BF16); dint("woq_out", [NCORE * 1024, D], BF16)
    dint("z_in", [B * L, DSL]); dint("z_out", [NCORE * B * L, DSL])
    dint("z_my", [TOK, D])
    dint("z1st", [TOK, D])
    dint("gv_dram", [NTT, 16, 128, 512])
    dint("gvqT_dram", [NTT, 128, HID], BF16)

    P = _Prog()
    for s in ("LD", "ST", "SD", "GD", "CC", "VE", "SC", "TE", "GP", "PH",
              "MF", "BF"):
        P.sem(s)
    marks = {}

    ctx = ExitStack()
    with ctx:
        E = ctx.enter_context
        S = {}
        S["identb"] = E(nc.sbuf_tensor("identb", [128, 128], BF16))
        S["identf"] = E(nc.sbuf_tensor("identf", [128, 128], F32))
        S["ones1"] = E(nc.sbuf_tensor("ones1", [1, 128], F32))
        S["onescol"] = E(nc.sbuf_tensor("onescol", [128, 1], F32))
        S["ones1b"] = E(nc.sbuf_tensor("ones1b", [1, 128], BF16))
        S["wgam"] = E(nc.sbuf_tensor("wgam", [128, 8], F32))
        S["dinv"] = E(nc.sbuf_tensor("dinv", [128, 1], F32))
        S["epsc"] = E(nc.sbuf_tensor("epsc", [128, 1], F32))
        S["c127"] = E(nc.sbuf_tensor("c127", [128, 1], F32))
        S["psf"] = [E(nc.psum_tensor(f"psf{i}", [128, 512], F32))
                    for i in range(6)]
        S["psb"] = [E(nc.psum_tensor(f"psb{i}", [128, 1024], BF16))
                    for i in range(2)]
        block = E(nc.Block())

        with ExitStack() as cscan:
            ES = cscan.enter_context
            S["xpj_h"] = ES(nc.sbuf_tensor("xpj_h", [96, B * L], BF16))
            S["xpj_l"] = ES(nc.sbuf_tensor("xpj_l", [96, B * L], BF16))

            _phase0(P, nc, T, S, marks)

            if DEBUG_STOP == "ph0":
                P.wait("gpsimd", "CC", marks["CC_GM"])
                P.dma("gpsimd", T["agdbg"], T["ag1_out"].ap(), "GD")
                P.dma("gpsimd", T["wqdbg"], T["wgq_out"].ap(), "GD")
                P.dma("gpsimd", T["gmdbg"], T["gm_out"].ap(), "GD")
                P.dma("gpsimd", T["wgdbg"], S["wgam"][:], "GD")
                P.wait("gpsimd", "GD", P.val("GD"))
                P.lower(nc, block, ctx)
                return nc

            _scan(P, nc, T, S, marks)

            if DEBUG_STOP == "mixer":
                P.wait("gpsimd", "ST", P.val("ST"))
                P.dma("gpsimd", T["zdbg"], T["z_in"].ap(), "GD")
                P.wait("gpsimd", "GD", P.val("GD"))
                P.lower(nc, block, ctx)
                return nc

            P.wait("gpsimd", "ST", P.val("ST"))
            P.wait("gpsimd", "SD", P.val("SD"))
            P.emit("gpsimd", lambda E: E.collective_compute(
                "AllGather", Alu.bypass, replica_groups=[list(range(NCORE))],
                ins=[T["z_in"].ap().opt()], outs=[T["z_out"].ap().opt()]),
                inc="CC")
            marks["CC_Z"] = P.val("CC")
            for a, b in (("wgq_in", "wgq_out"), ("wvq_in", "wvq_out"),
                         ("woq_in", "woq_out")):
                P.emit("gpsimd", lambda E, x=T[a], y=T[b]:
                       E.collective_compute(
                           "AllGather", Alu.bypass,
                           replica_groups=[list(range(NCORE))],
                           ins=[x.ap().opt()], outs=[y.ap().opt()]),
                       inc="CC")
            marks["CC_W"] = P.val("CC")
            P.barrier()
        _ffn(P, nc, T, S, marks)
        P.lower(nc, block, ctx)
    return nc


def _ag(P, T, name_in, name_out, mark, marks):
    P.wait("gpsimd", "ST", P.val("ST"))
    P.emit("gpsimd", lambda E, a=T[name_in], b=T[name_out]: E.collective_compute(
        "AllGather", Alu.bypass, replica_groups=[list(range(NCORE))],
        ins=[a.ap().opt()], outs=[b.ap().opt()]), inc="CC")
    marks[mark] = P.val("CC")


def _phase0(P, nc, T, S, marks):
    psf, psb = S["psf"], S["psb"]
    identb, identf = S["identb"], S["identf"]
    ones1, onescol, wgam = S["ones1"], S["onescol"], S["wgam"]

    with ExitStack() as c0:
        E0 = c0.enter_context
        xtk = E0(nc.sbuf_tensor("xtk", [128, D], F32))
        xth = E0(nc.sbuf_tensor("xth", [128, D], BF16))
        xtl = E0(nc.sbuf_tensor("xtl", [128, D], BF16))
        xtmp = E0(nc.sbuf_tensor("xtmp", [128, D], F32))
        xTh = E0(nc.sbuf_tensor("xTh", [128, NKD * 128], BF16))
        xTl = E0(nc.sbuf_tensor("xTl", [128, NKD * 128], BF16))
        wxs = E0(nc.sbuf_tensor("wxs", [96, D], F32))
        wxh = E0(nc.sbuf_tensor("wxh", [96, D], BF16))
        wxl = E0(nc.sbuf_tensor("wxl", [96, D], BF16))
        wxTh = E0(nc.sbuf_tensor("wxTh", [128, NKD * 96], BF16))
        wxTl = E0(nc.sbuf_tensor("wxTl", [128, NKD * 96], BF16))
        xpmine = E0(nc.sbuf_tensor("xpmine", [96, TOK], F32))
        wti = [E0(nc.sbuf_tensor(f"wti{i}", [128, 2048], F32)) for i in range(2)]
        wqo = [E0(nc.sbuf_tensor(f"wqo{i}", [128, 2048], BF16)) for i in range(2)]
        wtmp = E0(nc.sbuf_tensor("wtmp", [128, 2048], F32))
        wacc = E0(nc.sbuf_tensor("wacc", [128, 1], F32))
        wred = E0(nc.sbuf_tensor("wred", [128, 1], F32))
        gmsb = E0(nc.sbuf_tensor("gmsb", [1, 8], F32))
        gmrd = E0(nc.sbuf_tensor("gmrd", [1, 32], F32))

        P.emit("gpsimd", lambda E: E.memset(identb[:], 0.0))
        P.emit("gpsimd", lambda E: E.affine_select(
            out=identb[:], in_=identb[:], compare_op=Alu.not_equal,
            fill=1.0, base=0, pattern=[[-1, 128]], channel_multiplier=1))
        P.emit("gpsimd", lambda E: E.memset(identf[:], 0.0))
        P.emit("gpsimd", lambda E: E.affine_select(
            out=identf[:], in_=identf[:], compare_op=Alu.not_equal,
            fill=1.0, base=0, pattern=[[-1, 128]], channel_multiplier=1))
        P.emit("gpsimd", lambda E: E.memset(ones1[:], 1.0))
        P.emit("gpsimd", lambda E: E.memset(onescol[:], 1.0))
        P.emit("gpsimd", lambda E: E.memset(S["dinv"][:], 1.0 / D))
        P.emit("gpsimd", lambda E: E.memset(S["epsc"][:], EPS))
        P.emit("gpsimd", lambda E: E.memset(S["c127"][:], 127.0))
        P.emit("gpsimd", lambda E: E.memset(S["ones1b"][:], 1.0), inc="GP")
        gp_const = P.val("GP")

        P.dma("sync", wxs[:], T["wx"], "LD")
        P.wait("vector", "LD", P.val("LD"))
        P.emit("vector", lambda E: E.tensor_copy(wxh[:], wxs[:]))
        P.emit("vector", lambda E: E.tensor_tensor(
            xtmp[0:96, :], wxs[:], wxh[:], Alu.subtract))
        P.emit("vector", lambda E: E.tensor_copy(wxl[:], xtmp[0:96, :]), inc="VE")
        P.wait("tensor", "VE", P.val("VE"))
        P.wait("tensor", "GP", gp_const)
        tsq = 0
        for k in range(NKD):
            for src, dst in ((wxh, wxTh), (wxl, wxTl)):
                if tsq >= 2:
                    P.wait("tensor", "SC", P.val("SC") - 1)
                P.emit("tensor", lambda E, s=src, k=k, p=tsq % 2: E.transpose(
                    psb[p][:, 0:96], s[:, k * 128:(k + 1) * 128],
                    identb[0:96, 0:96]), inc="TE")
                P.wait("scalar", "TE", P.val("TE"))
                P.emit("scalar", lambda E, d=dst, k=k, p=tsq % 2: E.copy(
                    d[:, k * 96:(k + 1) * 96], psb[p][:, 0:96]), inc="SC")
                tsq += 1

        for tt in range(NTT):
            P.wait("sync", "VE", P.val("VE"))
            P.dma("sync", xtk[:], T["x_tok"][tt * 128:(tt + 1) * 128, :], "LD")
            P.wait("vector", "LD", P.val("LD"))
            P.emit("vector", lambda E: E.tensor_copy(xth[:], xtk[:]))
            P.emit("vector", lambda E: E.tensor_tensor(
                xtmp[:], xtk[:], xth[:], Alu.subtract))
            P.emit("vector", lambda E: E.tensor_copy(xtl[:], xtmp[:]), inc="VE")
            P.wait("tensor", "VE", P.val("VE"))
            for k in range(NKD):
                for src, dst in ((xth, xTh), (xtl, xTl)):
                    P.wait("tensor", "SC", P.val("SC") - 1)
                    P.emit("tensor", lambda E, s=src, k=k, p=tsq % 2: E.transpose(
                        psb[p][:, 0:128], s[:, k * 128:(k + 1) * 128], identb[:]),
                        inc="TE")
                    P.wait("scalar", "TE", P.val("TE"))
                    P.emit("scalar", lambda E, d=dst, k=k, p=tsq % 2: E.copy(
                        d[:, k * 128:(k + 1) * 128], psb[p][:, 0:128]), inc="SC")
                    tsq += 1
            P.wait("tensor", "SC", P.val("SC"))
            n_mm = NKD * 4
            i_mm = 0
            for k in range(NKD):
                for lh in (wxTh, wxTl):
                    for rh in (xTh, xTl):
                        P.emit("tensor",
                               lambda E, k=k, lh=lh, rh=rh, f=(i_mm == 0),
                               st=(i_mm == n_mm - 1): E.matmul(
                                   psf[4][0:96, 0:128],
                                   lh[:, k * 96:(k + 1) * 96],
                                   rh[:, k * 128:(k + 1) * 128],
                                   start=f, stop=st),
                               inc=("TE" if i_mm == n_mm - 1 else None))
                        i_mm += 1
            P.wait("vector", "TE", P.val("TE"))
            P.emit("vector", lambda E, tt=tt: E.tensor_copy(
                xpmine[:, tt * 128:(tt + 1) * 128], psf[4][0:96, 0:128]),
                inc="VE")
            P.wait("tensor", "VE", P.val("VE"))
        P.wait("sync", "VE", P.val("VE"))
        P.dma("sync", T["ag1_in"].ap(), xpmine[:], "ST")
        _ag(P, T, "ag1_in", "ag1_out", "CC_AG1", marks)

        wspec = [
            ("wgTd", "wgq_in", D // 128, 1024, 0),
            ("wvTd", "wvq_in", D // 128, 1024, 1),
            ("woTd", "woq_in", 1024 // 128, 2048, 2),
        ]
        for wsrc, wdst, ntile, width, widx in wspec:
            for it in range(ntile):
                buf = wti[it % 2]
                P.wait("sync", "VE", P.val("VE"))
                P.dma("sync", buf[:, 0:width],
                      T[wsrc][it * 128:(it + 1) * 128, :], "LD")
                P.wait("vector", "LD", P.val("LD"))
                P.emit("vector", lambda E, b=buf, w=width: E.tensor_reduce(
                    wred[:], b[:, 0:w], AXX, Alu.add, apply_absolute_value=True))
                P.emit("vector", lambda E: E.drain())
                if it == 0:
                    P.emit("vector", lambda E: E.tensor_copy(wacc[:], wred[:]),
                           inc="VE")
                else:
                    P.emit("vector", lambda E: E.tensor_tensor(
                        wacc[:], wacc[:], wred[:], Alu.add), inc="VE")
            P.wait("tensor", "VE", P.val("VE"))
            P.emit("tensor", lambda E: E.matmul(
                psf[5][0:1, 0:1], wacc[:], onescol[:], start=True, stop=True),
                inc="TE")
            P.wait("vector", "TE", P.val("TE"))
            P.emit("vector", lambda E, wi=widx: E.tensor_scalar(
                gmsb[0:1, wi:wi + 1], psf[5][0:1, 0:1], 1.0 / (HID * D), None,
                op0=Alu.mult), inc="VE")
            P.wait("tensor", "VE", P.val("VE"))
        P.wait("sync", "VE", P.val("VE"))
        P.dma("sync", T["gm_in"].ap(), gmsb[0:1, 0:4], "ST")
        _ag(P, T, "gm_in", "gm_out", "CC_GM", marks)
        P.wait("sync", "CC", marks["CC_GM"])
        P.dma("sync", gmrd[0:1, 0:32], T["gm_out"].ap(), "LD")
        P.wait("vector", "LD", P.val("LD"))
        gm3 = gmrd[0:1, 0:32].rearrange("p (r f) -> p r f", f=4)
        for widx in range(3):
            P.emit("vector", lambda E, wi=widx: E.tensor_reduce(
                gmsb[0:1, wi:wi + 1], gm3[:, :, wi], AXX, Alu.add))
        P.emit("vector", lambda E: E.drain(), inc="VE")
        P.wait("scalar", "VE", P.val("VE"))
        P.emit("scalar", lambda E: E.copy(gmsb[0:1, 4:7], gmsb[0:1, 0:3]),
               inc="SC")
        P.wait("vector", "SC", P.val("SC"))
        P.emit("vector", lambda E: E.tensor_scalar(
            gmsb[0:1, 0:3], gmsb[0:1, 4:7], EPS, None, op0=Alu.max))
        P.emit("vector", lambda E: E.drain())
        P.emit("vector", lambda E: E.tensor_scalar(
            gmsb[0:1, 4:7], gmsb[0:1, 0:3], 1.0 / 127.0, None, op0=Alu.mult))
        P.emit("vector", lambda E: E.drain())
        P.emit("vector", lambda E: E.reciprocal(gmsb[0:1, 0:3], gmsb[0:1, 0:3]))
        P.emit("vector", lambda E: E.drain(), inc="VE")
        P.wait("tensor", "VE", P.val("VE"))
        P.emit("tensor", lambda E: E.matmul(
            psf[5][:, 0:8], ones1[0:1, :], gmsb[0:1, 0:8], start=True,
            stop=True), inc="TE")
        P.wait("vector", "TE", P.val("TE"))
        P.emit("vector", lambda E: E.tensor_copy(wgam[:], psf[5][:, 0:8]),
               inc="VE")

        P.barrier()


def _scan(P, nc, T, S, marks):
    psf = S["psf"]
    identf = S["identf"]
    xpj_h, xpj_l = S["xpj_h"], S["xpj_l"]
    NP = B * NCH

    with ExitStack() as c1:
        E1 = c1.enter_context
        wdts = E1(nc.sbuf_tensor("wdts", [RDT, DSL], F32))
        wdth = E1(nc.sbuf_tensor("wdth", [RDT, DSL], BF16))
        wdtl = E1(nc.sbuf_tensor("wdtl", [RDT, DSL], BF16))
        wdtmp = E1(nc.sbuf_tensor("wdtmp", [RDT, DSL], F32))
        bdts = E1(nc.sbuf_tensor("bdts", [128, 2], F32))
        alogs = E1(nc.sbuf_tensor("alogs", [128, 2 * NST], F32))
        atile = E1(nc.sbuf_tensor("atile", [128, 2 * NST], F32))
        dp1 = E1(nc.sbuf_tensor("dp1", [128, 2], F32))
        xraw = [E1(nc.sbuf_tensor(f"xraw{i}", [128, 4 * DSL], F32))
                for i in range(2)]
        xT = [E1(nc.sbuf_tensor(f"xT{i}", [128, 2 * CH], F32))
              for i in range(2)]
        dtt = [E1(nc.sbuf_tensor(f"dtt{i}", [128, 2 * CH], F32))
               for i in range(2)]
        dtx = [E1(nc.sbuf_tensor(f"dtx{i}", [128, 2 * CH], F32))
               for i in range(2)]
        spm = E1(nc.sbuf_tensor("spm", [128, 2 * CH], F32))
        spa = E1(nc.sbuf_tensor("spa", [128, 2 * CH], F32))
        abuf = E1(nc.sbuf_tensor("abuf", [128, 4 * CH], F32))
        wbuf = E1(nc.sbuf_tensor("wbuf", [128, 2 * CH], F32))
        htr = [E1(nc.sbuf_tensor(f"htr{i}", [128, 2 * CH], F32))
               for i in range(2)]
        mbf = [E1(nc.sbuf_tensor(f"mbf{i}", [128, 2 * CH], F32))
               for i in range(2)]
        csb = E1(nc.sbuf_tensor("csb", [128, 2 * CH], F32))
        yacc = E1(nc.sbuf_tensor("yacc", [128, 2 * CH], F32))
        ztile = E1(nc.sbuf_tensor("ztile", [128, 2 * CH], F32))
        zstage = E1(nc.sbuf_tensor("zstage", [128, 4 * DSL], F32))
        hst = E1(nc.sbuf_tensor("hst", [128, 4 * NST], F32))

        for eng in ("sync", "vector", "scalar", "tensor", "gpsimd"):
            P.wait(eng, "CC", marks["CC_GM"])
        P.dma("sync", wdts[:], T["wdtT"], "LD")
        P.dma("sync", bdts[:], T["bdt"], "LD")
        P.dma("sync", alogs[:], T["alog"], "LD")
        P.dma("sync", dp1[:], T["dvec"], "LD")
        P.wait("vector", "LD", P.val("LD"))
        P.emit("vector", lambda E: E.tensor_copy(wdth[:], wdts[:]))
        P.emit("vector", lambda E: E.tensor_tensor(
            wdtmp[:], wdts[:], wdth[:], Alu.subtract))
        P.emit("vector", lambda E: E.tensor_copy(wdtl[:], wdtmp[:]))
        P.emit("vector", lambda E: E.tensor_scalar(
            dp1[:], dp1[:], 1.0, None, op0=Alu.add), inc="VE")
        P.wait("scalar", "LD", P.val("LD"))
        P.emit("scalar", lambda E: E.activation(atile[:], alogs[:], Act.Exp),
               inc="SC")
        P.wait("vector", "SC", P.val("SC"))
        P.emit("vector", lambda E: E.tensor_scalar(
            atile[:], atile[:], -1.0, None, op0=Alu.mult), inc="VE")
        P.emit("vector", lambda E: E.memset(hst[:], 0.0), inc="GP")
        gp_hst = P.val("GP")

        P.wait("sync", "CC", marks["CC_AG1"])
        with ExitStack() as cx:
            xprojT = cx.enter_context(
                nc.sbuf_tensor("xprojT", [96, B * L], F32))
            losc = cx.enter_context(nc.sbuf_tensor("losc", [96, 1024], F32))
            for r in range(NCORE):
                P.dma("sync", xprojT[:, r * TOK:(r + 1) * TOK],
                      T["ag1_out"][r * 96:(r + 1) * 96, :], "LD")
            P.wait("vector", "LD", P.val("LD"))
            P.emit("vector", lambda E: E.tensor_copy(xpj_h[:], xprojT[:, :]))
            for cs in range(0, B * L, 1024):
                P.emit("vector", lambda E, cs=cs: E.tensor_tensor(
                    losc[:, :], xprojT[:, cs:cs + 1024],
                    xpj_h[:, cs:cs + 1024], Alu.subtract))
                P.emit("vector", lambda E, cs=cs: E.tensor_copy(
                    xpj_l[:, cs:cs + 1024], losc[:, :]),
                    inc=("VE" if cs == B * L - 1024 else None))
        stg_h = E1(nc.sbuf_tensor("stg_h", [1, 32 * CH], BF16))
        stg_l = E1(nc.sbuf_tensor("stg_l", [1, 32 * CH], BF16))
        qwi = [E1(nc.sbuf_tensor(f"qwi{i}", [128, 1024], F32))
               for i in range(2)]
        qwo = [E1(nc.sbuf_tensor(f"qwo{i}", [128, 1024], BF16))
               for i in range(2)]
        qtmp = E1(nc.sbuf_tensor("qtmp", [128, 1024], F32))
        qcst = E1(nc.sbuf_tensor("qcst", [128, 4], F32))
        ve_prep = P.val("VE")
        P.wait("tensor", "VE", ve_prep)
        P.wait("tensor", "GP", gp_hst)
        P.wait("vector", "GP", gp_hst)
        P.wait("scalar", "VE", ve_prep)
        P.wait("sync", "VE", ve_prep)

        ld_ready, ld_stage = {}, {}
        xtr_done, xT_ready, ztr_done, zst_done = {}, {}, {}, {}
        sc_sp, gp_dtt, gp_dtx, ve_zt = {}, {}, {}, {}
        te_bc, sc_exp, sc_evC, ve_tts, ve_m0, gp_n = {}, {}, {}, {}, {}, {}
        mf_marks = []

        def emit_head(li):
            b, ci = divmod(li, NCH)
            colb = b * L + ci * CH
            P.wait("tensor", "LD", ld_ready[li])
            first_evac = True
            for s in range(4):
                for d in range(2):
                    if len(mf_marks) >= 2:
                        P.wait("tensor", "SC", mf_marks[-2])
                    P.emit("tensor", lambda E, s=s, d=d, li=li: E.transpose(
                        psf[4 + d][:, 0:128],
                        xraw[li % 2][:, s * DSL + d * 128:s * DSL + (d + 1) * 128],
                        identf[:]), inc="TE")
                    P.wait("scalar", "TE", P.val("TE"))
                    if first_evac:
                        if li - 2 in gp_dtx:
                            P.wait("scalar", "GP", gp_dtx[li - 2])
                        if li - 2 in ve_zt:
                            P.wait("scalar", "VE", ve_zt[li - 2])
                        first_evac = False
                    P.emit("scalar", lambda E, s=s, d=d, li=li: E.copy(
                        xT[li % 2][:, d * CH + s * 128:d * CH + (s + 1) * 128],
                        psf[4 + d][:, 0:128]), inc="SC")
                    mf_marks.append(P.val("SC"))
            xT_ready[li] = P.val("SC")
            xtr_done[li] = P.val("TE")
            if DEBUG_STOP == "mixer" and DEBUG_DUMPS:
                P.wait("sync", "TE", xtr_done[li])
                P.dma("sync", T["xrdbg"][:, li * 4 * DSL:(li + 1) * 4 * DSL],
                      xraw[li % 2][:], "ST")
            for d in range(2):
                if len(mf_marks) >= 2:
                    P.wait("tensor", "SC", mf_marks[-2])
                mmi = 0
                for lh in (wdth, wdtl):
                    for rh in (xpj_h, xpj_l):
                        P.emit("tensor", lambda E, lh=lh, rh=rh, d=d,
                               f=(mmi == 0), st=(mmi == 3), cb=colb:
                               E.matmul(psf[4 + d][:, 0:CH],
                                        lh[:, d * 128:(d + 1) * 128],
                                        rh[0:RDT, cb:cb + CH],
                                        start=f, stop=st),
                               inc=("TE" if mmi == 3 else None))
                        mmi += 1
                P.wait("scalar", "TE", P.val("TE"))
                if d == 0 and li - 1 in gp_dtt:
                    P.wait("scalar", "GP", gp_dtt[li - 1])
                P.emit("scalar", lambda E, d=d: E.activation(
                    spm[:, d * CH:(d + 1) * CH], psf[4 + d][:, 0:CH],
                    Act.Relu, bias=bdts[:, d:d + 1]))
                P.emit("scalar", lambda E, d=d: E.activation(
                    spa[:, d * CH:(d + 1) * CH], psf[4 + d][:, 0:CH],
                    Act.Abs, bias=bdts[:, d:d + 1]), inc="SC")
                mf_marks.append(P.val("SC"))
                P.emit("scalar", lambda E, d=d: E.activation(
                    spa[:, d * CH:(d + 1) * CH], spa[:, d * CH:(d + 1) * CH],
                    Act.Exp, scale=-1.0))
                P.emit("scalar", lambda E, d=d: E.activation(
                    spa[:, d * CH:(d + 1) * CH], spa[:, d * CH:(d + 1) * CH],
                    Act.Ln, bias=1.0), inc="SC")
            sc_sp[li] = P.val("SC")
            P.wait("gpsimd", "SC", sc_sp[li])
            if (li - 1) * NST - 1 in sc_exp:
                P.wait("gpsimd", "SC", sc_exp[(li - 1) * NST - 1])
            P.emit("gpsimd", lambda E, li=li: E.tensor_tensor(
                dtt[li % 2][:], spm[:], spa[:], Alu.add), inc="GP")
            gp_dtt[li] = P.val("GP")
            P.wait("gpsimd", "SC", xT_ready[li])
            if (li - 1) * NST - 1 in ve_tts:
                P.wait("gpsimd", "VE", ve_tts[(li - 1) * NST - 1])
            P.emit("gpsimd", lambda E, li=li: E.tensor_tensor(
                dtx[li % 2][:], dtt[li % 2][:], xT[li % 2][:], Alu.mult),
                inc="GP")
            gp_dtx[li] = P.val("GP")
            if DEBUG_STOP == "mixer" and DEBUG_DUMPS:
                P.wait("sync", "GP", P.val("GP"))
                P.dma("sync", T["dtdbg"][:, li * 2 * CH:(li + 1) * 2 * CH],
                      dtt[li % 2][:], "ST")
                P.dma("sync", T["xdbg"][:, li * 2 * CH:(li + 1) * 2 * CH],
                      dtx[li % 2][:], "ST")
                P.dma("sync", T["xtdbg"][:, li * 2 * CH:(li + 1) * 2 * CH],
                      xT[li % 2][:], "ST")
                for eng in ("vector", "scalar", "tensor", "gpsimd"):
                    P.wait(eng, "ST", P.val("ST"))

        def emit_stage(li):
            b, ci = divmod(li, NCH)
            colb = b * L + ci * CH
            if li * NST - 1 in te_bc:
                P.wait("sync", "TE", te_bc[li * NST - 1])
            P.dma("sync", stg_h[0:1, :], xpj_h[RDT:96, colb:colb + CH], "LD")
            P.dma("sync", stg_l[0:1, :], xpj_l[RDT:96, colb:colb + CH], "LD")
            ld_stage[li] = P.val("LD")

        def emit_n(li, n):
            nseq = li * NST + n
            sl = nseq % 2
            b, ci = divmod(li, NCH)
            if nseq - 2 in ve_m0:
                P.wait("tensor", "VE", ve_m0[nseq - 2])
                P.wait("tensor", "SC", sc_evC[nseq - 2])
            P.wait("tensor", "LD", ld_stage[li])
            for pj, base in ((0, n), (1, NST + n)):
                for hl, last in ((stg_h, False), (stg_l, True)):
                    P.emit("tensor", lambda E, pj=pj, base=base, hl=hl,
                           last=last, sl=sl: E.matmul(
                               psf[2 * sl + pj][:, 0:CH], S["ones1b"][0:1, :],
                               hl[0:1, base * CH:(base + 1) * CH],
                               start=not last, stop=last),
                           inc=("TE" if pj == 1 and last else None))
            te_bc[nseq] = P.val("TE")
            if nseq - 2 in ve_tts:
                P.wait("scalar", "VE", ve_tts[nseq - 2])
            if n == 0:
                P.wait("scalar", "GP", gp_dtt[li])
            for d in range(2):
                P.emit("scalar", lambda E, d=d, n=n, sl=sl, li=li:
                       E.activation(
                           abuf[:, sl * 2 * CH + d * CH:
                                sl * 2 * CH + (d + 1) * CH],
                           dtt[li % 2][:, d * CH:(d + 1) * CH], Act.Exp,
                           scale=atile[:, d * NST + n:d * NST + n + 1]))
            P.emit("scalar", lambda E: E.drain(), inc="SC")
            sc_exp[nseq] = P.val("SC")
            if nseq - 2 in gp_n:
                P.wait("scalar", "GP", gp_n[nseq - 2])
            P.wait("scalar", "TE", te_bc[nseq])
            P.emit("scalar", lambda E, sl=sl: E.copy(
                csb[:, sl * CH:(sl + 1) * CH], psf[2 * sl + 1][:, 0:CH]),
                inc="SC")
            sc_evC[nseq] = P.val("SC")
            P.wait("vector", "TE", te_bc[nseq])
            P.wait("vector", "SC", sc_exp[nseq])
            if nseq - 2 in gp_n:
                P.wait("vector", "GP", gp_n[nseq - 2])
            if n == 0:
                P.wait("vector", "GP", gp_dtx[li])
            for d in range(2):
                hcol = (2 * b + d) * NST + n
                P.emit("vector", lambda E, d=d, sl=sl, li=li: E.tensor_tensor(
                    wbuf[:, d * CH:(d + 1) * CH],
                    dtx[li % 2][:, d * CH:(d + 1) * CH],
                    psf[2 * sl][:, 0:CH], Alu.mult))
                P.emit("vector", lambda E, d=d, sl=sl, hc=hcol:
                       E.tensor_tensor_scan(
                           htr[sl][:, d * CH:(d + 1) * CH],
                           abuf[:, sl * 2 * CH + d * CH:
                                sl * 2 * CH + (d + 1) * CH],
                           wbuf[:, d * CH:(d + 1) * CH],
                           hst[:, hc:hc + 1],
                           op0=Alu.mult, op1=Alu.add),
                       inc=("VE" if d == 1 else None))
            ve_tts[nseq] = P.val("VE")
            P.emit("vector", lambda E, sl=sl: E.tensor_tensor(
                mbf[sl][:, 0:CH], htr[sl][:, 0:CH], psf[2 * sl + 1][:, 0:CH],
                Alu.mult), inc="VE")
            ve_m0[nseq] = P.val("VE")
            P.emit("vector", lambda E: E.drain())
            for d in range(2):
                hcol = (2 * b + d) * NST + n
                P.emit("vector", lambda E, d=d, sl=sl, hc=hcol: E.tensor_copy(
                    hst[:, hc:hc + 1],
                    htr[sl][:, (d + 1) * CH - 1:(d + 1) * CH]))
            if DEBUG_STOP == "mixer" and DEBUG_DUMPS and li == 1:
                P.wait("sync", "VE", P.val("VE"))
                P.wait("sync", "GP", P.val("GP"))
                P.dma("sync", T["abdbg"][:, n * 4 * CH:n * 4 * CH + 2 * CH],
                      abuf[:, sl * 2 * CH:(sl + 1) * 2 * CH], "ST")
                P.dma("sync", T["abdbg"][:, n * 4 * CH + 2 * CH:
                                         (n + 1) * 4 * CH],
                      htr[sl][:], "ST")
                P.dma("sync", T["atdbg"], atile[:], "ST")
                for eng in ("vector", "scalar", "tensor", "gpsimd"):
                    P.wait(eng, "ST", P.val("ST"))
            P.wait("gpsimd", "VE", ve_tts[nseq])
            P.wait("gpsimd", "SC", sc_evC[nseq])
            P.emit("gpsimd", lambda E, sl=sl: E.tensor_tensor(
                mbf[sl][:, CH:2 * CH], htr[sl][:, CH:2 * CH],
                csb[:, sl * CH:(sl + 1) * CH], Alu.mult))
            P.wait("gpsimd", "VE", ve_m0[nseq])
            if n == 0:
                if li - 1 in ve_zt:
                    P.wait("gpsimd", "VE", ve_zt[li - 1])
                P.emit("gpsimd", lambda E, sl=sl: E.tensor_copy(
                    yacc[:], mbf[sl][:]), inc="GP")
            else:
                P.emit("gpsimd", lambda E, sl=sl: E.tensor_tensor(
                    yacc[:], yacc[:], mbf[sl][:], Alu.add), inc="GP")
            gp_n[nseq] = P.val("GP")

        def emit_ztile(li):
            P.wait("vector", "GP", gp_n[li * NST + NST - 1])
            if li - 1 in ztr_done:
                P.wait("vector", "TE", ztr_done[li - 1])
            for d in range(2):
                P.emit("vector", lambda E, d=d, li=li: E.scalar_tensor_tensor(
                    ztile[:, d * CH:(d + 1) * CH],
                    xT[li % 2][:, d * CH:(d + 1) * CH],
                    dp1[:, d:d + 1], yacc[:, d * CH:(d + 1) * CH],
                    op0=Alu.mult, op1=Alu.add),
                    inc=("VE" if d == 1 else None))
            ve_zt[li] = P.val("VE")
            if DEBUG_STOP == "mixer" and DEBUG_DUMPS:
                P.wait("sync", "VE", P.val("VE"))
                P.dma("sync", T["ydbg"][:, li * 2 * CH:(li + 1) * 2 * CH],
                      yacc[:], "ST")
                for eng in ("vector", "scalar", "tensor", "gpsimd"):
                    P.wait(eng, "ST", P.val("ST"))

        def emit_ztrans(li):
            b, ci = divmod(li, NCH)
            P.wait("tensor", "VE", ve_zt[li])
            first = True
            for s in range(4):
                for d in range(2):
                    if len(mf_marks) >= 2:
                        P.wait("tensor", "SC", mf_marks[-2])
                    P.emit("tensor", lambda E, s=s, d=d: E.transpose(
                        psf[4 + d][:, 0:128],
                        ztile[:, d * CH + s * 128:d * CH + (s + 1) * 128],
                        identf[:]), inc="TE")
                    P.wait("scalar", "TE", P.val("TE"))
                    if first and li - 1 in zst_done:
                        P.wait("scalar", "ST", zst_done[li - 1])
                        first = False
                    P.emit("scalar", lambda E, s=s, d=d: E.copy(
                        zstage[:, s * DSL + d * 128:s * DSL + (d + 1) * 128],
                        psf[4 + d][:, 0:128]), inc="SC")
                    mf_marks.append(P.val("SC"))
            ztr_done[li] = P.val("TE")
            P.wait("sync", "SC", P.val("SC"))
            rowb = b * L + ci * CH
            for s in range(4):
                P.dma("sync",
                      T["z_in"][rowb + s * 128:rowb + (s + 1) * 128, :],
                      zstage[:, s * DSL:(s + 1) * DSL], "ST")
            zst_done[li] = P.val("ST")

        wgam = S["wgam"]
        P.emit("gpsimd", lambda E: E.memset(qcst[:, 0:1], MAGIC))
        P.emit("gpsimd", lambda E: E.memset(qcst[:, 1:2], -(MAGIC - 1.0)))
        P.emit("gpsimd", lambda E: E.memset(qcst[:, 2:3], 2.0))
        P.emit("gpsimd", lambda E: E.memset(qcst[:, 3:4], 1.0), inc="GP")
        P.wait("scalar", "GP", P.val("GP"))
        qdesc = []
        for wsrc, wdst, nrt, wid in (("wgTd", "wgq_in", 16, 0),
                                     ("wvTd", "wvq_in", 16, 1)):
            for it in range(nrt):
                qdesc.append((wsrc, wdst, it * 128, 0, wid))
        for it in range(8):
            for cb in (0, 1024):
                qdesc.append(("woTd", "woq_in", it * 128, cb, 2))
        q_sc, q_st = {}, {}
        qpos = [0]

        def q_pump(k):
            for _ in range(k):
                i = qpos[0]
                if i >= len(qdesc):
                    return
                qpos[0] += 1
                wsrc, wdst, r0, c0, wid = qdesc[i]
                if i - 2 in q_sc:
                    P.wait("sync", "SC", q_sc[i - 2])
                    P.wait("sync", "ST", q_st[i - 2])
                P.dma("sync", qwi[i % 2][:],
                      T[wsrc][r0:r0 + 128, c0:c0 + 1024], "LD")
                P.wait("scalar", "LD", P.val("LD"))
                P.emit("scalar", lambda E, i=i, wid=wid: E.activation(
                    qtmp[:], qwi[i % 2][:], Act.Copy,
                    scale=wgam[:, wid:wid + 1], bias=MAGIC))
                P.emit("scalar", lambda E: E.activation(
                    qtmp[:], qtmp[:], Act.Relu, bias=qcst[:, 1:2]))
                P.emit("scalar", lambda E: E.activation(
                    qtmp[:], qtmp[:], Act.Relu, scale=-1.0, bias=qcst[:, 2:3]))
                P.emit("scalar", lambda E, i=i: E.activation(
                    qwo[i % 2][:], qtmp[:], Act.Copy, scale=-1.0, bias=1.0))
                P.emit("scalar", lambda E: E.drain(), inc="SC")
                q_sc[i] = P.val("SC")
                P.wait("sync", "SC", q_sc[i])
                P.dma("sync", T[wdst][r0:r0 + 128, c0:c0 + 1024],
                      qwo[i % 2][:], "ST")
                q_st[i] = P.val("ST")

        for s in range(4):
            P.dma("sync", xraw[0][:, s * DSL:(s + 1) * DSL],
                  T["x_dsl"][0, s * 128:(s + 1) * 128, :], "LD")
        ld_ready[0] = P.val("LD")
        emit_stage(0)
        emit_head(0)
        for li in range(NP):
            if li + 1 < NP:
                nb, nci = divmod(li + 1, NCH)
                if li - 1 in xtr_done:
                    P.wait("sync", "TE", xtr_done[li - 1])
                for s in range(4):
                    P.dma("sync",
                          xraw[(li + 1) % 2][:, s * DSL:(s + 1) * DSL],
                          T["x_dsl"][nb,
                                     nci * CH + s * 128:nci * CH + (s + 1) * 128,
                                     :], "LD")
                ld_ready[li + 1] = P.val("LD")
            if li >= 1:
                emit_ztile(li - 1)
            if li >= 1:
                q_pump(4)
            for n in range(NST):
                emit_n(li, n)
                if n == 2 and li >= 1:
                    emit_ztrans(li - 1)
                if n == 7 and li + 1 < NP:
                    emit_head(li + 1)
                if n == 15 and li + 1 < NP:
                    emit_stage(li + 1)
        emit_ztile(NP - 1)
        emit_ztrans(NP - 1)
        q_pump(len(qdesc))
        P.wait("sync", "ST", P.val("ST"))


def _ffn(P, nc, T, S, marks):
    psf, psb = S["psf"], S["psb"]
    identb, ones1, wgam = S["identb"], S["ones1"], S["wgam"]
    hold = {}

    with ExitStack() as cO:
        EO = cO.enter_context
        sxg = EO(nc.sbuf_tensor("sxg", [128, NTT], F32))
        sxv = EO(nc.sbuf_tensor("sxv", [128, NTT], F32))
        sO = EO(nc.sbuf_tensor("sO", [128, NTT], F32))
        gvmax = EO(nc.sbuf_tensor("gvmax", [128, NTT], F32))
        for eng in ("sync", "vector", "scalar", "tensor", "gpsimd"):
            P.wait(eng, "CC", marks["CC_W"])
        P.emit("gpsimd", lambda E: E.memset(gvmax[:], 0.0), inc="GP")
        P.wait("vector", "GP", P.val("GP"))

        def emit_ln(src_fn, dst_fn, tn):
            P.emit("vector", lambda E: E.reduce_sum(tn["red"][:], src_fn(), AXX))
            P.emit("vector", lambda E: E.drain(), inc="VE")
            P.wait("scalar", "VE", P.val("VE"))
            P.emit("scalar", lambda E: E.activation(
                tn["sqs"][:], src_fn(), Act.Square))
            P.emit("scalar", lambda E: E.drain(), inc="SC")
            P.wait("vector", "SC", P.val("SC"))
            P.emit("vector", lambda E: E.reduce_sum(tn["ms"][:], tn["sqs"][:], AXX))
            P.emit("vector", lambda E: E.drain())
            P.emit("vector", lambda E: E.tensor_tensor(
                tn["mu"][:], tn["red"][:], S["dinv"][:], Alu.mult))
            P.emit("vector", lambda E: E.drain())
            P.emit("vector", lambda E: E.tensor_tensor(
                tn["t1"][:], tn["mu"][:], tn["mu"][:], Alu.mult))
            P.emit("vector", lambda E: E.tensor_tensor(
                tn["ms"][:], tn["ms"][:], S["dinv"][:], Alu.mult))
            P.emit("vector", lambda E: E.drain())
            P.emit("vector", lambda E: E.tensor_tensor(
                tn["vpe"][:], tn["ms"][:], tn["t1"][:], Alu.subtract))
            P.emit("vector", lambda E: E.drain())
            P.emit("vector", lambda E: E.tensor_tensor(
                tn["vpe"][:], tn["vpe"][:], S["epsc"][:], Alu.add))
            P.emit("vector", lambda E: E.drain(), inc="VE")
            P.wait("scalar", "VE", P.val("VE"))
            P.emit("scalar", lambda E: E.activation(
                tn["y0"][:], tn["vpe"][:], Act.Ln))
            P.emit("scalar", lambda E: E.drain())
            P.emit("scalar", lambda E: E.activation(
                tn["y0"][:], tn["y0"][:], Act.Exp, scale=-0.5))
            P.emit("scalar", lambda E: E.drain(), inc="SC")
            P.wait("vector", "SC", P.val("SC"))
            P.emit("vector", lambda E: E.tensor_tensor(
                tn["t1"][:], tn["y0"][:], tn["y0"][:], Alu.mult))
            P.emit("vector", lambda E: E.drain())
            P.emit("vector", lambda E: E.tensor_tensor(
                tn["t1"][:], tn["t1"][:], tn["vpe"][:], Alu.mult))
            P.emit("vector", lambda E: E.drain())
            P.emit("vector", lambda E: E.tensor_scalar(
                tn["t1"][:], tn["t1"][:], -0.5, 1.5, op0=Alu.mult, op1=Alu.add))
            P.emit("vector", lambda E: E.drain())
            P.emit("vector", lambda E: E.tensor_tensor(
                tn["rstd"][:], tn["y0"][:], tn["t1"][:], Alu.mult))
            P.emit("vector", lambda E: E.drain())
            P.emit("vector", lambda E: E.tensor_scalar(
                dst_fn(), src_fn(), tn["mu"][:], tn["rstd"][:],
                op0=Alu.subtract, op1=Alu.mult), inc="VE")

        with ExitStack() as cA:
            EA = cA.enter_context
            zraw = EA(nc.sbuf_tensor("zraw", [128, D], F32))
            sqs = EA(nc.sbuf_tensor("sqs", [128, D], F32))
            z1 = EA(nc.sbuf_tensor("z1", [128, D], F32))
            Xq = EA(nc.sbuf_tensor("Xq", [128, D], BF16))
            XqT = EA(nc.sbuf_tensor("XqT", [128, NKD * 1024], BF16))
            tn = {k: EA(nc.sbuf_tensor("ln_" + k, [128, 1], F32))
                  for k in ("red", "mu", "ms", "vpe", "y0", "t1", "rstd",
                            "gmx", "gam", "inv", "sx")}
            tn["sqs"] = sqs
            lnsb = EA(nc.sbuf_tensor("lnsb", [128, 8 * NTT], F32))
            wgsl = [EA(nc.sbuf_tensor(f"wgsl{i}", [128, HID], BF16))
                    for i in range(2)]
            wvsl = [EA(nc.sbuf_tensor(f"wvsl{i}", [128, HID], BF16))
                    for i in range(2)]
            gbm = EA(nc.sbuf_tensor("gbm", [1, HID], F32))
            vbm = EA(nc.sbuf_tensor("vbm", [1, HID], F32))
            gbj = EA(nc.sbuf_tensor("gbj", [128, 512], F32))
            vbj = EA(nc.sbuf_tensor("vbj", [128, 512], F32))
            upre = EA(nc.sbuf_tensor("upre", [128, 512], F32))
            gsil = EA(nc.sbuf_tensor("gsil", [128, 512], F32))
            vtile = EA(nc.sbuf_tensor("vtile", [128, 512], F32))
            gvst = [EA(nc.sbuf_tensor(f"gvst{i}", [128, 512], F32))
                    for i in range(2)]
            mjt = EA(nc.sbuf_tensor("mjt", [128, 1], F32))

            P.dma("sync", gbm[:], T["gbias"], "LD")
            P.dma("sync", vbm[:], T["vbias"], "LD")
            ld_bias = P.val("LD")
            P.wait("tensor", "LD", ld_bias)
            P.wait("sync", "CC", marks["CC_Z"])

            def _mk_zoffs(E):
                hold["zb"] = E.snap(E.partition_id() * TOK,
                                    min_val=0, max_val=(NCORE - 1) * TOK,
                                    guaranteed_mod_val=TOK)
                return E.nop()
            P.emit("sync", _mk_zoffs)
            z3 = T["z_out"].ap().rearrange("(r n) d -> r n d", r=NCORE)
            for r in range(NCORE):
                P.emit("sync", lambda E, r=r: E.dma_start(
                    out=T["z_my"][:, r * DSL:(r + 1) * DSL],
                    in_=z3[r][bass.ds(hold["zb"], TOK), :]),
                    inc="LD", k=16)
            P.wait("sync", "LD", P.val("LD"))

            tsq = 0
            xqtr_done = {}
            sd_z1 = {}
            for tt in range(NTT):
                P.wait("sync", "VE", P.val("VE"))
                P.dma("sync", zraw[:],
                      T["z_my"][tt * 128:(tt + 1) * 128, :], "LD")
                P.wait("vector", "LD", P.val("LD"))
                if tt - 1 in sd_z1:
                    P.wait("vector", "SD", sd_z1[tt - 1])
                emit_ln(lambda: zraw[:], lambda: z1[:], tn)
                if DEBUG_STOP == "ffn":
                    for k_i, k_n in enumerate(("red", "ms", "mu", "vpe", "y0",
                                               "rstd", "t1", "t1")):
                        P.emit("vector", lambda E, tt=tt, k_i=k_i, k_n=k_n:
                               E.tensor_copy(lnsb[:, tt * 8 + k_i:tt * 8 + k_i + 1],
                                             tn[k_n][:]))
                    P.emit("vector", lambda E: E.nop(), inc="VE")
                ve_z1 = P.val("VE")
                P.wait("scalar", "VE", ve_z1)
                P.emit("scalar", lambda E, tt=tt: E.dma_start(
                    out=T["z1st"][tt * 128:(tt + 1) * 128, :], in_=z1[:]),
                    inc="SD", k=16)
                sd_z1[tt] = P.val("SD")
                P.emit("vector", lambda E: E.tensor_reduce(
                    tn["gmx"][:], z1[:], AXX, Alu.max,
                    apply_absolute_value=True))
                P.emit("vector", lambda E: E.drain())
                P.emit("vector", lambda E: E.tensor_tensor(
                    tn["gam"][:], tn["gmx"][:], S["epsc"][:], Alu.max))
                P.emit("vector", lambda E: E.drain())
                P.emit("vector", lambda E: E.reciprocal(
                    tn["inv"][:], tn["gam"][:]))
                P.emit("vector", lambda E: E.drain())
                P.emit("vector", lambda E: E.tensor_tensor(
                    tn["sx"][:], tn["inv"][:], S["c127"][:], Alu.mult))
                P.emit("vector", lambda E, tt=tt: E.scalar_tensor_tensor(
                    sxg[:, tt:tt + 1], tn["gam"][:], 1.0, wgam[:, 4:5],
                    op0=Alu.mult, op1=Alu.mult))
                P.emit("vector", lambda E, tt=tt: E.scalar_tensor_tensor(
                    sxv[:, tt:tt + 1], tn["gam"][:], 1.0, wgam[:, 5:6],
                    op0=Alu.mult, op1=Alu.mult))
                P.emit("vector", lambda E: E.drain())
                P.emit("vector", lambda E: E.tensor_scalar(
                    sqs[:], z1[:], tn["sx"][:], MAGIC, op0=Alu.mult,
                    op1=Alu.add))
                if tt - 1 in xqtr_done:
                    P.wait("vector", "TE", xqtr_done[tt - 1])
                P.emit("vector", lambda E: E.tensor_scalar(
                    Xq[:], sqs[:], MAGIC, None, op0=Alu.subtract), inc="VE")
                P.wait("tensor", "VE", P.val("VE"))
                for k in range(NKD):
                    P.wait("tensor", "SC", P.val("SC") - 1)
                    P.emit("tensor", lambda E, k=k, p=tsq % 2: E.transpose(
                        psb[p][:, 0:128], Xq[:, k * 128:(k + 1) * 128],
                        identb[:]), inc="TE")
                    P.wait("scalar", "TE", P.val("TE"))
                    P.emit("scalar", lambda E, k=k, tt=tt, p=tsq % 2: E.copy(
                        XqT[:, k * 1024 + tt * 128:k * 1024 + (tt + 1) * 128],
                        psb[p][:, 0:128]), inc="SC")
                    tsq += 1
                xqtr_done[tt] = P.val("TE")
            P.wait("tensor", "SC", P.val("SC"))

            P.wait("sync", "CC", marks["CC_W"])
            ld_slab = {}
            mm_done = {}
            epi_done = {}
            sd_gv = {}
            for j in range(16):
                r, half = divmod(j, 2)
                if j - 2 in mm_done:
                    P.wait("sync", "TE", mm_done[j - 2])
                for k in range(NKD):
                    P.dma("sync", wgsl[j % 2][:, k * 512:(k + 1) * 512],
                          T["wgq_out"][r * D + k * 128:r * D + (k + 1) * 128,
                                       half * 512:(half + 1) * 512], "LD")
                    P.dma("sync", wvsl[j % 2][:, k * 512:(k + 1) * 512],
                          T["wvq_out"][r * D + k * 128:r * D + (k + 1) * 128,
                                       half * 512:(half + 1) * 512], "LD")
                ld_slab[j] = P.val("LD")
                if (j - 1) * NTT + NTT - 1 in epi_done:
                    P.wait("tensor", "VE", epi_done[(j - 1) * NTT + NTT - 1])
                P.emit("tensor", lambda E, j=j: E.matmul(
                    psf[4][:, 0:512], ones1[0:1, :],
                    gbm[0:1, j * 512:(j + 1) * 512],
                    start=True, stop=True))
                P.emit("tensor", lambda E, j=j: E.matmul(
                    psf[5][:, 0:512], ones1[0:1, :],
                    vbm[0:1, j * 512:(j + 1) * 512],
                    start=True, stop=True), inc="TE")
                P.wait("vector", "TE", P.val("TE"))
                P.emit("vector", lambda E: E.tensor_copy(gbj[:], psf[4][:, 0:512]))
                P.emit("vector", lambda E: E.tensor_copy(vbj[:], psf[5][:, 0:512]),
                       inc="VE")
                for t in range(NTT):
                    gq = j * NTT + t
                    P.wait("tensor", "LD", ld_slab[j])
                    if gq - 2 in epi_done:
                        P.wait("tensor", "VE", epi_done[gq - 2])
                    for k in range(NKD):
                        P.emit("tensor", lambda E, k=k, t=t, j=j,
                               f=(k == 0), st=(k == NKD - 1): E.matmul(
                                   psf[t % 2][:, 0:512],
                                   XqT[:, k * 1024 + t * 128:k * 1024 + (t + 1) * 128],
                                   wgsl[j % 2][:, k * 512:(k + 1) * 512],
                                   start=f, stop=st),
                               inc=("TE" if k == NKD - 1 else None))
                    for k in range(NKD):
                        P.emit("tensor", lambda E, k=k, t=t, j=j,
                               f=(k == 0), st=(k == NKD - 1): E.matmul(
                                   psf[2 + t % 2][:, 0:512],
                                   XqT[:, k * 1024 + t * 128:k * 1024 + (t + 1) * 128],
                                   wvsl[j % 2][:, k * 512:(k + 1) * 512],
                                   start=f, stop=st),
                               inc=("TE" if k == NKD - 1 else None))
                    te_gv = P.val("TE")
                    P.wait("vector", "TE", te_gv)
                    P.emit("vector", lambda E, t=t: E.scalar_tensor_tensor(
                        upre[:], psf[t % 2][:, 0:512], sxg[:, t:t + 1], gbj[:],
                        op0=Alu.mult, op1=Alu.add), inc="VE")
                    P.wait("scalar", "VE", P.val("VE"))
                    P.emit("scalar", lambda E: E.activation(
                        gsil[:], upre[:], Act.Silu))
                    P.emit("scalar", lambda E: E.drain(), inc="SC")
                    P.emit("vector", lambda E, t=t: E.scalar_tensor_tensor(
                        vtile[:], psf[2 + t % 2][:, 0:512], sxv[:, t:t + 1],
                        vbj[:], op0=Alu.mult, op1=Alu.add))
                    P.wait("vector", "SC", P.val("SC"))
                    if gq - 2 in sd_gv:
                        P.wait("vector", "SD", sd_gv[gq - 2])
                    P.emit("vector", lambda E, sl=gq % 2: E.tensor_tensor(
                        gvst[sl][:], gsil[:], vtile[:], Alu.mult))
                    P.emit("vector", lambda E, sl=gq % 2: E.tensor_reduce(
                        mjt[:], gvst[sl][:], AXX, Alu.max,
                        apply_absolute_value=True))
                    P.emit("vector", lambda E: E.drain())
                    P.emit("vector", lambda E, t=t: E.tensor_tensor(
                        gvmax[:, t:t + 1], gvmax[:, t:t + 1], mjt[:], Alu.max),
                        inc="VE")
                    epi_done[gq] = P.val("VE")
                    P.wait("scalar", "VE", epi_done[gq])
                    P.emit("scalar", lambda E, t=t, j=j, sl=gq % 2: E.dma_start(
                        out=T["gv_dram"][t, j, :, :], in_=gvst[sl][:]),
                        inc="SD", k=16)
                    sd_gv[gq] = P.val("SD")
                mm_done[j] = P.val("TE")
            P.wait("sync", "SD", P.val("SD"))
            if DEBUG_STOP == "ffn":
                P.wait("sync", "VE", P.val("VE"))
                P.dma("sync", T["lndbg"], lnsb[:], "ST")
                P.wait("sync", "ST", P.val("ST"))
            P.barrier()

        with ExitStack() as cB:
            EB = cB.enter_context
            gvrow = EB(nc.sbuf_tensor("gvrow", [128, HID], F32))
            gvq32 = EB(nc.sbuf_tensor("gvq32", [128, HID], F32))
            GVq = EB(nc.sbuf_tensor("GVq", [128, HID], BF16))
            gvqTst = EB(nc.sbuf_tensor("gvqTst", [128, HID], BF16))
            g2 = {k: EB(nc.sbuf_tensor("g2_" + k, [128, 1], F32))
                  for k in ("gam", "inv", "sgv")}
            tsq = 0
            for t in range(NTT):
                P.wait("sync", "VE", P.val("VE"))
                P.wait("sync", "ST", P.val("ST"))
                for j in range(16):
                    P.dma("sync", gvrow[:, j * 512:(j + 1) * 512],
                          T["gv_dram"][t, j, :, :], "LD")
                P.wait("vector", "LD", P.val("LD"))
                P.emit("vector", lambda E, t=t: E.tensor_scalar(
                    g2["gam"][:], gvmax[:, t:t + 1], EPS, None, op0=Alu.max))
                P.emit("vector", lambda E: E.drain())
                P.emit("vector", lambda E: E.reciprocal(
                    g2["inv"][:], g2["gam"][:]))
                P.emit("vector", lambda E: E.drain())
                P.emit("vector", lambda E: E.tensor_scalar(
                    g2["sgv"][:], g2["inv"][:], 127.0, None, op0=Alu.mult))
                P.emit("vector", lambda E, t=t: E.scalar_tensor_tensor(
                    sO[:, t:t + 1], g2["gam"][:], 1.0, wgam[:, 6:7],
                    op0=Alu.mult, op1=Alu.mult))
                P.emit("vector", lambda E: E.drain())
                P.emit("vector", lambda E: E.tensor_scalar(
                    gvq32[:], gvrow[:], g2["sgv"][:], MAGIC, op0=Alu.mult,
                    op1=Alu.add))
                P.emit("vector", lambda E: E.tensor_scalar(
                    GVq[:], gvq32[:], MAGIC, None, op0=Alu.subtract), inc="VE")
                P.wait("tensor", "VE", P.val("VE"))
                for k in range(NKH):
                    P.wait("tensor", "SC", P.val("SC") - 1)
                    P.emit("tensor", lambda E, k=k, p=tsq % 2: E.transpose(
                        psb[p][:, 0:128], GVq[:, k * 128:(k + 1) * 128],
                        identb[:]), inc="TE")
                    P.wait("scalar", "TE", P.val("TE"))
                    P.emit("scalar", lambda E, k=k, p=tsq % 2: E.copy(
                        gvqTst[:, k * 128:(k + 1) * 128], psb[p][:, 0:128]),
                        inc="SC")
                    tsq += 1
                P.wait("sync", "SC", P.val("SC"))
                P.dma("sync", T["gvqT_dram"][t, :, :], gvqTst[:], "ST")
            P.wait("sync", "ST", P.val("ST"))
            P.barrier()

        with ExitStack() as cC:
            EC = cC.enter_context
            wosl = EC(nc.sbuf_tensor("wosl", [128, NKH * 512], BF16))
            gvqrd = [EC(nc.sbuf_tensor(f"gvqrd{i}", [128, HID], BF16))
                     for i in range(2)]
            obm = EC(nc.sbuf_tensor("obm", [1, D], F32))
            obj = EC(nc.sbuf_tensor("obj", [128, 512], F32))
            Oacc = EC(nc.sbuf_tensor("Oacc", [128, NTT * D], F32))
            z1rd = EC(nc.sbuf_tensor("z1rd", [128, D], F32))
            z2b = EC(nc.sbuf_tensor("z2b", [128, D], F32))
            ofin = EC(nc.sbuf_tensor("ofin", [128, D], F32))
            sq2 = EC(nc.sbuf_tensor("sq2", [128, D], F32))
            tn2 = {k: EC(nc.sbuf_tensor("l2_" + k, [128, 1], F32))
                   for k in ("red", "mu", "ms", "vpe", "y0", "t1", "rstd")}
            tn2["sqs"] = sq2

            P.dma("sync", obm[:], T["obias"], "LD")
            P.wait("tensor", "LD", P.val("LD"))
            oq_mm = {}
            slab_mm = {}
            epi_o = {}
            for j in range(4):
                if j - 1 in slab_mm:
                    P.wait("sync", "TE", slab_mm[j - 1])
                for k in range(NKH):
                    P.dma("sync", wosl[:, k * 512:(k + 1) * 512],
                          T["woq_out"][k * 128:(k + 1) * 128,
                                       j * 512:(j + 1) * 512], "LD")
                ld_wo = P.val("LD")
                if (j - 1) * NTT + NTT - 1 in epi_o:
                    P.wait("tensor", "VE", epi_o[(j - 1) * NTT + NTT - 1])
                P.emit("tensor", lambda E, j=j: E.matmul(
                    psf[4][:, 0:512], ones1[0:1, :],
                    obm[0:1, j * 512:(j + 1) * 512],
                    start=True, stop=True), inc="TE")
                P.wait("vector", "TE", P.val("TE"))
                P.emit("vector", lambda E: E.tensor_copy(obj[:], psf[4][:, 0:512]),
                       inc="VE")
                for t in range(NTT):
                    oq = j * NTT + t
                    if oq - 2 in oq_mm:
                        P.wait("sync", "TE", oq_mm[oq - 2])
                    P.dma("sync", gvqrd[t % 2][:], T["gvqT_dram"][t, :, :], "LD")
                    P.wait("tensor", "LD", P.val("LD"))
                    if oq - 3 in epi_o:
                        P.wait("tensor", "VE", epi_o[oq - 3])
                    for k in range(NKH):
                        P.emit("tensor", lambda E, k=k, t=t, b=oq % 3,
                               f=(k == 0), st=(k == NKH - 1): E.matmul(
                                   psf[b][:, 0:512],
                                   gvqrd[t % 2][:, k * 128:(k + 1) * 128],
                                   wosl[:, k * 512:(k + 1) * 512],
                                   start=f, stop=st),
                               inc=("TE" if k == NKH - 1 else None))
                    oq_mm[oq] = P.val("TE")
                    P.wait("vector", "TE", oq_mm[oq])
                    P.emit("vector", lambda E, t=t, j=j, b=oq % 3:
                           E.scalar_tensor_tensor(
                               Oacc[:, t * D + j * 512:t * D + (j + 1) * 512],
                               psf[b][:, 0:512], sO[:, t:t + 1], obj[:],
                               op0=Alu.mult, op1=Alu.add), inc="VE")
                    epi_o[oq] = P.val("VE")
                slab_mm[j] = P.val("TE")

            st_out = {}
            for t in range(NTT):
                P.wait("sync", "VE", P.val("VE"))
                P.dma("sync", z1rd[:], T["z1st"][t * 128:(t + 1) * 128, :], "LD")
                P.wait("vector", "LD", P.val("LD"))
                if t - 1 in st_out:
                    P.wait("vector", "ST", st_out[t - 1])
                P.emit("vector", lambda E, t=t: E.tensor_tensor(
                    z2b[:], z1rd[:], Oacc[:, t * D:(t + 1) * D], Alu.add))
                emit_ln(lambda: z2b[:], lambda: ofin[:], tn2)
                P.wait("sync", "VE", P.val("VE"))
                P.dma("sync", T["out"][t * 128:(t + 1) * 128, :], ofin[:], "ST")
                st_out[t] = P.val("ST")
            P.wait("sync", "ST", P.val("ST"))
            P.wait("sync", "SD", P.val("SD"))
            if DEBUG_STOP == "ffn":
                P.dma("gpsimd", T["z1dbg"], T["z1st"].ap(), "GD")
                P.dma("gpsimd", T["gvdbg"],
                      T["gv_dram"].ap().rearrange("a b c d -> (a b c) d"), "GD")
                P.dma("gpsimd", T["gvqdbg"],
                      T["gvqT_dram"].ap().rearrange("a b c -> (a b) c"), "GD")
                P.dma("gpsimd", T["zmydbg"], T["z_my"].ap(), "GD")
                P.wait("gpsimd", "GD", P.val("GD"))


_NC_CACHE = {}


def _get_nc():
    key = DEBUG_STOP
    if key not in _NC_CACHE:
        _NC_CACHE[key] = build_nc()
    return _NC_CACHE[key]


def kernel(L0_attn_x, L0_attn_W_x, L0_attn_W_dt, L0_attn_b_dt, L0_attn_A_log,
           L0_attn_D, L0_ffn_gate_w, L0_ffn_gate_b, L0_ffn_val_w, L0_ffn_val_b,
           L0_ffn_out_w, L0_ffn_out_b):
    f32 = np.float32
    x = np.asarray(L0_attn_x, f32)
    W_x = np.ascontiguousarray(np.asarray(L0_attn_W_x, f32))
    W_dt = np.asarray(L0_attn_W_dt, f32)
    b_dt = np.asarray(L0_attn_b_dt, f32)
    A_log = np.asarray(L0_attn_A_log, f32)
    Dv = np.asarray(L0_attn_D, f32)
    WgT = np.ascontiguousarray(np.asarray(L0_ffn_gate_w, f32).T)
    WvT = np.ascontiguousarray(np.asarray(L0_ffn_val_w, f32).T)
    WoT = np.ascontiguousarray(np.asarray(L0_ffn_out_w, f32).T)
    gb = np.ascontiguousarray(np.asarray(L0_ffn_gate_b, f32).reshape(1, HID))
    vb = np.ascontiguousarray(np.asarray(L0_ffn_val_b, f32).reshape(1, HID))
    ob = np.ascontiguousarray(np.asarray(L0_ffn_out_b, f32).reshape(1, D))

    xf = np.ascontiguousarray(x.reshape(B * L, D))
    in_maps = []
    for c in range(NCORE):
        dsl = slice(DSL * c, DSL * (c + 1))
        hsl = slice(1024 * c, 1024 * (c + 1))
        in_maps.append(dict(
            x_tok=np.ascontiguousarray(xf[TOK * c:TOK * (c + 1)]),
            x_dsl=np.ascontiguousarray(x[:, :, dsl]),
            wx=W_x,
            wdtT=np.ascontiguousarray(W_dt[dsl].T),
            bdt=np.ascontiguousarray(b_dt[dsl].reshape(2, 128).T),
            alog=np.ascontiguousarray(
                A_log[dsl].reshape(2, 128, NST).transpose(1, 0, 2)
                .reshape(128, 2 * NST)),
            dvec=np.ascontiguousarray(Dv[dsl].reshape(2, 128).T),
            wgTd=np.ascontiguousarray(WgT[:, hsl]),
            wvTd=np.ascontiguousarray(WvT[:, hsl]),
            woTd=np.ascontiguousarray(WoT[hsl]),
            gbias=gb, vbias=vb, obias=ob,
        ))
    nc = _get_nc()
    res = run_bass_kernel_spmd(nc, in_maps, core_ids=list(range(NCORE)))
    if DEBUG_STOP is not None:
        return res
    o = np.concatenate([r["out"] for r in res.results], axis=0)
    return o.reshape(B, L, D)


if __name__ == "__main__":
    nc = build_nc()
    n = sum(1 for _ in nc.m.functions[0].blocks for _ in [0])
    print("built ok")


# version 12
# speedup vs baseline: 1.1462x; 1.1462x over previous
import numpy as np
import ml_dtypes
from contextlib import ExitStack

import concourse.bass as bass
import concourse.mybir as mybir
from concourse.bass_utils import run_bass_kernel_spmd

F32 = mybir.dt.float32
BF16 = mybir.dt.bfloat16
Alu = mybir.AluOpType
Act = mybir.ActivationFunctionType
AXX = mybir.AxisListType.X

MAGIC = float(np.float32(1.5 * 2 ** 23))
NCORE = 8
B, L, D, NST, RDT, HID = 2, 4096, 2048, 16, 64, 8192
TOK = (B * L) // NCORE
DSL = D // NCORE
CH = 512
NCH = L // CH
NKD = D // 128
NTT = TOK // 128
NKH = HID // 128
EPS = 1e-5

DEBUG_STOP = None
DEBUG_DUMPS = False
DEBUG_BC1 = False
DEBUG_A1 = False
DEBUG_NOBC = False


class _Prog:

    ENGINES = ("sync", "vector", "scalar", "tensor", "gpsimd")

    def __init__(self):
        self.ops = []
        self.cnt = {}

    def sem(self, name):
        self.cnt.setdefault(name, 0)

    def emit(self, eng, fn, inc=None, k=1):
        if inc is not None:
            self.sem(inc)
            self.cnt[inc] += k
        self.ops.append(("op", eng, fn, inc, k))

    def dma(self, eng, out, in_, inc):
        self.emit(eng, lambda E, o=out, i=in_: E.dma_start(out=o, in_=i),
                  inc=inc, k=16)

    def wait(self, eng, semname, val=None):
        v = self.cnt.get(semname, 0) if val is None else val
        if v > 0:
            self.ops.append(("wait", eng, semname, v, None))

    def val(self, semname):
        return self.cnt.get(semname, 0)

    def barrier(self):
        self.sem("PH")
        for e in self.ENGINES:
            if e in ("vector", "scalar", "tensor"):
                self.emit(e, lambda E: E.drain(), inc="PH")
            else:
                self.emit(e, lambda E: E.nop(), inc="PH")
        tgt = self.cnt["PH"]
        for e in self.ENGINES:
            self.wait(e, "PH", tgt)

    def lower(self, nc, block, ctx):
        handles = {name: ctx.enter_context(nc.semaphore(name))
                   for name in self.cnt}
        for eng_name in self.ENGINES:
            my = [op for op in self.ops if op[1] == eng_name]
            if not my:
                continue

            def section(E, my=my, handles=handles):
                for kind, _eng, a, b, c in my:
                    if kind == "wait":
                        E.wait_ge(handles[a], b)
                    else:
                        instr = a(E)
                        if b is not None:
                            instr.then_inc(handles[b], c)
            getattr(block, eng_name)(section)


def build_nc():
    nc = bass.Bass("TRN2", target_bir_lowering=False)

    T = {}

    def din(name, shape, dt=F32):
        T[name] = nc.dram_tensor(name, shape, dt, kind="ExternalInput").ap()

    def dout(name, shape, dt=F32):
        T[name] = nc.dram_tensor(name, shape, dt, kind="ExternalOutput").ap()

    def dint(name, shape, dt=F32, shared=False):
        T[name] = nc.dram_tensor(name, shape, dt,
                                 addr_space="Shared" if shared else "Local")

    din("x_tok", [TOK, D]); din("x_dsl", [B, L, DSL]); din("wx", [96, D])
    din("wdtT", [RDT, DSL]); din("bdt", [128, 2]); din("alog", [128, 2 * NST])
    din("dvec", [128, 2]); din("wgTd", [D, 1024]); din("wvTd", [D, 1024])
    din("woTd", [1024, D]); din("gbias", [1, HID]); din("vbias", [1, HID])
    din("obias", [1, D])
    dout("out", [TOK, D])
    if DEBUG_STOP == "mixer":
        dout("zdbg", [B * L, DSL])
        dout("dtdbg", [128, 2 * CH * B * NCH])
        dout("xdbg", [128, 2 * CH * B * NCH])
        dout("xtdbg", [128, 2 * CH * B * NCH])
        dout("abdbg", [128, 4 * CH * NST])
        dout("atdbg", [128, 2 * NST])
        dout("xrdbg", [128, 4 * DSL * B * NCH])
        dout("ydbg", [128, 2 * CH * B * NCH])
    if DEBUG_STOP == "ffn":
        dout("z1dbg", [TOK, D])
        dout("gvdbg", [NTT * 16 * 128, 512])
        dout("gvqdbg", [NTT * 128, HID], BF16)
        dout("zmydbg", [TOK, D])
        dout("lndbg", [128, 8 * NTT])
    if DEBUG_STOP == "ph0":
        dout("agdbg", [NCORE * 96, TOK])
        dout("wqdbg", [NCORE * D, 1024], BF16)
        dout("gmdbg", [NCORE, 4])
        dout("wgdbg", [128, 8])
    dint("ag1_in", [96, TOK])
    dint("ag1_out", [NCORE * 96, TOK], shared=True)
    dint("gm_in", [1, 4]); dint("gm_out", [NCORE, 4], shared=True)
    dint("wgq_in", [D, 1024], BF16)
    dint("wgq_out", [NCORE * D, 1024], BF16, shared=True)
    dint("wvq_in", [D, 1024], BF16)
    dint("wvq_out", [NCORE * D, 1024], BF16, shared=True)
    dint("woq_in", [1024, D], BF16)
    dint("woq_out", [NCORE * 1024, D], BF16, shared=True)
    dint("z_in", [B * L, DSL])
    dint("z_out", [NCORE * B * L, DSL], shared=True)
    dint("z_my", [TOK, D])
    dint("z1st", [TOK, D])
    dint("gv_dram", [NTT, 16, 128, 512])
    dint("gvqT_dram", [NTT, 128, HID], BF16)

    P = _Prog()
    for s in ("LD", "ST", "SD", "GD", "CC", "VE", "SC", "TE", "GP", "PH",
              "MF", "BF"):
        P.sem(s)
    marks = {}

    ctx = ExitStack()
    with ctx:
        E = ctx.enter_context
        S = {}
        S["identb"] = E(nc.sbuf_tensor("identb", [128, 128], BF16))
        S["identf"] = E(nc.sbuf_tensor("identf", [128, 128], F32))
        S["ones1"] = E(nc.sbuf_tensor("ones1", [1, 128], F32))
        S["onescol"] = E(nc.sbuf_tensor("onescol", [128, 1], F32))
        S["ones1b"] = E(nc.sbuf_tensor("ones1b", [1, 128], BF16))
        S["wgam"] = E(nc.sbuf_tensor("wgam", [128, 8], F32))
        S["dinv"] = E(nc.sbuf_tensor("dinv", [128, 1], F32))
        S["epsc"] = E(nc.sbuf_tensor("epsc", [128, 1], F32))
        S["c127"] = E(nc.sbuf_tensor("c127", [128, 1], F32))
        S["psf"] = [E(nc.psum_tensor(f"psf{i}", [128, 512], F32))
                    for i in range(6)]
        S["psb"] = [E(nc.psum_tensor(f"psb{i}", [128, 1024], BF16))
                    for i in range(2)]
        block = E(nc.Block())

        with ExitStack() as cscan:
            ES = cscan.enter_context
            S["xpj_h"] = ES(nc.sbuf_tensor("xpj_h", [96, B * L], BF16))
            S["xpj_l"] = ES(nc.sbuf_tensor("xpj_l", [96, B * L], BF16))

            _phase0(P, nc, T, S, marks)

            if DEBUG_STOP == "ph0":
                P.wait("gpsimd", "CC", marks["CC_GM"])
                P.dma("gpsimd", T["agdbg"], T["ag1_out"].ap(), "GD")
                P.dma("gpsimd", T["wqdbg"], T["wgq_out"].ap(), "GD")
                P.dma("gpsimd", T["gmdbg"], T["gm_out"].ap(), "GD")
                P.dma("gpsimd", T["wgdbg"], S["wgam"][:], "GD")
                P.wait("gpsimd", "GD", P.val("GD"))
                P.lower(nc, block, ctx)
                return nc

            _scan(P, nc, T, S, marks)

            if DEBUG_STOP == "mixer":
                P.wait("gpsimd", "ST", P.val("ST"))
                P.dma("gpsimd", T["zdbg"], T["z_in"].ap(), "GD")
                P.wait("gpsimd", "GD", P.val("GD"))
                P.lower(nc, block, ctx)
                return nc

            P.wait("gpsimd", "ST", P.val("ST"))
            P.wait("gpsimd", "SD", P.val("SD"))
            P.emit("gpsimd", lambda E: E.collective_compute(
                "AllGather", Alu.bypass, replica_groups=[list(range(NCORE))],
                ins=[T["z_in"].ap().opt()], outs=[T["z_out"].ap().opt()]),
                inc="CC")
            marks["CC_Z"] = P.val("CC")
            for a, b in (("wgq_in", "wgq_out"), ("wvq_in", "wvq_out"),
                         ("woq_in", "woq_out")):
                P.emit("gpsimd", lambda E, x=T[a], y=T[b]:
                       E.collective_compute(
                           "AllGather", Alu.bypass,
                           replica_groups=[list(range(NCORE))],
                           ins=[x.ap().opt()], outs=[y.ap().opt()]),
                       inc="CC")
            marks["CC_W"] = P.val("CC")
            P.barrier()
        _ffn(P, nc, T, S, marks)
        P.lower(nc, block, ctx)
    return nc


def _ag(P, T, name_in, name_out, mark, marks):
    P.wait("gpsimd", "ST", P.val("ST"))
    P.emit("gpsimd", lambda E, a=T[name_in], b=T[name_out]: E.collective_compute(
        "AllGather", Alu.bypass, replica_groups=[list(range(NCORE))],
        ins=[a.ap().opt()], outs=[b.ap().opt()]), inc="CC")
    marks[mark] = P.val("CC")


def _phase0(P, nc, T, S, marks):
    psf, psb = S["psf"], S["psb"]
    identb, identf = S["identb"], S["identf"]
    ones1, onescol, wgam = S["ones1"], S["onescol"], S["wgam"]

    with ExitStack() as c0:
        E0 = c0.enter_context
        xtk = E0(nc.sbuf_tensor("xtk", [128, D], F32))
        xth = E0(nc.sbuf_tensor("xth", [128, D], BF16))
        xtl = E0(nc.sbuf_tensor("xtl", [128, D], BF16))
        xtmp = E0(nc.sbuf_tensor("xtmp", [128, D], F32))
        xTh = E0(nc.sbuf_tensor("xTh", [128, NKD * 128], BF16))
        xTl = E0(nc.sbuf_tensor("xTl", [128, NKD * 128], BF16))
        wxs = E0(nc.sbuf_tensor("wxs", [96, D], F32))
        wxh = E0(nc.sbuf_tensor("wxh", [96, D], BF16))
        wxl = E0(nc.sbuf_tensor("wxl", [96, D], BF16))
        wxTh = E0(nc.sbuf_tensor("wxTh", [128, NKD * 96], BF16))
        wxTl = E0(nc.sbuf_tensor("wxTl", [128, NKD * 96], BF16))
        xpmine = E0(nc.sbuf_tensor("xpmine", [96, TOK], F32))
        wti = [E0(nc.sbuf_tensor(f"wti{i}", [128, 2048], F32)) for i in range(2)]
        wqo = [E0(nc.sbuf_tensor(f"wqo{i}", [128, 2048], BF16)) for i in range(2)]
        wtmp = E0(nc.sbuf_tensor("wtmp", [128, 2048], F32))
        wacc = E0(nc.sbuf_tensor("wacc", [128, 1], F32))
        wred = E0(nc.sbuf_tensor("wred", [128, 1], F32))
        gmsb = E0(nc.sbuf_tensor("gmsb", [1, 8], F32))
        gmrd = E0(nc.sbuf_tensor("gmrd", [1, 32], F32))

        P.emit("gpsimd", lambda E: E.memset(identb[:], 0.0))
        P.emit("gpsimd", lambda E: E.affine_select(
            out=identb[:], in_=identb[:], compare_op=Alu.not_equal,
            fill=1.0, base=0, pattern=[[-1, 128]], channel_multiplier=1))
        P.emit("gpsimd", lambda E: E.memset(identf[:], 0.0))
        P.emit("gpsimd", lambda E: E.affine_select(
            out=identf[:], in_=identf[:], compare_op=Alu.not_equal,
            fill=1.0, base=0, pattern=[[-1, 128]], channel_multiplier=1))
        P.emit("gpsimd", lambda E: E.memset(ones1[:], 1.0))
        P.emit("gpsimd", lambda E: E.memset(onescol[:], 1.0))
        P.emit("gpsimd", lambda E: E.memset(S["dinv"][:], 1.0 / D))
        P.emit("gpsimd", lambda E: E.memset(S["epsc"][:], EPS))
        P.emit("gpsimd", lambda E: E.memset(S["c127"][:], 127.0))
        P.emit("gpsimd", lambda E: E.memset(S["ones1b"][:], 1.0), inc="GP")
        gp_const = P.val("GP")

        P.dma("sync", wxs[:], T["wx"], "LD")
        P.wait("vector", "LD", P.val("LD"))
        P.emit("vector", lambda E: E.tensor_copy(wxh[:], wxs[:]))
        P.emit("vector", lambda E: E.tensor_tensor(
            xtmp[0:96, :], wxs[:], wxh[:], Alu.subtract))
        P.emit("vector", lambda E: E.tensor_copy(wxl[:], xtmp[0:96, :]), inc="VE")
        P.wait("tensor", "VE", P.val("VE"))
        P.wait("tensor", "GP", gp_const)
        tsq = 0
        for k in range(NKD):
            for src, dst in ((wxh, wxTh), (wxl, wxTl)):
                if tsq >= 2:
                    P.wait("tensor", "SC", P.val("SC") - 1)
                P.emit("tensor", lambda E, s=src, k=k, p=tsq % 2: E.transpose(
                    psb[p][:, 0:96], s[:, k * 128:(k + 1) * 128],
                    identb[0:96, 0:96]), inc="TE")
                P.wait("scalar", "TE", P.val("TE"))
                P.emit("scalar", lambda E, d=dst, k=k, p=tsq % 2: E.copy(
                    d[:, k * 96:(k + 1) * 96], psb[p][:, 0:96]), inc="SC")
                tsq += 1

        for tt in range(NTT):
            P.wait("sync", "VE", P.val("VE"))
            P.dma("sync", xtk[:], T["x_tok"][tt * 128:(tt + 1) * 128, :], "LD")
            P.wait("vector", "LD", P.val("LD"))
            P.emit("vector", lambda E: E.tensor_copy(xth[:], xtk[:]))
            P.emit("vector", lambda E: E.tensor_tensor(
                xtmp[:], xtk[:], xth[:], Alu.subtract))
            P.emit("vector", lambda E: E.tensor_copy(xtl[:], xtmp[:]), inc="VE")
            P.wait("tensor", "VE", P.val("VE"))
            for k in range(NKD):
                for src, dst in ((xth, xTh), (xtl, xTl)):
                    P.wait("tensor", "SC", P.val("SC") - 1)
                    P.emit("tensor", lambda E, s=src, k=k, p=tsq % 2: E.transpose(
                        psb[p][:, 0:128], s[:, k * 128:(k + 1) * 128], identb[:]),
                        inc="TE")
                    P.wait("scalar", "TE", P.val("TE"))
                    P.emit("scalar", lambda E, d=dst, k=k, p=tsq % 2: E.copy(
                        d[:, k * 128:(k + 1) * 128], psb[p][:, 0:128]), inc="SC")
                    tsq += 1
            P.wait("tensor", "SC", P.val("SC"))
            n_mm = NKD * 4
            i_mm = 0
            for k in range(NKD):
                for lh in (wxTh, wxTl):
                    for rh in (xTh, xTl):
                        P.emit("tensor",
                               lambda E, k=k, lh=lh, rh=rh, f=(i_mm == 0),
                               st=(i_mm == n_mm - 1): E.matmul(
                                   psf[4][0:96, 0:128],
                                   lh[:, k * 96:(k + 1) * 96],
                                   rh[:, k * 128:(k + 1) * 128],
                                   start=f, stop=st),
                               inc=("TE" if i_mm == n_mm - 1 else None))
                        i_mm += 1
            P.wait("vector", "TE", P.val("TE"))
            P.emit("vector", lambda E, tt=tt: E.tensor_copy(
                xpmine[:, tt * 128:(tt + 1) * 128], psf[4][0:96, 0:128]),
                inc="VE")
            P.wait("tensor", "VE", P.val("VE"))
        P.wait("sync", "VE", P.val("VE"))
        P.dma("sync", T["ag1_in"].ap(), xpmine[:], "ST")
        _ag(P, T, "ag1_in", "ag1_out", "CC_AG1", marks)

        wspec = [
            ("wgTd", "wgq_in", D // 128, 1024, 0),
            ("wvTd", "wvq_in", D // 128, 1024, 1),
            ("woTd", "woq_in", 1024 // 128, 2048, 2),
        ]
        for wsrc, wdst, ntile, width, widx in wspec:
            for it in range(ntile):
                buf = wti[it % 2]
                P.wait("sync", "VE", P.val("VE"))
                P.dma("sync", buf[:, 0:width],
                      T[wsrc][it * 128:(it + 1) * 128, :], "LD")
                P.wait("vector", "LD", P.val("LD"))
                P.emit("vector", lambda E, b=buf, w=width: E.tensor_reduce(
                    wred[:], b[:, 0:w], AXX, Alu.add, apply_absolute_value=True))
                P.emit("vector", lambda E: E.drain())
                if it == 0:
                    P.emit("vector", lambda E: E.tensor_copy(wacc[:], wred[:]),
                           inc="VE")
                else:
                    P.emit("vector", lambda E: E.tensor_tensor(
                        wacc[:], wacc[:], wred[:], Alu.add), inc="VE")
            P.wait("tensor", "VE", P.val("VE"))
            P.emit("tensor", lambda E: E.matmul(
                psf[5][0:1, 0:1], wacc[:], onescol[:], start=True, stop=True),
                inc="TE")
            P.wait("vector", "TE", P.val("TE"))
            P.emit("vector", lambda E, wi=widx: E.tensor_scalar(
                gmsb[0:1, wi:wi + 1], psf[5][0:1, 0:1], 1.0 / (HID * D), None,
                op0=Alu.mult), inc="VE")
            P.wait("tensor", "VE", P.val("VE"))
        P.wait("sync", "VE", P.val("VE"))
        P.dma("sync", T["gm_in"].ap(), gmsb[0:1, 0:4], "ST")
        _ag(P, T, "gm_in", "gm_out", "CC_GM", marks)
        P.wait("sync", "CC", marks["CC_GM"])
        P.dma("sync", gmrd[0:1, 0:32], T["gm_out"].ap(), "LD")
        P.wait("vector", "LD", P.val("LD"))
        gm3 = gmrd[0:1, 0:32].rearrange("p (r f) -> p r f", f=4)
        for widx in range(3):
            P.emit("vector", lambda E, wi=widx: E.tensor_reduce(
                gmsb[0:1, wi:wi + 1], gm3[:, :, wi], AXX, Alu.add))
        P.emit("vector", lambda E: E.drain(), inc="VE")
        P.wait("scalar", "VE", P.val("VE"))
        P.emit("scalar", lambda E: E.copy(gmsb[0:1, 4:7], gmsb[0:1, 0:3]),
               inc="SC")
        P.wait("vector", "SC", P.val("SC"))
        P.emit("vector", lambda E: E.tensor_scalar(
            gmsb[0:1, 0:3], gmsb[0:1, 4:7], EPS, None, op0=Alu.max))
        P.emit("vector", lambda E: E.drain())
        P.emit("vector", lambda E: E.tensor_scalar(
            gmsb[0:1, 4:7], gmsb[0:1, 0:3], 1.0 / 127.0, None, op0=Alu.mult))
        P.emit("vector", lambda E: E.drain())
        P.emit("vector", lambda E: E.reciprocal(gmsb[0:1, 0:3], gmsb[0:1, 0:3]))
        P.emit("vector", lambda E: E.drain(), inc="VE")
        P.wait("tensor", "VE", P.val("VE"))
        P.emit("tensor", lambda E: E.matmul(
            psf[5][:, 0:8], ones1[0:1, :], gmsb[0:1, 0:8], start=True,
            stop=True), inc="TE")
        P.wait("vector", "TE", P.val("TE"))
        P.emit("vector", lambda E: E.tensor_copy(wgam[:], psf[5][:, 0:8]),
               inc="VE")

        P.barrier()


def _scan(P, nc, T, S, marks):
    psf = S["psf"]
    identf = S["identf"]
    xpj_h, xpj_l = S["xpj_h"], S["xpj_l"]
    NP = B * NCH

    with ExitStack() as c1:
        E1 = c1.enter_context
        wdts = E1(nc.sbuf_tensor("wdts", [RDT, DSL], F32))
        wdth = E1(nc.sbuf_tensor("wdth", [RDT, DSL], BF16))
        wdtl = E1(nc.sbuf_tensor("wdtl", [RDT, DSL], BF16))
        wdtmp = E1(nc.sbuf_tensor("wdtmp", [RDT, DSL], F32))
        bdts = E1(nc.sbuf_tensor("bdts", [128, 2], F32))
        alogs = E1(nc.sbuf_tensor("alogs", [128, 2 * NST], F32))
        atile = E1(nc.sbuf_tensor("atile", [128, 2 * NST], F32))
        dp1 = E1(nc.sbuf_tensor("dp1", [128, 2], F32))
        xraw = [E1(nc.sbuf_tensor(f"xraw{i}", [128, 4 * DSL], F32))
                for i in range(2)]
        xT = [E1(nc.sbuf_tensor(f"xT{i}", [128, 2 * CH], F32))
              for i in range(2)]
        dtt = [E1(nc.sbuf_tensor(f"dtt{i}", [128, 2 * CH], F32))
               for i in range(2)]
        dtx = [E1(nc.sbuf_tensor(f"dtx{i}", [128, 2 * CH], F32))
               for i in range(2)]
        spm = E1(nc.sbuf_tensor("spm", [128, 2 * CH], F32))
        spa = E1(nc.sbuf_tensor("spa", [128, 2 * CH], F32))
        abuf = E1(nc.sbuf_tensor("abuf", [128, 4 * CH], F32))
        wbuf = E1(nc.sbuf_tensor("wbuf", [128, 2 * CH], F32))
        htr = [E1(nc.sbuf_tensor(f"htr{i}", [128, 2 * CH], F32))
               for i in range(2)]
        mbf = [E1(nc.sbuf_tensor(f"mbf{i}", [128, 2 * CH], F32))
               for i in range(2)]
        csb = E1(nc.sbuf_tensor("csb", [128, 2 * CH], F32))
        yacc = E1(nc.sbuf_tensor("yacc", [128, 2 * CH], F32))
        ztile = E1(nc.sbuf_tensor("ztile", [128, 2 * CH], F32))
        zstage = E1(nc.sbuf_tensor("zstage", [128, 4 * DSL], F32))
        hst = E1(nc.sbuf_tensor("hst", [128, 4 * NST], F32))

        for eng in ("sync", "vector", "scalar", "tensor", "gpsimd"):
            P.wait(eng, "CC", marks["CC_GM"])
        P.dma("sync", wdts[:], T["wdtT"], "LD")
        P.dma("sync", bdts[:], T["bdt"], "LD")
        P.dma("sync", alogs[:], T["alog"], "LD")
        P.dma("sync", dp1[:], T["dvec"], "LD")
        P.wait("vector", "LD", P.val("LD"))
        P.emit("vector", lambda E: E.tensor_copy(wdth[:], wdts[:]))
        P.emit("vector", lambda E: E.tensor_tensor(
            wdtmp[:], wdts[:], wdth[:], Alu.subtract))
        P.emit("vector", lambda E: E.tensor_copy(wdtl[:], wdtmp[:]))
        P.emit("vector", lambda E: E.tensor_scalar(
            dp1[:], dp1[:], 1.0, None, op0=Alu.add), inc="VE")
        P.wait("scalar", "LD", P.val("LD"))
        P.emit("scalar", lambda E: E.activation(atile[:], alogs[:], Act.Exp),
               inc="SC")
        P.wait("vector", "SC", P.val("SC"))
        P.emit("vector", lambda E: E.tensor_scalar(
            atile[:], atile[:], -1.0, None, op0=Alu.mult), inc="VE")
        P.emit("vector", lambda E: E.memset(hst[:], 0.0), inc="GP")
        gp_hst = P.val("GP")

        P.wait("sync", "CC", marks["CC_AG1"])
        with ExitStack() as cx:
            xprojT = cx.enter_context(
                nc.sbuf_tensor("xprojT", [96, B * L], F32))
            losc = cx.enter_context(nc.sbuf_tensor("losc", [96, 1024], F32))
            for r in range(NCORE):
                P.dma("sync", xprojT[:, r * TOK:(r + 1) * TOK],
                      T["ag1_out"][r * 96:(r + 1) * 96, :], "LD")
            P.wait("vector", "LD", P.val("LD"))
            P.emit("vector", lambda E: E.tensor_copy(xpj_h[:], xprojT[:, :]))
            for cs in range(0, B * L, 1024):
                P.emit("vector", lambda E, cs=cs: E.tensor_tensor(
                    losc[:, :], xprojT[:, cs:cs + 1024],
                    xpj_h[:, cs:cs + 1024], Alu.subtract))
                P.emit("vector", lambda E, cs=cs: E.tensor_copy(
                    xpj_l[:, cs:cs + 1024], losc[:, :]),
                    inc=("VE" if cs == B * L - 1024 else None))
        stg_h = E1(nc.sbuf_tensor("stg_h", [1, 32 * CH], BF16))
        stg_l = E1(nc.sbuf_tensor("stg_l", [1, 32 * CH], BF16))
        qwi = [E1(nc.sbuf_tensor(f"qwi{i}", [128, 1024], F32))
               for i in range(2)]
        qwo = [E1(nc.sbuf_tensor(f"qwo{i}", [128, 1024], BF16))
               for i in range(2)]
        qtmp = E1(nc.sbuf_tensor("qtmp", [128, 1024], F32))
        qcst = E1(nc.sbuf_tensor("qcst", [128, 4], F32))
        ve_prep = P.val("VE")
        P.wait("tensor", "VE", ve_prep)
        P.wait("tensor", "GP", gp_hst)
        P.wait("vector", "GP", gp_hst)
        P.wait("scalar", "VE", ve_prep)
        P.wait("sync", "VE", ve_prep)

        ld_ready, ld_stage = {}, {}
        xtr_done, xT_ready, ztr_done, zst_done = {}, {}, {}, {}
        sc_sp, gp_dtt, gp_dtx, ve_zt = {}, {}, {}, {}
        te_bc, sc_exp, sc_evC, ve_tts, ve_m0, gp_n = {}, {}, {}, {}, {}, {}
        mf_marks = []

        def emit_head(li):
            b, ci = divmod(li, NCH)
            colb = b * L + ci * CH
            P.wait("tensor", "LD", ld_ready[li])
            first_evac = True
            for s in range(4):
                for d in range(2):
                    if len(mf_marks) >= 2:
                        P.wait("tensor", "SC", mf_marks[-2])
                    P.emit("tensor", lambda E, s=s, d=d, li=li: E.transpose(
                        psf[4 + d][:, 0:128],
                        xraw[li % 2][:, s * DSL + d * 128:s * DSL + (d + 1) * 128],
                        identf[:]), inc="TE")
                    P.wait("scalar", "TE", P.val("TE"))
                    if first_evac:
                        if li - 2 in gp_dtx:
                            P.wait("scalar", "GP", gp_dtx[li - 2])
                        if li - 2 in ve_zt:
                            P.wait("scalar", "VE", ve_zt[li - 2])
                        first_evac = False
                    P.emit("scalar", lambda E, s=s, d=d, li=li: E.copy(
                        xT[li % 2][:, d * CH + s * 128:d * CH + (s + 1) * 128],
                        psf[4 + d][:, 0:128]), inc="SC")
                    mf_marks.append(P.val("SC"))
            xT_ready[li] = P.val("SC")
            xtr_done[li] = P.val("TE")
            if DEBUG_STOP == "mixer" and DEBUG_DUMPS:
                P.wait("sync", "TE", xtr_done[li])
                P.dma("sync", T["xrdbg"][:, li * 4 * DSL:(li + 1) * 4 * DSL],
                      xraw[li % 2][:], "ST")
            for d in range(2):
                if len(mf_marks) >= 2:
                    P.wait("tensor", "SC", mf_marks[-2])
                mmi = 0
                for lh in (wdth, wdtl):
                    for rh in (xpj_h, xpj_l):
                        P.emit("tensor", lambda E, lh=lh, rh=rh, d=d,
                               f=(mmi == 0), st=(mmi == 3), cb=colb:
                               E.matmul(psf[4 + d][:, 0:CH],
                                        lh[:, d * 128:(d + 1) * 128],
                                        rh[0:RDT, cb:cb + CH],
                                        start=f, stop=st),
                               inc=("TE" if mmi == 3 else None))
                        mmi += 1
                P.wait("scalar", "TE", P.val("TE"))
                if d == 0 and li - 1 in gp_dtt:
                    P.wait("scalar", "GP", gp_dtt[li - 1])
                P.emit("scalar", lambda E, d=d: E.activation(
                    spm[:, d * CH:(d + 1) * CH], psf[4 + d][:, 0:CH],
                    Act.Relu, bias=bdts[:, d:d + 1]))
                P.emit("scalar", lambda E, d=d: E.activation(
                    spa[:, d * CH:(d + 1) * CH], psf[4 + d][:, 0:CH],
                    Act.Abs, bias=bdts[:, d:d + 1]), inc="SC")
                mf_marks.append(P.val("SC"))
                P.emit("scalar", lambda E, d=d: E.activation(
                    spa[:, d * CH:(d + 1) * CH], spa[:, d * CH:(d + 1) * CH],
                    Act.Exp, scale=-1.0))
                P.emit("scalar", lambda E, d=d: E.activation(
                    spa[:, d * CH:(d + 1) * CH], spa[:, d * CH:(d + 1) * CH],
                    Act.Ln, bias=1.0), inc="SC")
            sc_sp[li] = P.val("SC")
            P.wait("gpsimd", "SC", sc_sp[li])
            if (li - 1) * NST - 1 in sc_exp:
                P.wait("gpsimd", "SC", sc_exp[(li - 1) * NST - 1])
            P.emit("gpsimd", lambda E, li=li: E.tensor_tensor(
                dtt[li % 2][:], spm[:], spa[:], Alu.add), inc="GP")
            gp_dtt[li] = P.val("GP")
            P.wait("gpsimd", "SC", xT_ready[li])
            if (li - 1) * NST - 1 in ve_tts:
                P.wait("gpsimd", "VE", ve_tts[(li - 1) * NST - 1])
            P.emit("gpsimd", lambda E, li=li: E.tensor_tensor(
                dtx[li % 2][:], dtt[li % 2][:], xT[li % 2][:], Alu.mult),
                inc="GP")
            gp_dtx[li] = P.val("GP")
            if DEBUG_STOP == "mixer" and DEBUG_DUMPS:
                P.wait("sync", "GP", P.val("GP"))
                P.dma("sync", T["dtdbg"][:, li * 2 * CH:(li + 1) * 2 * CH],
                      dtt[li % 2][:], "ST")
                P.dma("sync", T["xdbg"][:, li * 2 * CH:(li + 1) * 2 * CH],
                      dtx[li % 2][:], "ST")
                P.dma("sync", T["xtdbg"][:, li * 2 * CH:(li + 1) * 2 * CH],
                      xT[li % 2][:], "ST")
                for eng in ("vector", "scalar", "tensor", "gpsimd"):
                    P.wait(eng, "ST", P.val("ST"))

        def emit_stage(li):
            b, ci = divmod(li, NCH)
            colb = b * L + ci * CH
            if li * NST - 1 in te_bc:
                P.wait("sync", "TE", te_bc[li * NST - 1])
            P.dma("sync", stg_h[0:1, :], xpj_h[RDT:96, colb:colb + CH], "LD")
            P.dma("sync", stg_l[0:1, :], xpj_l[RDT:96, colb:colb + CH], "LD")
            ld_stage[li] = P.val("LD")

        def emit_n(li, n):
            nseq = li * NST + n
            sl = nseq % 2
            b, ci = divmod(li, NCH)
            if nseq - 2 in ve_m0:
                P.wait("tensor", "VE", ve_m0[nseq - 2])
                P.wait("tensor", "SC", sc_evC[nseq - 2])
            P.wait("tensor", "LD", ld_stage[li])
            for pj, base in ((0, n), (1, NST + n)):
                for hl, last in ((stg_h, False), (stg_l, True)):
                    P.emit("tensor", lambda E, pj=pj, base=base, hl=hl,
                           last=last, sl=sl: E.matmul(
                               psf[2 * sl + pj][:, 0:CH], S["ones1b"][0:1, :],
                               hl[0:1, base * CH:(base + 1) * CH],
                               start=not last, stop=last),
                           inc=("TE" if pj == 1 and last else None))
            te_bc[nseq] = P.val("TE")
            if nseq - 2 in ve_tts:
                P.wait("scalar", "VE", ve_tts[nseq - 2])
            if n == 0:
                P.wait("scalar", "GP", gp_dtt[li])
            for d in range(2):
                P.emit("scalar", lambda E, d=d, n=n, sl=sl, li=li:
                       E.activation(
                           abuf[:, sl * 2 * CH + d * CH:
                                sl * 2 * CH + (d + 1) * CH],
                           dtt[li % 2][:, d * CH:(d + 1) * CH], Act.Exp,
                           scale=atile[:, d * NST + n:d * NST + n + 1]))
            P.emit("scalar", lambda E: E.drain(), inc="SC")
            sc_exp[nseq] = P.val("SC")
            if nseq - 2 in gp_n:
                P.wait("scalar", "GP", gp_n[nseq - 2])
            P.wait("scalar", "TE", te_bc[nseq])
            P.emit("scalar", lambda E, sl=sl: E.copy(
                csb[:, sl * CH:(sl + 1) * CH], psf[2 * sl + 1][:, 0:CH]),
                inc="SC")
            sc_evC[nseq] = P.val("SC")
            P.wait("vector", "TE", te_bc[nseq])
            P.wait("vector", "SC", sc_exp[nseq])
            if nseq - 2 in gp_n:
                P.wait("vector", "GP", gp_n[nseq - 2])
            if n == 0:
                P.wait("vector", "GP", gp_dtx[li])
            for d in range(2):
                hcol = (2 * b + d) * NST + n
                P.emit("vector", lambda E, d=d, sl=sl, li=li: E.tensor_tensor(
                    wbuf[:, d * CH:(d + 1) * CH],
                    dtx[li % 2][:, d * CH:(d + 1) * CH],
                    psf[2 * sl][:, 0:CH], Alu.mult))
                P.emit("vector", lambda E, d=d, sl=sl, hc=hcol:
                       E.tensor_tensor_scan(
                           htr[sl][:, d * CH:(d + 1) * CH],
                           abuf[:, sl * 2 * CH + d * CH:
                                sl * 2 * CH + (d + 1) * CH],
                           wbuf[:, d * CH:(d + 1) * CH],
                           hst[:, hc:hc + 1],
                           op0=Alu.mult, op1=Alu.add),
                       inc=("VE" if d == 1 else None))
            ve_tts[nseq] = P.val("VE")
            P.emit("vector", lambda E, sl=sl: E.tensor_tensor(
                mbf[sl][:, 0:CH], htr[sl][:, 0:CH], psf[2 * sl + 1][:, 0:CH],
                Alu.mult), inc="VE")
            ve_m0[nseq] = P.val("VE")
            P.emit("vector", lambda E: E.drain())
            for d in range(2):
                hcol = (2 * b + d) * NST + n
                P.emit("vector", lambda E, d=d, sl=sl, hc=hcol: E.tensor_copy(
                    hst[:, hc:hc + 1],
                    htr[sl][:, (d + 1) * CH - 1:(d + 1) * CH]))
            if DEBUG_STOP == "mixer" and DEBUG_DUMPS and li == 1:
                P.wait("sync", "VE", P.val("VE"))
                P.wait("sync", "GP", P.val("GP"))
                P.dma("sync", T["abdbg"][:, n * 4 * CH:n * 4 * CH + 2 * CH],
                      abuf[:, sl * 2 * CH:(sl + 1) * 2 * CH], "ST")
                P.dma("sync", T["abdbg"][:, n * 4 * CH + 2 * CH:
                                         (n + 1) * 4 * CH],
                      htr[sl][:], "ST")
                P.dma("sync", T["atdbg"], atile[:], "ST")
                for eng in ("vector", "scalar", "tensor", "gpsimd"):
                    P.wait(eng, "ST", P.val("ST"))
            P.wait("gpsimd", "VE", ve_tts[nseq])
            P.wait("gpsimd", "SC", sc_evC[nseq])
            P.emit("gpsimd", lambda E, sl=sl: E.tensor_tensor(
                mbf[sl][:, CH:2 * CH], htr[sl][:, CH:2 * CH],
                csb[:, sl * CH:(sl + 1) * CH], Alu.mult))
            P.wait("gpsimd", "VE", ve_m0[nseq])
            if n == 0:
                if li - 1 in ve_zt:
                    P.wait("gpsimd", "VE", ve_zt[li - 1])
                P.emit("gpsimd", lambda E, sl=sl: E.tensor_copy(
                    yacc[:], mbf[sl][:]), inc="GP")
            else:
                P.emit("gpsimd", lambda E, sl=sl: E.tensor_tensor(
                    yacc[:], yacc[:], mbf[sl][:], Alu.add), inc="GP")
            gp_n[nseq] = P.val("GP")

        def emit_ztile(li):
            P.wait("vector", "GP", gp_n[li * NST + NST - 1])
            if li - 1 in ztr_done:
                P.wait("vector", "TE", ztr_done[li - 1])
            for d in range(2):
                P.emit("vector", lambda E, d=d, li=li: E.scalar_tensor_tensor(
                    ztile[:, d * CH:(d + 1) * CH],
                    xT[li % 2][:, d * CH:(d + 1) * CH],
                    dp1[:, d:d + 1], yacc[:, d * CH:(d + 1) * CH],
                    op0=Alu.mult, op1=Alu.add),
                    inc=("VE" if d == 1 else None))
            ve_zt[li] = P.val("VE")
            if DEBUG_STOP == "mixer" and DEBUG_DUMPS:
                P.wait("sync", "VE", P.val("VE"))
                P.dma("sync", T["ydbg"][:, li * 2 * CH:(li + 1) * 2 * CH],
                      yacc[:], "ST")
                for eng in ("vector", "scalar", "tensor", "gpsimd"):
                    P.wait(eng, "ST", P.val("ST"))

        def emit_ztrans(li):
            b, ci = divmod(li, NCH)
            P.wait("tensor", "VE", ve_zt[li])
            first = True
            for s in range(4):
                for d in range(2):
                    if len(mf_marks) >= 2:
                        P.wait("tensor", "SC", mf_marks[-2])
                    P.emit("tensor", lambda E, s=s, d=d: E.transpose(
                        psf[4 + d][:, 0:128],
                        ztile[:, d * CH + s * 128:d * CH + (s + 1) * 128],
                        identf[:]), inc="TE")
                    P.wait("scalar", "TE", P.val("TE"))
                    if first and li - 1 in zst_done:
                        P.wait("scalar", "ST", zst_done[li - 1])
                        first = False
                    P.emit("scalar", lambda E, s=s, d=d: E.copy(
                        zstage[:, s * DSL + d * 128:s * DSL + (d + 1) * 128],
                        psf[4 + d][:, 0:128]), inc="SC")
                    mf_marks.append(P.val("SC"))
            ztr_done[li] = P.val("TE")
            P.wait("sync", "SC", P.val("SC"))
            rowb = b * L + ci * CH
            for s in range(4):
                P.dma("sync",
                      T["z_in"][rowb + s * 128:rowb + (s + 1) * 128, :],
                      zstage[:, s * DSL:(s + 1) * DSL], "ST")
            zst_done[li] = P.val("ST")

        wgam = S["wgam"]
        P.emit("gpsimd", lambda E: E.memset(qcst[:, 0:1], MAGIC))
        P.emit("gpsimd", lambda E: E.memset(qcst[:, 1:2], -(MAGIC - 1.0)))
        P.emit("gpsimd", lambda E: E.memset(qcst[:, 2:3], 2.0))
        P.emit("gpsimd", lambda E: E.memset(qcst[:, 3:4], 1.0), inc="GP")
        P.wait("scalar", "GP", P.val("GP"))
        qdesc = []
        for wsrc, wdst, nrt, wid in (("wgTd", "wgq_in", 16, 0),
                                     ("wvTd", "wvq_in", 16, 1)):
            for it in range(nrt):
                qdesc.append((wsrc, wdst, it * 128, 0, wid))
        for it in range(8):
            for cb in (0, 1024):
                qdesc.append(("woTd", "woq_in", it * 128, cb, 2))
        q_sc, q_st = {}, {}
        qpos = [0]

        def q_pump(k):
            for _ in range(k):
                i = qpos[0]
                if i >= len(qdesc):
                    return
                qpos[0] += 1
                wsrc, wdst, r0, c0, wid = qdesc[i]
                if i - 2 in q_sc:
                    P.wait("sync", "SC", q_sc[i - 2])
                    P.wait("sync", "ST", q_st[i - 2])
                P.dma("sync", qwi[i % 2][:],
                      T[wsrc][r0:r0 + 128, c0:c0 + 1024], "LD")
                P.wait("scalar", "LD", P.val("LD"))
                P.emit("scalar", lambda E, i=i, wid=wid: E.activation(
                    qtmp[:], qwi[i % 2][:], Act.Copy,
                    scale=wgam[:, wid:wid + 1], bias=MAGIC))
                P.emit("scalar", lambda E: E.activation(
                    qtmp[:], qtmp[:], Act.Relu, bias=qcst[:, 1:2]))
                P.emit("scalar", lambda E: E.activation(
                    qtmp[:], qtmp[:], Act.Relu, scale=-1.0, bias=qcst[:, 2:3]))
                P.emit("scalar", lambda E, i=i: E.activation(
                    qwo[i % 2][:], qtmp[:], Act.Copy, scale=-1.0, bias=1.0))
                P.emit("scalar", lambda E: E.drain(), inc="SC")
                q_sc[i] = P.val("SC")
                P.wait("sync", "SC", q_sc[i])
                P.dma("sync", T[wdst][r0:r0 + 128, c0:c0 + 1024],
                      qwo[i % 2][:], "ST")
                q_st[i] = P.val("ST")

        for s in range(4):
            P.dma("sync", xraw[0][:, s * DSL:(s + 1) * DSL],
                  T["x_dsl"][0, s * 128:(s + 1) * 128, :], "LD")
        ld_ready[0] = P.val("LD")
        emit_stage(0)
        emit_head(0)
        for li in range(NP):
            if li + 1 < NP:
                nb, nci = divmod(li + 1, NCH)
                if li - 1 in xtr_done:
                    P.wait("sync", "TE", xtr_done[li - 1])
                for s in range(4):
                    P.dma("sync",
                          xraw[(li + 1) % 2][:, s * DSL:(s + 1) * DSL],
                          T["x_dsl"][nb,
                                     nci * CH + s * 128:nci * CH + (s + 1) * 128,
                                     :], "LD")
                ld_ready[li + 1] = P.val("LD")
            if li >= 1:
                emit_ztile(li - 1)
            for n in range(NST):
                emit_n(li, n)
                if li >= 1 and n in (3, 7, 11, 15):
                    q_pump(1)
                if n == 2 and li >= 1:
                    emit_ztrans(li - 1)
                if n == 7 and li + 1 < NP:
                    emit_head(li + 1)
                if n == 15 and li + 1 < NP:
                    emit_stage(li + 1)
        emit_ztile(NP - 1)
        emit_ztrans(NP - 1)
        q_pump(len(qdesc))
        P.wait("sync", "ST", P.val("ST"))


def _ffn(P, nc, T, S, marks):
    psf, psb = S["psf"], S["psb"]
    identb, ones1, wgam = S["identb"], S["ones1"], S["wgam"]
    hold = {}

    with ExitStack() as cO:
        EO = cO.enter_context
        sxg = EO(nc.sbuf_tensor("sxg", [128, NTT], F32))
        sxv = EO(nc.sbuf_tensor("sxv", [128, NTT], F32))
        sO = EO(nc.sbuf_tensor("sO", [128, NTT], F32))
        gvmax = EO(nc.sbuf_tensor("gvmax", [128, NTT], F32))
        for eng in ("sync", "vector", "scalar", "tensor", "gpsimd"):
            P.wait(eng, "CC", marks["CC_W"])
        P.emit("gpsimd", lambda E: E.memset(gvmax[:], 0.0), inc="GP")
        P.wait("vector", "GP", P.val("GP"))

        def emit_ln(src_fn, dst_fn, tn):
            P.emit("vector", lambda E: E.reduce_sum(tn["red"][:], src_fn(), AXX))
            P.emit("vector", lambda E: E.drain(), inc="VE")
            P.wait("scalar", "VE", P.val("VE"))
            P.emit("scalar", lambda E: E.activation(
                tn["sqs"][:], src_fn(), Act.Square))
            P.emit("scalar", lambda E: E.drain(), inc="SC")
            P.wait("vector", "SC", P.val("SC"))
            P.emit("vector", lambda E: E.reduce_sum(tn["ms"][:], tn["sqs"][:], AXX))
            P.emit("vector", lambda E: E.drain())
            P.emit("vector", lambda E: E.tensor_tensor(
                tn["mu"][:], tn["red"][:], S["dinv"][:], Alu.mult))
            P.emit("vector", lambda E: E.drain())
            P.emit("vector", lambda E: E.tensor_tensor(
                tn["t1"][:], tn["mu"][:], tn["mu"][:], Alu.mult))
            P.emit("vector", lambda E: E.tensor_tensor(
                tn["ms"][:], tn["ms"][:], S["dinv"][:], Alu.mult))
            P.emit("vector", lambda E: E.drain())
            P.emit("vector", lambda E: E.tensor_tensor(
                tn["vpe"][:], tn["ms"][:], tn["t1"][:], Alu.subtract))
            P.emit("vector", lambda E: E.drain())
            P.emit("vector", lambda E: E.tensor_tensor(
                tn["vpe"][:], tn["vpe"][:], S["epsc"][:], Alu.add))
            P.emit("vector", lambda E: E.drain(), inc="VE")
            P.wait("scalar", "VE", P.val("VE"))
            P.emit("scalar", lambda E: E.activation(
                tn["y0"][:], tn["vpe"][:], Act.Ln))
            P.emit("scalar", lambda E: E.drain())
            P.emit("scalar", lambda E: E.activation(
                tn["y0"][:], tn["y0"][:], Act.Exp, scale=-0.5))
            P.emit("scalar", lambda E: E.drain(), inc="SC")
            P.wait("vector", "SC", P.val("SC"))
            P.emit("vector", lambda E: E.tensor_tensor(
                tn["t1"][:], tn["y0"][:], tn["y0"][:], Alu.mult))
            P.emit("vector", lambda E: E.drain())
            P.emit("vector", lambda E: E.tensor_tensor(
                tn["t1"][:], tn["t1"][:], tn["vpe"][:], Alu.mult))
            P.emit("vector", lambda E: E.drain())
            P.emit("vector", lambda E: E.tensor_scalar(
                tn["t1"][:], tn["t1"][:], -0.5, 1.5, op0=Alu.mult, op1=Alu.add))
            P.emit("vector", lambda E: E.drain())
            P.emit("vector", lambda E: E.tensor_tensor(
                tn["rstd"][:], tn["y0"][:], tn["t1"][:], Alu.mult))
            P.emit("vector", lambda E: E.drain())
            P.emit("vector", lambda E: E.tensor_scalar(
                dst_fn(), src_fn(), tn["mu"][:], tn["rstd"][:],
                op0=Alu.subtract, op1=Alu.mult), inc="VE")

        with ExitStack() as cA:
            EA = cA.enter_context
            zraw = EA(nc.sbuf_tensor("zraw", [128, D], F32))
            sqs = EA(nc.sbuf_tensor("sqs", [128, D], F32))
            z1 = EA(nc.sbuf_tensor("z1", [128, D], F32))
            Xq = EA(nc.sbuf_tensor("Xq", [128, D], BF16))
            XqT = EA(nc.sbuf_tensor("XqT", [128, NKD * 1024], BF16))
            tn = {k: EA(nc.sbuf_tensor("ln_" + k, [128, 1], F32))
                  for k in ("red", "mu", "ms", "vpe", "y0", "t1", "rstd",
                            "gmx", "gam", "inv", "sx")}
            tn["sqs"] = sqs
            lnsb = EA(nc.sbuf_tensor("lnsb", [128, 8 * NTT], F32))
            wgsl = [EA(nc.sbuf_tensor(f"wgsl{i}", [128, HID], BF16))
                    for i in range(2)]
            wvsl = [EA(nc.sbuf_tensor(f"wvsl{i}", [128, HID], BF16))
                    for i in range(2)]
            gbm = EA(nc.sbuf_tensor("gbm", [1, HID], F32))
            vbm = EA(nc.sbuf_tensor("vbm", [1, HID], F32))
            gbj = EA(nc.sbuf_tensor("gbj", [128, 512], F32))
            vbj = EA(nc.sbuf_tensor("vbj", [128, 512], F32))
            upre = EA(nc.sbuf_tensor("upre", [128, 512], F32))
            gsil = EA(nc.sbuf_tensor("gsil", [128, 512], F32))
            vtile = EA(nc.sbuf_tensor("vtile", [128, 512], F32))
            gvst = [EA(nc.sbuf_tensor(f"gvst{i}", [128, 512], F32))
                    for i in range(2)]
            mjt = EA(nc.sbuf_tensor("mjt", [128, 1], F32))

            P.dma("sync", gbm[:], T["gbias"], "LD")
            P.dma("sync", vbm[:], T["vbias"], "LD")
            ld_bias = P.val("LD")
            P.wait("tensor", "LD", ld_bias)
            P.wait("sync", "CC", marks["CC_Z"])

            def _mk_zoffs(E):
                hold["zb"] = E.snap(E.partition_id() * TOK,
                                    min_val=0, max_val=(NCORE - 1) * TOK,
                                    guaranteed_mod_val=TOK)
                return E.nop()
            P.emit("sync", _mk_zoffs)
            z3 = T["z_out"].ap().rearrange("(r n) d -> r n d", r=NCORE)
            for r in range(NCORE):
                P.emit("sync", lambda E, r=r: E.dma_start(
                    out=T["z_my"][:, r * DSL:(r + 1) * DSL],
                    in_=z3[r][bass.ds(hold["zb"], TOK), :]),
                    inc="LD", k=16)
            P.wait("sync", "LD", P.val("LD"))

            tsq = 0
            xqtr_done = {}
            sd_z1 = {}
            for tt in range(NTT):
                P.wait("sync", "VE", P.val("VE"))
                P.dma("sync", zraw[:],
                      T["z_my"][tt * 128:(tt + 1) * 128, :], "LD")
                P.wait("vector", "LD", P.val("LD"))
                if tt - 1 in sd_z1:
                    P.wait("vector", "SD", sd_z1[tt - 1])
                emit_ln(lambda: zraw[:], lambda: z1[:], tn)
                if DEBUG_STOP == "ffn":
                    for k_i, k_n in enumerate(("red", "ms", "mu", "vpe", "y0",
                                               "rstd", "t1", "t1")):
                        P.emit("vector", lambda E, tt=tt, k_i=k_i, k_n=k_n:
                               E.tensor_copy(lnsb[:, tt * 8 + k_i:tt * 8 + k_i + 1],
                                             tn[k_n][:]))
                    P.emit("vector", lambda E: E.nop(), inc="VE")
                ve_z1 = P.val("VE")
                P.wait("scalar", "VE", ve_z1)
                P.emit("scalar", lambda E, tt=tt: E.dma_start(
                    out=T["z1st"][tt * 128:(tt + 1) * 128, :], in_=z1[:]),
                    inc="SD", k=16)
                sd_z1[tt] = P.val("SD")
                P.emit("vector", lambda E: E.tensor_reduce(
                    tn["gmx"][:], z1[:], AXX, Alu.max,
                    apply_absolute_value=True))
                P.emit("vector", lambda E: E.drain())
                P.emit("vector", lambda E: E.tensor_tensor(
                    tn["gam"][:], tn["gmx"][:], S["epsc"][:], Alu.max))
                P.emit("vector", lambda E: E.drain())
                P.emit("vector", lambda E: E.reciprocal(
                    tn["inv"][:], tn["gam"][:]))
                P.emit("vector", lambda E: E.drain())
                P.emit("vector", lambda E: E.tensor_tensor(
                    tn["sx"][:], tn["inv"][:], S["c127"][:], Alu.mult))
                P.emit("vector", lambda E, tt=tt: E.scalar_tensor_tensor(
                    sxg[:, tt:tt + 1], tn["gam"][:], 1.0, wgam[:, 4:5],
                    op0=Alu.mult, op1=Alu.mult))
                P.emit("vector", lambda E, tt=tt: E.scalar_tensor_tensor(
                    sxv[:, tt:tt + 1], tn["gam"][:], 1.0, wgam[:, 5:6],
                    op0=Alu.mult, op1=Alu.mult))
                P.emit("vector", lambda E: E.drain())
                P.emit("vector", lambda E: E.tensor_scalar(
                    sqs[:], z1[:], tn["sx"][:], MAGIC, op0=Alu.mult,
                    op1=Alu.add))
                if tt - 1 in xqtr_done:
                    P.wait("vector", "TE", xqtr_done[tt - 1])
                P.emit("vector", lambda E: E.tensor_scalar(
                    Xq[:], sqs[:], MAGIC, None, op0=Alu.subtract), inc="VE")
                P.wait("tensor", "VE", P.val("VE"))
                for k in range(NKD):
                    P.wait("tensor", "SC", P.val("SC") - 1)
                    P.emit("tensor", lambda E, k=k, p=tsq % 2: E.transpose(
                        psb[p][:, 0:128], Xq[:, k * 128:(k + 1) * 128],
                        identb[:]), inc="TE")
                    P.wait("scalar", "TE", P.val("TE"))
                    P.emit("scalar", lambda E, k=k, tt=tt, p=tsq % 2: E.copy(
                        XqT[:, k * 1024 + tt * 128:k * 1024 + (tt + 1) * 128],
                        psb[p][:, 0:128]), inc="SC")
                    tsq += 1
                xqtr_done[tt] = P.val("TE")
            P.wait("tensor", "SC", P.val("SC"))

            P.wait("sync", "CC", marks["CC_W"])
            ld_slab = {}
            mm_done = {}
            epi_done = {}
            sd_gv = {}
            for j in range(16):
                r, half = divmod(j, 2)
                if j - 2 in mm_done:
                    P.wait("sync", "TE", mm_done[j - 2])
                for k in range(NKD):
                    P.dma("sync", wgsl[j % 2][:, k * 512:(k + 1) * 512],
                          T["wgq_out"][r * D + k * 128:r * D + (k + 1) * 128,
                                       half * 512:(half + 1) * 512], "LD")
                    P.dma("sync", wvsl[j % 2][:, k * 512:(k + 1) * 512],
                          T["wvq_out"][r * D + k * 128:r * D + (k + 1) * 128,
                                       half * 512:(half + 1) * 512], "LD")
                ld_slab[j] = P.val("LD")
                if (j - 1) * NTT + NTT - 1 in epi_done:
                    P.wait("tensor", "VE", epi_done[(j - 1) * NTT + NTT - 1])
                P.emit("tensor", lambda E, j=j: E.matmul(
                    psf[4][:, 0:512], ones1[0:1, :],
                    gbm[0:1, j * 512:(j + 1) * 512],
                    start=True, stop=True))
                P.emit("tensor", lambda E, j=j: E.matmul(
                    psf[5][:, 0:512], ones1[0:1, :],
                    vbm[0:1, j * 512:(j + 1) * 512],
                    start=True, stop=True), inc="TE")
                P.wait("vector", "TE", P.val("TE"))
                P.emit("vector", lambda E: E.tensor_copy(gbj[:], psf[4][:, 0:512]))
                P.emit("vector", lambda E: E.tensor_copy(vbj[:], psf[5][:, 0:512]),
                       inc="VE")
                for t in range(NTT):
                    gq = j * NTT + t
                    P.wait("tensor", "LD", ld_slab[j])
                    if gq - 2 in epi_done:
                        P.wait("tensor", "VE", epi_done[gq - 2])
                    for k in range(NKD):
                        P.emit("tensor", lambda E, k=k, t=t, j=j,
                               f=(k == 0), st=(k == NKD - 1): E.matmul(
                                   psf[t % 2][:, 0:512],
                                   XqT[:, k * 1024 + t * 128:k * 1024 + (t + 1) * 128],
                                   wgsl[j % 2][:, k * 512:(k + 1) * 512],
                                   start=f, stop=st),
                               inc=("TE" if k == NKD - 1 else None))
                    for k in range(NKD):
                        P.emit("tensor", lambda E, k=k, t=t, j=j,
                               f=(k == 0), st=(k == NKD - 1): E.matmul(
                                   psf[2 + t % 2][:, 0:512],
                                   XqT[:, k * 1024 + t * 128:k * 1024 + (t + 1) * 128],
                                   wvsl[j % 2][:, k * 512:(k + 1) * 512],
                                   start=f, stop=st),
                               inc=("TE" if k == NKD - 1 else None))
                    te_gv = P.val("TE")
                    P.wait("vector", "TE", te_gv)
                    P.emit("vector", lambda E, t=t: E.scalar_tensor_tensor(
                        upre[:], psf[t % 2][:, 0:512], sxg[:, t:t + 1], gbj[:],
                        op0=Alu.mult, op1=Alu.add), inc="VE")
                    P.wait("scalar", "VE", P.val("VE"))
                    P.emit("scalar", lambda E: E.activation(
                        gsil[:], upre[:], Act.Silu))
                    P.emit("scalar", lambda E: E.drain(), inc="SC")
                    P.emit("vector", lambda E, t=t: E.scalar_tensor_tensor(
                        vtile[:], psf[2 + t % 2][:, 0:512], sxv[:, t:t + 1],
                        vbj[:], op0=Alu.mult, op1=Alu.add))
                    P.wait("vector", "SC", P.val("SC"))
                    if gq - 2 in sd_gv:
                        P.wait("vector", "SD", sd_gv[gq - 2])
                    P.emit("vector", lambda E, sl=gq % 2: E.tensor_tensor(
                        gvst[sl][:], gsil[:], vtile[:], Alu.mult))
                    P.emit("vector", lambda E, sl=gq % 2: E.tensor_reduce(
                        mjt[:], gvst[sl][:], AXX, Alu.max,
                        apply_absolute_value=True))
                    P.emit("vector", lambda E: E.drain())
                    P.emit("vector", lambda E, t=t: E.tensor_tensor(
                        gvmax[:, t:t + 1], gvmax[:, t:t + 1], mjt[:], Alu.max),
                        inc="VE")
                    epi_done[gq] = P.val("VE")
                    P.wait("scalar", "VE", epi_done[gq])
                    P.emit("scalar", lambda E, t=t, j=j, sl=gq % 2: E.dma_start(
                        out=T["gv_dram"][t, j, :, :], in_=gvst[sl][:]),
                        inc="SD", k=16)
                    sd_gv[gq] = P.val("SD")
                mm_done[j] = P.val("TE")
            P.wait("sync", "SD", P.val("SD"))
            if DEBUG_STOP == "ffn":
                P.wait("sync", "VE", P.val("VE"))
                P.dma("sync", T["lndbg"], lnsb[:], "ST")
                P.wait("sync", "ST", P.val("ST"))
            P.barrier()

        with ExitStack() as cB:
            EB = cB.enter_context
            gvrow = [EB(nc.sbuf_tensor(f"gvrow{i}", [128, HID], F32))
                     for i in range(2)]
            gvq32 = EB(nc.sbuf_tensor("gvq32", [128, HID], F32))
            GVq = [EB(nc.sbuf_tensor(f"GVq{i}", [128, HID], BF16))
                   for i in range(2)]
            gvqTst = [EB(nc.sbuf_tensor(f"gvqTst{i}", [128, HID], BF16))
                      for i in range(2)]
            g2 = {k: EB(nc.sbuf_tensor("g2_" + k, [128, 1], F32))
                  for k in ("gam", "inv", "sgv")}
            tsq = 0
            ld_gv, ve_q, tr_done, ev_tr, st_q = {}, {}, {}, {}, {}
            for t in range(NTT):
                if t - 2 in ve_q:
                    P.wait("sync", "VE", ve_q[t - 2])
                for j in range(16):
                    P.dma("sync", gvrow[t % 2][:, j * 512:(j + 1) * 512],
                          T["gv_dram"][t, j, :, :], "LD")
                ld_gv[t] = P.val("LD")
                P.wait("vector", "LD", ld_gv[t])
                P.emit("vector", lambda E, t=t: E.tensor_scalar(
                    g2["gam"][:], gvmax[:, t:t + 1], EPS, None, op0=Alu.max))
                P.emit("vector", lambda E: E.drain())
                P.emit("vector", lambda E: E.reciprocal(
                    g2["inv"][:], g2["gam"][:]))
                P.emit("vector", lambda E: E.drain())
                P.emit("vector", lambda E: E.tensor_scalar(
                    g2["sgv"][:], g2["inv"][:], 127.0, None, op0=Alu.mult))
                P.emit("vector", lambda E, t=t: E.scalar_tensor_tensor(
                    sO[:, t:t + 1], g2["gam"][:], 1.0, wgam[:, 6:7],
                    op0=Alu.mult, op1=Alu.mult))
                P.emit("vector", lambda E: E.drain())
                P.emit("vector", lambda E, t=t: E.tensor_scalar(
                    gvq32[:], gvrow[t % 2][:], g2["sgv"][:], MAGIC,
                    op0=Alu.mult, op1=Alu.add))
                if t - 2 in tr_done:
                    P.wait("vector", "TE", tr_done[t - 2])
                P.emit("vector", lambda E, t=t: E.tensor_scalar(
                    GVq[t % 2][:], gvq32[:], MAGIC, None, op0=Alu.subtract),
                    inc="VE")
                ve_q[t] = P.val("VE")
                P.wait("tensor", "VE", ve_q[t])
                first = True
                for k in range(NKH):
                    P.wait("tensor", "SC", P.val("SC") - 1)
                    P.emit("tensor", lambda E, k=k, t=t, p=tsq % 2:
                           E.transpose(psb[p][:, 0:128],
                                       GVq[t % 2][:, k * 128:(k + 1) * 128],
                                       identb[:]), inc="TE")
                    P.wait("scalar", "TE", P.val("TE"))
                    if first and t - 2 in st_q:
                        P.wait("scalar", "ST", st_q[t - 2])
                        first = False
                    P.emit("scalar", lambda E, k=k, t=t, p=tsq % 2: E.copy(
                        gvqTst[t % 2][:, k * 128:(k + 1) * 128],
                        psb[p][:, 0:128]), inc="SC")
                    tsq += 1
                tr_done[t] = P.val("TE")
                ev_tr[t] = P.val("SC")
                P.wait("sync", "SC", ev_tr[t])
                P.dma("sync", T["gvqT_dram"][t, :, :], gvqTst[t % 2][:], "ST")
                st_q[t] = P.val("ST")
            P.wait("sync", "ST", P.val("ST"))
            P.barrier()

        with ExitStack() as cC:
            EC = cC.enter_context
            wosl = EC(nc.sbuf_tensor("wosl", [128, NKH * 512], BF16))
            gvqrd = [EC(nc.sbuf_tensor(f"gvqrd{i}", [128, HID], BF16))
                     for i in range(2)]
            obm = EC(nc.sbuf_tensor("obm", [1, D], F32))
            obj = EC(nc.sbuf_tensor("obj", [128, 512], F32))
            Oacc = EC(nc.sbuf_tensor("Oacc", [128, NTT * D], F32))
            z1rd = EC(nc.sbuf_tensor("z1rd", [128, D], F32))
            z2b = EC(nc.sbuf_tensor("z2b", [128, D], F32))
            ofin = EC(nc.sbuf_tensor("ofin", [128, D], F32))
            sq2 = EC(nc.sbuf_tensor("sq2", [128, D], F32))
            tn2 = {k: EC(nc.sbuf_tensor("l2_" + k, [128, 1], F32))
                   for k in ("red", "mu", "ms", "vpe", "y0", "t1", "rstd")}
            tn2["sqs"] = sq2

            P.dma("sync", obm[:], T["obias"], "LD")
            P.wait("tensor", "LD", P.val("LD"))
            oq_mm = {}
            slab_mm = {}
            epi_o = {}
            for j in range(4):
                if j - 1 in slab_mm:
                    P.wait("sync", "TE", slab_mm[j - 1])
                for k in range(NKH):
                    P.dma("sync", wosl[:, k * 512:(k + 1) * 512],
                          T["woq_out"][k * 128:(k + 1) * 128,
                                       j * 512:(j + 1) * 512], "LD")
                ld_wo = P.val("LD")
                if (j - 1) * NTT + NTT - 1 in epi_o:
                    P.wait("tensor", "VE", epi_o[(j - 1) * NTT + NTT - 1])
                P.emit("tensor", lambda E, j=j: E.matmul(
                    psf[4][:, 0:512], ones1[0:1, :],
                    obm[0:1, j * 512:(j + 1) * 512],
                    start=True, stop=True), inc="TE")
                P.wait("vector", "TE", P.val("TE"))
                P.emit("vector", lambda E: E.tensor_copy(obj[:], psf[4][:, 0:512]),
                       inc="VE")
                for t in range(NTT):
                    oq = j * NTT + t
                    if oq - 2 in oq_mm:
                        P.wait("sync", "TE", oq_mm[oq - 2])
                    P.dma("sync", gvqrd[t % 2][:], T["gvqT_dram"][t, :, :], "LD")
                    P.wait("tensor", "LD", P.val("LD"))
                    if oq - 3 in epi_o:
                        P.wait("tensor", "VE", epi_o[oq - 3])
                    for k in range(NKH):
                        P.emit("tensor", lambda E, k=k, t=t, b=oq % 3,
                               f=(k == 0), st=(k == NKH - 1): E.matmul(
                                   psf[b][:, 0:512],
                                   gvqrd[t % 2][:, k * 128:(k + 1) * 128],
                                   wosl[:, k * 512:(k + 1) * 512],
                                   start=f, stop=st),
                               inc=("TE" if k == NKH - 1 else None))
                    oq_mm[oq] = P.val("TE")
                    P.wait("vector", "TE", oq_mm[oq])
                    P.emit("vector", lambda E, t=t, j=j, b=oq % 3:
                           E.scalar_tensor_tensor(
                               Oacc[:, t * D + j * 512:t * D + (j + 1) * 512],
                               psf[b][:, 0:512], sO[:, t:t + 1], obj[:],
                               op0=Alu.mult, op1=Alu.add), inc="VE")
                    epi_o[oq] = P.val("VE")
                slab_mm[j] = P.val("TE")

            st_out = {}
            for t in range(NTT):
                P.wait("sync", "VE", P.val("VE"))
                P.dma("sync", z1rd[:], T["z1st"][t * 128:(t + 1) * 128, :], "LD")
                P.wait("vector", "LD", P.val("LD"))
                if t - 1 in st_out:
                    P.wait("vector", "ST", st_out[t - 1])
                P.emit("vector", lambda E, t=t: E.tensor_tensor(
                    z2b[:], z1rd[:], Oacc[:, t * D:(t + 1) * D], Alu.add))
                emit_ln(lambda: z2b[:], lambda: ofin[:], tn2)
                P.wait("sync", "VE", P.val("VE"))
                P.dma("sync", T["out"][t * 128:(t + 1) * 128, :], ofin[:], "ST")
                st_out[t] = P.val("ST")
            P.wait("sync", "ST", P.val("ST"))
            P.wait("sync", "SD", P.val("SD"))
            if DEBUG_STOP == "ffn":
                P.dma("gpsimd", T["z1dbg"], T["z1st"].ap(), "GD")
                P.dma("gpsimd", T["gvdbg"],
                      T["gv_dram"].ap().rearrange("a b c d -> (a b c) d"), "GD")
                P.dma("gpsimd", T["gvqdbg"],
                      T["gvqT_dram"].ap().rearrange("a b c -> (a b) c"), "GD")
                P.dma("gpsimd", T["zmydbg"], T["z_my"].ap(), "GD")
                P.wait("gpsimd", "GD", P.val("GD"))


_NC_CACHE = {}


def _get_nc():
    key = DEBUG_STOP
    if key not in _NC_CACHE:
        _NC_CACHE[key] = build_nc()
    return _NC_CACHE[key]


def kernel(L0_attn_x, L0_attn_W_x, L0_attn_W_dt, L0_attn_b_dt, L0_attn_A_log,
           L0_attn_D, L0_ffn_gate_w, L0_ffn_gate_b, L0_ffn_val_w, L0_ffn_val_b,
           L0_ffn_out_w, L0_ffn_out_b):
    f32 = np.float32
    x = np.asarray(L0_attn_x, f32)
    W_x = np.ascontiguousarray(np.asarray(L0_attn_W_x, f32))
    W_dt = np.asarray(L0_attn_W_dt, f32)
    b_dt = np.asarray(L0_attn_b_dt, f32)
    A_log = np.asarray(L0_attn_A_log, f32)
    Dv = np.asarray(L0_attn_D, f32)
    WgT = np.ascontiguousarray(np.asarray(L0_ffn_gate_w, f32).T)
    WvT = np.ascontiguousarray(np.asarray(L0_ffn_val_w, f32).T)
    WoT = np.ascontiguousarray(np.asarray(L0_ffn_out_w, f32).T)
    gb = np.ascontiguousarray(np.asarray(L0_ffn_gate_b, f32).reshape(1, HID))
    vb = np.ascontiguousarray(np.asarray(L0_ffn_val_b, f32).reshape(1, HID))
    ob = np.ascontiguousarray(np.asarray(L0_ffn_out_b, f32).reshape(1, D))

    xf = np.ascontiguousarray(x.reshape(B * L, D))
    in_maps = []
    for c in range(NCORE):
        dsl = slice(DSL * c, DSL * (c + 1))
        hsl = slice(1024 * c, 1024 * (c + 1))
        in_maps.append(dict(
            x_tok=np.ascontiguousarray(xf[TOK * c:TOK * (c + 1)]),
            x_dsl=np.ascontiguousarray(x[:, :, dsl]),
            wx=W_x,
            wdtT=np.ascontiguousarray(W_dt[dsl].T),
            bdt=np.ascontiguousarray(b_dt[dsl].reshape(2, 128).T),
            alog=np.ascontiguousarray(
                A_log[dsl].reshape(2, 128, NST).transpose(1, 0, 2)
                .reshape(128, 2 * NST)),
            dvec=np.ascontiguousarray(Dv[dsl].reshape(2, 128).T),
            wgTd=np.ascontiguousarray(WgT[:, hsl]),
            wvTd=np.ascontiguousarray(WvT[:, hsl]),
            woTd=np.ascontiguousarray(WoT[hsl]),
            gbias=gb, vbias=vb, obias=ob,
        ))
    nc = _get_nc()
    res = run_bass_kernel_spmd(nc, in_maps, core_ids=list(range(NCORE)))
    if DEBUG_STOP is not None:
        return res
    o = np.concatenate([r["out"] for r in res.results], axis=0)
    return o.reshape(B, L, D)


if __name__ == "__main__":
    nc = build_nc()
    n = sum(1 for _ in nc.m.functions[0].blocks for _ in [0])
    print("built ok")


# version 13
# speedup vs baseline: 1.1625x; 1.0142x over previous
import numpy as np
import ml_dtypes
from contextlib import ExitStack

import concourse.bass as bass
import concourse.mybir as mybir
from concourse.bass_utils import run_bass_kernel_spmd

F32 = mybir.dt.float32
BF16 = mybir.dt.bfloat16
Alu = mybir.AluOpType
Act = mybir.ActivationFunctionType
AXX = mybir.AxisListType.X

MAGIC = float(np.float32(1.5 * 2 ** 23))
NCORE = 8
B, L, D, NST, RDT, HID = 2, 4096, 2048, 16, 64, 8192
TOK = (B * L) // NCORE
DSL = D // NCORE
CH = 512
NCH = L // CH
NKD = D // 128
NTT = TOK // 128
NKH = HID // 128
EPS = 1e-5

DEBUG_STOP = None
DEBUG_DUMPS = False
DEBUG_BC1 = False
DEBUG_A1 = False
DEBUG_NOBC = False


class _Prog:

    ENGINES = ("sync", "vector", "scalar", "tensor", "gpsimd")

    def __init__(self):
        self.ops = []
        self.cnt = {}

    def sem(self, name):
        self.cnt.setdefault(name, 0)

    def emit(self, eng, fn, inc=None, k=1):
        if inc is not None:
            self.sem(inc)
            self.cnt[inc] += k
        self.ops.append(("op", eng, fn, inc, k))

    def dma(self, eng, out, in_, inc):
        self.emit(eng, lambda E, o=out, i=in_: E.dma_start(out=o, in_=i),
                  inc=inc, k=16)

    def wait(self, eng, semname, val=None):
        v = self.cnt.get(semname, 0) if val is None else val
        if v > 0:
            self.ops.append(("wait", eng, semname, v, None))

    def val(self, semname):
        return self.cnt.get(semname, 0)

    def barrier(self):
        self.sem("PH")
        for e in self.ENGINES:
            if e in ("vector", "scalar", "tensor"):
                self.emit(e, lambda E: E.drain(), inc="PH")
            else:
                self.emit(e, lambda E: E.nop(), inc="PH")
        tgt = self.cnt["PH"]
        for e in self.ENGINES:
            self.wait(e, "PH", tgt)

    def lower(self, nc, block, ctx):
        handles = {name: ctx.enter_context(nc.semaphore(name))
                   for name in self.cnt}
        for eng_name in self.ENGINES:
            my = [op for op in self.ops if op[1] == eng_name]
            if not my:
                continue

            def section(E, my=my, handles=handles):
                for kind, _eng, a, b, c in my:
                    if kind == "wait":
                        E.wait_ge(handles[a], b)
                    else:
                        instr = a(E)
                        if b is not None:
                            instr.then_inc(handles[b], c)
            getattr(block, eng_name)(section)


def build_nc():
    nc = bass.Bass("TRN2", target_bir_lowering=False)

    T = {}

    def din(name, shape, dt=F32):
        T[name] = nc.dram_tensor(name, shape, dt, kind="ExternalInput").ap()

    def dout(name, shape, dt=F32):
        T[name] = nc.dram_tensor(name, shape, dt, kind="ExternalOutput").ap()

    def dint(name, shape, dt=F32, shared=False):
        T[name] = nc.dram_tensor(name, shape, dt,
                                 addr_space="Shared" if shared else "Local")

    din("x_tok", [TOK, D]); din("x_dsl", [B, L, DSL]); din("wx", [96, D])
    din("wdtT", [RDT, DSL]); din("bdt", [128, 2]); din("alog", [128, 2 * NST])
    din("dvec", [128, 2]); din("wgTd", [D, 1024]); din("wvTd", [D, 1024])
    din("woTd", [1024, D]); din("gbias", [1, HID]); din("vbias", [1, HID])
    din("obias", [1, D])
    dout("out", [TOK, D])
    if DEBUG_STOP == "mixer":
        dout("zdbg", [B * L, DSL])
        dout("dtdbg", [128, 2 * CH * B * NCH])
        dout("xdbg", [128, 2 * CH * B * NCH])
        dout("xtdbg", [128, 2 * CH * B * NCH])
        dout("abdbg", [128, 4 * CH * NST])
        dout("atdbg", [128, 2 * NST])
        dout("xrdbg", [128, 4 * DSL * B * NCH])
        dout("ydbg", [128, 2 * CH * B * NCH])
    if DEBUG_STOP == "ffn":
        dout("z1dbg", [TOK, D])
        dout("gvdbg", [NTT * 16 * 128, 512])
        dout("gvqdbg", [NTT * 128, HID], BF16)
        dout("zmydbg", [TOK, D])
        dout("lndbg", [128, 8 * NTT])
    if DEBUG_STOP == "ph0":
        dout("agdbg", [NCORE * 96, TOK])
        dout("wqdbg", [NCORE * D, 1024], BF16)
        dout("gmdbg", [NCORE, 4])
        dout("wgdbg", [128, 8])
    dint("ag1_in", [96, TOK])
    dint("ag1_out", [NCORE * 96, TOK], shared=True)
    dint("gm_in", [1, 4]); dint("gm_out", [NCORE, 4], shared=True)
    dint("wgq_in", [D, 1024], BF16)
    dint("wgq_out", [NCORE * D, 1024], BF16, shared=True)
    dint("wvq_in", [D, 1024], BF16)
    dint("wvq_out", [NCORE * D, 1024], BF16, shared=True)
    dint("woq_in", [1024, D], BF16)
    dint("woq_out", [NCORE * 1024, D], BF16, shared=True)
    dint("z_in", [B * L, DSL])
    dint("z_out", [NCORE * B * L, DSL], shared=True)
    dint("z_my", [TOK, D])
    dint("z1st", [TOK, D])
    dint("gv_dram", [NTT, 16, 128, 512])
    dint("gvqT_dram", [NTT, 128, HID], BF16)

    P = _Prog()
    for s in ("LD", "ST", "SD", "GD", "CC", "VE", "SC", "TE", "GP", "PH",
              "MF", "BF"):
        P.sem(s)
    marks = {}

    ctx = ExitStack()
    with ctx:
        E = ctx.enter_context
        S = {}
        S["identb"] = E(nc.sbuf_tensor("identb", [128, 128], BF16))
        S["identf"] = E(nc.sbuf_tensor("identf", [128, 128], F32))
        S["ones1"] = E(nc.sbuf_tensor("ones1", [1, 128], F32))
        S["onescol"] = E(nc.sbuf_tensor("onescol", [128, 1], F32))
        S["ones1b"] = E(nc.sbuf_tensor("ones1b", [1, 128], BF16))
        S["wgam"] = E(nc.sbuf_tensor("wgam", [128, 8], F32))
        S["dinv"] = E(nc.sbuf_tensor("dinv", [128, 1], F32))
        S["epsc"] = E(nc.sbuf_tensor("epsc", [128, 1], F32))
        S["c127"] = E(nc.sbuf_tensor("c127", [128, 1], F32))
        S["psf"] = [E(nc.psum_tensor(f"psf{i}", [128, 512], F32))
                    for i in range(6)]
        S["psb"] = [E(nc.psum_tensor(f"psb{i}", [128, 1024], BF16))
                    for i in range(2)]
        block = E(nc.Block())

        with ExitStack() as cscan:
            ES = cscan.enter_context
            S["xpj_h"] = ES(nc.sbuf_tensor("xpj_h", [96, B * L], BF16))
            S["xpj_l"] = ES(nc.sbuf_tensor("xpj_l", [96, B * L], BF16))

            _phase0(P, nc, T, S, marks)

            if DEBUG_STOP == "ph0":
                P.wait("gpsimd", "CC", marks["CC_GM"])
                P.dma("gpsimd", T["agdbg"], T["ag1_out"].ap(), "GD")
                P.dma("gpsimd", T["wqdbg"], T["wgq_out"].ap(), "GD")
                P.dma("gpsimd", T["gmdbg"], T["gm_out"].ap(), "GD")
                P.dma("gpsimd", T["wgdbg"], S["wgam"][:], "GD")
                P.wait("gpsimd", "GD", P.val("GD"))
                P.lower(nc, block, ctx)
                return nc

            _scan(P, nc, T, S, marks)

            if DEBUG_STOP == "mixer":
                P.wait("gpsimd", "ST", P.val("ST"))
                P.dma("gpsimd", T["zdbg"], T["z_in"].ap(), "GD")
                P.wait("gpsimd", "GD", P.val("GD"))
                P.lower(nc, block, ctx)
                return nc

            P.wait("gpsimd", "ST", P.val("ST"))
            P.wait("gpsimd", "SD", P.val("SD"))
            P.emit("gpsimd", lambda E: E.collective_compute(
                "AllGather", Alu.bypass, replica_groups=[list(range(NCORE))],
                ins=[T["z_in"].ap().opt()], outs=[T["z_out"].ap().opt()]),
                inc="CC")
            marks["CC_Z"] = P.val("CC")
            P.barrier()
        _ffn(P, nc, T, S, marks)
        P.lower(nc, block, ctx)
    return nc


def _ag(P, T, name_in, name_out, mark, marks):
    P.wait("gpsimd", "ST", P.val("ST"))
    P.emit("gpsimd", lambda E, a=T[name_in], b=T[name_out]: E.collective_compute(
        "AllGather", Alu.bypass, replica_groups=[list(range(NCORE))],
        ins=[a.ap().opt()], outs=[b.ap().opt()]), inc="CC")
    marks[mark] = P.val("CC")


def _phase0(P, nc, T, S, marks):
    psf, psb = S["psf"], S["psb"]
    identb, identf = S["identb"], S["identf"]
    ones1, onescol, wgam = S["ones1"], S["onescol"], S["wgam"]

    with ExitStack() as c0:
        E0 = c0.enter_context
        xtk = E0(nc.sbuf_tensor("xtk", [128, D], F32))
        xth = E0(nc.sbuf_tensor("xth", [128, D], BF16))
        xtl = E0(nc.sbuf_tensor("xtl", [128, D], BF16))
        xtmp = E0(nc.sbuf_tensor("xtmp", [128, D], F32))
        xTh = E0(nc.sbuf_tensor("xTh", [128, NKD * 128], BF16))
        xTl = E0(nc.sbuf_tensor("xTl", [128, NKD * 128], BF16))
        wxs = E0(nc.sbuf_tensor("wxs", [96, D], F32))
        wxh = E0(nc.sbuf_tensor("wxh", [96, D], BF16))
        wxl = E0(nc.sbuf_tensor("wxl", [96, D], BF16))
        wxTh = E0(nc.sbuf_tensor("wxTh", [128, NKD * 96], BF16))
        wxTl = E0(nc.sbuf_tensor("wxTl", [128, NKD * 96], BF16))
        xpmine = E0(nc.sbuf_tensor("xpmine", [96, TOK], F32))
        wti = [E0(nc.sbuf_tensor(f"wti{i}", [128, 2048], F32)) for i in range(2)]
        wqo = [E0(nc.sbuf_tensor(f"wqo{i}", [128, 2048], BF16)) for i in range(2)]
        wtmp = E0(nc.sbuf_tensor("wtmp", [128, 2048], F32))
        wacc = E0(nc.sbuf_tensor("wacc", [128, 1], F32))
        wred = E0(nc.sbuf_tensor("wred", [128, 1], F32))
        gmsb = E0(nc.sbuf_tensor("gmsb", [1, 8], F32))
        gmrd = E0(nc.sbuf_tensor("gmrd", [1, 32], F32))

        P.emit("gpsimd", lambda E: E.memset(identb[:], 0.0))
        P.emit("gpsimd", lambda E: E.affine_select(
            out=identb[:], in_=identb[:], compare_op=Alu.not_equal,
            fill=1.0, base=0, pattern=[[-1, 128]], channel_multiplier=1))
        P.emit("gpsimd", lambda E: E.memset(identf[:], 0.0))
        P.emit("gpsimd", lambda E: E.affine_select(
            out=identf[:], in_=identf[:], compare_op=Alu.not_equal,
            fill=1.0, base=0, pattern=[[-1, 128]], channel_multiplier=1))
        P.emit("gpsimd", lambda E: E.memset(ones1[:], 1.0))
        P.emit("gpsimd", lambda E: E.memset(onescol[:], 1.0))
        P.emit("gpsimd", lambda E: E.memset(S["dinv"][:], 1.0 / D))
        P.emit("gpsimd", lambda E: E.memset(S["epsc"][:], EPS))
        P.emit("gpsimd", lambda E: E.memset(S["c127"][:], 127.0))
        P.emit("gpsimd", lambda E: E.memset(S["ones1b"][:], 1.0), inc="GP")
        gp_const = P.val("GP")

        P.dma("sync", wxs[:], T["wx"], "LD")
        P.wait("vector", "LD", P.val("LD"))
        P.emit("vector", lambda E: E.tensor_copy(wxh[:], wxs[:]))
        P.emit("vector", lambda E: E.tensor_tensor(
            xtmp[0:96, :], wxs[:], wxh[:], Alu.subtract))
        P.emit("vector", lambda E: E.tensor_copy(wxl[:], xtmp[0:96, :]), inc="VE")
        P.wait("tensor", "VE", P.val("VE"))
        P.wait("tensor", "GP", gp_const)
        tsq = 0
        for k in range(NKD):
            for src, dst in ((wxh, wxTh), (wxl, wxTl)):
                if tsq >= 2:
                    P.wait("tensor", "SC", P.val("SC") - 1)
                P.emit("tensor", lambda E, s=src, k=k, p=tsq % 2: E.transpose(
                    psb[p][:, 0:96], s[:, k * 128:(k + 1) * 128],
                    identb[0:96, 0:96]), inc="TE")
                P.wait("scalar", "TE", P.val("TE"))
                P.emit("scalar", lambda E, d=dst, k=k, p=tsq % 2: E.copy(
                    d[:, k * 96:(k + 1) * 96], psb[p][:, 0:96]), inc="SC")
                tsq += 1

        for tt in range(NTT):
            P.wait("sync", "VE", P.val("VE"))
            P.dma("sync", xtk[:], T["x_tok"][tt * 128:(tt + 1) * 128, :], "LD")
            P.wait("vector", "LD", P.val("LD"))
            P.emit("vector", lambda E: E.tensor_copy(xth[:], xtk[:]))
            P.emit("vector", lambda E: E.tensor_tensor(
                xtmp[:], xtk[:], xth[:], Alu.subtract))
            P.emit("vector", lambda E: E.tensor_copy(xtl[:], xtmp[:]), inc="VE")
            P.wait("tensor", "VE", P.val("VE"))
            for k in range(NKD):
                for src, dst in ((xth, xTh), (xtl, xTl)):
                    P.wait("tensor", "SC", P.val("SC") - 1)
                    P.emit("tensor", lambda E, s=src, k=k, p=tsq % 2: E.transpose(
                        psb[p][:, 0:128], s[:, k * 128:(k + 1) * 128], identb[:]),
                        inc="TE")
                    P.wait("scalar", "TE", P.val("TE"))
                    P.emit("scalar", lambda E, d=dst, k=k, p=tsq % 2: E.copy(
                        d[:, k * 128:(k + 1) * 128], psb[p][:, 0:128]), inc="SC")
                    tsq += 1
            P.wait("tensor", "SC", P.val("SC"))
            n_mm = NKD * 4
            i_mm = 0
            for k in range(NKD):
                for lh in (wxTh, wxTl):
                    for rh in (xTh, xTl):
                        P.emit("tensor",
                               lambda E, k=k, lh=lh, rh=rh, f=(i_mm == 0),
                               st=(i_mm == n_mm - 1): E.matmul(
                                   psf[4][0:96, 0:128],
                                   lh[:, k * 96:(k + 1) * 96],
                                   rh[:, k * 128:(k + 1) * 128],
                                   start=f, stop=st),
                               inc=("TE" if i_mm == n_mm - 1 else None))
                        i_mm += 1
            P.wait("vector", "TE", P.val("TE"))
            P.emit("vector", lambda E, tt=tt: E.tensor_copy(
                xpmine[:, tt * 128:(tt + 1) * 128], psf[4][0:96, 0:128]),
                inc="VE")
            P.wait("tensor", "VE", P.val("VE"))
        P.wait("sync", "VE", P.val("VE"))
        P.dma("sync", T["ag1_in"].ap(), xpmine[:], "ST")
        _ag(P, T, "ag1_in", "ag1_out", "CC_AG1", marks)

        wspec = [
            ("wgTd", "wgq_in", D // 128, 1024, 0),
            ("wvTd", "wvq_in", D // 128, 1024, 1),
            ("woTd", "woq_in", 1024 // 128, 2048, 2),
        ]
        for wsrc, wdst, ntile, width, widx in wspec:
            for it in range(ntile):
                buf = wti[it % 2]
                P.wait("sync", "VE", P.val("VE"))
                P.dma("sync", buf[:, 0:width],
                      T[wsrc][it * 128:(it + 1) * 128, :], "LD")
                P.wait("vector", "LD", P.val("LD"))
                P.emit("vector", lambda E, b=buf, w=width: E.tensor_reduce(
                    wred[:], b[:, 0:w], AXX, Alu.add, apply_absolute_value=True))
                P.emit("vector", lambda E: E.drain())
                if it == 0:
                    P.emit("vector", lambda E: E.tensor_copy(wacc[:], wred[:]),
                           inc="VE")
                else:
                    P.emit("vector", lambda E: E.tensor_tensor(
                        wacc[:], wacc[:], wred[:], Alu.add), inc="VE")
            P.wait("tensor", "VE", P.val("VE"))
            P.emit("tensor", lambda E: E.matmul(
                psf[5][0:1, 0:1], wacc[:], onescol[:], start=True, stop=True),
                inc="TE")
            P.wait("vector", "TE", P.val("TE"))
            P.emit("vector", lambda E, wi=widx: E.tensor_scalar(
                gmsb[0:1, wi:wi + 1], psf[5][0:1, 0:1], 1.0 / (HID * D), None,
                op0=Alu.mult), inc="VE")
            P.wait("tensor", "VE", P.val("VE"))
        P.wait("sync", "VE", P.val("VE"))
        P.dma("sync", T["gm_in"].ap(), gmsb[0:1, 0:4], "ST")
        _ag(P, T, "gm_in", "gm_out", "CC_GM", marks)
        P.wait("sync", "CC", marks["CC_GM"])
        P.dma("sync", gmrd[0:1, 0:32], T["gm_out"].ap(), "LD")
        P.wait("vector", "LD", P.val("LD"))
        gm3 = gmrd[0:1, 0:32].rearrange("p (r f) -> p r f", f=4)
        for widx in range(3):
            P.emit("vector", lambda E, wi=widx: E.tensor_reduce(
                gmsb[0:1, wi:wi + 1], gm3[:, :, wi], AXX, Alu.add))
        P.emit("vector", lambda E: E.drain(), inc="VE")
        P.wait("scalar", "VE", P.val("VE"))
        P.emit("scalar", lambda E: E.copy(gmsb[0:1, 4:7], gmsb[0:1, 0:3]),
               inc="SC")
        P.wait("vector", "SC", P.val("SC"))
        P.emit("vector", lambda E: E.tensor_scalar(
            gmsb[0:1, 0:3], gmsb[0:1, 4:7], EPS, None, op0=Alu.max))
        P.emit("vector", lambda E: E.drain())
        P.emit("vector", lambda E: E.tensor_scalar(
            gmsb[0:1, 4:7], gmsb[0:1, 0:3], 1.0 / 127.0, None, op0=Alu.mult))
        P.emit("vector", lambda E: E.drain())
        P.emit("vector", lambda E: E.reciprocal(gmsb[0:1, 0:3], gmsb[0:1, 0:3]))
        P.emit("vector", lambda E: E.drain(), inc="VE")
        P.wait("tensor", "VE", P.val("VE"))
        P.emit("tensor", lambda E: E.matmul(
            psf[5][:, 0:8], ones1[0:1, :], gmsb[0:1, 0:8], start=True,
            stop=True), inc="TE")
        P.wait("vector", "TE", P.val("TE"))
        P.emit("vector", lambda E: E.tensor_copy(wgam[:], psf[5][:, 0:8]),
               inc="VE")

        P.barrier()


def _scan(P, nc, T, S, marks):
    psf = S["psf"]
    identf = S["identf"]
    xpj_h, xpj_l = S["xpj_h"], S["xpj_l"]
    NP = B * NCH

    with ExitStack() as c1:
        E1 = c1.enter_context
        wdts = E1(nc.sbuf_tensor("wdts", [RDT, DSL], F32))
        wdth = E1(nc.sbuf_tensor("wdth", [RDT, DSL], BF16))
        wdtl = E1(nc.sbuf_tensor("wdtl", [RDT, DSL], BF16))
        wdtmp = E1(nc.sbuf_tensor("wdtmp", [RDT, DSL], F32))
        bdts = E1(nc.sbuf_tensor("bdts", [128, 2], F32))
        alogs = E1(nc.sbuf_tensor("alogs", [128, 2 * NST], F32))
        atile = E1(nc.sbuf_tensor("atile", [128, 2 * NST], F32))
        dp1 = E1(nc.sbuf_tensor("dp1", [128, 2], F32))
        xraw = [E1(nc.sbuf_tensor(f"xraw{i}", [128, 4 * DSL], F32))
                for i in range(2)]
        xT = [E1(nc.sbuf_tensor(f"xT{i}", [128, 2 * CH], F32))
              for i in range(2)]
        dtt = [E1(nc.sbuf_tensor(f"dtt{i}", [128, 2 * CH], F32))
               for i in range(2)]
        dtx = [E1(nc.sbuf_tensor(f"dtx{i}", [128, 2 * CH], F32))
               for i in range(2)]
        spm = E1(nc.sbuf_tensor("spm", [128, 2 * CH], F32))
        spa = E1(nc.sbuf_tensor("spa", [128, 2 * CH], F32))
        abuf = E1(nc.sbuf_tensor("abuf", [128, 4 * CH], F32))
        wbuf = E1(nc.sbuf_tensor("wbuf", [128, 2 * CH], F32))
        htr = [E1(nc.sbuf_tensor(f"htr{i}", [128, 2 * CH], F32))
               for i in range(2)]
        mbf = [E1(nc.sbuf_tensor(f"mbf{i}", [128, 2 * CH], F32))
               for i in range(2)]
        csb = E1(nc.sbuf_tensor("csb", [128, 2 * CH], F32))
        yacc = E1(nc.sbuf_tensor("yacc", [128, 2 * CH], F32))
        ztile = E1(nc.sbuf_tensor("ztile", [128, 2 * CH], F32))
        zstage = E1(nc.sbuf_tensor("zstage", [128, 4 * DSL], F32))
        hst = E1(nc.sbuf_tensor("hst", [128, 4 * NST], F32))

        for eng in ("sync", "vector", "scalar", "tensor", "gpsimd"):
            P.wait(eng, "CC", marks["CC_GM"])
        P.dma("sync", wdts[:], T["wdtT"], "LD")
        P.dma("sync", bdts[:], T["bdt"], "LD")
        P.dma("sync", alogs[:], T["alog"], "LD")
        P.dma("sync", dp1[:], T["dvec"], "LD")
        P.wait("vector", "LD", P.val("LD"))
        P.emit("vector", lambda E: E.tensor_copy(wdth[:], wdts[:]))
        P.emit("vector", lambda E: E.tensor_tensor(
            wdtmp[:], wdts[:], wdth[:], Alu.subtract))
        P.emit("vector", lambda E: E.tensor_copy(wdtl[:], wdtmp[:]))
        P.emit("vector", lambda E: E.tensor_scalar(
            dp1[:], dp1[:], 1.0, None, op0=Alu.add), inc="VE")
        P.wait("scalar", "LD", P.val("LD"))
        P.emit("scalar", lambda E: E.activation(atile[:], alogs[:], Act.Exp),
               inc="SC")
        P.wait("vector", "SC", P.val("SC"))
        P.emit("vector", lambda E: E.tensor_scalar(
            atile[:], atile[:], -1.0, None, op0=Alu.mult), inc="VE")
        P.emit("vector", lambda E: E.memset(hst[:], 0.0), inc="GP")
        gp_hst = P.val("GP")

        P.wait("sync", "CC", marks["CC_AG1"])
        with ExitStack() as cx:
            xprojT = cx.enter_context(
                nc.sbuf_tensor("xprojT", [96, B * L], F32))
            losc = cx.enter_context(nc.sbuf_tensor("losc", [96, 1024], F32))
            for r in range(NCORE):
                P.dma("sync", xprojT[:, r * TOK:(r + 1) * TOK],
                      T["ag1_out"][r * 96:(r + 1) * 96, :], "LD")
            P.wait("vector", "LD", P.val("LD"))
            P.emit("vector", lambda E: E.tensor_copy(xpj_h[:], xprojT[:, :]))
            for cs in range(0, B * L, 1024):
                P.emit("vector", lambda E, cs=cs: E.tensor_tensor(
                    losc[:, :], xprojT[:, cs:cs + 1024],
                    xpj_h[:, cs:cs + 1024], Alu.subtract))
                P.emit("vector", lambda E, cs=cs: E.tensor_copy(
                    xpj_l[:, cs:cs + 1024], losc[:, :]),
                    inc=("VE" if cs == B * L - 1024 else None))
        stg_h = E1(nc.sbuf_tensor("stg_h", [1, 32 * CH], BF16))
        stg_l = E1(nc.sbuf_tensor("stg_l", [1, 32 * CH], BF16))
        qwi = [E1(nc.sbuf_tensor(f"qwi{i}", [128, 1024], F32))
               for i in range(2)]
        qwo = [E1(nc.sbuf_tensor(f"qwo{i}", [128, 1024], BF16))
               for i in range(2)]
        qtmp = E1(nc.sbuf_tensor("qtmp", [128, 1024], F32))
        qcst = E1(nc.sbuf_tensor("qcst", [128, 4], F32))
        ve_prep = P.val("VE")
        P.wait("tensor", "VE", ve_prep)
        P.wait("tensor", "GP", gp_hst)
        P.wait("vector", "GP", gp_hst)
        P.wait("scalar", "VE", ve_prep)
        P.wait("sync", "VE", ve_prep)

        ld_ready, ld_stage = {}, {}
        xtr_done, xT_ready, ztr_done, zst_done = {}, {}, {}, {}
        sc_sp, gp_dtt, gp_dtx, ve_zt = {}, {}, {}, {}
        te_bc, sc_exp, sc_evC, ve_tts, ve_m0, gp_n = {}, {}, {}, {}, {}, {}
        mf_marks = []

        def emit_head(li):
            b, ci = divmod(li, NCH)
            colb = b * L + ci * CH
            P.wait("tensor", "LD", ld_ready[li])
            first_evac = True
            for s in range(4):
                for d in range(2):
                    if len(mf_marks) >= 2:
                        P.wait("tensor", "SC", mf_marks[-2])
                    P.emit("tensor", lambda E, s=s, d=d, li=li: E.transpose(
                        psf[4 + d][:, 0:128],
                        xraw[li % 2][:, s * DSL + d * 128:s * DSL + (d + 1) * 128],
                        identf[:]), inc="TE")
                    P.wait("scalar", "TE", P.val("TE"))
                    if first_evac:
                        if li - 2 in gp_dtx:
                            P.wait("scalar", "GP", gp_dtx[li - 2])
                        if li - 2 in ve_zt:
                            P.wait("scalar", "VE", ve_zt[li - 2])
                        first_evac = False
                    P.emit("scalar", lambda E, s=s, d=d, li=li: E.copy(
                        xT[li % 2][:, d * CH + s * 128:d * CH + (s + 1) * 128],
                        psf[4 + d][:, 0:128]), inc="SC")
                    mf_marks.append(P.val("SC"))
            xT_ready[li] = P.val("SC")
            xtr_done[li] = P.val("TE")
            if DEBUG_STOP == "mixer" and DEBUG_DUMPS:
                P.wait("sync", "TE", xtr_done[li])
                P.dma("sync", T["xrdbg"][:, li * 4 * DSL:(li + 1) * 4 * DSL],
                      xraw[li % 2][:], "ST")
            for d in range(2):
                if len(mf_marks) >= 2:
                    P.wait("tensor", "SC", mf_marks[-2])
                mmi = 0
                for lh in (wdth, wdtl):
                    for rh in (xpj_h, xpj_l):
                        P.emit("tensor", lambda E, lh=lh, rh=rh, d=d,
                               f=(mmi == 0), st=(mmi == 3), cb=colb:
                               E.matmul(psf[4 + d][:, 0:CH],
                                        lh[:, d * 128:(d + 1) * 128],
                                        rh[0:RDT, cb:cb + CH],
                                        start=f, stop=st),
                               inc=("TE" if mmi == 3 else None))
                        mmi += 1
                P.wait("scalar", "TE", P.val("TE"))
                if d == 0 and li - 1 in gp_dtt:
                    P.wait("scalar", "GP", gp_dtt[li - 1])
                P.emit("scalar", lambda E, d=d: E.activation(
                    spm[:, d * CH:(d + 1) * CH], psf[4 + d][:, 0:CH],
                    Act.Relu, bias=bdts[:, d:d + 1]))
                P.emit("scalar", lambda E, d=d: E.activation(
                    spa[:, d * CH:(d + 1) * CH], psf[4 + d][:, 0:CH],
                    Act.Abs, bias=bdts[:, d:d + 1]), inc="SC")
                mf_marks.append(P.val("SC"))
                P.emit("scalar", lambda E, d=d: E.activation(
                    spa[:, d * CH:(d + 1) * CH], spa[:, d * CH:(d + 1) * CH],
                    Act.Exp, scale=-1.0))
                P.emit("scalar", lambda E, d=d: E.activation(
                    spa[:, d * CH:(d + 1) * CH], spa[:, d * CH:(d + 1) * CH],
                    Act.Ln, bias=1.0), inc="SC")
            sc_sp[li] = P.val("SC")
            P.wait("gpsimd", "SC", sc_sp[li])
            if (li - 1) * NST - 1 in sc_exp:
                P.wait("gpsimd", "SC", sc_exp[(li - 1) * NST - 1])
            P.emit("gpsimd", lambda E, li=li: E.tensor_tensor(
                dtt[li % 2][:], spm[:], spa[:], Alu.add), inc="GP")
            gp_dtt[li] = P.val("GP")
            P.wait("gpsimd", "SC", xT_ready[li])
            if (li - 1) * NST - 1 in ve_tts:
                P.wait("gpsimd", "VE", ve_tts[(li - 1) * NST - 1])
            P.emit("gpsimd", lambda E, li=li: E.tensor_tensor(
                dtx[li % 2][:], dtt[li % 2][:], xT[li % 2][:], Alu.mult),
                inc="GP")
            gp_dtx[li] = P.val("GP")
            if DEBUG_STOP == "mixer" and DEBUG_DUMPS:
                P.wait("sync", "GP", P.val("GP"))
                P.dma("sync", T["dtdbg"][:, li * 2 * CH:(li + 1) * 2 * CH],
                      dtt[li % 2][:], "ST")
                P.dma("sync", T["xdbg"][:, li * 2 * CH:(li + 1) * 2 * CH],
                      dtx[li % 2][:], "ST")
                P.dma("sync", T["xtdbg"][:, li * 2 * CH:(li + 1) * 2 * CH],
                      xT[li % 2][:], "ST")
                for eng in ("vector", "scalar", "tensor", "gpsimd"):
                    P.wait(eng, "ST", P.val("ST"))

        def emit_stage(li):
            b, ci = divmod(li, NCH)
            colb = b * L + ci * CH
            if li * NST - 1 in te_bc:
                P.wait("sync", "TE", te_bc[li * NST - 1])
            P.dma("sync", stg_h[0:1, :], xpj_h[RDT:96, colb:colb + CH], "LD")
            P.dma("sync", stg_l[0:1, :], xpj_l[RDT:96, colb:colb + CH], "LD")
            ld_stage[li] = P.val("LD")

        def emit_n(li, n):
            nseq = li * NST + n
            sl = nseq % 2
            b, ci = divmod(li, NCH)
            if nseq - 2 in ve_m0:
                P.wait("tensor", "VE", ve_m0[nseq - 2])
                P.wait("tensor", "SC", sc_evC[nseq - 2])
            P.wait("tensor", "LD", ld_stage[li])
            for pj, base in ((0, n), (1, NST + n)):
                for hl, last in ((stg_h, False), (stg_l, True)):
                    P.emit("tensor", lambda E, pj=pj, base=base, hl=hl,
                           last=last, sl=sl: E.matmul(
                               psf[2 * sl + pj][:, 0:CH], S["ones1b"][0:1, :],
                               hl[0:1, base * CH:(base + 1) * CH],
                               start=not last, stop=last),
                           inc=("TE" if pj == 1 and last else None))
            te_bc[nseq] = P.val("TE")
            if nseq - 2 in ve_tts:
                P.wait("scalar", "VE", ve_tts[nseq - 2])
            if n == 0:
                P.wait("scalar", "GP", gp_dtt[li])
            for d in range(2):
                P.emit("scalar", lambda E, d=d, n=n, sl=sl, li=li:
                       E.activation(
                           abuf[:, sl * 2 * CH + d * CH:
                                sl * 2 * CH + (d + 1) * CH],
                           dtt[li % 2][:, d * CH:(d + 1) * CH], Act.Exp,
                           scale=atile[:, d * NST + n:d * NST + n + 1]))
            P.emit("scalar", lambda E: E.drain(), inc="SC")
            sc_exp[nseq] = P.val("SC")
            if nseq - 2 in gp_n:
                P.wait("scalar", "GP", gp_n[nseq - 2])
            P.wait("scalar", "TE", te_bc[nseq])
            P.emit("scalar", lambda E, sl=sl: E.copy(
                csb[:, sl * CH:(sl + 1) * CH], psf[2 * sl + 1][:, 0:CH]),
                inc="SC")
            sc_evC[nseq] = P.val("SC")
            P.wait("vector", "TE", te_bc[nseq])
            P.wait("vector", "SC", sc_exp[nseq])
            if nseq - 2 in gp_n:
                P.wait("vector", "GP", gp_n[nseq - 2])
            if n == 0:
                P.wait("vector", "GP", gp_dtx[li])
            for d in range(2):
                hcol = (2 * b + d) * NST + n
                P.emit("vector", lambda E, d=d, sl=sl, li=li: E.tensor_tensor(
                    wbuf[:, d * CH:(d + 1) * CH],
                    dtx[li % 2][:, d * CH:(d + 1) * CH],
                    psf[2 * sl][:, 0:CH], Alu.mult))
                P.emit("vector", lambda E, d=d, sl=sl, hc=hcol:
                       E.tensor_tensor_scan(
                           htr[sl][:, d * CH:(d + 1) * CH],
                           abuf[:, sl * 2 * CH + d * CH:
                                sl * 2 * CH + (d + 1) * CH],
                           wbuf[:, d * CH:(d + 1) * CH],
                           hst[:, hc:hc + 1],
                           op0=Alu.mult, op1=Alu.add),
                       inc=("VE" if d == 1 else None))
            ve_tts[nseq] = P.val("VE")
            P.emit("vector", lambda E, sl=sl: E.tensor_tensor(
                mbf[sl][:, 0:CH], htr[sl][:, 0:CH], psf[2 * sl + 1][:, 0:CH],
                Alu.mult), inc="VE")
            ve_m0[nseq] = P.val("VE")
            P.emit("vector", lambda E: E.drain())
            for d in range(2):
                hcol = (2 * b + d) * NST + n
                P.emit("vector", lambda E, d=d, sl=sl, hc=hcol: E.tensor_copy(
                    hst[:, hc:hc + 1],
                    htr[sl][:, (d + 1) * CH - 1:(d + 1) * CH]))
            if DEBUG_STOP == "mixer" and DEBUG_DUMPS and li == 1:
                P.wait("sync", "VE", P.val("VE"))
                P.wait("sync", "GP", P.val("GP"))
                P.dma("sync", T["abdbg"][:, n * 4 * CH:n * 4 * CH + 2 * CH],
                      abuf[:, sl * 2 * CH:(sl + 1) * 2 * CH], "ST")
                P.dma("sync", T["abdbg"][:, n * 4 * CH + 2 * CH:
                                         (n + 1) * 4 * CH],
                      htr[sl][:], "ST")
                P.dma("sync", T["atdbg"], atile[:], "ST")
                for eng in ("vector", "scalar", "tensor", "gpsimd"):
                    P.wait(eng, "ST", P.val("ST"))
            P.wait("gpsimd", "VE", ve_tts[nseq])
            P.wait("gpsimd", "SC", sc_evC[nseq])
            P.emit("gpsimd", lambda E, sl=sl: E.tensor_tensor(
                mbf[sl][:, CH:2 * CH], htr[sl][:, CH:2 * CH],
                csb[:, sl * CH:(sl + 1) * CH], Alu.mult))
            P.wait("gpsimd", "VE", ve_m0[nseq])
            if n == 0:
                if li - 1 in ve_zt:
                    P.wait("gpsimd", "VE", ve_zt[li - 1])
                P.emit("gpsimd", lambda E, sl=sl: E.tensor_copy(
                    yacc[:], mbf[sl][:]), inc="GP")
            else:
                P.emit("gpsimd", lambda E, sl=sl: E.tensor_tensor(
                    yacc[:], yacc[:], mbf[sl][:], Alu.add), inc="GP")
            gp_n[nseq] = P.val("GP")

        def emit_ztile(li):
            P.wait("vector", "GP", gp_n[li * NST + NST - 1])
            if li - 1 in ztr_done:
                P.wait("vector", "TE", ztr_done[li - 1])
            for d in range(2):
                P.emit("vector", lambda E, d=d, li=li: E.scalar_tensor_tensor(
                    ztile[:, d * CH:(d + 1) * CH],
                    xT[li % 2][:, d * CH:(d + 1) * CH],
                    dp1[:, d:d + 1], yacc[:, d * CH:(d + 1) * CH],
                    op0=Alu.mult, op1=Alu.add),
                    inc=("VE" if d == 1 else None))
            ve_zt[li] = P.val("VE")
            if DEBUG_STOP == "mixer" and DEBUG_DUMPS:
                P.wait("sync", "VE", P.val("VE"))
                P.dma("sync", T["ydbg"][:, li * 2 * CH:(li + 1) * 2 * CH],
                      yacc[:], "ST")
                for eng in ("vector", "scalar", "tensor", "gpsimd"):
                    P.wait(eng, "ST", P.val("ST"))

        def emit_ztrans(li):
            b, ci = divmod(li, NCH)
            P.wait("tensor", "VE", ve_zt[li])
            first = True
            for s in range(4):
                for d in range(2):
                    if len(mf_marks) >= 2:
                        P.wait("tensor", "SC", mf_marks[-2])
                    P.emit("tensor", lambda E, s=s, d=d: E.transpose(
                        psf[4 + d][:, 0:128],
                        ztile[:, d * CH + s * 128:d * CH + (s + 1) * 128],
                        identf[:]), inc="TE")
                    P.wait("scalar", "TE", P.val("TE"))
                    if first and li - 1 in zst_done:
                        P.wait("scalar", "ST", zst_done[li - 1])
                        first = False
                    P.emit("scalar", lambda E, s=s, d=d: E.copy(
                        zstage[:, s * DSL + d * 128:s * DSL + (d + 1) * 128],
                        psf[4 + d][:, 0:128]), inc="SC")
                    mf_marks.append(P.val("SC"))
            ztr_done[li] = P.val("TE")
            P.wait("sync", "SC", P.val("SC"))
            rowb = b * L + ci * CH
            for s in range(4):
                P.dma("sync",
                      T["z_in"][rowb + s * 128:rowb + (s + 1) * 128, :],
                      zstage[:, s * DSL:(s + 1) * DSL], "ST")
            zst_done[li] = P.val("ST")

        wgam = S["wgam"]
        P.emit("gpsimd", lambda E: E.memset(qcst[:, 0:1], MAGIC))
        P.emit("gpsimd", lambda E: E.memset(qcst[:, 1:2], -(MAGIC - 1.0)))
        P.emit("gpsimd", lambda E: E.memset(qcst[:, 2:3], 2.0))
        P.emit("gpsimd", lambda E: E.memset(qcst[:, 3:4], 1.0), inc="GP")
        P.wait("scalar", "GP", P.val("GP"))
        qdesc = []
        for wsrc, wdst, nrt, wid in (("wgTd", "wgq_in", 16, 0),
                                     ("wvTd", "wvq_in", 16, 1)):
            for it in range(nrt):
                qdesc.append((wsrc, wdst, it * 128, 0, wid))
        for it in range(8):
            for cb in (0, 1024):
                qdesc.append(("woTd", "woq_in", it * 128, cb, 2))
        q_sc, q_st = {}, {}
        qpos = [0]

        def q_pump(k):
            for _ in range(k):
                i = qpos[0]
                if i >= len(qdesc):
                    return
                qpos[0] += 1
                wsrc, wdst, r0, c0, wid = qdesc[i]
                if i - 2 in q_sc:
                    P.wait("sync", "SC", q_sc[i - 2])
                    P.wait("sync", "ST", q_st[i - 2])
                P.dma("sync", qwi[i % 2][:],
                      T[wsrc][r0:r0 + 128, c0:c0 + 1024], "LD")
                P.wait("scalar", "LD", P.val("LD"))
                P.emit("scalar", lambda E, i=i, wid=wid: E.activation(
                    qtmp[:], qwi[i % 2][:], Act.Copy,
                    scale=wgam[:, wid:wid + 1], bias=MAGIC))
                P.emit("scalar", lambda E: E.activation(
                    qtmp[:], qtmp[:], Act.Relu, bias=qcst[:, 1:2]))
                P.emit("scalar", lambda E: E.activation(
                    qtmp[:], qtmp[:], Act.Relu, scale=-1.0, bias=qcst[:, 2:3]))
                P.emit("scalar", lambda E, i=i: E.activation(
                    qwo[i % 2][:], qtmp[:], Act.Copy, scale=-1.0, bias=1.0))
                P.emit("scalar", lambda E: E.drain(), inc="SC")
                q_sc[i] = P.val("SC")
                P.wait("sync", "SC", q_sc[i])
                P.dma("sync", T[wdst][r0:r0 + 128, c0:c0 + 1024],
                      qwo[i % 2][:], "ST")
                q_st[i] = P.val("ST")

        for s in range(4):
            P.dma("sync", xraw[0][:, s * DSL:(s + 1) * DSL],
                  T["x_dsl"][0, s * 128:(s + 1) * 128, :], "LD")
        ld_ready[0] = P.val("LD")
        emit_stage(0)
        emit_head(0)
        for li in range(NP):
            if li + 1 < NP:
                nb, nci = divmod(li + 1, NCH)
                if li - 1 in xtr_done:
                    P.wait("sync", "TE", xtr_done[li - 1])
                for s in range(4):
                    P.dma("sync",
                          xraw[(li + 1) % 2][:, s * DSL:(s + 1) * DSL],
                          T["x_dsl"][nb,
                                     nci * CH + s * 128:nci * CH + (s + 1) * 128,
                                     :], "LD")
                ld_ready[li + 1] = P.val("LD")
            if li >= 1:
                emit_ztile(li - 1)
            for n in range(NST):
                emit_n(li, n)
                if li >= 1 and n in (3, 7, 11, 15):
                    q_pump(1)
                if n == 2 and li >= 1:
                    emit_ztrans(li - 1)
                if n == 7 and li + 1 < NP:
                    emit_head(li + 1)
                if n == 15 and li + 1 < NP:
                    emit_stage(li + 1)
        emit_ztile(NP - 1)
        emit_ztrans(NP - 1)
        q_pump(len(qdesc))
        P.wait("sync", "ST", P.val("ST"))


def _ffn(P, nc, T, S, marks):
    psf, psb = S["psf"], S["psb"]
    identb, ones1, wgam = S["identb"], S["ones1"], S["wgam"]
    hold = {}

    with ExitStack() as cO:
        EO = cO.enter_context
        sxg = EO(nc.sbuf_tensor("sxg", [128, NTT], F32))
        sxv = EO(nc.sbuf_tensor("sxv", [128, NTT], F32))
        sO = EO(nc.sbuf_tensor("sO", [128, NTT], F32))
        gvmax = EO(nc.sbuf_tensor("gvmax", [128, NTT], F32))
        for eng in ("sync", "vector", "scalar", "tensor", "gpsimd"):
            P.wait(eng, "CC", marks["CC_Z"])
        P.emit("gpsimd", lambda E: E.memset(gvmax[:], 0.0), inc="GP")
        P.wait("vector", "GP", P.val("GP"))

        def emit_ln(src_fn, dst_fn, tn):
            P.emit("vector", lambda E: E.reduce_sum(tn["red"][:], src_fn(), AXX))
            P.emit("vector", lambda E: E.drain(), inc="VE")
            P.wait("scalar", "VE", P.val("VE"))
            P.emit("scalar", lambda E: E.activation(
                tn["sqs"][:], src_fn(), Act.Square))
            P.emit("scalar", lambda E: E.drain(), inc="SC")
            P.wait("vector", "SC", P.val("SC"))
            P.emit("vector", lambda E: E.reduce_sum(tn["ms"][:], tn["sqs"][:], AXX))
            P.emit("vector", lambda E: E.drain())
            P.emit("vector", lambda E: E.tensor_tensor(
                tn["mu"][:], tn["red"][:], S["dinv"][:], Alu.mult))
            P.emit("vector", lambda E: E.drain())
            P.emit("vector", lambda E: E.tensor_tensor(
                tn["t1"][:], tn["mu"][:], tn["mu"][:], Alu.mult))
            P.emit("vector", lambda E: E.tensor_tensor(
                tn["ms"][:], tn["ms"][:], S["dinv"][:], Alu.mult))
            P.emit("vector", lambda E: E.drain())
            P.emit("vector", lambda E: E.tensor_tensor(
                tn["vpe"][:], tn["ms"][:], tn["t1"][:], Alu.subtract))
            P.emit("vector", lambda E: E.drain())
            P.emit("vector", lambda E: E.tensor_tensor(
                tn["vpe"][:], tn["vpe"][:], S["epsc"][:], Alu.add))
            P.emit("vector", lambda E: E.drain(), inc="VE")
            P.wait("scalar", "VE", P.val("VE"))
            P.emit("scalar", lambda E: E.activation(
                tn["y0"][:], tn["vpe"][:], Act.Ln))
            P.emit("scalar", lambda E: E.drain())
            P.emit("scalar", lambda E: E.activation(
                tn["y0"][:], tn["y0"][:], Act.Exp, scale=-0.5))
            P.emit("scalar", lambda E: E.drain(), inc="SC")
            P.wait("vector", "SC", P.val("SC"))
            P.emit("vector", lambda E: E.tensor_tensor(
                tn["t1"][:], tn["y0"][:], tn["y0"][:], Alu.mult))
            P.emit("vector", lambda E: E.drain())
            P.emit("vector", lambda E: E.tensor_tensor(
                tn["t1"][:], tn["t1"][:], tn["vpe"][:], Alu.mult))
            P.emit("vector", lambda E: E.drain())
            P.emit("vector", lambda E: E.tensor_scalar(
                tn["t1"][:], tn["t1"][:], -0.5, 1.5, op0=Alu.mult, op1=Alu.add))
            P.emit("vector", lambda E: E.drain())
            P.emit("vector", lambda E: E.tensor_tensor(
                tn["rstd"][:], tn["y0"][:], tn["t1"][:], Alu.mult))
            P.emit("vector", lambda E: E.drain())
            P.emit("vector", lambda E: E.tensor_scalar(
                dst_fn(), src_fn(), tn["mu"][:], tn["rstd"][:],
                op0=Alu.subtract, op1=Alu.mult), inc="VE")

        with ExitStack() as cA:
            EA = cA.enter_context
            XqT = EA(nc.sbuf_tensor("XqT", [128, NKD * 1024], BF16))
            lnsb = EA(nc.sbuf_tensor("lnsb", [128, 8 * NTT], F32))
            cA1 = EA(ExitStack())
            EA1 = cA1.enter_context
            zall = EA1(nc.sbuf_tensor("zall", [128, NTT * D], F32))
            sqs = EA1(nc.sbuf_tensor("sqs", [128, D], F32))
            z1 = EA1(nc.sbuf_tensor("z1", [128, D], F32))
            Xq = EA1(nc.sbuf_tensor("Xq", [128, D], BF16))
            tn = {k: EA1(nc.sbuf_tensor("ln_" + k, [128, 1], F32))
                  for k in ("red", "mu", "ms", "vpe", "y0", "t1", "rstd",
                            "gmx", "gam", "inv", "sx")}
            tn["sqs"] = sqs
            P.wait("sync", "CC", marks["CC_Z"])

            def _mk_zoffs(E):
                hold["zb"] = E.snap(E.partition_id() * TOK,
                                    min_val=0, max_val=(NCORE - 1) * TOK,
                                    guaranteed_mod_val=TOK)
                return E.nop()
            P.emit("sync", _mk_zoffs)
            z3 = T["z_out"].ap().rearrange("(r n) d -> r n d", r=NCORE)
            for r in range(NCORE):
                P.emit("sync", lambda E, r=r: E.dma_start(
                    out=T["z_my"][:, r * DSL:(r + 1) * DSL],
                    in_=z3[r][bass.ds(hold["zb"], TOK), :]),
                    inc="LD", k=16)
            P.wait("sync", "LD", P.val("LD"))
            for tt in range(NTT):
                P.dma("sync", zall[:, tt * D:(tt + 1) * D],
                      T["z_my"][tt * 128:(tt + 1) * 128, :], "LD")
            ld_all = P.val("LD")
            P.wait("gpsimd", "LD", ld_all)
            P.wait("gpsimd", "ST", P.val("ST"))
            for a, b in (("wgq_in", "wgq_out"), ("wvq_in", "wvq_out"),
                         ("woq_in", "woq_out")):
                P.emit("gpsimd", lambda E, x=T[a], y=T[b]:
                       E.collective_compute(
                           "AllGather", Alu.bypass,
                           replica_groups=[list(range(NCORE))],
                           ins=[x.ap().opt()], outs=[y.ap().opt()]),
                       inc="CC")
            marks["CC_W"] = P.val("CC")

            tsq = 0
            xqtr_done = {}
            sd_z1 = {}
            P.wait("vector", "LD", ld_all)
            for tt in range(NTT):
                if tt - 1 in sd_z1:
                    P.wait("vector", "SD", sd_z1[tt - 1])
                emit_ln(lambda tt=tt: zall[:, tt * D:(tt + 1) * D],
                        lambda: z1[:], tn)
                if DEBUG_STOP == "ffn":
                    for k_i, k_n in enumerate(("red", "ms", "mu", "vpe", "y0",
                                               "rstd", "t1", "t1")):
                        P.emit("vector", lambda E, tt=tt, k_i=k_i, k_n=k_n:
                               E.tensor_copy(lnsb[:, tt * 8 + k_i:tt * 8 + k_i + 1],
                                             tn[k_n][:]))
                    P.emit("vector", lambda E: E.nop(), inc="VE")
                ve_z1 = P.val("VE")
                P.wait("scalar", "VE", ve_z1)
                P.emit("scalar", lambda E, tt=tt: E.dma_start(
                    out=T["z1st"][tt * 128:(tt + 1) * 128, :], in_=z1[:]),
                    inc="SD", k=16)
                sd_z1[tt] = P.val("SD")
                P.emit("vector", lambda E: E.tensor_reduce(
                    tn["gmx"][:], z1[:], AXX, Alu.max,
                    apply_absolute_value=True))
                P.emit("vector", lambda E: E.drain())
                P.emit("vector", lambda E: E.tensor_tensor(
                    tn["gam"][:], tn["gmx"][:], S["epsc"][:], Alu.max))
                P.emit("vector", lambda E: E.drain())
                P.emit("vector", lambda E: E.reciprocal(
                    tn["inv"][:], tn["gam"][:]))
                P.emit("vector", lambda E: E.drain())
                P.emit("vector", lambda E: E.tensor_tensor(
                    tn["sx"][:], tn["inv"][:], S["c127"][:], Alu.mult))
                P.emit("vector", lambda E, tt=tt: E.scalar_tensor_tensor(
                    sxg[:, tt:tt + 1], tn["gam"][:], 1.0, wgam[:, 4:5],
                    op0=Alu.mult, op1=Alu.mult))
                P.emit("vector", lambda E, tt=tt: E.scalar_tensor_tensor(
                    sxv[:, tt:tt + 1], tn["gam"][:], 1.0, wgam[:, 5:6],
                    op0=Alu.mult, op1=Alu.mult))
                P.emit("vector", lambda E: E.drain())
                P.emit("vector", lambda E: E.tensor_scalar(
                    sqs[:], z1[:], tn["sx"][:], MAGIC, op0=Alu.mult,
                    op1=Alu.add))
                if tt - 1 in xqtr_done:
                    P.wait("vector", "TE", xqtr_done[tt - 1])
                P.emit("vector", lambda E: E.tensor_scalar(
                    Xq[:], sqs[:], MAGIC, None, op0=Alu.subtract), inc="VE")
                P.wait("tensor", "VE", P.val("VE"))
                for k in range(NKD):
                    P.wait("tensor", "SC", P.val("SC") - 1)
                    P.emit("tensor", lambda E, k=k, p=tsq % 2: E.transpose(
                        psb[p][:, 0:128], Xq[:, k * 128:(k + 1) * 128],
                        identb[:]), inc="TE")
                    P.wait("scalar", "TE", P.val("TE"))
                    P.emit("scalar", lambda E, k=k, tt=tt, p=tsq % 2: E.copy(
                        XqT[:, k * 1024 + tt * 128:k * 1024 + (tt + 1) * 128],
                        psb[p][:, 0:128]), inc="SC")
                    tsq += 1
                xqtr_done[tt] = P.val("TE")
            P.wait("tensor", "SC", P.val("SC"))

            cA1.close()
            wgsl = [EA(nc.sbuf_tensor(f"wgsl{i}", [128, HID], BF16))
                    for i in range(2)]
            wvsl = [EA(nc.sbuf_tensor(f"wvsl{i}", [128, HID], BF16))
                    for i in range(2)]
            gbr = [EA(nc.sbuf_tensor(f"gbr{i}", [1, 512], F32))
                   for i in range(2)]
            vbr = [EA(nc.sbuf_tensor(f"vbr{i}", [1, 512], F32))
                   for i in range(2)]
            gbj = EA(nc.sbuf_tensor("gbj", [128, 512], F32))
            vbj = EA(nc.sbuf_tensor("vbj", [128, 512], F32))
            upre = EA(nc.sbuf_tensor("upre", [128, 512], F32))
            gsil = EA(nc.sbuf_tensor("gsil", [128, 512], F32))
            vtile = EA(nc.sbuf_tensor("vtile", [128, 512], F32))
            gvst = [EA(nc.sbuf_tensor(f"gvst{i}", [128, 512], F32))
                    for i in range(2)]
            mjt = EA(nc.sbuf_tensor("mjt", [128, 1], F32))
            P.wait("sync", "CC", marks["CC_W"])
            ld_slab = {}
            mm_done = {}
            epi_done = {}
            sd_gv = {}
            for j in range(16):
                r, half = divmod(j, 2)
                if j - 2 in mm_done:
                    P.wait("sync", "TE", mm_done[j - 2])
                for k in range(NKD):
                    P.dma("sync", wgsl[j % 2][:, k * 512:(k + 1) * 512],
                          T["wgq_out"][r * D + k * 128:r * D + (k + 1) * 128,
                                       half * 512:(half + 1) * 512], "LD")
                    P.dma("sync", wvsl[j % 2][:, k * 512:(k + 1) * 512],
                          T["wvq_out"][r * D + k * 128:r * D + (k + 1) * 128,
                                       half * 512:(half + 1) * 512], "LD")
                P.dma("sync", gbr[j % 2][0:1, :],
                      T["gbias"][0:1, j * 512:(j + 1) * 512], "LD")
                P.dma("sync", vbr[j % 2][0:1, :],
                      T["vbias"][0:1, j * 512:(j + 1) * 512], "LD")
                ld_slab[j] = P.val("LD")
                if (j - 1) * NTT + NTT - 1 in epi_done:
                    P.wait("tensor", "VE", epi_done[(j - 1) * NTT + NTT - 1])
                P.wait("tensor", "LD", ld_slab[j])
                P.emit("tensor", lambda E, j=j: E.matmul(
                    psf[4][:, 0:512], ones1[0:1, :],
                    gbr[j % 2][0:1, 0:512],
                    start=True, stop=True))
                P.emit("tensor", lambda E, j=j: E.matmul(
                    psf[5][:, 0:512], ones1[0:1, :],
                    vbr[j % 2][0:1, 0:512],
                    start=True, stop=True), inc="TE")
                P.wait("vector", "TE", P.val("TE"))
                P.emit("vector", lambda E: E.tensor_copy(gbj[:], psf[4][:, 0:512]))
                P.emit("vector", lambda E: E.tensor_copy(vbj[:], psf[5][:, 0:512]),
                       inc="VE")
                for t in range(NTT):
                    gq = j * NTT + t
                    P.wait("tensor", "LD", ld_slab[j])
                    if gq - 2 in epi_done:
                        P.wait("tensor", "VE", epi_done[gq - 2])
                    for k in range(NKD):
                        P.emit("tensor", lambda E, k=k, t=t, j=j,
                               f=(k == 0), st=(k == NKD - 1): E.matmul(
                                   psf[t % 2][:, 0:512],
                                   XqT[:, k * 1024 + t * 128:k * 1024 + (t + 1) * 128],
                                   wgsl[j % 2][:, k * 512:(k + 1) * 512],
                                   start=f, stop=st),
                               inc=("TE" if k == NKD - 1 else None))
                    for k in range(NKD):
                        P.emit("tensor", lambda E, k=k, t=t, j=j,
                               f=(k == 0), st=(k == NKD - 1): E.matmul(
                                   psf[2 + t % 2][:, 0:512],
                                   XqT[:, k * 1024 + t * 128:k * 1024 + (t + 1) * 128],
                                   wvsl[j % 2][:, k * 512:(k + 1) * 512],
                                   start=f, stop=st),
                               inc=("TE" if k == NKD - 1 else None))
                    te_gv = P.val("TE")
                    P.wait("vector", "TE", te_gv)
                    P.emit("vector", lambda E, t=t: E.scalar_tensor_tensor(
                        upre[:], psf[t % 2][:, 0:512], sxg[:, t:t + 1], gbj[:],
                        op0=Alu.mult, op1=Alu.add), inc="VE")
                    P.wait("scalar", "VE", P.val("VE"))
                    P.emit("scalar", lambda E: E.activation(
                        gsil[:], upre[:], Act.Silu))
                    P.emit("scalar", lambda E: E.drain(), inc="SC")
                    P.emit("vector", lambda E, t=t: E.scalar_tensor_tensor(
                        vtile[:], psf[2 + t % 2][:, 0:512], sxv[:, t:t + 1],
                        vbj[:], op0=Alu.mult, op1=Alu.add))
                    P.wait("vector", "SC", P.val("SC"))
                    if gq - 2 in sd_gv:
                        P.wait("vector", "SD", sd_gv[gq - 2])
                    P.emit("vector", lambda E, sl=gq % 2: E.tensor_tensor(
                        gvst[sl][:], gsil[:], vtile[:], Alu.mult))
                    P.emit("vector", lambda E, sl=gq % 2: E.tensor_reduce(
                        mjt[:], gvst[sl][:], AXX, Alu.max,
                        apply_absolute_value=True))
                    P.emit("vector", lambda E: E.drain())
                    P.emit("vector", lambda E, t=t: E.tensor_tensor(
                        gvmax[:, t:t + 1], gvmax[:, t:t + 1], mjt[:], Alu.max),
                        inc="VE")
                    epi_done[gq] = P.val("VE")
                    P.wait("scalar", "VE", epi_done[gq])
                    P.emit("scalar", lambda E, t=t, j=j, sl=gq % 2: E.dma_start(
                        out=T["gv_dram"][t, j, :, :], in_=gvst[sl][:]),
                        inc="SD", k=16)
                    sd_gv[gq] = P.val("SD")
                mm_done[j] = P.val("TE")
            P.wait("sync", "SD", P.val("SD"))
            if DEBUG_STOP == "ffn":
                P.wait("sync", "VE", P.val("VE"))
                P.dma("sync", T["lndbg"], lnsb[:], "ST")
                P.wait("sync", "ST", P.val("ST"))
            P.barrier()

        with ExitStack() as cB:
            EB = cB.enter_context
            gvrow = [EB(nc.sbuf_tensor(f"gvrow{i}", [128, HID], F32))
                     for i in range(2)]
            gvq32 = EB(nc.sbuf_tensor("gvq32", [128, HID], F32))
            GVq = [EB(nc.sbuf_tensor(f"GVq{i}", [128, HID], BF16))
                   for i in range(2)]
            gvqTst = [EB(nc.sbuf_tensor(f"gvqTst{i}", [128, HID], BF16))
                      for i in range(2)]
            g2 = {k: EB(nc.sbuf_tensor("g2_" + k, [128, 1], F32))
                  for k in ("gam", "inv", "sgv")}
            tsq = 0
            ld_gv, ve_q, tr_done, ev_tr, st_q = {}, {}, {}, {}, {}
            for t in range(NTT):
                if t - 2 in ve_q:
                    P.wait("sync", "VE", ve_q[t - 2])
                for j in range(16):
                    P.dma("sync", gvrow[t % 2][:, j * 512:(j + 1) * 512],
                          T["gv_dram"][t, j, :, :], "LD")
                ld_gv[t] = P.val("LD")
                P.wait("vector", "LD", ld_gv[t])
                P.emit("vector", lambda E, t=t: E.tensor_scalar(
                    g2["gam"][:], gvmax[:, t:t + 1], EPS, None, op0=Alu.max))
                P.emit("vector", lambda E: E.drain())
                P.emit("vector", lambda E: E.reciprocal(
                    g2["inv"][:], g2["gam"][:]))
                P.emit("vector", lambda E: E.drain())
                P.emit("vector", lambda E: E.tensor_scalar(
                    g2["sgv"][:], g2["inv"][:], 127.0, None, op0=Alu.mult))
                P.emit("vector", lambda E, t=t: E.scalar_tensor_tensor(
                    sO[:, t:t + 1], g2["gam"][:], 1.0, wgam[:, 6:7],
                    op0=Alu.mult, op1=Alu.mult))
                P.emit("vector", lambda E: E.drain())
                P.emit("vector", lambda E, t=t: E.tensor_scalar(
                    gvq32[:], gvrow[t % 2][:], g2["sgv"][:], MAGIC,
                    op0=Alu.mult, op1=Alu.add))
                if t - 2 in tr_done:
                    P.wait("vector", "TE", tr_done[t - 2])
                P.emit("vector", lambda E, t=t: E.tensor_scalar(
                    GVq[t % 2][:], gvq32[:], MAGIC, None, op0=Alu.subtract),
                    inc="VE")
                ve_q[t] = P.val("VE")
                P.wait("tensor", "VE", ve_q[t])
                first = True
                for k in range(NKH):
                    P.wait("tensor", "SC", P.val("SC") - 1)
                    P.emit("tensor", lambda E, k=k, t=t, p=tsq % 2:
                           E.transpose(psb[p][:, 0:128],
                                       GVq[t % 2][:, k * 128:(k + 1) * 128],
                                       identb[:]), inc="TE")
                    P.wait("scalar", "TE", P.val("TE"))
                    if first and t - 2 in st_q:
                        P.wait("scalar", "ST", st_q[t - 2])
                        first = False
                    P.emit("scalar", lambda E, k=k, t=t, p=tsq % 2: E.copy(
                        gvqTst[t % 2][:, k * 128:(k + 1) * 128],
                        psb[p][:, 0:128]), inc="SC")
                    tsq += 1
                tr_done[t] = P.val("TE")
                ev_tr[t] = P.val("SC")
                P.wait("sync", "SC", ev_tr[t])
                P.dma("sync", T["gvqT_dram"][t, :, :], gvqTst[t % 2][:], "ST")
                st_q[t] = P.val("ST")
            P.wait("sync", "ST", P.val("ST"))
            P.barrier()

        with ExitStack() as cC:
            EC = cC.enter_context
            wosl = EC(nc.sbuf_tensor("wosl", [128, NKH * 512], BF16))
            gvqrd = [EC(nc.sbuf_tensor(f"gvqrd{i}", [128, HID], BF16))
                     for i in range(2)]
            obm = EC(nc.sbuf_tensor("obm", [1, D], F32))
            obj = EC(nc.sbuf_tensor("obj", [128, 512], F32))
            Oacc = EC(nc.sbuf_tensor("Oacc", [128, NTT * D], F32))
            z1rd = EC(nc.sbuf_tensor("z1rd", [128, D], F32))
            z2b = EC(nc.sbuf_tensor("z2b", [128, D], F32))
            ofin = EC(nc.sbuf_tensor("ofin", [128, D], F32))
            sq2 = EC(nc.sbuf_tensor("sq2", [128, D], F32))
            tn2 = {k: EC(nc.sbuf_tensor("l2_" + k, [128, 1], F32))
                   for k in ("red", "mu", "ms", "vpe", "y0", "t1", "rstd")}
            tn2["sqs"] = sq2

            P.dma("sync", obm[:], T["obias"], "LD")
            P.wait("tensor", "LD", P.val("LD"))
            oq_mm = {}
            slab_mm = {}
            epi_o = {}
            for j in range(4):
                if j - 1 in slab_mm:
                    P.wait("sync", "TE", slab_mm[j - 1])
                for k in range(NKH):
                    P.dma("sync", wosl[:, k * 512:(k + 1) * 512],
                          T["woq_out"][k * 128:(k + 1) * 128,
                                       j * 512:(j + 1) * 512], "LD")
                ld_wo = P.val("LD")
                if (j - 1) * NTT + NTT - 1 in epi_o:
                    P.wait("tensor", "VE", epi_o[(j - 1) * NTT + NTT - 1])
                P.emit("tensor", lambda E, j=j: E.matmul(
                    psf[4][:, 0:512], ones1[0:1, :],
                    obm[0:1, j * 512:(j + 1) * 512],
                    start=True, stop=True), inc="TE")
                P.wait("vector", "TE", P.val("TE"))
                P.emit("vector", lambda E: E.tensor_copy(obj[:], psf[4][:, 0:512]),
                       inc="VE")
                for t in range(NTT):
                    oq = j * NTT + t
                    if oq - 2 in oq_mm:
                        P.wait("sync", "TE", oq_mm[oq - 2])
                    P.dma("sync", gvqrd[t % 2][:], T["gvqT_dram"][t, :, :], "LD")
                    P.wait("tensor", "LD", P.val("LD"))
                    if oq - 3 in epi_o:
                        P.wait("tensor", "VE", epi_o[oq - 3])
                    for k in range(NKH):
                        P.emit("tensor", lambda E, k=k, t=t, b=oq % 3,
                               f=(k == 0), st=(k == NKH - 1): E.matmul(
                                   psf[b][:, 0:512],
                                   gvqrd[t % 2][:, k * 128:(k + 1) * 128],
                                   wosl[:, k * 512:(k + 1) * 512],
                                   start=f, stop=st),
                               inc=("TE" if k == NKH - 1 else None))
                    oq_mm[oq] = P.val("TE")
                    P.wait("vector", "TE", oq_mm[oq])
                    P.emit("vector", lambda E, t=t, j=j, b=oq % 3:
                           E.scalar_tensor_tensor(
                               Oacc[:, t * D + j * 512:t * D + (j + 1) * 512],
                               psf[b][:, 0:512], sO[:, t:t + 1], obj[:],
                               op0=Alu.mult, op1=Alu.add), inc="VE")
                    epi_o[oq] = P.val("VE")
                slab_mm[j] = P.val("TE")

            st_out = {}
            for t in range(NTT):
                P.wait("sync", "VE", P.val("VE"))
                P.dma("sync", z1rd[:], T["z1st"][t * 128:(t + 1) * 128, :], "LD")
                P.wait("vector", "LD", P.val("LD"))
                if t - 1 in st_out:
                    P.wait("vector", "ST", st_out[t - 1])
                P.emit("vector", lambda E, t=t: E.tensor_tensor(
                    z2b[:], z1rd[:], Oacc[:, t * D:(t + 1) * D], Alu.add))
                emit_ln(lambda: z2b[:], lambda: ofin[:], tn2)
                P.wait("sync", "VE", P.val("VE"))
                P.dma("sync", T["out"][t * 128:(t + 1) * 128, :], ofin[:], "ST")
                st_out[t] = P.val("ST")
            P.wait("sync", "ST", P.val("ST"))
            P.wait("sync", "SD", P.val("SD"))
            if DEBUG_STOP == "ffn":
                P.dma("gpsimd", T["z1dbg"], T["z1st"].ap(), "GD")
                P.dma("gpsimd", T["gvdbg"],
                      T["gv_dram"].ap().rearrange("a b c d -> (a b c) d"), "GD")
                P.dma("gpsimd", T["gvqdbg"],
                      T["gvqT_dram"].ap().rearrange("a b c -> (a b) c"), "GD")
                P.dma("gpsimd", T["zmydbg"], T["z_my"].ap(), "GD")
                P.wait("gpsimd", "GD", P.val("GD"))


_NC_CACHE = {}


def _get_nc():
    key = DEBUG_STOP
    if key not in _NC_CACHE:
        _NC_CACHE[key] = build_nc()
    return _NC_CACHE[key]


def kernel(L0_attn_x, L0_attn_W_x, L0_attn_W_dt, L0_attn_b_dt, L0_attn_A_log,
           L0_attn_D, L0_ffn_gate_w, L0_ffn_gate_b, L0_ffn_val_w, L0_ffn_val_b,
           L0_ffn_out_w, L0_ffn_out_b):
    f32 = np.float32
    x = np.asarray(L0_attn_x, f32)
    W_x = np.ascontiguousarray(np.asarray(L0_attn_W_x, f32))
    W_dt = np.asarray(L0_attn_W_dt, f32)
    b_dt = np.asarray(L0_attn_b_dt, f32)
    A_log = np.asarray(L0_attn_A_log, f32)
    Dv = np.asarray(L0_attn_D, f32)
    WgT = np.ascontiguousarray(np.asarray(L0_ffn_gate_w, f32).T)
    WvT = np.ascontiguousarray(np.asarray(L0_ffn_val_w, f32).T)
    WoT = np.ascontiguousarray(np.asarray(L0_ffn_out_w, f32).T)
    gb = np.ascontiguousarray(np.asarray(L0_ffn_gate_b, f32).reshape(1, HID))
    vb = np.ascontiguousarray(np.asarray(L0_ffn_val_b, f32).reshape(1, HID))
    ob = np.ascontiguousarray(np.asarray(L0_ffn_out_b, f32).reshape(1, D))

    xf = np.ascontiguousarray(x.reshape(B * L, D))
    in_maps = []
    for c in range(NCORE):
        dsl = slice(DSL * c, DSL * (c + 1))
        hsl = slice(1024 * c, 1024 * (c + 1))
        in_maps.append(dict(
            x_tok=np.ascontiguousarray(xf[TOK * c:TOK * (c + 1)]),
            x_dsl=np.ascontiguousarray(x[:, :, dsl]),
            wx=W_x,
            wdtT=np.ascontiguousarray(W_dt[dsl].T),
            bdt=np.ascontiguousarray(b_dt[dsl].reshape(2, 128).T),
            alog=np.ascontiguousarray(
                A_log[dsl].reshape(2, 128, NST).transpose(1, 0, 2)
                .reshape(128, 2 * NST)),
            dvec=np.ascontiguousarray(Dv[dsl].reshape(2, 128).T),
            wgTd=np.ascontiguousarray(WgT[:, hsl]),
            wvTd=np.ascontiguousarray(WvT[:, hsl]),
            woTd=np.ascontiguousarray(WoT[hsl]),
            gbias=gb, vbias=vb, obias=ob,
        ))
    nc = _get_nc()
    res = run_bass_kernel_spmd(nc, in_maps, core_ids=list(range(NCORE)))
    if DEBUG_STOP is not None:
        return res
    o = np.concatenate([r["out"] for r in res.results], axis=0)
    return o.reshape(B, L, D)


if __name__ == "__main__":
    nc = build_nc()
    n = sum(1 for _ in nc.m.functions[0].blocks for _ in [0])
    print("built ok")


# version 15 (final)
# speedup vs baseline: 1.1639x; 1.0012x over previous
import numpy as np
import ml_dtypes
from contextlib import ExitStack

import concourse.bass as bass
import concourse.mybir as mybir
from concourse.bass_utils import run_bass_kernel_spmd

F32 = mybir.dt.float32
BF16 = mybir.dt.bfloat16
Alu = mybir.AluOpType
Act = mybir.ActivationFunctionType
AXX = mybir.AxisListType.X

MAGIC = float(np.float32(1.5 * 2 ** 23))
NCORE = 8
B, L, D, NST, RDT, HID = 2, 4096, 2048, 16, 64, 8192
TOK = (B * L) // NCORE
DSL = D // NCORE
CH = 512
NCH = L // CH
NKD = D // 128
NTT = TOK // 128
NKH = HID // 128
EPS = 1e-5

DEBUG_STOP = None
DEBUG_DUMPS = False
DEBUG_BC1 = False
DEBUG_A1 = False
DEBUG_NOBC = False


class _Prog:

    ENGINES = ("sync", "vector", "scalar", "tensor", "gpsimd")

    def __init__(self):
        self.ops = []
        self.cnt = {}

    def sem(self, name):
        self.cnt.setdefault(name, 0)

    def emit(self, eng, fn, inc=None, k=1):
        if inc is not None:
            self.sem(inc)
            self.cnt[inc] += k
        self.ops.append(("op", eng, fn, inc, k))

    def dma(self, eng, out, in_, inc):
        self.emit(eng, lambda E, o=out, i=in_: E.dma_start(out=o, in_=i),
                  inc=inc, k=16)

    def wait(self, eng, semname, val=None):
        v = self.cnt.get(semname, 0) if val is None else val
        if v > 0:
            self.ops.append(("wait", eng, semname, v, None))

    def val(self, semname):
        return self.cnt.get(semname, 0)

    def barrier(self):
        self.sem("PH")
        for e in self.ENGINES:
            if e in ("vector", "scalar", "tensor"):
                self.emit(e, lambda E: E.drain(), inc="PH")
            else:
                self.emit(e, lambda E: E.nop(), inc="PH")
        tgt = self.cnt["PH"]
        for e in self.ENGINES:
            self.wait(e, "PH", tgt)

    def lower(self, nc, block, ctx):
        handles = {name: ctx.enter_context(nc.semaphore(name))
                   for name in self.cnt}
        for eng_name in self.ENGINES:
            my = [op for op in self.ops if op[1] == eng_name]
            if not my:
                continue

            def section(E, my=my, handles=handles):
                for kind, _eng, a, b, c in my:
                    if kind == "wait":
                        E.wait_ge(handles[a], b)
                    else:
                        instr = a(E)
                        if b is not None:
                            instr.then_inc(handles[b], c)
            getattr(block, eng_name)(section)


def build_nc():
    nc = bass.Bass("TRN2", target_bir_lowering=False)

    T = {}

    def din(name, shape, dt=F32):
        T[name] = nc.dram_tensor(name, shape, dt, kind="ExternalInput").ap()

    def dout(name, shape, dt=F32):
        T[name] = nc.dram_tensor(name, shape, dt, kind="ExternalOutput").ap()

    def dint(name, shape, dt=F32, shared=False):
        T[name] = nc.dram_tensor(name, shape, dt,
                                 addr_space="Shared" if shared else "Local")

    din("x_tok", [TOK, D]); din("x_dsl", [B, L, DSL]); din("wx", [96, D])
    din("wdtT", [RDT, DSL]); din("bdt", [128, 2]); din("alog", [128, 2 * NST])
    din("dvec", [128, 2]); din("wgTd", [D, 1024]); din("wvTd", [D, 1024])
    din("woTd", [1024, D]); din("gbias", [1, HID]); din("vbias", [1, HID])
    din("obias", [1, D])
    dout("out", [TOK, D])
    if DEBUG_STOP == "mixer":
        dout("zdbg", [B * L, DSL])
        dout("dtdbg", [128, 2 * CH * B * NCH])
        dout("xdbg", [128, 2 * CH * B * NCH])
        dout("xtdbg", [128, 2 * CH * B * NCH])
        dout("abdbg", [128, 4 * CH * NST])
        dout("atdbg", [128, 2 * NST])
        dout("xrdbg", [128, 4 * DSL * B * NCH])
        dout("ydbg", [128, 2 * CH * B * NCH])
    if DEBUG_STOP == "ffn":
        dout("z1dbg", [TOK, D])
        dout("gvdbg", [NTT * 16 * 128, 512])
        dout("gvqdbg", [NTT * 128, HID], BF16)
        dout("zmydbg", [TOK, D])
        dout("lndbg", [128, 8 * NTT])
    if DEBUG_STOP == "ph0":
        dout("agdbg", [NCORE * 96, TOK])
        dout("wqdbg", [NCORE * D, 1024], BF16)
        dout("gmdbg", [NCORE, 4])
        dout("wgdbg", [128, 8])
    dint("ag1_in", [96, TOK])
    dint("ag1_out", [NCORE * 96, TOK], shared=True)
    dint("gm_in", [1, 4]); dint("gm_out", [NCORE, 4], shared=True)
    dint("wgq_in", [D, 1024], BF16)
    dint("wgq_out", [NCORE * D, 1024], BF16, shared=True)
    dint("wvq_in", [D, 1024], BF16)
    dint("wvq_out", [NCORE * D, 1024], BF16, shared=True)
    dint("woq_in", [1024, D], BF16)
    dint("woq_out", [NCORE * 1024, D], BF16, shared=True)
    dint("z_in", [B * L, DSL])
    dint("z_out", [NCORE * B * L, DSL], shared=True)
    dint("z_my", [TOK, D])
    dint("z1st", [TOK, D])
    dint("gv_dram", [NTT, 16, 128, 512])
    dint("gvqT_dram", [NTT, 128, HID], BF16)

    P = _Prog()
    for s in ("LD", "ST", "SD", "GD", "CC", "VE", "SC", "TE", "GP", "PH",
              "MF", "BF"):
        P.sem(s)
    marks = {}

    ctx = ExitStack()
    with ctx:
        E = ctx.enter_context
        S = {}
        S["identb"] = E(nc.sbuf_tensor("identb", [128, 128], BF16))
        S["identf"] = E(nc.sbuf_tensor("identf", [128, 128], F32))
        S["ones1"] = E(nc.sbuf_tensor("ones1", [1, 128], F32))
        S["onescol"] = E(nc.sbuf_tensor("onescol", [128, 1], F32))
        S["ones1b"] = E(nc.sbuf_tensor("ones1b", [1, 128], BF16))
        S["wgam"] = E(nc.sbuf_tensor("wgam", [128, 8], F32))
        S["dinv"] = E(nc.sbuf_tensor("dinv", [128, 1], F32))
        S["epsc"] = E(nc.sbuf_tensor("epsc", [128, 1], F32))
        S["c127"] = E(nc.sbuf_tensor("c127", [128, 1], F32))
        S["psf"] = [E(nc.psum_tensor(f"psf{i}", [128, 512], F32))
                    for i in range(6)]
        S["psb"] = [E(nc.psum_tensor(f"psb{i}", [128, 1024], BF16))
                    for i in range(2)]
        block = E(nc.Block())

        with ExitStack() as cscan:
            ES = cscan.enter_context
            S["xpj_h"] = ES(nc.sbuf_tensor("xpj_h", [96, B * L], BF16))
            S["xpj_l"] = ES(nc.sbuf_tensor("xpj_l", [96, B * L], BF16))

            _phase0(P, nc, T, S, marks)

            if DEBUG_STOP == "ph0":
                P.wait("gpsimd", "CC", marks["CC_GM"])
                P.dma("gpsimd", T["agdbg"], T["ag1_out"].ap(), "GD")
                P.dma("gpsimd", T["wqdbg"], T["wgq_out"].ap(), "GD")
                P.dma("gpsimd", T["gmdbg"], T["gm_out"].ap(), "GD")
                P.dma("gpsimd", T["wgdbg"], S["wgam"][:], "GD")
                P.wait("gpsimd", "GD", P.val("GD"))
                P.lower(nc, block, ctx)
                return nc

            _scan(P, nc, T, S, marks)

            if DEBUG_STOP == "mixer":
                P.wait("gpsimd", "ST", P.val("ST"))
                P.dma("gpsimd", T["zdbg"], T["z_in"].ap(), "GD")
                P.wait("gpsimd", "GD", P.val("GD"))
                P.lower(nc, block, ctx)
                return nc

            P.wait("gpsimd", "ST", P.val("ST"))
            P.wait("gpsimd", "SD", P.val("SD"))
            P.emit("gpsimd", lambda E: E.collective_compute(
                "AllGather", Alu.bypass, replica_groups=[list(range(NCORE))],
                ins=[T["z_in"].ap().opt()], outs=[T["z_out"].ap().opt()]),
                inc="CC")
            marks["CC_Z"] = P.val("CC")
            P.barrier()
        _ffn(P, nc, T, S, marks)
        P.lower(nc, block, ctx)
    return nc


def _ag(P, T, name_in, name_out, mark, marks):
    P.wait("gpsimd", "ST", P.val("ST"))
    P.emit("gpsimd", lambda E, a=T[name_in], b=T[name_out]: E.collective_compute(
        "AllGather", Alu.bypass, replica_groups=[list(range(NCORE))],
        ins=[a.ap().opt()], outs=[b.ap().opt()]), inc="CC")
    marks[mark] = P.val("CC")


def _phase0(P, nc, T, S, marks):
    psf, psb = S["psf"], S["psb"]
    identb, identf = S["identb"], S["identf"]
    ones1, onescol, wgam = S["ones1"], S["onescol"], S["wgam"]

    with ExitStack() as c0:
        E0 = c0.enter_context
        xtk = E0(nc.sbuf_tensor("xtk", [128, D], F32))
        xth = E0(nc.sbuf_tensor("xth", [128, D], BF16))
        xtl = E0(nc.sbuf_tensor("xtl", [128, D], BF16))
        xtmp = E0(nc.sbuf_tensor("xtmp", [128, D], F32))
        xTh = E0(nc.sbuf_tensor("xTh", [128, NKD * 128], BF16))
        xTl = E0(nc.sbuf_tensor("xTl", [128, NKD * 128], BF16))
        wxs = E0(nc.sbuf_tensor("wxs", [96, D], F32))
        wxh = E0(nc.sbuf_tensor("wxh", [96, D], BF16))
        wxl = E0(nc.sbuf_tensor("wxl", [96, D], BF16))
        wxTh = E0(nc.sbuf_tensor("wxTh", [128, NKD * 96], BF16))
        wxTl = E0(nc.sbuf_tensor("wxTl", [128, NKD * 96], BF16))
        xpmine = E0(nc.sbuf_tensor("xpmine", [96, TOK], F32))
        wti = [E0(nc.sbuf_tensor(f"wti{i}", [128, 2048], F32)) for i in range(2)]
        wqo = [E0(nc.sbuf_tensor(f"wqo{i}", [128, 2048], BF16)) for i in range(2)]
        wtmp = E0(nc.sbuf_tensor("wtmp", [128, 2048], F32))
        wacc = E0(nc.sbuf_tensor("wacc", [128, 1], F32))
        wred = E0(nc.sbuf_tensor("wred", [128, 1], F32))
        gmsb = E0(nc.sbuf_tensor("gmsb", [1, 8], F32))
        gmrd = E0(nc.sbuf_tensor("gmrd", [1, 32], F32))

        P.emit("gpsimd", lambda E: E.memset(identb[:], 0.0))
        P.emit("gpsimd", lambda E: E.affine_select(
            out=identb[:], in_=identb[:], compare_op=Alu.not_equal,
            fill=1.0, base=0, pattern=[[-1, 128]], channel_multiplier=1))
        P.emit("gpsimd", lambda E: E.memset(identf[:], 0.0))
        P.emit("gpsimd", lambda E: E.affine_select(
            out=identf[:], in_=identf[:], compare_op=Alu.not_equal,
            fill=1.0, base=0, pattern=[[-1, 128]], channel_multiplier=1))
        P.emit("gpsimd", lambda E: E.memset(ones1[:], 1.0))
        P.emit("gpsimd", lambda E: E.memset(onescol[:], 1.0))
        P.emit("gpsimd", lambda E: E.memset(S["dinv"][:], 1.0 / D))
        P.emit("gpsimd", lambda E: E.memset(S["epsc"][:], EPS))
        P.emit("gpsimd", lambda E: E.memset(S["c127"][:], 127.0))
        P.emit("gpsimd", lambda E: E.memset(S["ones1b"][:], 1.0), inc="GP")
        gp_const = P.val("GP")

        P.dma("sync", wxs[:], T["wx"], "LD")
        P.wait("vector", "LD", P.val("LD"))
        P.emit("vector", lambda E: E.tensor_copy(wxh[:], wxs[:]))
        P.emit("vector", lambda E: E.tensor_tensor(
            xtmp[0:96, :], wxs[:], wxh[:], Alu.subtract))
        P.emit("vector", lambda E: E.tensor_copy(wxl[:], xtmp[0:96, :]), inc="VE")
        P.wait("tensor", "VE", P.val("VE"))
        P.wait("tensor", "GP", gp_const)
        tsq = 0
        for k in range(NKD):
            for src, dst in ((wxh, wxTh), (wxl, wxTl)):
                if tsq >= 2:
                    P.wait("tensor", "SC", P.val("SC") - 1)
                P.emit("tensor", lambda E, s=src, k=k, p=tsq % 2: E.transpose(
                    psb[p][:, 0:96], s[:, k * 128:(k + 1) * 128],
                    identb[0:96, 0:96]), inc="TE")
                P.wait("scalar", "TE", P.val("TE"))
                P.emit("scalar", lambda E, d=dst, k=k, p=tsq % 2: E.copy(
                    d[:, k * 96:(k + 1) * 96], psb[p][:, 0:96]), inc="SC")
                tsq += 1

        for tt in range(NTT):
            P.wait("sync", "VE", P.val("VE"))
            P.dma("sync", xtk[:], T["x_tok"][tt * 128:(tt + 1) * 128, :], "LD")
            P.wait("vector", "LD", P.val("LD"))
            P.emit("vector", lambda E: E.tensor_copy(xth[:], xtk[:]))
            P.emit("vector", lambda E: E.tensor_tensor(
                xtmp[:], xtk[:], xth[:], Alu.subtract))
            P.emit("vector", lambda E: E.tensor_copy(xtl[:], xtmp[:]), inc="VE")
            P.wait("tensor", "VE", P.val("VE"))
            for k in range(NKD):
                for src, dst in ((xth, xTh), (xtl, xTl)):
                    P.wait("tensor", "SC", P.val("SC") - 1)
                    P.emit("tensor", lambda E, s=src, k=k, p=tsq % 2: E.transpose(
                        psb[p][:, 0:128], s[:, k * 128:(k + 1) * 128], identb[:]),
                        inc="TE")
                    P.wait("scalar", "TE", P.val("TE"))
                    P.emit("scalar", lambda E, d=dst, k=k, p=tsq % 2: E.copy(
                        d[:, k * 128:(k + 1) * 128], psb[p][:, 0:128]), inc="SC")
                    tsq += 1
            P.wait("tensor", "SC", P.val("SC"))
            n_mm = NKD * 4
            i_mm = 0
            for k in range(NKD):
                for lh in (wxTh, wxTl):
                    for rh in (xTh, xTl):
                        P.emit("tensor",
                               lambda E, k=k, lh=lh, rh=rh, f=(i_mm == 0),
                               st=(i_mm == n_mm - 1): E.matmul(
                                   psf[4][0:96, 0:128],
                                   lh[:, k * 96:(k + 1) * 96],
                                   rh[:, k * 128:(k + 1) * 128],
                                   start=f, stop=st),
                               inc=("TE" if i_mm == n_mm - 1 else None))
                        i_mm += 1
            P.wait("vector", "TE", P.val("TE"))
            P.emit("vector", lambda E, tt=tt: E.tensor_copy(
                xpmine[:, tt * 128:(tt + 1) * 128], psf[4][0:96, 0:128]),
                inc="VE")
            P.wait("tensor", "VE", P.val("VE"))
        P.wait("sync", "VE", P.val("VE"))
        P.dma("sync", T["ag1_in"].ap(), xpmine[:], "ST")
        _ag(P, T, "ag1_in", "ag1_out", "CC_AG1", marks)

        wspec = [
            ("wgTd", "wgq_in", D // 128, 1024, 0),
            ("wvTd", "wvq_in", D // 128, 1024, 1),
            ("woTd", "woq_in", 1024 // 128, 2048, 2),
        ]
        for wsrc, wdst, ntile, width, widx in wspec:
            for it in range(ntile):
                buf = wti[it % 2]
                P.wait("sync", "VE", P.val("VE"))
                P.dma("sync", buf[:, 0:width],
                      T[wsrc][it * 128:(it + 1) * 128, :], "LD")
                P.wait("vector", "LD", P.val("LD"))
                P.emit("vector", lambda E, b=buf, w=width: E.tensor_reduce(
                    wred[:], b[:, 0:w], AXX, Alu.add, apply_absolute_value=True))
                P.emit("vector", lambda E: E.drain())
                if it == 0:
                    P.emit("vector", lambda E: E.tensor_copy(wacc[:], wred[:]),
                           inc="VE")
                else:
                    P.emit("vector", lambda E: E.tensor_tensor(
                        wacc[:], wacc[:], wred[:], Alu.add), inc="VE")
            P.wait("tensor", "VE", P.val("VE"))
            P.emit("tensor", lambda E: E.matmul(
                psf[5][0:1, 0:1], wacc[:], onescol[:], start=True, stop=True),
                inc="TE")
            P.wait("vector", "TE", P.val("TE"))
            P.emit("vector", lambda E, wi=widx: E.tensor_scalar(
                gmsb[0:1, wi:wi + 1], psf[5][0:1, 0:1], 1.0 / (HID * D), None,
                op0=Alu.mult), inc="VE")
            P.wait("tensor", "VE", P.val("VE"))
        P.wait("sync", "VE", P.val("VE"))
        P.dma("sync", T["gm_in"].ap(), gmsb[0:1, 0:4], "ST")
        _ag(P, T, "gm_in", "gm_out", "CC_GM", marks)
        P.wait("sync", "CC", marks["CC_GM"])
        P.dma("sync", gmrd[0:1, 0:32], T["gm_out"].ap(), "LD")
        P.wait("vector", "LD", P.val("LD"))
        gm3 = gmrd[0:1, 0:32].rearrange("p (r f) -> p r f", f=4)
        for widx in range(3):
            P.emit("vector", lambda E, wi=widx: E.tensor_reduce(
                gmsb[0:1, wi:wi + 1], gm3[:, :, wi], AXX, Alu.add))
        P.emit("vector", lambda E: E.drain(), inc="VE")
        P.wait("scalar", "VE", P.val("VE"))
        P.emit("scalar", lambda E: E.copy(gmsb[0:1, 4:7], gmsb[0:1, 0:3]),
               inc="SC")
        P.wait("vector", "SC", P.val("SC"))
        P.emit("vector", lambda E: E.tensor_scalar(
            gmsb[0:1, 0:3], gmsb[0:1, 4:7], EPS, None, op0=Alu.max))
        P.emit("vector", lambda E: E.drain())
        P.emit("vector", lambda E: E.tensor_scalar(
            gmsb[0:1, 4:7], gmsb[0:1, 0:3], 1.0 / 127.0, None, op0=Alu.mult))
        P.emit("vector", lambda E: E.drain())
        P.emit("vector", lambda E: E.reciprocal(gmsb[0:1, 0:3], gmsb[0:1, 0:3]))
        P.emit("vector", lambda E: E.drain(), inc="VE")
        P.wait("tensor", "VE", P.val("VE"))
        P.emit("tensor", lambda E: E.matmul(
            psf[5][:, 0:8], ones1[0:1, :], gmsb[0:1, 0:8], start=True,
            stop=True), inc="TE")
        P.wait("vector", "TE", P.val("TE"))
        P.emit("vector", lambda E: E.tensor_copy(wgam[:], psf[5][:, 0:8]),
               inc="VE")

        P.barrier()


def _scan(P, nc, T, S, marks):
    psf = S["psf"]
    identf = S["identf"]
    xpj_h, xpj_l = S["xpj_h"], S["xpj_l"]
    NP = B * NCH

    with ExitStack() as c1:
        E1 = c1.enter_context
        wdts = E1(nc.sbuf_tensor("wdts", [RDT, DSL], F32))
        wdth = E1(nc.sbuf_tensor("wdth", [RDT, DSL], BF16))
        wdtl = E1(nc.sbuf_tensor("wdtl", [RDT, DSL], BF16))
        wdtmp = E1(nc.sbuf_tensor("wdtmp", [RDT, DSL], F32))
        bdts = E1(nc.sbuf_tensor("bdts", [128, 2], F32))
        alogs = E1(nc.sbuf_tensor("alogs", [128, 2 * NST], F32))
        atile = E1(nc.sbuf_tensor("atile", [128, 2 * NST], F32))
        dp1 = E1(nc.sbuf_tensor("dp1", [128, 2], F32))
        xraw = [E1(nc.sbuf_tensor(f"xraw{i}", [128, 4 * DSL], F32))
                for i in range(2)]
        xT = [E1(nc.sbuf_tensor(f"xT{i}", [128, 2 * CH], F32))
              for i in range(2)]
        dtt = [E1(nc.sbuf_tensor(f"dtt{i}", [128, 2 * CH], F32))
               for i in range(2)]
        dtx = [E1(nc.sbuf_tensor(f"dtx{i}", [128, 2 * CH], F32))
               for i in range(2)]
        spm = E1(nc.sbuf_tensor("spm", [128, 2 * CH], F32))
        spa = E1(nc.sbuf_tensor("spa", [128, 2 * CH], F32))
        abuf = E1(nc.sbuf_tensor("abuf", [128, 4 * CH], F32))
        wbuf = E1(nc.sbuf_tensor("wbuf", [128, 2 * CH], F32))
        htr = [E1(nc.sbuf_tensor(f"htr{i}", [128, 2 * CH], F32))
               for i in range(2)]
        mbf = [E1(nc.sbuf_tensor(f"mbf{i}", [128, 2 * CH], F32))
               for i in range(2)]
        csb = E1(nc.sbuf_tensor("csb", [128, 2 * CH], F32))
        yacc = E1(nc.sbuf_tensor("yacc", [128, 2 * CH], F32))
        ztile = E1(nc.sbuf_tensor("ztile", [128, 2 * CH], F32))
        zstage = E1(nc.sbuf_tensor("zstage", [128, 4 * DSL], F32))
        hst = E1(nc.sbuf_tensor("hst", [128, 4 * NST], F32))

        for eng in ("sync", "vector", "scalar", "tensor", "gpsimd"):
            P.wait(eng, "CC", marks["CC_GM"])
        P.dma("sync", wdts[:], T["wdtT"], "LD")
        P.dma("sync", bdts[:], T["bdt"], "LD")
        P.dma("sync", alogs[:], T["alog"], "LD")
        P.dma("sync", dp1[:], T["dvec"], "LD")
        P.wait("vector", "LD", P.val("LD"))
        P.emit("vector", lambda E: E.tensor_copy(wdth[:], wdts[:]))
        P.emit("vector", lambda E: E.tensor_tensor(
            wdtmp[:], wdts[:], wdth[:], Alu.subtract))
        P.emit("vector", lambda E: E.tensor_copy(wdtl[:], wdtmp[:]))
        P.emit("vector", lambda E: E.tensor_scalar(
            dp1[:], dp1[:], 1.0, None, op0=Alu.add), inc="VE")
        P.wait("scalar", "LD", P.val("LD"))
        P.emit("scalar", lambda E: E.activation(atile[:], alogs[:], Act.Exp),
               inc="SC")
        P.wait("vector", "SC", P.val("SC"))
        P.emit("vector", lambda E: E.tensor_scalar(
            atile[:], atile[:], -1.0, None, op0=Alu.mult), inc="VE")
        P.emit("vector", lambda E: E.memset(hst[:], 0.0), inc="GP")
        gp_hst = P.val("GP")

        P.wait("sync", "CC", marks["CC_AG1"])
        with ExitStack() as cx:
            xprojT = cx.enter_context(
                nc.sbuf_tensor("xprojT", [96, B * L], F32))
            losc = cx.enter_context(nc.sbuf_tensor("losc", [96, 1024], F32))
            for r in range(NCORE):
                P.dma("sync", xprojT[:, r * TOK:(r + 1) * TOK],
                      T["ag1_out"][r * 96:(r + 1) * 96, :], "LD")
            P.wait("vector", "LD", P.val("LD"))
            P.emit("vector", lambda E: E.tensor_copy(xpj_h[:], xprojT[:, :]))
            for cs in range(0, B * L, 1024):
                P.emit("vector", lambda E, cs=cs: E.tensor_tensor(
                    losc[:, :], xprojT[:, cs:cs + 1024],
                    xpj_h[:, cs:cs + 1024], Alu.subtract))
                P.emit("vector", lambda E, cs=cs: E.tensor_copy(
                    xpj_l[:, cs:cs + 1024], losc[:, :]),
                    inc=("VE" if cs == B * L - 1024 else None))
        stg_h = E1(nc.sbuf_tensor("stg_h", [1, 32 * CH], BF16))
        stg_l = E1(nc.sbuf_tensor("stg_l", [1, 32 * CH], BF16))
        qwi = [E1(nc.sbuf_tensor(f"qwi{i}", [128, 1024], F32))
               for i in range(2)]
        qwo = [E1(nc.sbuf_tensor(f"qwo{i}", [128, 1024], BF16))
               for i in range(2)]
        qtmp = E1(nc.sbuf_tensor("qtmp", [128, 1024], F32))
        qcst = E1(nc.sbuf_tensor("qcst", [128, 4], F32))
        ve_prep = P.val("VE")
        P.wait("tensor", "VE", ve_prep)
        P.wait("tensor", "GP", gp_hst)
        P.wait("vector", "GP", gp_hst)
        P.wait("scalar", "VE", ve_prep)
        P.wait("sync", "VE", ve_prep)

        ld_ready, ld_stage = {}, {}
        xtr_done, xT_ready, ztr_done, zst_done = {}, {}, {}, {}
        sc_sp, gp_dtt, gp_dtx, ve_zt = {}, {}, {}, {}
        te_bc, sc_exp, sc_evC, ve_tts, ve_m0, gp_n = {}, {}, {}, {}, {}, {}
        mf_marks = []

        def emit_head(li):
            b, ci = divmod(li, NCH)
            colb = b * L + ci * CH
            P.wait("tensor", "LD", ld_ready[li])
            first_evac = True
            for s in range(4):
                for d in range(2):
                    if len(mf_marks) >= 2:
                        P.wait("tensor", "SC", mf_marks[-2])
                    P.emit("tensor", lambda E, s=s, d=d, li=li: E.transpose(
                        psf[4 + d][:, 0:128],
                        xraw[li % 2][:, s * DSL + d * 128:s * DSL + (d + 1) * 128],
                        identf[:]), inc="TE")
                    P.wait("scalar", "TE", P.val("TE"))
                    if first_evac:
                        if li - 2 in gp_dtx:
                            P.wait("scalar", "GP", gp_dtx[li - 2])
                        if li - 2 in ve_zt:
                            P.wait("scalar", "VE", ve_zt[li - 2])
                        first_evac = False
                    P.emit("scalar", lambda E, s=s, d=d, li=li: E.copy(
                        xT[li % 2][:, d * CH + s * 128:d * CH + (s + 1) * 128],
                        psf[4 + d][:, 0:128]), inc="SC")
                    mf_marks.append(P.val("SC"))
            xT_ready[li] = P.val("SC")
            xtr_done[li] = P.val("TE")
            if DEBUG_STOP == "mixer" and DEBUG_DUMPS:
                P.wait("sync", "TE", xtr_done[li])
                P.dma("sync", T["xrdbg"][:, li * 4 * DSL:(li + 1) * 4 * DSL],
                      xraw[li % 2][:], "ST")
            for d in range(2):
                if len(mf_marks) >= 2:
                    P.wait("tensor", "SC", mf_marks[-2])
                mmi = 0
                for lh in (wdth, wdtl):
                    for rh in (xpj_h, xpj_l):
                        P.emit("tensor", lambda E, lh=lh, rh=rh, d=d,
                               f=(mmi == 0), st=(mmi == 3), cb=colb:
                               E.matmul(psf[4 + d][:, 0:CH],
                                        lh[:, d * 128:(d + 1) * 128],
                                        rh[0:RDT, cb:cb + CH],
                                        start=f, stop=st),
                               inc=("TE" if mmi == 3 else None))
                        mmi += 1
                P.wait("scalar", "TE", P.val("TE"))
                if d == 0 and li - 1 in gp_dtt:
                    P.wait("scalar", "GP", gp_dtt[li - 1])
                P.emit("scalar", lambda E, d=d: E.activation(
                    spm[:, d * CH:(d + 1) * CH], psf[4 + d][:, 0:CH],
                    Act.Relu, bias=bdts[:, d:d + 1]))
                P.emit("scalar", lambda E, d=d: E.activation(
                    spa[:, d * CH:(d + 1) * CH], psf[4 + d][:, 0:CH],
                    Act.Abs, bias=bdts[:, d:d + 1]), inc="SC")
                mf_marks.append(P.val("SC"))
                P.emit("scalar", lambda E, d=d: E.activation(
                    spa[:, d * CH:(d + 1) * CH], spa[:, d * CH:(d + 1) * CH],
                    Act.Exp, scale=-1.0))
                P.emit("scalar", lambda E, d=d: E.activation(
                    spa[:, d * CH:(d + 1) * CH], spa[:, d * CH:(d + 1) * CH],
                    Act.Ln, bias=1.0), inc="SC")
            sc_sp[li] = P.val("SC")
            P.wait("gpsimd", "SC", sc_sp[li])
            if (li - 1) * NST - 1 in sc_exp:
                P.wait("gpsimd", "SC", sc_exp[(li - 1) * NST - 1])
            P.emit("gpsimd", lambda E, li=li: E.tensor_tensor(
                dtt[li % 2][:], spm[:], spa[:], Alu.add), inc="GP")
            gp_dtt[li] = P.val("GP")
            P.wait("gpsimd", "SC", xT_ready[li])
            if (li - 1) * NST - 1 in ve_tts:
                P.wait("gpsimd", "VE", ve_tts[(li - 1) * NST - 1])
            P.emit("gpsimd", lambda E, li=li: E.tensor_tensor(
                dtx[li % 2][:], dtt[li % 2][:], xT[li % 2][:], Alu.mult),
                inc="GP")
            gp_dtx[li] = P.val("GP")
            if DEBUG_STOP == "mixer" and DEBUG_DUMPS:
                P.wait("sync", "GP", P.val("GP"))
                P.dma("sync", T["dtdbg"][:, li * 2 * CH:(li + 1) * 2 * CH],
                      dtt[li % 2][:], "ST")
                P.dma("sync", T["xdbg"][:, li * 2 * CH:(li + 1) * 2 * CH],
                      dtx[li % 2][:], "ST")
                P.dma("sync", T["xtdbg"][:, li * 2 * CH:(li + 1) * 2 * CH],
                      xT[li % 2][:], "ST")
                for eng in ("vector", "scalar", "tensor", "gpsimd"):
                    P.wait(eng, "ST", P.val("ST"))

        def emit_stage(li):
            b, ci = divmod(li, NCH)
            colb = b * L + ci * CH
            if li * NST - 1 in te_bc:
                P.wait("sync", "TE", te_bc[li * NST - 1])
            P.dma("sync", stg_h[0:1, :], xpj_h[RDT:96, colb:colb + CH], "LD")
            P.dma("sync", stg_l[0:1, :], xpj_l[RDT:96, colb:colb + CH], "LD")
            ld_stage[li] = P.val("LD")

        def emit_n(li, n):
            nseq = li * NST + n
            sl = nseq % 2
            b, ci = divmod(li, NCH)
            if nseq - 2 in ve_m0:
                P.wait("tensor", "VE", ve_m0[nseq - 2])
                P.wait("tensor", "SC", sc_evC[nseq - 2])
            P.wait("tensor", "LD", ld_stage[li])
            for pj, base in ((0, n), (1, NST + n)):
                for hl, last in ((stg_h, False), (stg_l, True)):
                    P.emit("tensor", lambda E, pj=pj, base=base, hl=hl,
                           last=last, sl=sl: E.matmul(
                               psf[2 * sl + pj][:, 0:CH], S["ones1b"][0:1, :],
                               hl[0:1, base * CH:(base + 1) * CH],
                               start=not last, stop=last),
                           inc=("TE" if pj == 1 and last else None))
            te_bc[nseq] = P.val("TE")
            if nseq - 2 in ve_tts:
                P.wait("scalar", "VE", ve_tts[nseq - 2])
            if n == 0:
                P.wait("scalar", "GP", gp_dtt[li])
            for d in range(2):
                P.emit("scalar", lambda E, d=d, n=n, sl=sl, li=li:
                       E.activation(
                           abuf[:, sl * 2 * CH + d * CH:
                                sl * 2 * CH + (d + 1) * CH],
                           dtt[li % 2][:, d * CH:(d + 1) * CH], Act.Exp,
                           scale=atile[:, d * NST + n:d * NST + n + 1]))
            P.emit("scalar", lambda E: E.drain(), inc="SC")
            sc_exp[nseq] = P.val("SC")
            if nseq - 2 in gp_n:
                P.wait("scalar", "GP", gp_n[nseq - 2])
            P.wait("scalar", "TE", te_bc[nseq])
            P.emit("scalar", lambda E, sl=sl: E.copy(
                csb[:, sl * CH:(sl + 1) * CH], psf[2 * sl + 1][:, 0:CH]),
                inc="SC")
            sc_evC[nseq] = P.val("SC")
            P.wait("vector", "TE", te_bc[nseq])
            P.wait("vector", "SC", sc_exp[nseq])
            if nseq - 2 in gp_n:
                P.wait("vector", "GP", gp_n[nseq - 2])
            if n == 0:
                P.wait("vector", "GP", gp_dtx[li])
            for d in range(2):
                hcol = (2 * b + d) * NST + n
                P.emit("vector", lambda E, d=d, sl=sl, li=li: E.tensor_tensor(
                    wbuf[:, d * CH:(d + 1) * CH],
                    dtx[li % 2][:, d * CH:(d + 1) * CH],
                    psf[2 * sl][:, 0:CH], Alu.mult))
                P.emit("vector", lambda E, d=d, sl=sl, hc=hcol:
                       E.tensor_tensor_scan(
                           htr[sl][:, d * CH:(d + 1) * CH],
                           abuf[:, sl * 2 * CH + d * CH:
                                sl * 2 * CH + (d + 1) * CH],
                           wbuf[:, d * CH:(d + 1) * CH],
                           hst[:, hc:hc + 1],
                           op0=Alu.mult, op1=Alu.add),
                       inc=("VE" if d == 1 else None))
            ve_tts[nseq] = P.val("VE")
            P.emit("vector", lambda E, sl=sl: E.tensor_tensor(
                mbf[sl][:, 0:CH], htr[sl][:, 0:CH], psf[2 * sl + 1][:, 0:CH],
                Alu.mult), inc="VE")
            ve_m0[nseq] = P.val("VE")
            P.emit("vector", lambda E: E.drain())
            for d in range(2):
                hcol = (2 * b + d) * NST + n
                P.emit("vector", lambda E, d=d, sl=sl, hc=hcol: E.tensor_copy(
                    hst[:, hc:hc + 1],
                    htr[sl][:, (d + 1) * CH - 1:(d + 1) * CH]))
            if DEBUG_STOP == "mixer" and DEBUG_DUMPS and li == 1:
                P.wait("sync", "VE", P.val("VE"))
                P.wait("sync", "GP", P.val("GP"))
                P.dma("sync", T["abdbg"][:, n * 4 * CH:n * 4 * CH + 2 * CH],
                      abuf[:, sl * 2 * CH:(sl + 1) * 2 * CH], "ST")
                P.dma("sync", T["abdbg"][:, n * 4 * CH + 2 * CH:
                                         (n + 1) * 4 * CH],
                      htr[sl][:], "ST")
                P.dma("sync", T["atdbg"], atile[:], "ST")
                for eng in ("vector", "scalar", "tensor", "gpsimd"):
                    P.wait(eng, "ST", P.val("ST"))
            P.wait("gpsimd", "VE", ve_tts[nseq])
            P.wait("gpsimd", "SC", sc_evC[nseq])
            P.emit("gpsimd", lambda E, sl=sl: E.tensor_tensor(
                mbf[sl][:, CH:2 * CH], htr[sl][:, CH:2 * CH],
                csb[:, sl * CH:(sl + 1) * CH], Alu.mult))
            P.wait("gpsimd", "VE", ve_m0[nseq])
            if n == 0:
                if li - 1 in ve_zt:
                    P.wait("gpsimd", "VE", ve_zt[li - 1])
                P.emit("gpsimd", lambda E, sl=sl: E.tensor_copy(
                    yacc[:], mbf[sl][:]), inc="GP")
            else:
                P.emit("gpsimd", lambda E, sl=sl: E.tensor_tensor(
                    yacc[:], yacc[:], mbf[sl][:], Alu.add), inc="GP")
            gp_n[nseq] = P.val("GP")

        def emit_ztile(li):
            P.wait("vector", "GP", gp_n[li * NST + NST - 1])
            if li - 1 in ztr_done:
                P.wait("vector", "TE", ztr_done[li - 1])
            for d in range(2):
                P.emit("vector", lambda E, d=d, li=li: E.scalar_tensor_tensor(
                    ztile[:, d * CH:(d + 1) * CH],
                    xT[li % 2][:, d * CH:(d + 1) * CH],
                    dp1[:, d:d + 1], yacc[:, d * CH:(d + 1) * CH],
                    op0=Alu.mult, op1=Alu.add),
                    inc=("VE" if d == 1 else None))
            ve_zt[li] = P.val("VE")
            if DEBUG_STOP == "mixer" and DEBUG_DUMPS:
                P.wait("sync", "VE", P.val("VE"))
                P.dma("sync", T["ydbg"][:, li * 2 * CH:(li + 1) * 2 * CH],
                      yacc[:], "ST")
                for eng in ("vector", "scalar", "tensor", "gpsimd"):
                    P.wait(eng, "ST", P.val("ST"))

        def emit_ztrans(li):
            b, ci = divmod(li, NCH)
            P.wait("tensor", "VE", ve_zt[li])
            first = True
            for s in range(4):
                for d in range(2):
                    if len(mf_marks) >= 2:
                        P.wait("tensor", "SC", mf_marks[-2])
                    P.emit("tensor", lambda E, s=s, d=d: E.transpose(
                        psf[4 + d][:, 0:128],
                        ztile[:, d * CH + s * 128:d * CH + (s + 1) * 128],
                        identf[:]), inc="TE")
                    P.wait("scalar", "TE", P.val("TE"))
                    if first and li - 1 in zst_done:
                        P.wait("scalar", "ST", zst_done[li - 1])
                        first = False
                    P.emit("scalar", lambda E, s=s, d=d: E.copy(
                        zstage[:, s * DSL + d * 128:s * DSL + (d + 1) * 128],
                        psf[4 + d][:, 0:128]), inc="SC")
                    mf_marks.append(P.val("SC"))
            ztr_done[li] = P.val("TE")
            P.wait("sync", "SC", P.val("SC"))
            rowb = b * L + ci * CH
            for s in range(4):
                P.dma("sync",
                      T["z_in"][rowb + s * 128:rowb + (s + 1) * 128, :],
                      zstage[:, s * DSL:(s + 1) * DSL], "ST")
            zst_done[li] = P.val("ST")

        wgam = S["wgam"]
        P.emit("gpsimd", lambda E: E.memset(qcst[:, 0:1], MAGIC))
        P.emit("gpsimd", lambda E: E.memset(qcst[:, 1:2], -(MAGIC - 1.0)))
        P.emit("gpsimd", lambda E: E.memset(qcst[:, 2:3], 2.0))
        P.emit("gpsimd", lambda E: E.memset(qcst[:, 3:4], 1.0), inc="GP")
        P.wait("scalar", "GP", P.val("GP"))
        qdesc = []
        for wsrc, wdst, nrt, wid in (("wgTd", "wgq_in", 16, 0),
                                     ("wvTd", "wvq_in", 16, 1)):
            for it in range(nrt):
                qdesc.append((wsrc, wdst, it * 128, 0, wid))
        for it in range(8):
            for cb in (0, 1024):
                qdesc.append(("woTd", "woq_in", it * 128, cb, 2))
        q_sc, q_st = {}, {}
        qpos = [0]

        def q_pump(k):
            for _ in range(k):
                i = qpos[0]
                if i >= len(qdesc):
                    return
                qpos[0] += 1
                wsrc, wdst, r0, c0, wid = qdesc[i]
                if i - 2 in q_sc:
                    P.wait("sync", "SC", q_sc[i - 2])
                    P.wait("sync", "ST", q_st[i - 2])
                P.dma("sync", qwi[i % 2][:],
                      T[wsrc][r0:r0 + 128, c0:c0 + 1024], "LD")
                P.wait("scalar", "LD", P.val("LD"))
                P.emit("scalar", lambda E, i=i, wid=wid: E.activation(
                    qtmp[:], qwi[i % 2][:], Act.Copy,
                    scale=wgam[:, wid:wid + 1], bias=MAGIC))
                P.emit("scalar", lambda E: E.activation(
                    qtmp[:], qtmp[:], Act.Relu, bias=qcst[:, 1:2]))
                P.emit("scalar", lambda E: E.activation(
                    qtmp[:], qtmp[:], Act.Relu, scale=-1.0, bias=qcst[:, 2:3]))
                P.emit("scalar", lambda E, i=i: E.activation(
                    qwo[i % 2][:], qtmp[:], Act.Copy, scale=-1.0, bias=1.0))
                P.emit("scalar", lambda E: E.drain(), inc="SC")
                q_sc[i] = P.val("SC")
                P.wait("sync", "SC", q_sc[i])
                P.dma("sync", T[wdst][r0:r0 + 128, c0:c0 + 1024],
                      qwo[i % 2][:], "ST")
                q_st[i] = P.val("ST")

        for s in range(4):
            P.dma("sync", xraw[0][:, s * DSL:(s + 1) * DSL],
                  T["x_dsl"][0, s * 128:(s + 1) * 128, :], "LD")
        ld_ready[0] = P.val("LD")
        emit_stage(0)
        emit_head(0)
        for li in range(NP):
            if li + 1 < NP:
                nb, nci = divmod(li + 1, NCH)
                if li - 1 in xtr_done:
                    P.wait("sync", "TE", xtr_done[li - 1])
                for s in range(4):
                    P.dma("sync",
                          xraw[(li + 1) % 2][:, s * DSL:(s + 1) * DSL],
                          T["x_dsl"][nb,
                                     nci * CH + s * 128:nci * CH + (s + 1) * 128,
                                     :], "LD")
                ld_ready[li + 1] = P.val("LD")
            if li >= 1:
                emit_ztile(li - 1)
            for n in range(NST):
                emit_n(li, n)
                if li >= 1 and n in (3, 7, 11, 15):
                    q_pump(1)
                if n == 2 and li >= 1:
                    emit_ztrans(li - 1)
                if n == 7 and li + 1 < NP:
                    emit_head(li + 1)
                if n == 15 and li + 1 < NP:
                    emit_stage(li + 1)
        emit_ztile(NP - 1)
        emit_ztrans(NP - 1)
        q_pump(len(qdesc))
        P.wait("sync", "ST", P.val("ST"))


def _ffn(P, nc, T, S, marks):
    psf, psb = S["psf"], S["psb"]
    identb, ones1, wgam = S["identb"], S["ones1"], S["wgam"]
    hold = {}

    with ExitStack() as cO:
        EO = cO.enter_context
        sxg = EO(nc.sbuf_tensor("sxg", [128, NTT], F32))
        sxv = EO(nc.sbuf_tensor("sxv", [128, NTT], F32))
        sO = EO(nc.sbuf_tensor("sO", [128, NTT], F32))
        gvmax = EO(nc.sbuf_tensor("gvmax", [128, NTT], F32))
        for eng in ("sync", "vector", "scalar", "tensor", "gpsimd"):
            P.wait(eng, "CC", marks["CC_Z"])
        P.emit("gpsimd", lambda E: E.memset(gvmax[:], 0.0), inc="GP")
        P.wait("vector", "GP", P.val("GP"))

        def emit_ln(src_fn, dst_fn, tn):
            P.emit("vector", lambda E: E.reduce_sum(tn["red"][:], src_fn(), AXX))
            P.emit("vector", lambda E: E.drain(), inc="VE")
            P.wait("scalar", "VE", P.val("VE"))
            P.emit("scalar", lambda E: E.activation(
                tn["sqs"][:], src_fn(), Act.Square))
            P.emit("scalar", lambda E: E.drain(), inc="SC")
            P.wait("vector", "SC", P.val("SC"))
            P.emit("vector", lambda E: E.reduce_sum(tn["ms"][:], tn["sqs"][:], AXX))
            P.emit("vector", lambda E: E.drain())
            P.emit("vector", lambda E: E.tensor_tensor(
                tn["mu"][:], tn["red"][:], S["dinv"][:], Alu.mult))
            P.emit("vector", lambda E: E.drain())
            P.emit("vector", lambda E: E.tensor_tensor(
                tn["t1"][:], tn["mu"][:], tn["mu"][:], Alu.mult))
            P.emit("vector", lambda E: E.tensor_tensor(
                tn["ms"][:], tn["ms"][:], S["dinv"][:], Alu.mult))
            P.emit("vector", lambda E: E.drain())
            P.emit("vector", lambda E: E.tensor_tensor(
                tn["vpe"][:], tn["ms"][:], tn["t1"][:], Alu.subtract))
            P.emit("vector", lambda E: E.drain())
            P.emit("vector", lambda E: E.tensor_tensor(
                tn["vpe"][:], tn["vpe"][:], S["epsc"][:], Alu.add))
            P.emit("vector", lambda E: E.drain(), inc="VE")
            P.wait("scalar", "VE", P.val("VE"))
            P.emit("scalar", lambda E: E.activation(
                tn["y0"][:], tn["vpe"][:], Act.Ln))
            P.emit("scalar", lambda E: E.drain())
            P.emit("scalar", lambda E: E.activation(
                tn["y0"][:], tn["y0"][:], Act.Exp, scale=-0.5))
            P.emit("scalar", lambda E: E.drain(), inc="SC")
            P.wait("vector", "SC", P.val("SC"))
            P.emit("vector", lambda E: E.tensor_tensor(
                tn["t1"][:], tn["y0"][:], tn["y0"][:], Alu.mult))
            P.emit("vector", lambda E: E.drain())
            P.emit("vector", lambda E: E.tensor_tensor(
                tn["t1"][:], tn["t1"][:], tn["vpe"][:], Alu.mult))
            P.emit("vector", lambda E: E.drain())
            P.emit("vector", lambda E: E.tensor_scalar(
                tn["t1"][:], tn["t1"][:], -0.5, 1.5, op0=Alu.mult, op1=Alu.add))
            P.emit("vector", lambda E: E.drain())
            P.emit("vector", lambda E: E.tensor_tensor(
                tn["rstd"][:], tn["y0"][:], tn["t1"][:], Alu.mult))
            P.emit("vector", lambda E: E.drain())
            P.emit("vector", lambda E: E.tensor_scalar(
                dst_fn(), src_fn(), tn["mu"][:], tn["rstd"][:],
                op0=Alu.subtract, op1=Alu.mult), inc="VE")

        with ExitStack() as cA:
            EA = cA.enter_context
            XqT = EA(nc.sbuf_tensor("XqT", [128, NKD * 1024], BF16))
            lnsb = EA(nc.sbuf_tensor("lnsb", [128, 8 * NTT], F32))
            cA1 = EA(ExitStack())
            EA1 = cA1.enter_context
            zall = EA1(nc.sbuf_tensor("zall", [128, NTT * D], F32))
            sqs = EA1(nc.sbuf_tensor("sqs", [128, D], F32))
            z1 = EA1(nc.sbuf_tensor("z1", [128, D], F32))
            Xq = EA1(nc.sbuf_tensor("Xq", [128, D], BF16))
            tn = {k: EA1(nc.sbuf_tensor("ln_" + k, [128, 1], F32))
                  for k in ("red", "mu", "ms", "vpe", "y0", "t1", "rstd",
                            "gmx", "gam", "inv", "sx")}
            tn["sqs"] = sqs
            P.wait("sync", "CC", marks["CC_Z"])

            def _mk_zoffs(E):
                hold["zb"] = E.snap(E.partition_id() * TOK,
                                    min_val=0, max_val=(NCORE - 1) * TOK,
                                    guaranteed_mod_val=TOK)
                return E.nop()
            P.emit("sync", _mk_zoffs)
            z3 = T["z_out"].ap().rearrange("(r n) d -> r n d", r=NCORE)
            for r in range(NCORE):
                P.emit("sync", lambda E, r=r: E.dma_start(
                    out=T["z_my"][:, r * DSL:(r + 1) * DSL],
                    in_=z3[r][bass.ds(hold["zb"], TOK), :]),
                    inc="LD", k=16)
            P.wait("sync", "LD", P.val("LD"))
            for tt in range(NTT):
                P.dma("sync", zall[:, tt * D:(tt + 1) * D],
                      T["z_my"][tt * 128:(tt + 1) * 128, :], "LD")
            ld_all = P.val("LD")
            P.wait("gpsimd", "LD", ld_all)
            P.wait("gpsimd", "ST", P.val("ST"))
            for a, b in (("wgq_in", "wgq_out"), ("wvq_in", "wvq_out"),
                         ("woq_in", "woq_out")):
                P.emit("gpsimd", lambda E, x=T[a], y=T[b]:
                       E.collective_compute(
                           "AllGather", Alu.bypass,
                           replica_groups=[list(range(NCORE))],
                           ins=[x.ap().opt()], outs=[y.ap().opt()]),
                       inc="CC")
            marks["CC_W"] = P.val("CC")

            tsq = 0
            xqtr_done = {}
            sd_z1 = {}
            P.wait("vector", "LD", ld_all)
            for tt in range(NTT):
                if tt - 1 in sd_z1:
                    P.wait("vector", "SD", sd_z1[tt - 1])
                emit_ln(lambda tt=tt: zall[:, tt * D:(tt + 1) * D],
                        lambda: z1[:], tn)
                if DEBUG_STOP == "ffn":
                    for k_i, k_n in enumerate(("red", "ms", "mu", "vpe", "y0",
                                               "rstd", "t1", "t1")):
                        P.emit("vector", lambda E, tt=tt, k_i=k_i, k_n=k_n:
                               E.tensor_copy(lnsb[:, tt * 8 + k_i:tt * 8 + k_i + 1],
                                             tn[k_n][:]))
                    P.emit("vector", lambda E: E.nop(), inc="VE")
                ve_z1 = P.val("VE")
                P.wait("scalar", "VE", ve_z1)
                P.emit("scalar", lambda E, tt=tt: E.dma_start(
                    out=T["z1st"][tt * 128:(tt + 1) * 128, :], in_=z1[:]),
                    inc="SD", k=16)
                sd_z1[tt] = P.val("SD")
                P.emit("vector", lambda E: E.tensor_reduce(
                    tn["gmx"][:], z1[:], AXX, Alu.max,
                    apply_absolute_value=True))
                P.emit("vector", lambda E: E.drain())
                P.emit("vector", lambda E: E.tensor_tensor(
                    tn["gam"][:], tn["gmx"][:], S["epsc"][:], Alu.max))
                P.emit("vector", lambda E: E.drain())
                P.emit("vector", lambda E: E.reciprocal(
                    tn["inv"][:], tn["gam"][:]))
                P.emit("vector", lambda E: E.drain())
                P.emit("vector", lambda E: E.tensor_tensor(
                    tn["sx"][:], tn["inv"][:], S["c127"][:], Alu.mult))
                P.emit("vector", lambda E, tt=tt: E.scalar_tensor_tensor(
                    sxg[:, tt:tt + 1], tn["gam"][:], 1.0, wgam[:, 4:5],
                    op0=Alu.mult, op1=Alu.mult))
                P.emit("vector", lambda E, tt=tt: E.scalar_tensor_tensor(
                    sxv[:, tt:tt + 1], tn["gam"][:], 1.0, wgam[:, 5:6],
                    op0=Alu.mult, op1=Alu.mult))
                P.emit("vector", lambda E: E.drain())
                P.emit("vector", lambda E: E.tensor_scalar(
                    sqs[:], z1[:], tn["sx"][:], MAGIC, op0=Alu.mult,
                    op1=Alu.add))
                if tt - 1 in xqtr_done:
                    P.wait("vector", "TE", xqtr_done[tt - 1])
                P.emit("vector", lambda E: E.tensor_scalar(
                    Xq[:], sqs[:], MAGIC, None, op0=Alu.subtract), inc="VE")
                P.wait("tensor", "VE", P.val("VE"))
                for k in range(NKD):
                    P.wait("tensor", "SC", P.val("SC") - 1)
                    P.emit("tensor", lambda E, k=k, p=tsq % 2: E.transpose(
                        psb[p][:, 0:128], Xq[:, k * 128:(k + 1) * 128],
                        identb[:]), inc="TE")
                    P.wait("scalar", "TE", P.val("TE"))
                    P.emit("scalar", lambda E, k=k, tt=tt, p=tsq % 2: E.copy(
                        XqT[:, k * 1024 + tt * 128:k * 1024 + (tt + 1) * 128],
                        psb[p][:, 0:128]), inc="SC")
                    tsq += 1
                xqtr_done[tt] = P.val("TE")
            P.wait("tensor", "SC", P.val("SC"))

            cA1.close()
            wgsl = [EA(nc.sbuf_tensor(f"wgsl{i}", [128, HID], BF16))
                    for i in range(2)]
            wvsl = [EA(nc.sbuf_tensor(f"wvsl{i}", [128, HID], BF16))
                    for i in range(2)]
            gbr = [EA(nc.sbuf_tensor(f"gbr{i}", [1, 512], F32))
                   for i in range(2)]
            vbr = [EA(nc.sbuf_tensor(f"vbr{i}", [1, 512], F32))
                   for i in range(2)]
            gbj = EA(nc.sbuf_tensor("gbj", [128, 512], F32))
            vbj = EA(nc.sbuf_tensor("vbj", [128, 512], F32))
            upre = EA(nc.sbuf_tensor("upre", [128, 512], F32))
            gsil = EA(nc.sbuf_tensor("gsil", [128, 512], F32))
            vtile = EA(nc.sbuf_tensor("vtile", [128, 512], F32))
            gvst = [EA(nc.sbuf_tensor(f"gvst{i}", [128, 512], F32))
                    for i in range(2)]
            mjt = EA(nc.sbuf_tensor("mjt", [128, 1], F32))
            P.wait("sync", "CC", marks["CC_W"])
            ld_slab = {}
            mm_done = {}
            epi_done = {}
            sd_gv = {}
            for j in range(16):
                r, half = divmod(j, 2)
                if j - 2 in mm_done:
                    P.wait("sync", "TE", mm_done[j - 2])
                for k in range(NKD):
                    P.dma("sync", wgsl[j % 2][:, k * 512:(k + 1) * 512],
                          T["wgq_out"][r * D + k * 128:r * D + (k + 1) * 128,
                                       half * 512:(half + 1) * 512], "LD")
                    P.dma("sync", wvsl[j % 2][:, k * 512:(k + 1) * 512],
                          T["wvq_out"][r * D + k * 128:r * D + (k + 1) * 128,
                                       half * 512:(half + 1) * 512], "LD")
                P.dma("sync", gbr[j % 2][0:1, :],
                      T["gbias"][0:1, j * 512:(j + 1) * 512], "LD")
                P.dma("sync", vbr[j % 2][0:1, :],
                      T["vbias"][0:1, j * 512:(j + 1) * 512], "LD")
                ld_slab[j] = P.val("LD")
                if (j - 1) * NTT + NTT - 1 in epi_done:
                    P.wait("tensor", "VE", epi_done[(j - 1) * NTT + NTT - 1])
                P.wait("tensor", "LD", ld_slab[j])
                P.emit("tensor", lambda E, j=j: E.matmul(
                    psf[4][:, 0:512], ones1[0:1, :],
                    gbr[j % 2][0:1, 0:512],
                    start=True, stop=True))
                P.emit("tensor", lambda E, j=j: E.matmul(
                    psf[5][:, 0:512], ones1[0:1, :],
                    vbr[j % 2][0:1, 0:512],
                    start=True, stop=True), inc="TE")
                P.wait("vector", "TE", P.val("TE"))
                P.emit("vector", lambda E: E.tensor_copy(gbj[:], psf[4][:, 0:512]))
                P.emit("vector", lambda E: E.tensor_copy(vbj[:], psf[5][:, 0:512]),
                       inc="VE")
                for t in range(NTT):
                    gq = j * NTT + t
                    P.wait("tensor", "LD", ld_slab[j])
                    if gq - 2 in epi_done:
                        P.wait("tensor", "VE", epi_done[gq - 2])
                    for k in range(NKD):
                        P.emit("tensor", lambda E, k=k, t=t, j=j,
                               f=(k == 0), st=(k == NKD - 1): E.matmul(
                                   psf[t % 2][:, 0:512],
                                   XqT[:, k * 1024 + t * 128:k * 1024 + (t + 1) * 128],
                                   wgsl[j % 2][:, k * 512:(k + 1) * 512],
                                   start=f, stop=st),
                               inc=("TE" if k == NKD - 1 else None))
                    for k in range(NKD):
                        P.emit("tensor", lambda E, k=k, t=t, j=j,
                               f=(k == 0), st=(k == NKD - 1): E.matmul(
                                   psf[2 + t % 2][:, 0:512],
                                   XqT[:, k * 1024 + t * 128:k * 1024 + (t + 1) * 128],
                                   wvsl[j % 2][:, k * 512:(k + 1) * 512],
                                   start=f, stop=st),
                               inc=("TE" if k == NKD - 1 else None))
                    te_gv = P.val("TE")
                    P.wait("vector", "TE", te_gv)
                    P.emit("vector", lambda E, t=t: E.scalar_tensor_tensor(
                        upre[:], psf[t % 2][:, 0:512], sxg[:, t:t + 1], gbj[:],
                        op0=Alu.mult, op1=Alu.add), inc="VE")
                    P.wait("scalar", "VE", P.val("VE"))
                    P.emit("scalar", lambda E: E.activation(
                        gsil[:], upre[:], Act.Silu))
                    P.emit("scalar", lambda E: E.drain(), inc="SC")
                    P.emit("vector", lambda E, t=t: E.scalar_tensor_tensor(
                        vtile[:], psf[2 + t % 2][:, 0:512], sxv[:, t:t + 1],
                        vbj[:], op0=Alu.mult, op1=Alu.add))
                    P.wait("vector", "SC", P.val("SC"))
                    if gq - 2 in sd_gv:
                        P.wait("vector", "SD", sd_gv[gq - 2])
                    P.emit("vector", lambda E, sl=gq % 2: E.tensor_tensor(
                        gvst[sl][:], gsil[:], vtile[:], Alu.mult))
                    P.emit("vector", lambda E, sl=gq % 2: E.tensor_reduce(
                        mjt[:], gvst[sl][:], AXX, Alu.max,
                        apply_absolute_value=True))
                    P.emit("vector", lambda E: E.drain())
                    P.emit("vector", lambda E, t=t: E.tensor_tensor(
                        gvmax[:, t:t + 1], gvmax[:, t:t + 1], mjt[:], Alu.max),
                        inc="VE")
                    epi_done[gq] = P.val("VE")
                    P.wait("scalar", "VE", epi_done[gq])
                    P.emit("scalar", lambda E, t=t, j=j, sl=gq % 2: E.dma_start(
                        out=T["gv_dram"][t, j, :, :], in_=gvst[sl][:]),
                        inc="SD", k=16)
                    sd_gv[gq] = P.val("SD")
                mm_done[j] = P.val("TE")
            P.wait("sync", "SD", P.val("SD"))
            if DEBUG_STOP == "ffn":
                P.wait("sync", "VE", P.val("VE"))
                P.dma("sync", T["lndbg"], lnsb[:], "ST")
                P.wait("sync", "ST", P.val("ST"))
            P.barrier()

        with ExitStack() as cB:
            EB = cB.enter_context
            gvrow = [EB(nc.sbuf_tensor(f"gvrow{i}", [128, HID], F32))
                     for i in range(2)]
            gvq32 = EB(nc.sbuf_tensor("gvq32", [128, HID], F32))
            GVq = [EB(nc.sbuf_tensor(f"GVq{i}", [128, HID], BF16))
                   for i in range(2)]
            gvqTst = [EB(nc.sbuf_tensor(f"gvqTst{i}", [128, HID], BF16))
                      for i in range(2)]
            g2 = {k: EB(nc.sbuf_tensor("g2_" + k, [128, 1], F32))
                  for k in ("gam", "inv", "sgv")}
            tsq = 0
            ld_gv, ve_q, tr_done, st_q = {}, {}, {}, {}
            for t in range(NTT):
                if t - 2 in ve_q:
                    P.wait("sync", "VE", ve_q[t - 2])
                for j in range(16):
                    P.dma("sync", gvrow[t % 2][:, j * 512:(j + 1) * 512],
                          T["gv_dram"][t, j, :, :], "LD")
                ld_gv[t] = P.val("LD")
                P.wait("vector", "LD", ld_gv[t])
                P.emit("vector", lambda E, t=t: E.tensor_scalar(
                    g2["gam"][:], gvmax[:, t:t + 1], EPS, None, op0=Alu.max))
                P.emit("vector", lambda E: E.drain())
                P.emit("vector", lambda E: E.reciprocal(
                    g2["inv"][:], g2["gam"][:]))
                P.emit("vector", lambda E: E.drain())
                P.emit("vector", lambda E: E.tensor_scalar(
                    g2["sgv"][:], g2["inv"][:], 127.0, None, op0=Alu.mult))
                P.emit("vector", lambda E, t=t: E.scalar_tensor_tensor(
                    sO[:, t:t + 1], g2["gam"][:], 1.0, wgam[:, 6:7],
                    op0=Alu.mult, op1=Alu.mult))
                P.emit("vector", lambda E: E.drain())
                P.emit("vector", lambda E, t=t: E.tensor_scalar(
                    gvq32[:], gvrow[t % 2][:], g2["sgv"][:], MAGIC,
                    op0=Alu.mult, op1=Alu.add))
                if t - 2 in tr_done:
                    P.wait("vector", "TE", tr_done[t - 2])
                P.emit("vector", lambda E, t=t: E.tensor_scalar(
                    GVq[t % 2][:], gvq32[:], MAGIC, None, op0=Alu.subtract),
                    inc="VE")
                ve_q[t] = P.val("VE")
                P.wait("tensor", "VE", ve_q[t])
                first = True
                evm = []
                for k in range(NKH):
                    if len(evm) >= 2:
                        eng0, v0 = evm[-2]
                        P.wait("tensor", "SC" if eng0 == "scalar" else "VE",
                               v0)
                    P.emit("tensor", lambda E, k=k, t=t, p=tsq % 2:
                           E.transpose(psb[p][:, 0:128],
                                       GVq[t % 2][:, k * 128:(k + 1) * 128],
                                       identb[:]), inc="TE")
                    eng = "scalar" if k % 2 == 0 else "vector"
                    P.wait(eng, "TE", P.val("TE"))
                    if first and t - 2 in st_q:
                        P.wait(eng, "ST", st_q[t - 2])
                        first = False
                    if eng == "scalar":
                        P.emit("scalar", lambda E, k=k, t=t, p=tsq % 2:
                               E.copy(gvqTst[t % 2][:, k * 128:(k + 1) * 128],
                                      psb[p][:, 0:128]), inc="SC")
                        evm.append(("scalar", P.val("SC")))
                    else:
                        P.emit("vector", lambda E, k=k, t=t, p=tsq % 2:
                               E.tensor_copy(
                                   gvqTst[t % 2][:, k * 128:(k + 1) * 128],
                                   psb[p][:, 0:128]), inc="VE")
                        evm.append(("vector", P.val("VE")))
                    tsq += 1
                tr_done[t] = P.val("TE")
                P.wait("sync", "SC", P.val("SC"))
                P.wait("sync", "VE", P.val("VE"))
                P.dma("sync", T["gvqT_dram"][t, :, :], gvqTst[t % 2][:], "ST")
                st_q[t] = P.val("ST")
            P.wait("sync", "ST", P.val("ST"))
            P.barrier()

        with ExitStack() as cC:
            EC = cC.enter_context
            wosl = EC(nc.sbuf_tensor("wosl", [128, NKH * 512], BF16))
            gvqrd = [EC(nc.sbuf_tensor(f"gvqrd{i}", [128, HID], BF16))
                     for i in range(2)]
            obm = EC(nc.sbuf_tensor("obm", [1, D], F32))
            obj = EC(nc.sbuf_tensor("obj", [128, 512], F32))
            Oacc = EC(nc.sbuf_tensor("Oacc", [128, NTT * D], F32))
            z1rd = EC(nc.sbuf_tensor("z1rd", [128, D], F32))
            z2b = EC(nc.sbuf_tensor("z2b", [128, D], F32))
            ofin = EC(nc.sbuf_tensor("ofin", [128, D], F32))
            sq2 = EC(nc.sbuf_tensor("sq2", [128, D], F32))
            tn2 = {k: EC(nc.sbuf_tensor("l2_" + k, [128, 1], F32))
                   for k in ("red", "mu", "ms", "vpe", "y0", "t1", "rstd")}
            tn2["sqs"] = sq2

            P.dma("sync", obm[:], T["obias"], "LD")
            P.wait("tensor", "LD", P.val("LD"))
            oq_mm = {}
            slab_mm = {}
            epi_o = {}
            for j in range(4):
                if j - 1 in slab_mm:
                    P.wait("sync", "TE", slab_mm[j - 1])
                for k in range(NKH):
                    P.dma("sync", wosl[:, k * 512:(k + 1) * 512],
                          T["woq_out"][k * 128:(k + 1) * 128,
                                       j * 512:(j + 1) * 512], "LD")
                ld_wo = P.val("LD")
                if (j - 1) * NTT + NTT - 1 in epi_o:
                    P.wait("tensor", "VE", epi_o[(j - 1) * NTT + NTT - 1])
                P.emit("tensor", lambda E, j=j: E.matmul(
                    psf[4][:, 0:512], ones1[0:1, :],
                    obm[0:1, j * 512:(j + 1) * 512],
                    start=True, stop=True), inc="TE")
                P.wait("vector", "TE", P.val("TE"))
                P.emit("vector", lambda E: E.tensor_copy(obj[:], psf[4][:, 0:512]),
                       inc="VE")
                for t in range(NTT):
                    oq = j * NTT + t
                    if oq - 2 in oq_mm:
                        P.wait("sync", "TE", oq_mm[oq - 2])
                    P.dma("sync", gvqrd[t % 2][:], T["gvqT_dram"][t, :, :], "LD")
                    P.wait("tensor", "LD", P.val("LD"))
                    if oq - 3 in epi_o:
                        P.wait("tensor", "VE", epi_o[oq - 3])
                    for k in range(NKH):
                        P.emit("tensor", lambda E, k=k, t=t, b=oq % 3,
                               f=(k == 0), st=(k == NKH - 1): E.matmul(
                                   psf[b][:, 0:512],
                                   gvqrd[t % 2][:, k * 128:(k + 1) * 128],
                                   wosl[:, k * 512:(k + 1) * 512],
                                   start=f, stop=st),
                               inc=("TE" if k == NKH - 1 else None))
                    oq_mm[oq] = P.val("TE")
                    P.wait("vector", "TE", oq_mm[oq])
                    P.emit("vector", lambda E, t=t, j=j, b=oq % 3:
                           E.scalar_tensor_tensor(
                               Oacc[:, t * D + j * 512:t * D + (j + 1) * 512],
                               psf[b][:, 0:512], sO[:, t:t + 1], obj[:],
                               op0=Alu.mult, op1=Alu.add), inc="VE")
                    epi_o[oq] = P.val("VE")
                slab_mm[j] = P.val("TE")

            st_out = {}
            for t in range(NTT):
                P.wait("sync", "VE", P.val("VE"))
                P.dma("sync", z1rd[:], T["z1st"][t * 128:(t + 1) * 128, :], "LD")
                P.wait("vector", "LD", P.val("LD"))
                if t - 1 in st_out:
                    P.wait("vector", "ST", st_out[t - 1])
                P.emit("vector", lambda E, t=t: E.tensor_tensor(
                    z2b[:], z1rd[:], Oacc[:, t * D:(t + 1) * D], Alu.add))
                emit_ln(lambda: z2b[:], lambda: ofin[:], tn2)
                P.wait("sync", "VE", P.val("VE"))
                P.dma("sync", T["out"][t * 128:(t + 1) * 128, :], ofin[:], "ST")
                st_out[t] = P.val("ST")
            P.wait("sync", "ST", P.val("ST"))
            P.wait("sync", "SD", P.val("SD"))
            if DEBUG_STOP == "ffn":
                P.dma("gpsimd", T["z1dbg"], T["z1st"].ap(), "GD")
                P.dma("gpsimd", T["gvdbg"],
                      T["gv_dram"].ap().rearrange("a b c d -> (a b c) d"), "GD")
                P.dma("gpsimd", T["gvqdbg"],
                      T["gvqT_dram"].ap().rearrange("a b c -> (a b) c"), "GD")
                P.dma("gpsimd", T["zmydbg"], T["z_my"].ap(), "GD")
                P.wait("gpsimd", "GD", P.val("GD"))


_NC_CACHE = {}


def _get_nc():
    key = DEBUG_STOP
    if key not in _NC_CACHE:
        _NC_CACHE[key] = build_nc()
    return _NC_CACHE[key]


def kernel(L0_attn_x, L0_attn_W_x, L0_attn_W_dt, L0_attn_b_dt, L0_attn_A_log,
           L0_attn_D, L0_ffn_gate_w, L0_ffn_gate_b, L0_ffn_val_w, L0_ffn_val_b,
           L0_ffn_out_w, L0_ffn_out_b):
    f32 = np.float32
    x = np.asarray(L0_attn_x, f32)
    W_x = np.ascontiguousarray(np.asarray(L0_attn_W_x, f32))
    W_dt = np.asarray(L0_attn_W_dt, f32)
    b_dt = np.asarray(L0_attn_b_dt, f32)
    A_log = np.asarray(L0_attn_A_log, f32)
    Dv = np.asarray(L0_attn_D, f32)
    WgT = np.ascontiguousarray(np.asarray(L0_ffn_gate_w, f32).T)
    WvT = np.ascontiguousarray(np.asarray(L0_ffn_val_w, f32).T)
    WoT = np.ascontiguousarray(np.asarray(L0_ffn_out_w, f32).T)
    gb = np.ascontiguousarray(np.asarray(L0_ffn_gate_b, f32).reshape(1, HID))
    vb = np.ascontiguousarray(np.asarray(L0_ffn_val_b, f32).reshape(1, HID))
    ob = np.ascontiguousarray(np.asarray(L0_ffn_out_b, f32).reshape(1, D))

    xf = np.ascontiguousarray(x.reshape(B * L, D))
    in_maps = []
    for c in range(NCORE):
        dsl = slice(DSL * c, DSL * (c + 1))
        hsl = slice(1024 * c, 1024 * (c + 1))
        in_maps.append(dict(
            x_tok=np.ascontiguousarray(xf[TOK * c:TOK * (c + 1)]),
            x_dsl=np.ascontiguousarray(x[:, :, dsl]),
            wx=W_x,
            wdtT=np.ascontiguousarray(W_dt[dsl].T),
            bdt=np.ascontiguousarray(b_dt[dsl].reshape(2, 128).T),
            alog=np.ascontiguousarray(
                A_log[dsl].reshape(2, 128, NST).transpose(1, 0, 2)
                .reshape(128, 2 * NST)),
            dvec=np.ascontiguousarray(Dv[dsl].reshape(2, 128).T),
            wgTd=np.ascontiguousarray(WgT[:, hsl]),
            wvTd=np.ascontiguousarray(WvT[:, hsl]),
            woTd=np.ascontiguousarray(WoT[hsl]),
            gbias=gb, vbias=vb, obias=ob,
        ))
    nc = _get_nc()
    res = run_bass_kernel_spmd(nc, in_maps, core_ids=list(range(NCORE)))
    if DEBUG_STOP is not None:
        return res
    o = np.concatenate([r["out"] for r in res.results], axis=0)
    return o.reshape(B, L, D)


if __name__ == "__main__":
    nc = build_nc()
    n = sum(1 for _ in nc.m.functions[0].blocks for _ in [0])
    print("built ok")


# version 16
# speedup vs baseline: 1.1878x; 1.0205x over previous
import numpy as np
import ml_dtypes
from contextlib import ExitStack

import concourse.bass as bass
import concourse.mybir as mybir
from concourse.bass_utils import run_bass_kernel_spmd

F32 = mybir.dt.float32
BF16 = mybir.dt.bfloat16
Alu = mybir.AluOpType
Act = mybir.ActivationFunctionType
AXX = mybir.AxisListType.X

MAGIC = float(np.float32(1.5 * 2 ** 23))
NCORE = 8
B, L, D, NST, RDT, HID = 2, 4096, 2048, 16, 64, 8192
TOK = (B * L) // NCORE
DSL = D // NCORE
CH = 512
NCH = L // CH
NKD = D // 128
NTT = TOK // 128
NKH = HID // 128
EPS = 1e-5

DEBUG_STOP = None
DEBUG_DUMPS = False
DEBUG_BC1 = False
DEBUG_A1 = False
DEBUG_NOBC = False


class _Prog:

    ENGINES = ("sync", "vector", "scalar", "tensor", "gpsimd")

    def __init__(self):
        self.ops = []
        self.cnt = {}

    def sem(self, name):
        self.cnt.setdefault(name, 0)

    def emit(self, eng, fn, inc=None, k=1):
        if inc is not None:
            self.sem(inc)
            self.cnt[inc] += k
        self.ops.append(("op", eng, fn, inc, k))

    def dma(self, eng, out, in_, inc):
        self.emit(eng, lambda E, o=out, i=in_: E.dma_start(out=o, in_=i),
                  inc=inc, k=16)

    def wait(self, eng, semname, val=None):
        v = self.cnt.get(semname, 0) if val is None else val
        if v > 0:
            self.ops.append(("wait", eng, semname, v, None))

    def val(self, semname):
        return self.cnt.get(semname, 0)

    def barrier(self):
        self.sem("PH")
        for e in self.ENGINES:
            if e in ("vector", "scalar", "tensor"):
                self.emit(e, lambda E: E.drain(), inc="PH")
            else:
                self.emit(e, lambda E: E.nop(), inc="PH")
        tgt = self.cnt["PH"]
        for e in self.ENGINES:
            self.wait(e, "PH", tgt)

    def lower(self, nc, block, ctx):
        handles = {name: ctx.enter_context(nc.semaphore(name))
                   for name in self.cnt}
        for eng_name in self.ENGINES:
            my = [op for op in self.ops if op[1] == eng_name]
            if not my:
                continue

            def section(E, my=my, handles=handles):
                for kind, _eng, a, b, c in my:
                    if kind == "wait":
                        E.wait_ge(handles[a], b)
                    else:
                        instr = a(E)
                        if b is not None:
                            instr.then_inc(handles[b], c)
            getattr(block, eng_name)(section)


def build_nc():
    nc = bass.Bass("TRN2", target_bir_lowering=False)

    T = {}

    def din(name, shape, dt=F32):
        T[name] = nc.dram_tensor(name, shape, dt, kind="ExternalInput").ap()

    def dout(name, shape, dt=F32):
        T[name] = nc.dram_tensor(name, shape, dt, kind="ExternalOutput").ap()

    def dint(name, shape, dt=F32, shared=False):
        T[name] = nc.dram_tensor(name, shape, dt,
                                 addr_space="Shared" if shared else "Local")

    din("x_tok", [TOK, D]); din("x_dsl", [B, L, DSL]); din("wx", [96, D])
    din("wdtT", [RDT, DSL]); din("bdt", [128, 2]); din("alog", [128, 2 * NST])
    din("dvec", [128, 2]); din("wgTd", [D, 1024]); din("wvTd", [D, 1024])
    din("woTd", [1024, D]); din("gbias", [1, HID]); din("vbias", [1, HID])
    din("obias", [1, D])
    dout("out", [TOK, D])
    if DEBUG_STOP == "mixer":
        dout("zdbg", [B * L, DSL])
        dout("dtdbg", [128, 2 * CH * B * NCH])
        dout("xdbg", [128, 2 * CH * B * NCH])
        dout("xtdbg", [128, 2 * CH * B * NCH])
        dout("abdbg", [128, 4 * CH * NST])
        dout("atdbg", [128, 2 * NST])
        dout("xrdbg", [128, 4 * DSL * B * NCH])
        dout("ydbg", [128, 2 * CH * B * NCH])
    if DEBUG_STOP == "ffn":
        dout("z1dbg", [TOK, D])
        dout("gvdbg", [NTT * 16 * 128, 512])
        dout("gvqdbg", [NTT * 128, HID], BF16)
        dout("zmydbg", [TOK, D])
        dout("lndbg", [128, 8 * NTT])
    if DEBUG_STOP == "ph0":
        dout("agdbg", [NCORE * 96, TOK])
        dout("wqdbg", [NCORE * D, 1024], BF16)
        dout("gmdbg", [NCORE, 4])
        dout("wgdbg", [128, 8])
    dint("ag1_in", [96, TOK])
    dint("ag1_out", [NCORE * 96, TOK], shared=True)
    dint("gm_in", [1, 4]); dint("gm_out", [NCORE, 4], shared=True)
    dint("wgq_in", [D, 1024], BF16)
    dint("wgq_out", [NCORE * D, 1024], BF16, shared=True)
    dint("wvq_in", [D, 1024], BF16)
    dint("wvq_out", [NCORE * D, 1024], BF16, shared=True)
    dint("woq_in", [1024, D], BF16)
    dint("woq_out", [NCORE * 1024, D], BF16, shared=True)
    dint("z_in", [B * L, DSL])
    dint("z_out", [NCORE * B * L, DSL], shared=True)
    dint("z_my", [TOK, D])
    dint("z1st", [TOK, D])
    dint("gv_dram", [NTT, 16, 128, 512])
    dint("gvqT_dram", [NTT, 128, HID], BF16)

    P = _Prog()
    for s in ("LD", "ST", "SD", "GD", "CC", "VE", "SC", "TE", "GP", "PH",
              "MF", "BF"):
        P.sem(s)
    marks = {}

    ctx = ExitStack()
    with ctx:
        E = ctx.enter_context
        S = {}
        S["identb"] = E(nc.sbuf_tensor("identb", [128, 128], BF16))
        S["identf"] = E(nc.sbuf_tensor("identf", [128, 128], F32))
        S["ones1"] = E(nc.sbuf_tensor("ones1", [1, 128], F32))
        S["onescol"] = E(nc.sbuf_tensor("onescol", [128, 1], F32))
        S["ones1b"] = E(nc.sbuf_tensor("ones1b", [1, 128], BF16))
        S["wgam"] = E(nc.sbuf_tensor("wgam", [128, 8], F32))
        S["dinv"] = E(nc.sbuf_tensor("dinv", [128, 1], F32))
        S["epsc"] = E(nc.sbuf_tensor("epsc", [128, 1], F32))
        S["c127"] = E(nc.sbuf_tensor("c127", [128, 1], F32))
        S["psf"] = [E(nc.psum_tensor(f"psf{i}", [128, 512], F32))
                    for i in range(6)]
        S["psb"] = [E(nc.psum_tensor(f"psb{i}", [128, 1024], BF16))
                    for i in range(2)]
        block = E(nc.Block())

        with ExitStack() as cscan:
            ES = cscan.enter_context
            S["xpj_h"] = ES(nc.sbuf_tensor("xpj_h", [96, B * L], BF16))
            S["xpj_l"] = ES(nc.sbuf_tensor("xpj_l", [96, B * L], BF16))

            _phase0(P, nc, T, S, marks)

            if DEBUG_STOP == "ph0":
                P.wait("gpsimd", "CC", marks["CC_GM"])
                P.dma("gpsimd", T["agdbg"], T["ag1_out"].ap(), "GD")
                P.dma("gpsimd", T["wqdbg"], T["wgq_out"].ap(), "GD")
                P.dma("gpsimd", T["gmdbg"], T["gm_out"].ap(), "GD")
                P.dma("gpsimd", T["wgdbg"], S["wgam"][:], "GD")
                P.wait("gpsimd", "GD", P.val("GD"))
                P.lower(nc, block, ctx)
                return nc

            _scan(P, nc, T, S, marks)

            if DEBUG_STOP == "mixer":
                P.wait("gpsimd", "ST", P.val("ST"))
                P.dma("gpsimd", T["zdbg"], T["z_in"].ap(), "GD")
                P.wait("gpsimd", "GD", P.val("GD"))
                P.lower(nc, block, ctx)
                return nc

            P.wait("gpsimd", "ST", P.val("ST"))
            P.wait("gpsimd", "SD", P.val("SD"))
            P.emit("gpsimd", lambda E: E.collective_compute(
                "AllGather", Alu.bypass, replica_groups=[list(range(NCORE))],
                ins=[T["z_in"].ap().opt()], outs=[T["z_out"].ap().opt()]),
                inc="CC")
            marks["CC_Z"] = P.val("CC")
            P.barrier()
        _ffn(P, nc, T, S, marks)
        P.lower(nc, block, ctx)
    return nc


def _ag(P, T, name_in, name_out, mark, marks):
    P.wait("gpsimd", "ST", P.val("ST"))
    P.emit("gpsimd", lambda E, a=T[name_in], b=T[name_out]: E.collective_compute(
        "AllGather", Alu.bypass, replica_groups=[list(range(NCORE))],
        ins=[a.ap().opt()], outs=[b.ap().opt()]), inc="CC")
    marks[mark] = P.val("CC")


def _phase0(P, nc, T, S, marks):
    psf, psb = S["psf"], S["psb"]
    identb, identf = S["identb"], S["identf"]
    ones1, onescol, wgam = S["ones1"], S["onescol"], S["wgam"]

    with ExitStack() as c0:
        E0 = c0.enter_context
        xtk2 = [E0(nc.sbuf_tensor(f"xtk{i}", [128, D], F32))
                for i in range(2)]
        xth2 = [E0(nc.sbuf_tensor(f"xth{i}", [128, D], BF16))
                for i in range(2)]
        xtl2 = [E0(nc.sbuf_tensor(f"xtl{i}", [128, D], BF16))
                for i in range(2)]
        xtmp = E0(nc.sbuf_tensor("xtmp", [128, D], F32))
        xTh2 = [E0(nc.sbuf_tensor(f"xTh{i}", [128, NKD * 128], BF16))
                for i in range(2)]
        xTl2 = [E0(nc.sbuf_tensor(f"xTl{i}", [128, NKD * 128], BF16))
                for i in range(2)]
        wxs = E0(nc.sbuf_tensor("wxs", [96, D], F32))
        wxh = E0(nc.sbuf_tensor("wxh", [96, D], BF16))
        wxl = E0(nc.sbuf_tensor("wxl", [96, D], BF16))
        wxTh = E0(nc.sbuf_tensor("wxTh", [128, NKD * 96], BF16))
        wxTl = E0(nc.sbuf_tensor("wxTl", [128, NKD * 96], BF16))
        xpmine = E0(nc.sbuf_tensor("xpmine", [96, TOK], F32))
        wti = [E0(nc.sbuf_tensor(f"wti{i}", [128, 2048], F32)) for i in range(2)]
        wqo = [E0(nc.sbuf_tensor(f"wqo{i}", [128, 2048], BF16)) for i in range(2)]
        wtmp = E0(nc.sbuf_tensor("wtmp", [128, 2048], F32))
        wacc = E0(nc.sbuf_tensor("wacc", [128, 1], F32))
        wred = E0(nc.sbuf_tensor("wred", [128, 1], F32))
        gmsb = E0(nc.sbuf_tensor("gmsb", [1, 8], F32))
        gmrd = E0(nc.sbuf_tensor("gmrd", [1, 32], F32))

        P.emit("gpsimd", lambda E: E.memset(identb[:], 0.0))
        P.emit("gpsimd", lambda E: E.affine_select(
            out=identb[:], in_=identb[:], compare_op=Alu.not_equal,
            fill=1.0, base=0, pattern=[[-1, 128]], channel_multiplier=1))
        P.emit("gpsimd", lambda E: E.memset(identf[:], 0.0))
        P.emit("gpsimd", lambda E: E.affine_select(
            out=identf[:], in_=identf[:], compare_op=Alu.not_equal,
            fill=1.0, base=0, pattern=[[-1, 128]], channel_multiplier=1))
        P.emit("gpsimd", lambda E: E.memset(ones1[:], 1.0))
        P.emit("gpsimd", lambda E: E.memset(onescol[:], 1.0))
        P.emit("gpsimd", lambda E: E.memset(S["dinv"][:], 1.0 / D))
        P.emit("gpsimd", lambda E: E.memset(S["epsc"][:], EPS))
        P.emit("gpsimd", lambda E: E.memset(S["c127"][:], 127.0))
        P.emit("gpsimd", lambda E: E.memset(S["ones1b"][:], 1.0), inc="GP")
        gp_const = P.val("GP")

        P.dma("sync", wxs[:], T["wx"], "LD")
        P.wait("vector", "LD", P.val("LD"))
        P.emit("vector", lambda E: E.tensor_copy(wxh[:], wxs[:]))
        P.emit("vector", lambda E: E.tensor_tensor(
            xtmp[0:96, :], wxs[:], wxh[:], Alu.subtract))
        P.emit("vector", lambda E: E.tensor_copy(wxl[:], xtmp[0:96, :]), inc="VE")
        P.wait("tensor", "VE", P.val("VE"))
        P.wait("tensor", "GP", gp_const)
        tsq = 0
        for k in range(NKD):
            for src, dst in ((wxh, wxTh), (wxl, wxTl)):
                if tsq >= 2:
                    P.wait("tensor", "SC", P.val("SC") - 1)
                P.emit("tensor", lambda E, s=src, k=k, p=tsq % 2: E.transpose(
                    psb[p][:, 0:96], s[:, k * 128:(k + 1) * 128],
                    identb[0:96, 0:96]), inc="TE")
                P.wait("scalar", "TE", P.val("TE"))
                P.emit("scalar", lambda E, d=dst, k=k, p=tsq % 2: E.copy(
                    d[:, k * 96:(k + 1) * 96], psb[p][:, 0:96]), inc="SC")
                tsq += 1

        ld_x, ve_sp, te_tr, ve_out = {}, {}, {}, {}
        for tt in range(NTT):
            sx = tt % 2
            if tt - 2 in ve_sp:
                P.wait("sync", "VE", ve_sp[tt - 2])
            P.dma("sync", xtk2[sx][:],
                  T["x_tok"][tt * 128:(tt + 1) * 128, :], "LD")
            ld_x[tt] = P.val("LD")
            P.wait("vector", "LD", ld_x[tt])
            if tt - 2 in te_tr:
                P.wait("vector", "TE", te_tr[tt - 2])
            P.emit("vector", lambda E, sx=sx: E.tensor_copy(
                xth2[sx][:], xtk2[sx][:]))
            P.emit("vector", lambda E, sx=sx: E.tensor_tensor(
                xtmp[:], xtk2[sx][:], xth2[sx][:], Alu.subtract))
            P.emit("vector", lambda E, sx=sx: E.tensor_copy(
                xtl2[sx][:], xtmp[:]), inc="VE")
            ve_sp[tt] = P.val("VE")
            P.wait("tensor", "VE", ve_sp[tt])
            for k in range(NKD):
                for src, dst in ((xth2[sx], xTh2[sx]), (xtl2[sx], xTl2[sx])):
                    P.wait("tensor", "SC", P.val("SC") - 1)
                    P.emit("tensor", lambda E, s=src, k=k, p=tsq % 2: E.transpose(
                        psb[p][:, 0:128], s[:, k * 128:(k + 1) * 128], identb[:]),
                        inc="TE")
                    P.wait("scalar", "TE", P.val("TE"))
                    P.emit("scalar", lambda E, d=dst, k=k, p=tsq % 2: E.copy(
                        d[:, k * 128:(k + 1) * 128], psb[p][:, 0:128]), inc="SC")
                    tsq += 1
            te_tr[tt] = P.val("TE")
            P.wait("tensor", "SC", P.val("SC"))
            n_mm = NKD * 4
            i_mm = 0
            for k in range(NKD):
                for lh in (wxTh, wxTl):
                    for rh in (xTh2[sx], xTl2[sx]):
                        P.emit("tensor",
                               lambda E, k=k, lh=lh, rh=rh, f=(i_mm == 0),
                               st=(i_mm == n_mm - 1): E.matmul(
                                   psf[4][0:96, 0:128],
                                   lh[:, k * 96:(k + 1) * 96],
                                   rh[:, k * 128:(k + 1) * 128],
                                   start=f, stop=st),
                               inc=("TE" if i_mm == n_mm - 1 else None))
                        i_mm += 1
            P.wait("vector", "TE", P.val("TE"))
            P.emit("vector", lambda E, tt=tt: E.tensor_copy(
                xpmine[:, tt * 128:(tt + 1) * 128], psf[4][0:96, 0:128]),
                inc="VE")
            P.wait("tensor", "VE", P.val("VE"))
        P.wait("sync", "VE", P.val("VE"))
        P.dma("sync", T["ag1_in"].ap(), xpmine[:], "ST")
        _ag(P, T, "ag1_in", "ag1_out", "CC_AG1", marks)

        wspec = [
            ("wgTd", "wgq_in", D // 128, 1024, 0),
            ("wvTd", "wvq_in", D // 128, 1024, 1),
            ("woTd", "woq_in", 1024 // 128, 2048, 2),
        ]
        for wsrc, wdst, ntile, width, widx in wspec:
            for it in range(ntile):
                buf = wti[it % 2]
                P.wait("sync", "VE", P.val("VE"))
                P.dma("sync", buf[:, 0:width],
                      T[wsrc][it * 128:(it + 1) * 128, :], "LD")
                P.wait("vector", "LD", P.val("LD"))
                P.emit("vector", lambda E, b=buf, w=width: E.tensor_reduce(
                    wred[:], b[:, 0:w], AXX, Alu.add, apply_absolute_value=True))
                P.emit("vector", lambda E: E.drain())
                if it == 0:
                    P.emit("vector", lambda E: E.tensor_copy(wacc[:], wred[:]),
                           inc="VE")
                else:
                    P.emit("vector", lambda E: E.tensor_tensor(
                        wacc[:], wacc[:], wred[:], Alu.add), inc="VE")
            P.wait("tensor", "VE", P.val("VE"))
            P.emit("tensor", lambda E: E.matmul(
                psf[5][0:1, 0:1], wacc[:], onescol[:], start=True, stop=True),
                inc="TE")
            P.wait("vector", "TE", P.val("TE"))
            P.emit("vector", lambda E, wi=widx: E.tensor_scalar(
                gmsb[0:1, wi:wi + 1], psf[5][0:1, 0:1], 1.0 / (HID * D), None,
                op0=Alu.mult), inc="VE")
            P.wait("tensor", "VE", P.val("VE"))
        P.wait("sync", "VE", P.val("VE"))
        P.dma("sync", T["gm_in"].ap(), gmsb[0:1, 0:4], "ST")
        _ag(P, T, "gm_in", "gm_out", "CC_GM", marks)
        P.wait("sync", "CC", marks["CC_GM"])
        P.dma("sync", gmrd[0:1, 0:32], T["gm_out"].ap(), "LD")
        P.wait("vector", "LD", P.val("LD"))
        gm3 = gmrd[0:1, 0:32].rearrange("p (r f) -> p r f", f=4)
        for widx in range(3):
            P.emit("vector", lambda E, wi=widx: E.tensor_reduce(
                gmsb[0:1, wi:wi + 1], gm3[:, :, wi], AXX, Alu.add))
        P.emit("vector", lambda E: E.drain(), inc="VE")
        P.wait("scalar", "VE", P.val("VE"))
        P.emit("scalar", lambda E: E.copy(gmsb[0:1, 4:7], gmsb[0:1, 0:3]),
               inc="SC")
        P.wait("vector", "SC", P.val("SC"))
        P.emit("vector", lambda E: E.tensor_scalar(
            gmsb[0:1, 0:3], gmsb[0:1, 4:7], EPS, None, op0=Alu.max))
        P.emit("vector", lambda E: E.drain())
        P.emit("vector", lambda E: E.tensor_scalar(
            gmsb[0:1, 4:7], gmsb[0:1, 0:3], 1.0 / 127.0, None, op0=Alu.mult))
        P.emit("vector", lambda E: E.drain())
        P.emit("vector", lambda E: E.reciprocal(gmsb[0:1, 0:3], gmsb[0:1, 0:3]))
        P.emit("vector", lambda E: E.drain(), inc="VE")
        P.wait("tensor", "VE", P.val("VE"))
        P.emit("tensor", lambda E: E.matmul(
            psf[5][:, 0:8], ones1[0:1, :], gmsb[0:1, 0:8], start=True,
            stop=True), inc="TE")
        P.wait("vector", "TE", P.val("TE"))
        P.emit("vector", lambda E: E.tensor_copy(wgam[:], psf[5][:, 0:8]),
               inc="VE")

        P.barrier()


def _scan(P, nc, T, S, marks):
    psf = S["psf"]
    identf = S["identf"]
    xpj_h, xpj_l = S["xpj_h"], S["xpj_l"]
    NP = B * NCH

    with ExitStack() as c1:
        E1 = c1.enter_context
        wdts = E1(nc.sbuf_tensor("wdts", [RDT, DSL], F32))
        wdth = E1(nc.sbuf_tensor("wdth", [RDT, DSL], BF16))
        wdtl = E1(nc.sbuf_tensor("wdtl", [RDT, DSL], BF16))
        wdtmp = E1(nc.sbuf_tensor("wdtmp", [RDT, DSL], F32))
        bdts = E1(nc.sbuf_tensor("bdts", [128, 2], F32))
        alogs = E1(nc.sbuf_tensor("alogs", [128, 2 * NST], F32))
        atile = E1(nc.sbuf_tensor("atile", [128, 2 * NST], F32))
        dp1 = E1(nc.sbuf_tensor("dp1", [128, 2], F32))
        xraw = [E1(nc.sbuf_tensor(f"xraw{i}", [128, 4 * DSL], F32))
                for i in range(2)]
        xT = [E1(nc.sbuf_tensor(f"xT{i}", [128, 2 * CH], F32))
              for i in range(2)]
        dtt = [E1(nc.sbuf_tensor(f"dtt{i}", [128, 2 * CH], F32))
               for i in range(2)]
        dtx = [E1(nc.sbuf_tensor(f"dtx{i}", [128, 2 * CH], F32))
               for i in range(2)]
        spm = E1(nc.sbuf_tensor("spm", [128, 2 * CH], F32))
        spa = E1(nc.sbuf_tensor("spa", [128, 2 * CH], F32))
        abuf = E1(nc.sbuf_tensor("abuf", [128, 4 * CH], F32))
        wbuf = E1(nc.sbuf_tensor("wbuf", [128, 2 * CH], F32))
        htr = [E1(nc.sbuf_tensor(f"htr{i}", [128, 2 * CH], F32))
               for i in range(2)]
        mbf = [E1(nc.sbuf_tensor(f"mbf{i}", [128, 2 * CH], F32))
               for i in range(2)]
        csb = E1(nc.sbuf_tensor("csb", [128, 2 * CH], F32))
        yacc = E1(nc.sbuf_tensor("yacc", [128, 2 * CH], F32))
        ztile = E1(nc.sbuf_tensor("ztile", [128, 2 * CH], F32))
        zstage = E1(nc.sbuf_tensor("zstage", [128, 4 * DSL], F32))
        hst = E1(nc.sbuf_tensor("hst", [128, 4 * NST], F32))

        for eng in ("sync", "vector", "scalar", "tensor", "gpsimd"):
            P.wait(eng, "CC", marks["CC_GM"])
        P.dma("sync", wdts[:], T["wdtT"], "LD")
        P.dma("sync", bdts[:], T["bdt"], "LD")
        P.dma("sync", alogs[:], T["alog"], "LD")
        P.dma("sync", dp1[:], T["dvec"], "LD")
        P.wait("vector", "LD", P.val("LD"))
        P.emit("vector", lambda E: E.tensor_copy(wdth[:], wdts[:]))
        P.emit("vector", lambda E: E.tensor_tensor(
            wdtmp[:], wdts[:], wdth[:], Alu.subtract))
        P.emit("vector", lambda E: E.tensor_copy(wdtl[:], wdtmp[:]))
        P.emit("vector", lambda E: E.tensor_scalar(
            dp1[:], dp1[:], 1.0, None, op0=Alu.add), inc="VE")
        P.wait("scalar", "LD", P.val("LD"))
        P.emit("scalar", lambda E: E.activation(atile[:], alogs[:], Act.Exp),
               inc="SC")
        P.wait("vector", "SC", P.val("SC"))
        P.emit("vector", lambda E: E.tensor_scalar(
            atile[:], atile[:], -1.0, None, op0=Alu.mult), inc="VE")
        P.emit("vector", lambda E: E.memset(hst[:], 0.0), inc="GP")
        gp_hst = P.val("GP")

        P.wait("sync", "CC", marks["CC_AG1"])
        with ExitStack() as cx:
            xprojT = cx.enter_context(
                nc.sbuf_tensor("xprojT", [96, B * L], F32))
            losc = cx.enter_context(nc.sbuf_tensor("losc", [96, 1024], F32))
            for r in range(NCORE):
                P.dma("sync", xprojT[:, r * TOK:(r + 1) * TOK],
                      T["ag1_out"][r * 96:(r + 1) * 96, :], "LD")
            P.wait("vector", "LD", P.val("LD"))
            P.emit("vector", lambda E: E.tensor_copy(xpj_h[:], xprojT[:, :]))
            for cs in range(0, B * L, 1024):
                P.emit("vector", lambda E, cs=cs: E.tensor_tensor(
                    losc[:, :], xprojT[:, cs:cs + 1024],
                    xpj_h[:, cs:cs + 1024], Alu.subtract))
                P.emit("vector", lambda E, cs=cs: E.tensor_copy(
                    xpj_l[:, cs:cs + 1024], losc[:, :]),
                    inc=("VE" if cs == B * L - 1024 else None))
        stg_h = E1(nc.sbuf_tensor("stg_h", [1, 32 * CH], BF16))
        stg_l = E1(nc.sbuf_tensor("stg_l", [1, 32 * CH], BF16))
        qwi = [E1(nc.sbuf_tensor(f"qwi{i}", [128, 1024], F32))
               for i in range(2)]
        qwo = [E1(nc.sbuf_tensor(f"qwo{i}", [128, 1024], BF16))
               for i in range(2)]
        qtmp = E1(nc.sbuf_tensor("qtmp", [128, 1024], F32))
        qcst = E1(nc.sbuf_tensor("qcst", [128, 4], F32))
        ve_prep = P.val("VE")
        P.wait("tensor", "VE", ve_prep)
        P.wait("tensor", "GP", gp_hst)
        P.wait("vector", "GP", gp_hst)
        P.wait("scalar", "VE", ve_prep)
        P.wait("sync", "VE", ve_prep)

        ld_ready, ld_stage = {}, {}
        xtr_done, xT_ready, ztr_done, zst_done = {}, {}, {}, {}
        sc_sp, gp_dtt, gp_dtx, ve_zt = {}, {}, {}, {}
        te_bc, sc_exp, sc_evC, ve_tts, ve_m0, gp_n = {}, {}, {}, {}, {}, {}
        mf_marks = []

        def emit_head(li):
            b, ci = divmod(li, NCH)
            colb = b * L + ci * CH
            P.wait("tensor", "LD", ld_ready[li])
            first_evac = True
            for s in range(4):
                for d in range(2):
                    if len(mf_marks) >= 2:
                        P.wait("tensor", "SC", mf_marks[-2])
                    P.emit("tensor", lambda E, s=s, d=d, li=li: E.transpose(
                        psf[4 + d][:, 0:128],
                        xraw[li % 2][:, s * DSL + d * 128:s * DSL + (d + 1) * 128],
                        identf[:]), inc="TE")
                    P.wait("scalar", "TE", P.val("TE"))
                    if first_evac:
                        if li - 2 in gp_dtx:
                            P.wait("scalar", "GP", gp_dtx[li - 2])
                        if li - 2 in ve_zt:
                            P.wait("scalar", "VE", ve_zt[li - 2])
                        first_evac = False
                    P.emit("scalar", lambda E, s=s, d=d, li=li: E.copy(
                        xT[li % 2][:, d * CH + s * 128:d * CH + (s + 1) * 128],
                        psf[4 + d][:, 0:128]), inc="SC")
                    mf_marks.append(P.val("SC"))
            xT_ready[li] = P.val("SC")
            xtr_done[li] = P.val("TE")
            if DEBUG_STOP == "mixer" and DEBUG_DUMPS:
                P.wait("sync", "TE", xtr_done[li])
                P.dma("sync", T["xrdbg"][:, li * 4 * DSL:(li + 1) * 4 * DSL],
                      xraw[li % 2][:], "ST")
            for d in range(2):
                if len(mf_marks) >= 2:
                    P.wait("tensor", "SC", mf_marks[-2])
                mmi = 0
                for lh in (wdth, wdtl):
                    for rh in (xpj_h, xpj_l):
                        P.emit("tensor", lambda E, lh=lh, rh=rh, d=d,
                               f=(mmi == 0), st=(mmi == 3), cb=colb:
                               E.matmul(psf[4 + d][:, 0:CH],
                                        lh[:, d * 128:(d + 1) * 128],
                                        rh[0:RDT, cb:cb + CH],
                                        start=f, stop=st),
                               inc=("TE" if mmi == 3 else None))
                        mmi += 1
                P.wait("scalar", "TE", P.val("TE"))
                if d == 0 and li - 1 in gp_dtt:
                    P.wait("scalar", "GP", gp_dtt[li - 1])
                P.emit("scalar", lambda E, d=d: E.activation(
                    spm[:, d * CH:(d + 1) * CH], psf[4 + d][:, 0:CH],
                    Act.Relu, bias=bdts[:, d:d + 1]))
                P.emit("scalar", lambda E, d=d: E.activation(
                    spa[:, d * CH:(d + 1) * CH], psf[4 + d][:, 0:CH],
                    Act.Abs, bias=bdts[:, d:d + 1]), inc="SC")
                mf_marks.append(P.val("SC"))
                P.emit("scalar", lambda E, d=d: E.activation(
                    spa[:, d * CH:(d + 1) * CH], spa[:, d * CH:(d + 1) * CH],
                    Act.Exp, scale=-1.0))
                P.emit("scalar", lambda E, d=d: E.activation(
                    spa[:, d * CH:(d + 1) * CH], spa[:, d * CH:(d + 1) * CH],
                    Act.Ln, bias=1.0), inc="SC")
            sc_sp[li] = P.val("SC")
            P.wait("gpsimd", "SC", sc_sp[li])
            if (li - 1) * NST - 1 in sc_exp:
                P.wait("gpsimd", "SC", sc_exp[(li - 1) * NST - 1])
            P.emit("gpsimd", lambda E, li=li: E.tensor_tensor(
                dtt[li % 2][:], spm[:], spa[:], Alu.add), inc="GP")
            gp_dtt[li] = P.val("GP")
            P.wait("gpsimd", "SC", xT_ready[li])
            if (li - 1) * NST - 1 in ve_tts:
                P.wait("gpsimd", "VE", ve_tts[(li - 1) * NST - 1])
            P.emit("gpsimd", lambda E, li=li: E.tensor_tensor(
                dtx[li % 2][:], dtt[li % 2][:], xT[li % 2][:], Alu.mult),
                inc="GP")
            gp_dtx[li] = P.val("GP")
            if DEBUG_STOP == "mixer" and DEBUG_DUMPS:
                P.wait("sync", "GP", P.val("GP"))
                P.dma("sync", T["dtdbg"][:, li * 2 * CH:(li + 1) * 2 * CH],
                      dtt[li % 2][:], "ST")
                P.dma("sync", T["xdbg"][:, li * 2 * CH:(li + 1) * 2 * CH],
                      dtx[li % 2][:], "ST")
                P.dma("sync", T["xtdbg"][:, li * 2 * CH:(li + 1) * 2 * CH],
                      xT[li % 2][:], "ST")
                for eng in ("vector", "scalar", "tensor", "gpsimd"):
                    P.wait(eng, "ST", P.val("ST"))

        def emit_stage(li):
            b, ci = divmod(li, NCH)
            colb = b * L + ci * CH
            if li * NST - 1 in te_bc:
                P.wait("sync", "TE", te_bc[li * NST - 1])
            P.dma("sync", stg_h[0:1, :], xpj_h[RDT:96, colb:colb + CH], "LD")
            P.dma("sync", stg_l[0:1, :], xpj_l[RDT:96, colb:colb + CH], "LD")
            ld_stage[li] = P.val("LD")

        def emit_n(li, n):
            nseq = li * NST + n
            sl = nseq % 2
            b, ci = divmod(li, NCH)
            if nseq - 2 in ve_m0:
                P.wait("tensor", "VE", ve_m0[nseq - 2])
                P.wait("tensor", "SC", sc_evC[nseq - 2])
            P.wait("tensor", "LD", ld_stage[li])
            for pj, base in ((0, n), (1, NST + n)):
                for hl, last in ((stg_h, False), (stg_l, True)):
                    P.emit("tensor", lambda E, pj=pj, base=base, hl=hl,
                           last=last, sl=sl: E.matmul(
                               psf[2 * sl + pj][:, 0:CH], S["ones1b"][0:1, :],
                               hl[0:1, base * CH:(base + 1) * CH],
                               start=not last, stop=last),
                           inc=("TE" if pj == 1 and last else None))
            te_bc[nseq] = P.val("TE")
            if nseq - 2 in ve_tts:
                P.wait("scalar", "VE", ve_tts[nseq - 2])
            if n == 0:
                P.wait("scalar", "GP", gp_dtt[li])
            for d in range(2):
                P.emit("scalar", lambda E, d=d, n=n, sl=sl, li=li:
                       E.activation(
                           abuf[:, sl * 2 * CH + d * CH:
                                sl * 2 * CH + (d + 1) * CH],
                           dtt[li % 2][:, d * CH:(d + 1) * CH], Act.Exp,
                           scale=atile[:, d * NST + n:d * NST + n + 1]))
            P.emit("scalar", lambda E: E.drain(), inc="SC")
            sc_exp[nseq] = P.val("SC")
            if nseq - 2 in gp_n:
                P.wait("scalar", "GP", gp_n[nseq - 2])
            P.wait("scalar", "TE", te_bc[nseq])
            P.emit("scalar", lambda E, sl=sl: E.copy(
                csb[:, sl * CH:(sl + 1) * CH], psf[2 * sl + 1][:, 0:CH]),
                inc="SC")
            sc_evC[nseq] = P.val("SC")
            P.wait("vector", "TE", te_bc[nseq])
            P.wait("vector", "SC", sc_exp[nseq])
            if nseq - 2 in gp_n:
                P.wait("vector", "GP", gp_n[nseq - 2])
            if n == 0:
                P.wait("vector", "GP", gp_dtx[li])
            for d in range(2):
                hcol = (2 * b + d) * NST + n
                P.emit("vector", lambda E, d=d, sl=sl, li=li: E.tensor_tensor(
                    wbuf[:, d * CH:(d + 1) * CH],
                    dtx[li % 2][:, d * CH:(d + 1) * CH],
                    psf[2 * sl][:, 0:CH], Alu.mult))
                P.emit("vector", lambda E, d=d, sl=sl, hc=hcol:
                       E.tensor_tensor_scan(
                           htr[sl][:, d * CH:(d + 1) * CH],
                           abuf[:, sl * 2 * CH + d * CH:
                                sl * 2 * CH + (d + 1) * CH],
                           wbuf[:, d * CH:(d + 1) * CH],
                           hst[:, hc:hc + 1],
                           op0=Alu.mult, op1=Alu.add),
                       inc=("VE" if d == 1 else None))
            ve_tts[nseq] = P.val("VE")
            P.emit("vector", lambda E, sl=sl: E.tensor_tensor(
                mbf[sl][:, 0:CH], htr[sl][:, 0:CH], psf[2 * sl + 1][:, 0:CH],
                Alu.mult), inc="VE")
            ve_m0[nseq] = P.val("VE")
            P.emit("vector", lambda E: E.drain())
            for d in range(2):
                hcol = (2 * b + d) * NST + n
                P.emit("vector", lambda E, d=d, sl=sl, hc=hcol: E.tensor_copy(
                    hst[:, hc:hc + 1],
                    htr[sl][:, (d + 1) * CH - 1:(d + 1) * CH]))
            if DEBUG_STOP == "mixer" and DEBUG_DUMPS and li == 1:
                P.wait("sync", "VE", P.val("VE"))
                P.wait("sync", "GP", P.val("GP"))
                P.dma("sync", T["abdbg"][:, n * 4 * CH:n * 4 * CH + 2 * CH],
                      abuf[:, sl * 2 * CH:(sl + 1) * 2 * CH], "ST")
                P.dma("sync", T["abdbg"][:, n * 4 * CH + 2 * CH:
                                         (n + 1) * 4 * CH],
                      htr[sl][:], "ST")
                P.dma("sync", T["atdbg"], atile[:], "ST")
                for eng in ("vector", "scalar", "tensor", "gpsimd"):
                    P.wait(eng, "ST", P.val("ST"))
            P.wait("gpsimd", "VE", ve_tts[nseq])
            P.wait("gpsimd", "SC", sc_evC[nseq])
            P.emit("gpsimd", lambda E, sl=sl: E.tensor_tensor(
                mbf[sl][:, CH:2 * CH], htr[sl][:, CH:2 * CH],
                csb[:, sl * CH:(sl + 1) * CH], Alu.mult))
            P.wait("gpsimd", "VE", ve_m0[nseq])
            if n == 0:
                if li - 1 in ve_zt:
                    P.wait("gpsimd", "VE", ve_zt[li - 1])
                P.emit("gpsimd", lambda E, sl=sl: E.tensor_copy(
                    yacc[:], mbf[sl][:]), inc="GP")
            else:
                P.emit("gpsimd", lambda E, sl=sl: E.tensor_tensor(
                    yacc[:], yacc[:], mbf[sl][:], Alu.add), inc="GP")
            gp_n[nseq] = P.val("GP")

        def emit_ztile(li):
            P.wait("vector", "GP", gp_n[li * NST + NST - 1])
            if li - 1 in ztr_done:
                P.wait("vector", "TE", ztr_done[li - 1])
            for d in range(2):
                P.emit("vector", lambda E, d=d, li=li: E.scalar_tensor_tensor(
                    ztile[:, d * CH:(d + 1) * CH],
                    xT[li % 2][:, d * CH:(d + 1) * CH],
                    dp1[:, d:d + 1], yacc[:, d * CH:(d + 1) * CH],
                    op0=Alu.mult, op1=Alu.add),
                    inc=("VE" if d == 1 else None))
            ve_zt[li] = P.val("VE")
            if DEBUG_STOP == "mixer" and DEBUG_DUMPS:
                P.wait("sync", "VE", P.val("VE"))
                P.dma("sync", T["ydbg"][:, li * 2 * CH:(li + 1) * 2 * CH],
                      yacc[:], "ST")
                for eng in ("vector", "scalar", "tensor", "gpsimd"):
                    P.wait(eng, "ST", P.val("ST"))

        def emit_ztrans(li):
            b, ci = divmod(li, NCH)
            P.wait("tensor", "VE", ve_zt[li])
            first = True
            for s in range(4):
                for d in range(2):
                    if len(mf_marks) >= 2:
                        P.wait("tensor", "SC", mf_marks[-2])
                    P.emit("tensor", lambda E, s=s, d=d: E.transpose(
                        psf[4 + d][:, 0:128],
                        ztile[:, d * CH + s * 128:d * CH + (s + 1) * 128],
                        identf[:]), inc="TE")
                    P.wait("scalar", "TE", P.val("TE"))
                    if first and li - 1 in zst_done:
                        P.wait("scalar", "ST", zst_done[li - 1])
                        first = False
                    P.emit("scalar", lambda E, s=s, d=d: E.copy(
                        zstage[:, s * DSL + d * 128:s * DSL + (d + 1) * 128],
                        psf[4 + d][:, 0:128]), inc="SC")
                    mf_marks.append(P.val("SC"))
            ztr_done[li] = P.val("TE")
            P.wait("sync", "SC", P.val("SC"))
            rowb = b * L + ci * CH
            for s in range(4):
                P.dma("sync",
                      T["z_in"][rowb + s * 128:rowb + (s + 1) * 128, :],
                      zstage[:, s * DSL:(s + 1) * DSL], "ST")
            zst_done[li] = P.val("ST")

        wgam = S["wgam"]
        P.emit("gpsimd", lambda E: E.memset(qcst[:, 0:1], MAGIC))
        P.emit("gpsimd", lambda E: E.memset(qcst[:, 1:2], -(MAGIC - 1.0)))
        P.emit("gpsimd", lambda E: E.memset(qcst[:, 2:3], 2.0))
        P.emit("gpsimd", lambda E: E.memset(qcst[:, 3:4], 1.0), inc="GP")
        P.wait("scalar", "GP", P.val("GP"))
        qdesc = []
        for wsrc, wdst, nrt, wid in (("wgTd", "wgq_in", 16, 0),
                                     ("wvTd", "wvq_in", 16, 1)):
            for it in range(nrt):
                qdesc.append((wsrc, wdst, it * 128, 0, wid))
        for it in range(8):
            for cb in (0, 1024):
                qdesc.append(("woTd", "woq_in", it * 128, cb, 2))
        q_sc, q_st = {}, {}
        qpos = [0]

        def q_pump(k):
            for _ in range(k):
                i = qpos[0]
                if i >= len(qdesc):
                    return
                qpos[0] += 1
                wsrc, wdst, r0, c0, wid = qdesc[i]
                if i - 2 in q_sc:
                    P.wait("sync", "SC", q_sc[i - 2])
                    P.wait("sync", "ST", q_st[i - 2])
                P.dma("sync", qwi[i % 2][:],
                      T[wsrc][r0:r0 + 128, c0:c0 + 1024], "LD")
                P.wait("scalar", "LD", P.val("LD"))
                P.emit("scalar", lambda E, i=i, wid=wid: E.activation(
                    qtmp[:], qwi[i % 2][:], Act.Copy,
                    scale=wgam[:, wid:wid + 1], bias=MAGIC))
                P.emit("scalar", lambda E: E.activation(
                    qtmp[:], qtmp[:], Act.Relu, bias=qcst[:, 1:2]))
                P.emit("scalar", lambda E: E.activation(
                    qtmp[:], qtmp[:], Act.Relu, scale=-1.0, bias=qcst[:, 2:3]))
                P.emit("scalar", lambda E, i=i: E.activation(
                    qwo[i % 2][:], qtmp[:], Act.Copy, scale=-1.0, bias=1.0))
                P.emit("scalar", lambda E: E.drain(), inc="SC")
                q_sc[i] = P.val("SC")
                P.wait("sync", "SC", q_sc[i])
                P.dma("sync", T[wdst][r0:r0 + 128, c0:c0 + 1024],
                      qwo[i % 2][:], "ST")
                q_st[i] = P.val("ST")

        for s in range(4):
            P.dma("sync", xraw[0][:, s * DSL:(s + 1) * DSL],
                  T["x_dsl"][0, s * 128:(s + 1) * 128, :], "LD")
        ld_ready[0] = P.val("LD")
        emit_stage(0)
        emit_head(0)
        for li in range(NP):
            if li + 1 < NP:
                nb, nci = divmod(li + 1, NCH)
                if li - 1 in xtr_done:
                    P.wait("sync", "TE", xtr_done[li - 1])
                for s in range(4):
                    P.dma("sync",
                          xraw[(li + 1) % 2][:, s * DSL:(s + 1) * DSL],
                          T["x_dsl"][nb,
                                     nci * CH + s * 128:nci * CH + (s + 1) * 128,
                                     :], "LD")
                ld_ready[li + 1] = P.val("LD")
            if li >= 1:
                emit_ztile(li - 1)
            for n in range(NST):
                emit_n(li, n)
                if li >= 1 and n in (3, 7, 11, 15):
                    q_pump(1)
                if n == 2 and li >= 1:
                    emit_ztrans(li - 1)
                if n == 7 and li + 1 < NP:
                    emit_head(li + 1)
                if n == 15 and li + 1 < NP:
                    emit_stage(li + 1)
        emit_ztile(NP - 1)
        emit_ztrans(NP - 1)
        q_pump(len(qdesc))
        P.wait("sync", "ST", P.val("ST"))


def _ffn(P, nc, T, S, marks):
    psf, psb = S["psf"], S["psb"]
    identb, ones1, wgam = S["identb"], S["ones1"], S["wgam"]
    hold = {}

    with ExitStack() as cO:
        EO = cO.enter_context
        sxg = EO(nc.sbuf_tensor("sxg", [128, NTT], F32))
        sxv = EO(nc.sbuf_tensor("sxv", [128, NTT], F32))
        sO = EO(nc.sbuf_tensor("sO", [128, NTT], F32))
        gvmax = EO(nc.sbuf_tensor("gvmax", [128, NTT], F32))
        for eng in ("sync", "vector", "scalar", "tensor", "gpsimd"):
            P.wait(eng, "CC", marks["CC_Z"])
        P.emit("gpsimd", lambda E: E.memset(gvmax[:], 0.0), inc="GP")
        P.wait("vector", "GP", P.val("GP"))

        def emit_ln(src_fn, dst_fn, tn):
            P.emit("vector", lambda E: E.reduce_sum(tn["red"][:], src_fn(), AXX))
            P.emit("vector", lambda E: E.drain(), inc="VE")
            P.wait("scalar", "VE", P.val("VE"))
            P.emit("scalar", lambda E: E.activation(
                tn["sqs"][:], src_fn(), Act.Square))
            P.emit("scalar", lambda E: E.drain(), inc="SC")
            P.wait("vector", "SC", P.val("SC"))
            P.emit("vector", lambda E: E.reduce_sum(tn["ms"][:], tn["sqs"][:], AXX))
            P.emit("vector", lambda E: E.drain())
            P.emit("vector", lambda E: E.tensor_tensor(
                tn["mu"][:], tn["red"][:], S["dinv"][:], Alu.mult))
            P.emit("vector", lambda E: E.drain())
            P.emit("vector", lambda E: E.tensor_tensor(
                tn["t1"][:], tn["mu"][:], tn["mu"][:], Alu.mult))
            P.emit("vector", lambda E: E.tensor_tensor(
                tn["ms"][:], tn["ms"][:], S["dinv"][:], Alu.mult))
            P.emit("vector", lambda E: E.drain())
            P.emit("vector", lambda E: E.tensor_tensor(
                tn["vpe"][:], tn["ms"][:], tn["t1"][:], Alu.subtract))
            P.emit("vector", lambda E: E.drain())
            P.emit("vector", lambda E: E.tensor_tensor(
                tn["vpe"][:], tn["vpe"][:], S["epsc"][:], Alu.add))
            P.emit("vector", lambda E: E.drain(), inc="VE")
            P.wait("scalar", "VE", P.val("VE"))
            P.emit("scalar", lambda E: E.activation(
                tn["y0"][:], tn["vpe"][:], Act.Ln))
            P.emit("scalar", lambda E: E.drain())
            P.emit("scalar", lambda E: E.activation(
                tn["y0"][:], tn["y0"][:], Act.Exp, scale=-0.5))
            P.emit("scalar", lambda E: E.drain(), inc="SC")
            P.wait("vector", "SC", P.val("SC"))
            P.emit("vector", lambda E: E.tensor_tensor(
                tn["t1"][:], tn["y0"][:], tn["y0"][:], Alu.mult))
            P.emit("vector", lambda E: E.drain())
            P.emit("vector", lambda E: E.tensor_tensor(
                tn["t1"][:], tn["t1"][:], tn["vpe"][:], Alu.mult))
            P.emit("vector", lambda E: E.drain())
            P.emit("vector", lambda E: E.tensor_scalar(
                tn["t1"][:], tn["t1"][:], -0.5, 1.5, op0=Alu.mult, op1=Alu.add))
            P.emit("vector", lambda E: E.drain())
            P.emit("vector", lambda E: E.tensor_tensor(
                tn["rstd"][:], tn["y0"][:], tn["t1"][:], Alu.mult))
            P.emit("vector", lambda E: E.drain())
            P.emit("vector", lambda E: E.tensor_scalar(
                dst_fn(), src_fn(), tn["mu"][:], tn["rstd"][:],
                op0=Alu.subtract, op1=Alu.mult), inc="VE")

        with ExitStack() as cA:
            EA = cA.enter_context
            XqT = EA(nc.sbuf_tensor("XqT", [128, NKD * 1024], BF16))
            lnsb = EA(nc.sbuf_tensor("lnsb", [128, 8 * NTT], F32))
            cA1 = EA(ExitStack())
            EA1 = cA1.enter_context
            zall = EA1(nc.sbuf_tensor("zall", [128, NTT * D], F32))
            sqs = EA1(nc.sbuf_tensor("sqs", [128, D], F32))
            z1 = EA1(nc.sbuf_tensor("z1", [128, D], F32))
            Xq = EA1(nc.sbuf_tensor("Xq", [128, D], BF16))
            tn = {k: EA1(nc.sbuf_tensor("ln_" + k, [128, 1], F32))
                  for k in ("red", "mu", "ms", "vpe", "y0", "t1", "rstd",
                            "gmx", "gam", "inv", "sx")}
            tn["sqs"] = sqs
            P.wait("sync", "CC", marks["CC_Z"])

            def _mk_zoffs(E):
                hold["zb"] = E.snap(E.partition_id() * TOK,
                                    min_val=0, max_val=(NCORE - 1) * TOK,
                                    guaranteed_mod_val=TOK)
                return E.nop()
            P.emit("sync", _mk_zoffs)
            z3 = T["z_out"].ap().rearrange("(r n) d -> r n d", r=NCORE)
            for r in range(NCORE):
                P.emit("sync", lambda E, r=r: E.dma_start(
                    out=T["z_my"][:, r * DSL:(r + 1) * DSL],
                    in_=z3[r][bass.ds(hold["zb"], TOK), :]),
                    inc="LD", k=16)
            P.wait("sync", "LD", P.val("LD"))
            for tt in range(NTT):
                P.dma("sync", zall[:, tt * D:(tt + 1) * D],
                      T["z_my"][tt * 128:(tt + 1) * 128, :], "LD")
            ld_all = P.val("LD")
            P.wait("gpsimd", "LD", ld_all)
            P.wait("gpsimd", "ST", P.val("ST"))
            for a, b in (("wgq_in", "wgq_out"), ("wvq_in", "wvq_out"),
                         ("woq_in", "woq_out")):
                P.emit("gpsimd", lambda E, x=T[a], y=T[b]:
                       E.collective_compute(
                           "AllGather", Alu.bypass,
                           replica_groups=[list(range(NCORE))],
                           ins=[x.ap().opt()], outs=[y.ap().opt()]),
                       inc="CC")
            marks["CC_W"] = P.val("CC")

            tsq = 0
            xqtr_done = {}
            sd_z1 = {}
            P.wait("vector", "LD", ld_all)
            for tt in range(NTT):
                if tt - 1 in sd_z1:
                    P.wait("vector", "SD", sd_z1[tt - 1])
                emit_ln(lambda tt=tt: zall[:, tt * D:(tt + 1) * D],
                        lambda: z1[:], tn)
                if DEBUG_STOP == "ffn":
                    for k_i, k_n in enumerate(("red", "ms", "mu", "vpe", "y0",
                                               "rstd", "t1", "t1")):
                        P.emit("vector", lambda E, tt=tt, k_i=k_i, k_n=k_n:
                               E.tensor_copy(lnsb[:, tt * 8 + k_i:tt * 8 + k_i + 1],
                                             tn[k_n][:]))
                    P.emit("vector", lambda E: E.nop(), inc="VE")
                ve_z1 = P.val("VE")
                P.wait("scalar", "VE", ve_z1)
                P.emit("scalar", lambda E, tt=tt: E.dma_start(
                    out=T["z1st"][tt * 128:(tt + 1) * 128, :], in_=z1[:]),
                    inc="SD", k=16)
                sd_z1[tt] = P.val("SD")
                P.emit("vector", lambda E: E.tensor_reduce(
                    tn["gmx"][:], z1[:], AXX, Alu.max,
                    apply_absolute_value=True))
                P.emit("vector", lambda E: E.drain())
                P.emit("vector", lambda E: E.tensor_tensor(
                    tn["gam"][:], tn["gmx"][:], S["epsc"][:], Alu.max))
                P.emit("vector", lambda E: E.drain())
                P.emit("vector", lambda E: E.reciprocal(
                    tn["inv"][:], tn["gam"][:]))
                P.emit("vector", lambda E: E.drain())
                P.emit("vector", lambda E: E.tensor_tensor(
                    tn["sx"][:], tn["inv"][:], S["c127"][:], Alu.mult))
                P.emit("vector", lambda E, tt=tt: E.scalar_tensor_tensor(
                    sxg[:, tt:tt + 1], tn["gam"][:], 1.0, wgam[:, 4:5],
                    op0=Alu.mult, op1=Alu.mult))
                P.emit("vector", lambda E, tt=tt: E.scalar_tensor_tensor(
                    sxv[:, tt:tt + 1], tn["gam"][:], 1.0, wgam[:, 5:6],
                    op0=Alu.mult, op1=Alu.mult))
                P.emit("vector", lambda E: E.drain())
                P.emit("vector", lambda E: E.tensor_scalar(
                    sqs[:], z1[:], tn["sx"][:], MAGIC, op0=Alu.mult,
                    op1=Alu.add))
                if tt - 1 in xqtr_done:
                    P.wait("vector", "TE", xqtr_done[tt - 1])
                P.emit("vector", lambda E: E.tensor_scalar(
                    Xq[:], sqs[:], MAGIC, None, op0=Alu.subtract), inc="VE")
                P.wait("tensor", "VE", P.val("VE"))
                for k in range(NKD):
                    P.wait("tensor", "SC", P.val("SC") - 1)
                    P.emit("tensor", lambda E, k=k, p=tsq % 2: E.transpose(
                        psb[p][:, 0:128], Xq[:, k * 128:(k + 1) * 128],
                        identb[:]), inc="TE")
                    P.wait("scalar", "TE", P.val("TE"))
                    P.emit("scalar", lambda E, k=k, tt=tt, p=tsq % 2: E.copy(
                        XqT[:, k * 1024 + tt * 128:k * 1024 + (tt + 1) * 128],
                        psb[p][:, 0:128]), inc="SC")
                    tsq += 1
                xqtr_done[tt] = P.val("TE")
            P.wait("tensor", "SC", P.val("SC"))

            cA1.close()
            wgsl = [EA(nc.sbuf_tensor(f"wgsl{i}", [128, HID], BF16))
                    for i in range(2)]
            wvsl = [EA(nc.sbuf_tensor(f"wvsl{i}", [128, HID], BF16))
                    for i in range(2)]
            gbr = [EA(nc.sbuf_tensor(f"gbr{i}", [1, 512], F32))
                   for i in range(2)]
            vbr = [EA(nc.sbuf_tensor(f"vbr{i}", [1, 512], F32))
                   for i in range(2)]
            gbj = EA(nc.sbuf_tensor("gbj", [128, 512], F32))
            vbj = EA(nc.sbuf_tensor("vbj", [128, 512], F32))
            upre = EA(nc.sbuf_tensor("upre", [128, 512], F32))
            gsil = EA(nc.sbuf_tensor("gsil", [128, 512], F32))
            vtile = EA(nc.sbuf_tensor("vtile", [128, 512], F32))
            gvst = [EA(nc.sbuf_tensor(f"gvst{i}", [128, 512], F32))
                    for i in range(2)]
            mjt = EA(nc.sbuf_tensor("mjt", [128, 1], F32))
            P.wait("sync", "CC", marks["CC_W"])
            ld_slab = {}
            mm_done = {}
            epi_done = {}
            sd_gv = {}
            for j in range(16):
                r, half = divmod(j, 2)
                if j - 2 in mm_done:
                    P.wait("sync", "TE", mm_done[j - 2])
                for k in range(NKD):
                    P.dma("sync", wgsl[j % 2][:, k * 512:(k + 1) * 512],
                          T["wgq_out"][r * D + k * 128:r * D + (k + 1) * 128,
                                       half * 512:(half + 1) * 512], "LD")
                    P.dma("sync", wvsl[j % 2][:, k * 512:(k + 1) * 512],
                          T["wvq_out"][r * D + k * 128:r * D + (k + 1) * 128,
                                       half * 512:(half + 1) * 512], "LD")
                P.dma("sync", gbr[j % 2][0:1, :],
                      T["gbias"][0:1, j * 512:(j + 1) * 512], "LD")
                P.dma("sync", vbr[j % 2][0:1, :],
                      T["vbias"][0:1, j * 512:(j + 1) * 512], "LD")
                ld_slab[j] = P.val("LD")
                if (j - 1) * NTT + NTT - 1 in epi_done:
                    P.wait("tensor", "VE", epi_done[(j - 1) * NTT + NTT - 1])
                P.wait("tensor", "LD", ld_slab[j])
                P.emit("tensor", lambda E, j=j: E.matmul(
                    psf[4][:, 0:512], ones1[0:1, :],
                    gbr[j % 2][0:1, 0:512],
                    start=True, stop=True))
                P.emit("tensor", lambda E, j=j: E.matmul(
                    psf[5][:, 0:512], ones1[0:1, :],
                    vbr[j % 2][0:1, 0:512],
                    start=True, stop=True), inc="TE")
                P.wait("vector", "TE", P.val("TE"))
                P.emit("vector", lambda E: E.tensor_copy(gbj[:], psf[4][:, 0:512]))
                P.emit("vector", lambda E: E.tensor_copy(vbj[:], psf[5][:, 0:512]),
                       inc="VE")
                for t in range(NTT):
                    gq = j * NTT + t
                    P.wait("tensor", "LD", ld_slab[j])
                    if gq - 2 in epi_done:
                        P.wait("tensor", "VE", epi_done[gq - 2])
                    for k in range(NKD):
                        P.emit("tensor", lambda E, k=k, t=t, j=j,
                               f=(k == 0), st=(k == NKD - 1): E.matmul(
                                   psf[t % 2][:, 0:512],
                                   XqT[:, k * 1024 + t * 128:k * 1024 + (t + 1) * 128],
                                   wgsl[j % 2][:, k * 512:(k + 1) * 512],
                                   start=f, stop=st),
                               inc=("TE" if k == NKD - 1 else None))
                    for k in range(NKD):
                        P.emit("tensor", lambda E, k=k, t=t, j=j,
                               f=(k == 0), st=(k == NKD - 1): E.matmul(
                                   psf[2 + t % 2][:, 0:512],
                                   XqT[:, k * 1024 + t * 128:k * 1024 + (t + 1) * 128],
                                   wvsl[j % 2][:, k * 512:(k + 1) * 512],
                                   start=f, stop=st),
                               inc=("TE" if k == NKD - 1 else None))
                    te_gv = P.val("TE")
                    P.wait("vector", "TE", te_gv)
                    P.emit("vector", lambda E, t=t: E.scalar_tensor_tensor(
                        upre[:], psf[t % 2][:, 0:512], sxg[:, t:t + 1], gbj[:],
                        op0=Alu.mult, op1=Alu.add), inc="VE")
                    P.wait("scalar", "VE", P.val("VE"))
                    P.emit("scalar", lambda E: E.activation(
                        gsil[:], upre[:], Act.Silu))
                    P.emit("scalar", lambda E: E.drain(), inc="SC")
                    P.emit("vector", lambda E, t=t: E.scalar_tensor_tensor(
                        vtile[:], psf[2 + t % 2][:, 0:512], sxv[:, t:t + 1],
                        vbj[:], op0=Alu.mult, op1=Alu.add))
                    P.wait("vector", "SC", P.val("SC"))
                    if gq - 2 in sd_gv:
                        P.wait("vector", "SD", sd_gv[gq - 2])
                    P.emit("vector", lambda E, sl=gq % 2: E.tensor_tensor(
                        gvst[sl][:], gsil[:], vtile[:], Alu.mult))
                    P.emit("vector", lambda E, sl=gq % 2: E.tensor_reduce(
                        mjt[:], gvst[sl][:], AXX, Alu.max,
                        apply_absolute_value=True))
                    P.emit("vector", lambda E: E.drain())
                    P.emit("vector", lambda E, t=t: E.tensor_tensor(
                        gvmax[:, t:t + 1], gvmax[:, t:t + 1], mjt[:], Alu.max),
                        inc="VE")
                    epi_done[gq] = P.val("VE")
                    P.wait("scalar", "VE", epi_done[gq])
                    P.emit("scalar", lambda E, t=t, j=j, sl=gq % 2: E.dma_start(
                        out=T["gv_dram"][t, j, :, :], in_=gvst[sl][:]),
                        inc="SD", k=16)
                    sd_gv[gq] = P.val("SD")
                mm_done[j] = P.val("TE")
            P.wait("sync", "SD", P.val("SD"))
            if DEBUG_STOP == "ffn":
                P.wait("sync", "VE", P.val("VE"))
                P.dma("sync", T["lndbg"], lnsb[:], "ST")
                P.wait("sync", "ST", P.val("ST"))
            P.barrier()

        with ExitStack() as cB:
            EB = cB.enter_context
            gvrow = [EB(nc.sbuf_tensor(f"gvrow{i}", [128, HID], F32))
                     for i in range(2)]
            gvq32 = EB(nc.sbuf_tensor("gvq32", [128, HID], F32))
            GVq = [EB(nc.sbuf_tensor(f"GVq{i}", [128, HID], BF16))
                   for i in range(2)]
            gvqTst = [EB(nc.sbuf_tensor(f"gvqTst{i}", [128, HID], BF16))
                      for i in range(2)]
            g2 = {k: EB(nc.sbuf_tensor("g2_" + k, [128, 1], F32))
                  for k in ("gam", "inv", "sgv")}
            tsq = 0
            ld_gv, ve_q, tr_done, st_q = {}, {}, {}, {}
            for t in range(NTT):
                if t - 2 in ve_q:
                    P.wait("sync", "VE", ve_q[t - 2])
                for j in range(16):
                    P.dma("sync", gvrow[t % 2][:, j * 512:(j + 1) * 512],
                          T["gv_dram"][t, j, :, :], "LD")
                ld_gv[t] = P.val("LD")
                P.wait("vector", "LD", ld_gv[t])
                P.emit("vector", lambda E, t=t: E.tensor_scalar(
                    g2["gam"][:], gvmax[:, t:t + 1], EPS, None, op0=Alu.max))
                P.emit("vector", lambda E: E.drain())
                P.emit("vector", lambda E: E.reciprocal(
                    g2["inv"][:], g2["gam"][:]))
                P.emit("vector", lambda E: E.drain())
                P.emit("vector", lambda E: E.tensor_scalar(
                    g2["sgv"][:], g2["inv"][:], 127.0, None, op0=Alu.mult))
                P.emit("vector", lambda E, t=t: E.scalar_tensor_tensor(
                    sO[:, t:t + 1], g2["gam"][:], 1.0, wgam[:, 6:7],
                    op0=Alu.mult, op1=Alu.mult))
                P.emit("vector", lambda E: E.drain())
                P.emit("vector", lambda E, t=t: E.tensor_scalar(
                    gvq32[:], gvrow[t % 2][:], g2["sgv"][:], MAGIC,
                    op0=Alu.mult, op1=Alu.add))
                if t - 2 in tr_done:
                    P.wait("vector", "TE", tr_done[t - 2])
                P.emit("vector", lambda E, t=t: E.tensor_scalar(
                    GVq[t % 2][:], gvq32[:], MAGIC, None, op0=Alu.subtract),
                    inc="VE")
                ve_q[t] = P.val("VE")
                P.wait("tensor", "VE", ve_q[t])
                first = True
                evm = []
                for k in range(NKH):
                    if len(evm) >= 2:
                        eng0, v0 = evm[-2]
                        P.wait("tensor", "SC" if eng0 == "scalar" else "VE",
                               v0)
                    P.emit("tensor", lambda E, k=k, t=t, p=tsq % 2:
                           E.transpose(psb[p][:, 0:128],
                                       GVq[t % 2][:, k * 128:(k + 1) * 128],
                                       identb[:]), inc="TE")
                    eng = "scalar" if k % 2 == 0 else "vector"
                    P.wait(eng, "TE", P.val("TE"))
                    if first and t - 2 in st_q:
                        P.wait(eng, "ST", st_q[t - 2])
                        first = False
                    if eng == "scalar":
                        P.emit("scalar", lambda E, k=k, t=t, p=tsq % 2:
                               E.copy(gvqTst[t % 2][:, k * 128:(k + 1) * 128],
                                      psb[p][:, 0:128]), inc="SC")
                        evm.append(("scalar", P.val("SC")))
                    else:
                        P.emit("vector", lambda E, k=k, t=t, p=tsq % 2:
                               E.tensor_copy(
                                   gvqTst[t % 2][:, k * 128:(k + 1) * 128],
                                   psb[p][:, 0:128]), inc="VE")
                        evm.append(("vector", P.val("VE")))
                    tsq += 1
                tr_done[t] = P.val("TE")
                P.wait("sync", "SC", P.val("SC"))
                P.wait("sync", "VE", P.val("VE"))
                P.dma("sync", T["gvqT_dram"][t, :, :], gvqTst[t % 2][:], "ST")
                st_q[t] = P.val("ST")
            P.wait("sync", "ST", P.val("ST"))
            P.barrier()

        with ExitStack() as cC:
            EC = cC.enter_context
            wosl = EC(nc.sbuf_tensor("wosl", [128, NKH * 512], BF16))
            gvqrd = [EC(nc.sbuf_tensor(f"gvqrd{i}", [128, HID], BF16))
                     for i in range(2)]
            obm = EC(nc.sbuf_tensor("obm", [1, D], F32))
            obj = EC(nc.sbuf_tensor("obj", [128, 512], F32))
            Oacc = EC(nc.sbuf_tensor("Oacc", [128, NTT * D], F32))
            z1rd = EC(nc.sbuf_tensor("z1rd", [128, D], F32))
            z2b = EC(nc.sbuf_tensor("z2b", [128, D], F32))
            ofin = EC(nc.sbuf_tensor("ofin", [128, D], F32))
            sq2 = EC(nc.sbuf_tensor("sq2", [128, D], F32))
            tn2 = {k: EC(nc.sbuf_tensor("l2_" + k, [128, 1], F32))
                   for k in ("red", "mu", "ms", "vpe", "y0", "t1", "rstd")}
            tn2["sqs"] = sq2

            P.dma("sync", obm[:], T["obias"], "LD")
            P.wait("tensor", "LD", P.val("LD"))
            oq_mm = {}
            slab_mm = {}
            epi_o = {}
            for j in range(4):
                if j - 1 in slab_mm:
                    P.wait("sync", "TE", slab_mm[j - 1])
                for k in range(NKH):
                    P.dma("sync", wosl[:, k * 512:(k + 1) * 512],
                          T["woq_out"][k * 128:(k + 1) * 128,
                                       j * 512:(j + 1) * 512], "LD")
                ld_wo = P.val("LD")
                if (j - 1) * NTT + NTT - 1 in epi_o:
                    P.wait("tensor", "VE", epi_o[(j - 1) * NTT + NTT - 1])
                P.emit("tensor", lambda E, j=j: E.matmul(
                    psf[4][:, 0:512], ones1[0:1, :],
                    obm[0:1, j * 512:(j + 1) * 512],
                    start=True, stop=True), inc="TE")
                P.wait("vector", "TE", P.val("TE"))
                P.emit("vector", lambda E: E.tensor_copy(obj[:], psf[4][:, 0:512]),
                       inc="VE")
                for t in range(NTT):
                    oq = j * NTT + t
                    if oq - 2 in oq_mm:
                        P.wait("sync", "TE", oq_mm[oq - 2])
                    P.dma("sync", gvqrd[t % 2][:], T["gvqT_dram"][t, :, :], "LD")
                    P.wait("tensor", "LD", P.val("LD"))
                    if oq - 3 in epi_o:
                        P.wait("tensor", "VE", epi_o[oq - 3])
                    for k in range(NKH):
                        P.emit("tensor", lambda E, k=k, t=t, b=oq % 3,
                               f=(k == 0), st=(k == NKH - 1): E.matmul(
                                   psf[b][:, 0:512],
                                   gvqrd[t % 2][:, k * 128:(k + 1) * 128],
                                   wosl[:, k * 512:(k + 1) * 512],
                                   start=f, stop=st),
                               inc=("TE" if k == NKH - 1 else None))
                    oq_mm[oq] = P.val("TE")
                    P.wait("vector", "TE", oq_mm[oq])
                    P.emit("vector", lambda E, t=t, j=j, b=oq % 3:
                           E.scalar_tensor_tensor(
                               Oacc[:, t * D + j * 512:t * D + (j + 1) * 512],
                               psf[b][:, 0:512], sO[:, t:t + 1], obj[:],
                               op0=Alu.mult, op1=Alu.add), inc="VE")
                    epi_o[oq] = P.val("VE")
                slab_mm[j] = P.val("TE")

            st_out = {}
            for t in range(NTT):
                P.wait("sync", "VE", P.val("VE"))
                P.dma("sync", z1rd[:], T["z1st"][t * 128:(t + 1) * 128, :], "LD")
                P.wait("vector", "LD", P.val("LD"))
                if t - 1 in st_out:
                    P.wait("vector", "ST", st_out[t - 1])
                P.emit("vector", lambda E, t=t: E.tensor_tensor(
                    z2b[:], z1rd[:], Oacc[:, t * D:(t + 1) * D], Alu.add))
                emit_ln(lambda: z2b[:], lambda: ofin[:], tn2)
                P.wait("sync", "VE", P.val("VE"))
                P.dma("sync", T["out"][t * 128:(t + 1) * 128, :], ofin[:], "ST")
                st_out[t] = P.val("ST")
            P.wait("sync", "ST", P.val("ST"))
            P.wait("sync", "SD", P.val("SD"))
            if DEBUG_STOP == "ffn":
                P.dma("gpsimd", T["z1dbg"], T["z1st"].ap(), "GD")
                P.dma("gpsimd", T["gvdbg"],
                      T["gv_dram"].ap().rearrange("a b c d -> (a b c) d"), "GD")
                P.dma("gpsimd", T["gvqdbg"],
                      T["gvqT_dram"].ap().rearrange("a b c -> (a b) c"), "GD")
                P.dma("gpsimd", T["zmydbg"], T["z_my"].ap(), "GD")
                P.wait("gpsimd", "GD", P.val("GD"))


_NC_CACHE = {}


def _get_nc():
    key = DEBUG_STOP
    if key not in _NC_CACHE:
        _NC_CACHE[key] = build_nc()
    return _NC_CACHE[key]


def kernel(L0_attn_x, L0_attn_W_x, L0_attn_W_dt, L0_attn_b_dt, L0_attn_A_log,
           L0_attn_D, L0_ffn_gate_w, L0_ffn_gate_b, L0_ffn_val_w, L0_ffn_val_b,
           L0_ffn_out_w, L0_ffn_out_b):
    f32 = np.float32
    x = np.asarray(L0_attn_x, f32)
    W_x = np.ascontiguousarray(np.asarray(L0_attn_W_x, f32))
    W_dt = np.asarray(L0_attn_W_dt, f32)
    b_dt = np.asarray(L0_attn_b_dt, f32)
    A_log = np.asarray(L0_attn_A_log, f32)
    Dv = np.asarray(L0_attn_D, f32)
    WgT = np.ascontiguousarray(np.asarray(L0_ffn_gate_w, f32).T)
    WvT = np.ascontiguousarray(np.asarray(L0_ffn_val_w, f32).T)
    WoT = np.ascontiguousarray(np.asarray(L0_ffn_out_w, f32).T)
    gb = np.ascontiguousarray(np.asarray(L0_ffn_gate_b, f32).reshape(1, HID))
    vb = np.ascontiguousarray(np.asarray(L0_ffn_val_b, f32).reshape(1, HID))
    ob = np.ascontiguousarray(np.asarray(L0_ffn_out_b, f32).reshape(1, D))

    xf = np.ascontiguousarray(x.reshape(B * L, D))
    in_maps = []
    for c in range(NCORE):
        dsl = slice(DSL * c, DSL * (c + 1))
        hsl = slice(1024 * c, 1024 * (c + 1))
        in_maps.append(dict(
            x_tok=np.ascontiguousarray(xf[TOK * c:TOK * (c + 1)]),
            x_dsl=np.ascontiguousarray(x[:, :, dsl]),
            wx=W_x,
            wdtT=np.ascontiguousarray(W_dt[dsl].T),
            bdt=np.ascontiguousarray(b_dt[dsl].reshape(2, 128).T),
            alog=np.ascontiguousarray(
                A_log[dsl].reshape(2, 128, NST).transpose(1, 0, 2)
                .reshape(128, 2 * NST)),
            dvec=np.ascontiguousarray(Dv[dsl].reshape(2, 128).T),
            wgTd=np.ascontiguousarray(WgT[:, hsl]),
            wvTd=np.ascontiguousarray(WvT[:, hsl]),
            woTd=np.ascontiguousarray(WoT[hsl]),
            gbias=gb, vbias=vb, obias=ob,
        ))
    nc = _get_nc()
    res = run_bass_kernel_spmd(nc, in_maps, core_ids=list(range(NCORE)))
    if DEBUG_STOP is not None:
        return res
    o = np.concatenate([r["out"] for r in res.results], axis=0)
    return o.reshape(B, L, D)


if __name__ == "__main__":
    nc = build_nc()
    n = sum(1 for _ in nc.m.functions[0].blocks for _ in [0])
    print("built ok")


# version 17
# speedup vs baseline: 1.2032x; 1.0130x over previous
import numpy as np
import ml_dtypes
from contextlib import ExitStack

import concourse.bass as bass
import concourse.mybir as mybir
from concourse.bass_utils import run_bass_kernel_spmd

F32 = mybir.dt.float32
BF16 = mybir.dt.bfloat16
Alu = mybir.AluOpType
Act = mybir.ActivationFunctionType
AXX = mybir.AxisListType.X

MAGIC = float(np.float32(1.5 * 2 ** 23))
NCORE = 8
B, L, D, NST, RDT, HID = 2, 4096, 2048, 16, 64, 8192
TOK = (B * L) // NCORE
DSL = D // NCORE
CH = 512
NCH = L // CH
NKD = D // 128
NTT = TOK // 128
NKH = HID // 128
EPS = 1e-5

DEBUG_STOP = None
DEBUG_DUMPS = False
DEBUG_BC1 = False
DEBUG_A1 = False
DEBUG_NOBC = False


class _Prog:

    ENGINES = ("sync", "vector", "scalar", "tensor", "gpsimd")

    def __init__(self):
        self.ops = []
        self.cnt = {}

    def sem(self, name):
        self.cnt.setdefault(name, 0)

    def emit(self, eng, fn, inc=None, k=1):
        if inc is not None:
            self.sem(inc)
            self.cnt[inc] += k
        self.ops.append(("op", eng, fn, inc, k))

    def dma(self, eng, out, in_, inc):
        self.emit(eng, lambda E, o=out, i=in_: E.dma_start(out=o, in_=i),
                  inc=inc, k=16)

    def wait(self, eng, semname, val=None):
        v = self.cnt.get(semname, 0) if val is None else val
        if v > 0:
            self.ops.append(("wait", eng, semname, v, None))

    def val(self, semname):
        return self.cnt.get(semname, 0)

    def barrier(self):
        self.sem("PH")
        for e in self.ENGINES:
            if e in ("vector", "scalar", "tensor"):
                self.emit(e, lambda E: E.drain(), inc="PH")
            else:
                self.emit(e, lambda E: E.nop(), inc="PH")
        tgt = self.cnt["PH"]
        for e in self.ENGINES:
            self.wait(e, "PH", tgt)

    def lower(self, nc, block, ctx):
        handles = {name: ctx.enter_context(nc.semaphore(name))
                   for name in self.cnt}
        for eng_name in self.ENGINES:
            my = [op for op in self.ops if op[1] == eng_name]
            if not my:
                continue

            def section(E, my=my, handles=handles):
                for kind, _eng, a, b, c in my:
                    if kind == "wait":
                        E.wait_ge(handles[a], b)
                    else:
                        instr = a(E)
                        if b is not None:
                            instr.then_inc(handles[b], c)
            getattr(block, eng_name)(section)


def build_nc():
    nc = bass.Bass("TRN2", target_bir_lowering=False)

    T = {}

    def din(name, shape, dt=F32):
        T[name] = nc.dram_tensor(name, shape, dt, kind="ExternalInput").ap()

    def dout(name, shape, dt=F32):
        T[name] = nc.dram_tensor(name, shape, dt, kind="ExternalOutput").ap()

    def dint(name, shape, dt=F32, shared=False):
        T[name] = nc.dram_tensor(name, shape, dt,
                                 addr_space="Shared" if shared else "Local")

    din("x_tok", [TOK, D]); din("x_dsl", [B, L, DSL]); din("wx", [96, D])
    din("wdtT", [RDT, DSL]); din("bdt", [128, 2]); din("alog", [128, 2 * NST])
    din("dvec", [128, 2]); din("wgTd", [D, 1024]); din("wvTd", [D, 1024])
    din("woTd", [1024, D]); din("gbias", [1, HID]); din("vbias", [1, HID])
    din("obias", [1, D])
    dout("out", [TOK, D])
    if DEBUG_STOP == "mixer":
        dout("zdbg", [B * L, DSL])
        dout("dtdbg", [128, 2 * CH * B * NCH])
        dout("xdbg", [128, 2 * CH * B * NCH])
        dout("xtdbg", [128, 2 * CH * B * NCH])
        dout("abdbg", [128, 4 * CH * NST])
        dout("atdbg", [128, 2 * NST])
        dout("xrdbg", [128, 4 * DSL * B * NCH])
        dout("ydbg", [128, 2 * CH * B * NCH])
    if DEBUG_STOP == "ffn":
        dout("z1dbg", [TOK, D])
        dout("gvdbg", [NTT * 16 * 128, 512])
        dout("gvqdbg", [NTT * 128, HID], BF16)
        dout("zmydbg", [TOK, D])
        dout("lndbg", [128, 8 * NTT])
    if DEBUG_STOP == "ph0":
        dout("agdbg", [NCORE * 96, TOK])
        dout("wqdbg", [NCORE * D, 1024], BF16)
        dout("gmdbg", [NCORE, 4])
        dout("wgdbg", [128, 8])
    dint("ag1_in", [96, TOK])
    dint("ag1_out", [NCORE * 96, TOK], shared=True)
    dint("gm_in", [1, 4]); dint("gm_out", [NCORE, 4], shared=True)
    dint("wgq_in", [D, 1024], BF16)
    dint("wgq_out", [NCORE * D, 1024], BF16, shared=True)
    dint("wvq_in", [D, 1024], BF16)
    dint("wvq_out", [NCORE * D, 1024], BF16, shared=True)
    dint("woq_in", [1024, D], BF16)
    dint("woq_out", [NCORE * 1024, D], BF16, shared=True)
    dint("z_in", [B * L, DSL])
    dint("z_out", [NCORE * B * L, DSL], shared=True)
    dint("z_my", [TOK, D])
    dint("z1st", [TOK, D])
    dint("gv_dram", [NTT, 16, 128, 512])
    dint("gvqT_dram", [NTT, 128, HID], BF16)

    P = _Prog()
    for s in ("LD", "ST", "SD", "GD", "CC", "VE", "SC", "TE", "GP", "PH",
              "MF", "BF"):
        P.sem(s)
    marks = {}

    ctx = ExitStack()
    with ctx:
        E = ctx.enter_context
        S = {}
        S["identb"] = E(nc.sbuf_tensor("identb", [128, 128], BF16))
        S["identf"] = E(nc.sbuf_tensor("identf", [128, 128], F32))
        S["ones1"] = E(nc.sbuf_tensor("ones1", [1, 128], F32))
        S["onescol"] = E(nc.sbuf_tensor("onescol", [128, 1], F32))
        S["ones1b"] = E(nc.sbuf_tensor("ones1b", [1, 128], BF16))
        S["wgam"] = E(nc.sbuf_tensor("wgam", [128, 8], F32))
        S["dinv"] = E(nc.sbuf_tensor("dinv", [128, 1], F32))
        S["epsc"] = E(nc.sbuf_tensor("epsc", [128, 1], F32))
        S["c127"] = E(nc.sbuf_tensor("c127", [128, 1], F32))
        S["psf"] = [E(nc.psum_tensor(f"psf{i}", [128, 512], F32))
                    for i in range(6)]
        S["psb"] = [E(nc.psum_tensor(f"psb{i}", [128, 1024], BF16))
                    for i in range(2)]
        block = E(nc.Block())

        with ExitStack() as cscan:
            ES = cscan.enter_context
            S["xpj_h"] = ES(nc.sbuf_tensor("xpj_h", [96, B * L], BF16))
            S["xpj_l"] = ES(nc.sbuf_tensor("xpj_l", [96, B * L], BF16))

            _phase0(P, nc, T, S, marks)

            if DEBUG_STOP == "ph0":
                P.wait("gpsimd", "CC", marks["CC_GM"])
                P.dma("gpsimd", T["agdbg"], T["ag1_out"].ap(), "GD")
                P.dma("gpsimd", T["wqdbg"], T["wgq_out"].ap(), "GD")
                P.dma("gpsimd", T["gmdbg"], T["gm_out"].ap(), "GD")
                P.dma("gpsimd", T["wgdbg"], S["wgam"][:], "GD")
                P.wait("gpsimd", "GD", P.val("GD"))
                P.lower(nc, block, ctx)
                return nc

            _scan(P, nc, T, S, marks)

            if DEBUG_STOP == "mixer":
                P.wait("gpsimd", "ST", P.val("ST"))
                P.dma("gpsimd", T["zdbg"], T["z_in"].ap(), "GD")
                P.wait("gpsimd", "GD", P.val("GD"))
                P.lower(nc, block, ctx)
                return nc

            P.wait("gpsimd", "ST", P.val("ST"))
            P.wait("gpsimd", "SD", P.val("SD"))
            P.emit("gpsimd", lambda E: E.collective_compute(
                "AllGather", Alu.bypass, replica_groups=[list(range(NCORE))],
                ins=[T["z_in"].ap().opt()], outs=[T["z_out"].ap().opt()]),
                inc="CC")
            marks["CC_Z"] = P.val("CC")
            P.barrier()
        _ffn(P, nc, T, S, marks)
        P.lower(nc, block, ctx)
    return nc


def _ag(P, T, name_in, name_out, mark, marks):
    P.wait("gpsimd", "ST", P.val("ST"))
    P.emit("gpsimd", lambda E, a=T[name_in], b=T[name_out]: E.collective_compute(
        "AllGather", Alu.bypass, replica_groups=[list(range(NCORE))],
        ins=[a.ap().opt()], outs=[b.ap().opt()]), inc="CC")
    marks[mark] = P.val("CC")


def _phase0(P, nc, T, S, marks):
    psf, psb = S["psf"], S["psb"]
    identb, identf = S["identb"], S["identf"]
    ones1, onescol, wgam = S["ones1"], S["onescol"], S["wgam"]

    with ExitStack() as c0:
        E0 = c0.enter_context
        xtk = E0(nc.sbuf_tensor("xtk", [128, D], F32))
        xth = E0(nc.sbuf_tensor("xth", [128, D], BF16))
        xtl = E0(nc.sbuf_tensor("xtl", [128, D], BF16))
        xtmp = E0(nc.sbuf_tensor("xtmp", [128, D], F32))
        xTh = E0(nc.sbuf_tensor("xTh", [128, NKD * 128], BF16))
        xTl = E0(nc.sbuf_tensor("xTl", [128, NKD * 128], BF16))
        wxs = E0(nc.sbuf_tensor("wxs", [96, D], F32))
        wxh = E0(nc.sbuf_tensor("wxh", [96, D], BF16))
        wxl = E0(nc.sbuf_tensor("wxl", [96, D], BF16))
        wxTh = E0(nc.sbuf_tensor("wxTh", [128, NKD * 96], BF16))
        wxTl = E0(nc.sbuf_tensor("wxTl", [128, NKD * 96], BF16))
        xpmine = E0(nc.sbuf_tensor("xpmine", [96, TOK], F32))
        wti = [E0(nc.sbuf_tensor(f"wti{i}", [128, 2048], F32)) for i in range(2)]
        wqo = [E0(nc.sbuf_tensor(f"wqo{i}", [128, 2048], BF16)) for i in range(2)]
        wtmp = E0(nc.sbuf_tensor("wtmp", [128, 2048], F32))
        wacc = E0(nc.sbuf_tensor("wacc", [128, 1], F32))
        wred = E0(nc.sbuf_tensor("wred", [128, 1], F32))
        gmsb = E0(nc.sbuf_tensor("gmsb", [1, 8], F32))
        gmrd = E0(nc.sbuf_tensor("gmrd", [1, 32], F32))

        P.emit("gpsimd", lambda E: E.memset(identb[:], 0.0))
        P.emit("gpsimd", lambda E: E.affine_select(
            out=identb[:], in_=identb[:], compare_op=Alu.not_equal,
            fill=1.0, base=0, pattern=[[-1, 128]], channel_multiplier=1))
        P.emit("gpsimd", lambda E: E.memset(identf[:], 0.0))
        P.emit("gpsimd", lambda E: E.affine_select(
            out=identf[:], in_=identf[:], compare_op=Alu.not_equal,
            fill=1.0, base=0, pattern=[[-1, 128]], channel_multiplier=1))
        P.emit("gpsimd", lambda E: E.memset(ones1[:], 1.0))
        P.emit("gpsimd", lambda E: E.memset(onescol[:], 1.0))
        P.emit("gpsimd", lambda E: E.memset(S["dinv"][:], 1.0 / D))
        P.emit("gpsimd", lambda E: E.memset(S["epsc"][:], EPS))
        P.emit("gpsimd", lambda E: E.memset(S["c127"][:], 127.0))
        P.emit("gpsimd", lambda E: E.memset(S["ones1b"][:], 1.0), inc="GP")
        gp_const = P.val("GP")

        P.dma("sync", wxs[:], T["wx"], "LD")
        P.wait("vector", "LD", P.val("LD"))
        P.emit("vector", lambda E: E.tensor_copy(wxh[:], wxs[:]))
        P.emit("vector", lambda E: E.tensor_tensor(
            xtmp[0:96, :], wxs[:], wxh[:], Alu.subtract))
        P.emit("vector", lambda E: E.tensor_copy(wxl[:], xtmp[0:96, :]), inc="VE")
        P.wait("tensor", "VE", P.val("VE"))
        P.wait("tensor", "GP", gp_const)
        tsq = 0
        for k in range(NKD):
            for src, dst in ((wxh, wxTh), (wxl, wxTl)):
                if tsq >= 2:
                    P.wait("tensor", "SC", P.val("SC") - 1)
                P.emit("tensor", lambda E, s=src, k=k, p=tsq % 2: E.transpose(
                    psb[p][:, 0:96], s[:, k * 128:(k + 1) * 128],
                    identb[0:96, 0:96]), inc="TE")
                P.wait("scalar", "TE", P.val("TE"))
                P.emit("scalar", lambda E, d=dst, k=k, p=tsq % 2: E.copy(
                    d[:, k * 96:(k + 1) * 96], psb[p][:, 0:96]), inc="SC")
                tsq += 1

        for tt in range(NTT):
            P.wait("sync", "VE", P.val("VE"))
            P.dma("sync", xtk[:], T["x_tok"][tt * 128:(tt + 1) * 128, :], "LD")
            P.wait("vector", "LD", P.val("LD"))
            P.emit("vector", lambda E: E.tensor_copy(xth[:], xtk[:]))
            P.emit("vector", lambda E: E.tensor_tensor(
                xtmp[:], xtk[:], xth[:], Alu.subtract))
            P.emit("vector", lambda E: E.tensor_copy(xtl[:], xtmp[:]), inc="VE")
            P.wait("tensor", "VE", P.val("VE"))
            for k in range(NKD):
                for src, dst in ((xth, xTh), (xtl, xTl)):
                    P.wait("tensor", "SC", P.val("SC") - 1)
                    P.emit("tensor", lambda E, s=src, k=k, p=tsq % 2: E.transpose(
                        psb[p][:, 0:128], s[:, k * 128:(k + 1) * 128], identb[:]),
                        inc="TE")
                    P.wait("scalar", "TE", P.val("TE"))
                    P.emit("scalar", lambda E, d=dst, k=k, p=tsq % 2: E.copy(
                        d[:, k * 128:(k + 1) * 128], psb[p][:, 0:128]), inc="SC")
                    tsq += 1
            P.wait("tensor", "SC", P.val("SC"))
            n_mm = NKD * 4
            i_mm = 0
            for k in range(NKD):
                for lh in (wxTh, wxTl):
                    for rh in (xTh, xTl):
                        P.emit("tensor",
                               lambda E, k=k, lh=lh, rh=rh, f=(i_mm == 0),
                               st=(i_mm == n_mm - 1): E.matmul(
                                   psf[4][0:96, 0:128],
                                   lh[:, k * 96:(k + 1) * 96],
                                   rh[:, k * 128:(k + 1) * 128],
                                   start=f, stop=st),
                               inc=("TE" if i_mm == n_mm - 1 else None))
                        i_mm += 1
            P.wait("vector", "TE", P.val("TE"))
            P.emit("vector", lambda E, tt=tt: E.tensor_copy(
                xpmine[:, tt * 128:(tt + 1) * 128], psf[4][0:96, 0:128]),
                inc="VE")
            P.wait("tensor", "VE", P.val("VE"))
        P.wait("sync", "VE", P.val("VE"))
        P.dma("sync", T["ag1_in"].ap(), xpmine[:], "ST")
        _ag(P, T, "ag1_in", "ag1_out", "CC_AG1", marks)

        wspec = [
            ("wgTd", "wgq_in", D // 128, 1024, 0),
            ("wvTd", "wvq_in", D // 128, 1024, 1),
            ("woTd", "woq_in", 1024 // 128, 2048, 2),
        ]
        for wsrc, wdst, ntile, width, widx in wspec:
            for it in range(ntile):
                buf = wti[it % 2]
                P.wait("sync", "VE", P.val("VE"))
                P.dma("sync", buf[:, 0:width],
                      T[wsrc][it * 128:(it + 1) * 128, :], "LD")
                P.wait("vector", "LD", P.val("LD"))
                P.emit("vector", lambda E, b=buf, w=width: E.tensor_reduce(
                    wred[:], b[:, 0:w], AXX, Alu.add, apply_absolute_value=True))
                P.emit("vector", lambda E: E.drain())
                if it == 0:
                    P.emit("vector", lambda E: E.tensor_copy(wacc[:], wred[:]),
                           inc="VE")
                else:
                    P.emit("vector", lambda E: E.tensor_tensor(
                        wacc[:], wacc[:], wred[:], Alu.add), inc="VE")
            P.wait("tensor", "VE", P.val("VE"))
            P.emit("tensor", lambda E: E.matmul(
                psf[5][0:1, 0:1], wacc[:], onescol[:], start=True, stop=True),
                inc="TE")
            P.wait("vector", "TE", P.val("TE"))
            P.emit("vector", lambda E, wi=widx: E.tensor_scalar(
                gmsb[0:1, wi:wi + 1], psf[5][0:1, 0:1], 1.0 / (HID * D), None,
                op0=Alu.mult), inc="VE")
            P.wait("tensor", "VE", P.val("VE"))
        P.wait("sync", "VE", P.val("VE"))
        P.dma("sync", T["gm_in"].ap(), gmsb[0:1, 0:4], "ST")
        _ag(P, T, "gm_in", "gm_out", "CC_GM", marks)
        P.wait("sync", "CC", marks["CC_GM"])
        P.dma("sync", gmrd[0:1, 0:32], T["gm_out"].ap(), "LD")
        P.wait("vector", "LD", P.val("LD"))
        gm3 = gmrd[0:1, 0:32].rearrange("p (r f) -> p r f", f=4)
        for widx in range(3):
            P.emit("vector", lambda E, wi=widx: E.tensor_reduce(
                gmsb[0:1, wi:wi + 1], gm3[:, :, wi], AXX, Alu.add))
        P.emit("vector", lambda E: E.drain(), inc="VE")
        P.wait("scalar", "VE", P.val("VE"))
        P.emit("scalar", lambda E: E.copy(gmsb[0:1, 4:7], gmsb[0:1, 0:3]),
               inc="SC")
        P.wait("vector", "SC", P.val("SC"))
        P.emit("vector", lambda E: E.tensor_scalar(
            gmsb[0:1, 0:3], gmsb[0:1, 4:7], EPS, None, op0=Alu.max))
        P.emit("vector", lambda E: E.drain())
        P.emit("vector", lambda E: E.tensor_scalar(
            gmsb[0:1, 4:7], gmsb[0:1, 0:3], 1.0 / 127.0, None, op0=Alu.mult))
        P.emit("vector", lambda E: E.drain())
        P.emit("vector", lambda E: E.reciprocal(gmsb[0:1, 0:3], gmsb[0:1, 0:3]))
        P.emit("vector", lambda E: E.drain(), inc="VE")
        P.wait("tensor", "VE", P.val("VE"))
        P.emit("tensor", lambda E: E.matmul(
            psf[5][:, 0:8], ones1[0:1, :], gmsb[0:1, 0:8], start=True,
            stop=True), inc="TE")
        P.wait("vector", "TE", P.val("TE"))
        P.emit("vector", lambda E: E.tensor_copy(wgam[:], psf[5][:, 0:8]),
               inc="VE")

        P.barrier()


def _scan(P, nc, T, S, marks):
    psf = S["psf"]
    identf = S["identf"]
    xpj_h, xpj_l = S["xpj_h"], S["xpj_l"]
    NP = B * NCH

    with ExitStack() as c1:
        E1 = c1.enter_context
        wdts = E1(nc.sbuf_tensor("wdts", [RDT, DSL], F32))
        wdth = E1(nc.sbuf_tensor("wdth", [RDT, DSL], BF16))
        wdtl = E1(nc.sbuf_tensor("wdtl", [RDT, DSL], BF16))
        wdtmp = E1(nc.sbuf_tensor("wdtmp", [RDT, DSL], F32))
        bdts = E1(nc.sbuf_tensor("bdts", [128, 2], F32))
        alogs = E1(nc.sbuf_tensor("alogs", [128, 2 * NST], F32))
        atile = E1(nc.sbuf_tensor("atile", [128, 2 * NST], F32))
        dp1 = E1(nc.sbuf_tensor("dp1", [128, 2], F32))
        xraw = [E1(nc.sbuf_tensor(f"xraw{i}", [128, 4 * DSL], F32))
                for i in range(2)]
        xT = [E1(nc.sbuf_tensor(f"xT{i}", [128, 2 * CH], F32))
              for i in range(2)]
        dtt = [E1(nc.sbuf_tensor(f"dtt{i}", [128, 2 * CH], F32))
               for i in range(2)]
        dtx = [E1(nc.sbuf_tensor(f"dtx{i}", [128, 2 * CH], F32))
               for i in range(2)]
        spm = E1(nc.sbuf_tensor("spm", [128, 2 * CH], F32))
        spa = E1(nc.sbuf_tensor("spa", [128, 2 * CH], F32))
        abuf = E1(nc.sbuf_tensor("abuf", [128, 4 * CH], F32))
        wbuf = E1(nc.sbuf_tensor("wbuf", [128, 2 * CH], F32))
        htr = [E1(nc.sbuf_tensor(f"htr{i}", [128, 2 * CH], F32))
               for i in range(2)]
        mbf = [E1(nc.sbuf_tensor(f"mbf{i}", [128, 2 * CH], F32))
               for i in range(2)]
        csb = E1(nc.sbuf_tensor("csb", [128, 2 * CH], F32))
        yacc = E1(nc.sbuf_tensor("yacc", [128, 2 * CH], F32))
        ztile = E1(nc.sbuf_tensor("ztile", [128, 2 * CH], F32))
        zstage = E1(nc.sbuf_tensor("zstage", [128, 4 * DSL], F32))
        hst = E1(nc.sbuf_tensor("hst", [128, 4 * NST], F32))

        for eng in ("sync", "vector", "scalar", "tensor", "gpsimd"):
            P.wait(eng, "CC", marks["CC_GM"])
        P.dma("sync", wdts[:], T["wdtT"], "LD")
        P.dma("sync", bdts[:], T["bdt"], "LD")
        P.dma("sync", alogs[:], T["alog"], "LD")
        P.dma("sync", dp1[:], T["dvec"], "LD")
        P.wait("vector", "LD", P.val("LD"))
        P.emit("vector", lambda E: E.tensor_copy(wdth[:], wdts[:]))
        P.emit("vector", lambda E: E.tensor_tensor(
            wdtmp[:], wdts[:], wdth[:], Alu.subtract))
        P.emit("vector", lambda E: E.tensor_copy(wdtl[:], wdtmp[:]))
        P.emit("vector", lambda E: E.tensor_scalar(
            dp1[:], dp1[:], 1.0, None, op0=Alu.add), inc="VE")
        P.wait("scalar", "LD", P.val("LD"))
        P.emit("scalar", lambda E: E.activation(atile[:], alogs[:], Act.Exp),
               inc="SC")
        P.wait("vector", "SC", P.val("SC"))
        P.emit("vector", lambda E: E.tensor_scalar(
            atile[:], atile[:], -1.0, None, op0=Alu.mult), inc="VE")
        P.emit("vector", lambda E: E.memset(hst[:], 0.0), inc="GP")
        gp_hst = P.val("GP")

        P.wait("sync", "CC", marks["CC_AG1"])
        with ExitStack() as cx:
            xprojT = cx.enter_context(
                nc.sbuf_tensor("xprojT", [96, B * L], F32))
            losc = cx.enter_context(nc.sbuf_tensor("losc", [96, 1024], F32))
            for r in range(NCORE):
                P.dma("sync", xprojT[:, r * TOK:(r + 1) * TOK],
                      T["ag1_out"][r * 96:(r + 1) * 96, :], "LD")
            P.wait("vector", "LD", P.val("LD"))
            P.emit("vector", lambda E: E.tensor_copy(xpj_h[:], xprojT[:, :]))
            for cs in range(0, B * L, 1024):
                P.emit("vector", lambda E, cs=cs: E.tensor_tensor(
                    losc[:, :], xprojT[:, cs:cs + 1024],
                    xpj_h[:, cs:cs + 1024], Alu.subtract))
                P.emit("vector", lambda E, cs=cs: E.tensor_copy(
                    xpj_l[:, cs:cs + 1024], losc[:, :]),
                    inc=("VE" if cs == B * L - 1024 else None))
        stg_h = E1(nc.sbuf_tensor("stg_h", [1, 32 * CH], BF16))
        stg_l = E1(nc.sbuf_tensor("stg_l", [1, 32 * CH], BF16))
        qwi = [E1(nc.sbuf_tensor(f"qwi{i}", [128, 1024], F32))
               for i in range(2)]
        qwo = [E1(nc.sbuf_tensor(f"qwo{i}", [128, 1024], BF16))
               for i in range(2)]
        qtmp = E1(nc.sbuf_tensor("qtmp", [128, 1024], F32))
        qcst = E1(nc.sbuf_tensor("qcst", [128, 4], F32))
        ve_prep = P.val("VE")
        P.wait("tensor", "VE", ve_prep)
        P.wait("tensor", "GP", gp_hst)
        P.wait("vector", "GP", gp_hst)
        P.wait("scalar", "VE", ve_prep)
        P.wait("sync", "VE", ve_prep)

        ld_ready, ld_stage = {}, {}
        xtr_done, xT_ready, ztr_done, zst_done = {}, {}, {}, {}
        sc_sp, gp_dtt, gp_dtx, ve_zt = {}, {}, {}, {}
        te_bc, sc_exp, sc_evC, ve_tts, ve_m0, gp_n = {}, {}, {}, {}, {}, {}
        mf_marks = []

        def emit_head(li):
            b, ci = divmod(li, NCH)
            colb = b * L + ci * CH
            P.wait("tensor", "LD", ld_ready[li])
            first_evac = True
            for s in range(4):
                for d in range(2):
                    if len(mf_marks) >= 2:
                        P.wait("tensor", "SC", mf_marks[-2])
                    P.emit("tensor", lambda E, s=s, d=d, li=li: E.transpose(
                        psf[4 + d][:, 0:128],
                        xraw[li % 2][:, s * DSL + d * 128:s * DSL + (d + 1) * 128],
                        identf[:]), inc="TE")
                    P.wait("scalar", "TE", P.val("TE"))
                    if first_evac:
                        if li - 2 in gp_dtx:
                            P.wait("scalar", "GP", gp_dtx[li - 2])
                        if li - 2 in ve_zt:
                            P.wait("scalar", "VE", ve_zt[li - 2])
                        first_evac = False
                    P.emit("scalar", lambda E, s=s, d=d, li=li: E.copy(
                        xT[li % 2][:, d * CH + s * 128:d * CH + (s + 1) * 128],
                        psf[4 + d][:, 0:128]), inc="SC")
                    mf_marks.append(P.val("SC"))
            xT_ready[li] = P.val("SC")
            xtr_done[li] = P.val("TE")
            if DEBUG_STOP == "mixer" and DEBUG_DUMPS:
                P.wait("sync", "TE", xtr_done[li])
                P.dma("sync", T["xrdbg"][:, li * 4 * DSL:(li + 1) * 4 * DSL],
                      xraw[li % 2][:], "ST")
            for d in range(2):
                if len(mf_marks) >= 2:
                    P.wait("tensor", "SC", mf_marks[-2])
                mmi = 0
                for lh in (wdth, wdtl):
                    for rh in (xpj_h, xpj_l):
                        P.emit("tensor", lambda E, lh=lh, rh=rh, d=d,
                               f=(mmi == 0), st=(mmi == 3), cb=colb:
                               E.matmul(psf[4 + d][:, 0:CH],
                                        lh[:, d * 128:(d + 1) * 128],
                                        rh[0:RDT, cb:cb + CH],
                                        start=f, stop=st),
                               inc=("TE" if mmi == 3 else None))
                        mmi += 1
                P.wait("scalar", "TE", P.val("TE"))
                if d == 0 and li - 1 in gp_dtt:
                    P.wait("scalar", "GP", gp_dtt[li - 1])
                P.emit("scalar", lambda E, d=d: E.activation(
                    spm[:, d * CH:(d + 1) * CH], psf[4 + d][:, 0:CH],
                    Act.Relu, bias=bdts[:, d:d + 1]))
                P.emit("scalar", lambda E, d=d: E.activation(
                    spa[:, d * CH:(d + 1) * CH], psf[4 + d][:, 0:CH],
                    Act.Abs, bias=bdts[:, d:d + 1]), inc="SC")
                mf_marks.append(P.val("SC"))
                P.emit("scalar", lambda E, d=d: E.activation(
                    spa[:, d * CH:(d + 1) * CH], spa[:, d * CH:(d + 1) * CH],
                    Act.Exp, scale=-1.0))
                P.emit("scalar", lambda E, d=d: E.activation(
                    spa[:, d * CH:(d + 1) * CH], spa[:, d * CH:(d + 1) * CH],
                    Act.Ln, bias=1.0), inc="SC")
            sc_sp[li] = P.val("SC")
            P.wait("gpsimd", "SC", sc_sp[li])
            if (li - 1) * NST - 1 in sc_exp:
                P.wait("gpsimd", "SC", sc_exp[(li - 1) * NST - 1])
            P.emit("gpsimd", lambda E, li=li: E.tensor_tensor(
                dtt[li % 2][:], spm[:], spa[:], Alu.add), inc="GP")
            gp_dtt[li] = P.val("GP")
            P.wait("gpsimd", "SC", xT_ready[li])
            if (li - 1) * NST - 1 in ve_tts:
                P.wait("gpsimd", "VE", ve_tts[(li - 1) * NST - 1])
            P.emit("gpsimd", lambda E, li=li: E.tensor_tensor(
                dtx[li % 2][:], dtt[li % 2][:], xT[li % 2][:], Alu.mult),
                inc="GP")
            gp_dtx[li] = P.val("GP")
            if DEBUG_STOP == "mixer" and DEBUG_DUMPS:
                P.wait("sync", "GP", P.val("GP"))
                P.dma("sync", T["dtdbg"][:, li * 2 * CH:(li + 1) * 2 * CH],
                      dtt[li % 2][:], "ST")
                P.dma("sync", T["xdbg"][:, li * 2 * CH:(li + 1) * 2 * CH],
                      dtx[li % 2][:], "ST")
                P.dma("sync", T["xtdbg"][:, li * 2 * CH:(li + 1) * 2 * CH],
                      xT[li % 2][:], "ST")
                for eng in ("vector", "scalar", "tensor", "gpsimd"):
                    P.wait(eng, "ST", P.val("ST"))

        def emit_stage(li):
            b, ci = divmod(li, NCH)
            colb = b * L + ci * CH
            if li * NST - 1 in te_bc:
                P.wait("sync", "TE", te_bc[li * NST - 1])
            P.dma("sync", stg_h[0:1, :], xpj_h[RDT:96, colb:colb + CH], "LD")
            P.dma("sync", stg_l[0:1, :], xpj_l[RDT:96, colb:colb + CH], "LD")
            ld_stage[li] = P.val("LD")

        def emit_n(li, n):
            nseq = li * NST + n
            sl = nseq % 2
            b, ci = divmod(li, NCH)
            if nseq - 2 in ve_m0:
                P.wait("tensor", "VE", ve_m0[nseq - 2])
                P.wait("tensor", "SC", sc_evC[nseq - 2])
            P.wait("tensor", "LD", ld_stage[li])
            for pj, base in ((0, n), (1, NST + n)):
                for hl, last in ((stg_h, False), (stg_l, True)):
                    P.emit("tensor", lambda E, pj=pj, base=base, hl=hl,
                           last=last, sl=sl: E.matmul(
                               psf[2 * sl + pj][:, 0:CH], S["ones1b"][0:1, :],
                               hl[0:1, base * CH:(base + 1) * CH],
                               start=not last, stop=last),
                           inc=("TE" if pj == 1 and last else None))
            te_bc[nseq] = P.val("TE")
            if nseq - 2 in ve_tts:
                P.wait("scalar", "VE", ve_tts[nseq - 2])
            if n == 0:
                P.wait("scalar", "GP", gp_dtt[li])
            for d in range(2):
                P.emit("scalar", lambda E, d=d, n=n, sl=sl, li=li:
                       E.activation(
                           abuf[:, sl * 2 * CH + d * CH:
                                sl * 2 * CH + (d + 1) * CH],
                           dtt[li % 2][:, d * CH:(d + 1) * CH], Act.Exp,
                           scale=atile[:, d * NST + n:d * NST + n + 1]))
            P.emit("scalar", lambda E: E.drain(), inc="SC")
            sc_exp[nseq] = P.val("SC")
            if nseq - 2 in gp_n:
                P.wait("scalar", "GP", gp_n[nseq - 2])
            P.wait("scalar", "TE", te_bc[nseq])
            P.emit("scalar", lambda E, sl=sl: E.copy(
                csb[:, sl * CH:(sl + 1) * CH], psf[2 * sl + 1][:, 0:CH]),
                inc="SC")
            sc_evC[nseq] = P.val("SC")
            P.wait("vector", "TE", te_bc[nseq])
            P.wait("vector", "SC", sc_exp[nseq])
            if nseq - 2 in gp_n:
                P.wait("vector", "GP", gp_n[nseq - 2])
            if n == 0:
                P.wait("vector", "GP", gp_dtx[li])
            for d in range(2):
                hcol = (2 * b + d) * NST + n
                P.emit("vector", lambda E, d=d, sl=sl, li=li: E.tensor_tensor(
                    wbuf[:, d * CH:(d + 1) * CH],
                    dtx[li % 2][:, d * CH:(d + 1) * CH],
                    psf[2 * sl][:, 0:CH], Alu.mult))
                P.emit("vector", lambda E, d=d, sl=sl, hc=hcol:
                       E.tensor_tensor_scan(
                           htr[sl][:, d * CH:(d + 1) * CH],
                           abuf[:, sl * 2 * CH + d * CH:
                                sl * 2 * CH + (d + 1) * CH],
                           wbuf[:, d * CH:(d + 1) * CH],
                           hst[:, hc:hc + 1],
                           op0=Alu.mult, op1=Alu.add),
                       inc=("VE" if d == 1 else None))
            ve_tts[nseq] = P.val("VE")
            P.emit("vector", lambda E, sl=sl: E.tensor_tensor(
                mbf[sl][:, 0:CH], htr[sl][:, 0:CH], psf[2 * sl + 1][:, 0:CH],
                Alu.mult), inc="VE")
            ve_m0[nseq] = P.val("VE")
            P.emit("vector", lambda E: E.drain())
            for d in range(2):
                hcol = (2 * b + d) * NST + n
                P.emit("vector", lambda E, d=d, sl=sl, hc=hcol: E.tensor_copy(
                    hst[:, hc:hc + 1],
                    htr[sl][:, (d + 1) * CH - 1:(d + 1) * CH]))
            if DEBUG_STOP == "mixer" and DEBUG_DUMPS and li == 1:
                P.wait("sync", "VE", P.val("VE"))
                P.wait("sync", "GP", P.val("GP"))
                P.dma("sync", T["abdbg"][:, n * 4 * CH:n * 4 * CH + 2 * CH],
                      abuf[:, sl * 2 * CH:(sl + 1) * 2 * CH], "ST")
                P.dma("sync", T["abdbg"][:, n * 4 * CH + 2 * CH:
                                         (n + 1) * 4 * CH],
                      htr[sl][:], "ST")
                P.dma("sync", T["atdbg"], atile[:], "ST")
                for eng in ("vector", "scalar", "tensor", "gpsimd"):
                    P.wait(eng, "ST", P.val("ST"))
            P.wait("gpsimd", "VE", ve_tts[nseq])
            P.wait("gpsimd", "SC", sc_evC[nseq])
            P.emit("gpsimd", lambda E, sl=sl: E.tensor_tensor(
                mbf[sl][:, CH:2 * CH], htr[sl][:, CH:2 * CH],
                csb[:, sl * CH:(sl + 1) * CH], Alu.mult))
            P.wait("gpsimd", "VE", ve_m0[nseq])
            if n == 0:
                if li - 1 in ve_zt:
                    P.wait("gpsimd", "VE", ve_zt[li - 1])
                P.emit("gpsimd", lambda E, sl=sl: E.tensor_copy(
                    yacc[:], mbf[sl][:]), inc="GP")
            else:
                P.emit("gpsimd", lambda E, sl=sl: E.tensor_tensor(
                    yacc[:], yacc[:], mbf[sl][:], Alu.add), inc="GP")
            gp_n[nseq] = P.val("GP")

        def emit_ztile(li):
            P.wait("vector", "GP", gp_n[li * NST + NST - 1])
            if li - 1 in ztr_done:
                P.wait("vector", "TE", ztr_done[li - 1])
            for d in range(2):
                P.emit("vector", lambda E, d=d, li=li: E.scalar_tensor_tensor(
                    ztile[:, d * CH:(d + 1) * CH],
                    xT[li % 2][:, d * CH:(d + 1) * CH],
                    dp1[:, d:d + 1], yacc[:, d * CH:(d + 1) * CH],
                    op0=Alu.mult, op1=Alu.add),
                    inc=("VE" if d == 1 else None))
            ve_zt[li] = P.val("VE")
            if DEBUG_STOP == "mixer" and DEBUG_DUMPS:
                P.wait("sync", "VE", P.val("VE"))
                P.dma("sync", T["ydbg"][:, li * 2 * CH:(li + 1) * 2 * CH],
                      yacc[:], "ST")
                for eng in ("vector", "scalar", "tensor", "gpsimd"):
                    P.wait(eng, "ST", P.val("ST"))

        def emit_ztrans(li):
            b, ci = divmod(li, NCH)
            P.wait("tensor", "VE", ve_zt[li])
            first = True
            for s in range(4):
                for d in range(2):
                    if len(mf_marks) >= 2:
                        P.wait("tensor", "SC", mf_marks[-2])
                    P.emit("tensor", lambda E, s=s, d=d: E.transpose(
                        psf[4 + d][:, 0:128],
                        ztile[:, d * CH + s * 128:d * CH + (s + 1) * 128],
                        identf[:]), inc="TE")
                    P.wait("scalar", "TE", P.val("TE"))
                    if first and li - 1 in zst_done:
                        P.wait("scalar", "ST", zst_done[li - 1])
                        first = False
                    P.emit("scalar", lambda E, s=s, d=d: E.copy(
                        zstage[:, s * DSL + d * 128:s * DSL + (d + 1) * 128],
                        psf[4 + d][:, 0:128]), inc="SC")
                    mf_marks.append(P.val("SC"))
            ztr_done[li] = P.val("TE")
            P.wait("sync", "SC", P.val("SC"))
            rowb = b * L + ci * CH
            for s in range(4):
                P.dma("sync",
                      T["z_in"][rowb + s * 128:rowb + (s + 1) * 128, :],
                      zstage[:, s * DSL:(s + 1) * DSL], "ST")
            zst_done[li] = P.val("ST")

        wgam = S["wgam"]
        P.emit("gpsimd", lambda E: E.memset(qcst[:, 0:1], MAGIC))
        P.emit("gpsimd", lambda E: E.memset(qcst[:, 1:2], -(MAGIC - 1.0)))
        P.emit("gpsimd", lambda E: E.memset(qcst[:, 2:3], 2.0))
        P.emit("gpsimd", lambda E: E.memset(qcst[:, 3:4], 1.0), inc="GP")
        P.wait("scalar", "GP", P.val("GP"))
        qdesc = []
        for wsrc, wdst, nrt, wid in (("wgTd", "wgq_in", 16, 0),
                                     ("wvTd", "wvq_in", 16, 1)):
            for it in range(nrt):
                qdesc.append((wsrc, wdst, it * 128, 0, wid))
        for it in range(8):
            for cb in (0, 1024):
                qdesc.append(("woTd", "woq_in", it * 128, cb, 2))
        q_sc, q_st = {}, {}
        qpos = [0]

        def q_pump(k):
            for _ in range(k):
                i = qpos[0]
                if i >= len(qdesc):
                    return
                qpos[0] += 1
                wsrc, wdst, r0, c0, wid = qdesc[i]
                if i - 2 in q_sc:
                    P.wait("sync", "SC", q_sc[i - 2])
                    P.wait("sync", "ST", q_st[i - 2])
                P.dma("sync", qwi[i % 2][:],
                      T[wsrc][r0:r0 + 128, c0:c0 + 1024], "LD")
                P.wait("scalar", "LD", P.val("LD"))
                P.emit("scalar", lambda E, i=i, wid=wid: E.activation(
                    qtmp[:], qwi[i % 2][:], Act.Copy,
                    scale=wgam[:, wid:wid + 1], bias=MAGIC))
                P.emit("scalar", lambda E: E.activation(
                    qtmp[:], qtmp[:], Act.Relu, bias=qcst[:, 1:2]))
                P.emit("scalar", lambda E: E.activation(
                    qtmp[:], qtmp[:], Act.Relu, scale=-1.0, bias=qcst[:, 2:3]))
                P.emit("scalar", lambda E, i=i: E.activation(
                    qwo[i % 2][:], qtmp[:], Act.Copy, scale=-1.0, bias=1.0))
                P.emit("scalar", lambda E: E.drain(), inc="SC")
                q_sc[i] = P.val("SC")
                P.wait("sync", "SC", q_sc[i])
                P.dma("sync", T[wdst][r0:r0 + 128, c0:c0 + 1024],
                      qwo[i % 2][:], "ST")
                q_st[i] = P.val("ST")

        for s in range(4):
            P.dma("sync", xraw[0][:, s * DSL:(s + 1) * DSL],
                  T["x_dsl"][0, s * 128:(s + 1) * 128, :], "LD")
        ld_ready[0] = P.val("LD")
        emit_stage(0)
        emit_head(0)
        for li in range(NP):
            if li + 1 < NP:
                nb, nci = divmod(li + 1, NCH)
                if li - 1 in xtr_done:
                    P.wait("sync", "TE", xtr_done[li - 1])
                for s in range(4):
                    P.dma("sync",
                          xraw[(li + 1) % 2][:, s * DSL:(s + 1) * DSL],
                          T["x_dsl"][nb,
                                     nci * CH + s * 128:nci * CH + (s + 1) * 128,
                                     :], "LD")
                ld_ready[li + 1] = P.val("LD")
            if li >= 1:
                emit_ztile(li - 1)
            for n in range(NST):
                emit_n(li, n)
                if li >= 1 and n in (3, 7, 11, 15):
                    q_pump(1)
                if n == 2 and li >= 1:
                    emit_ztrans(li - 1)
                if n == 7 and li + 1 < NP:
                    emit_head(li + 1)
                if n == 15 and li + 1 < NP:
                    emit_stage(li + 1)
        emit_ztile(NP - 1)
        emit_ztrans(NP - 1)
        q_pump(len(qdesc))
        P.wait("sync", "ST", P.val("ST"))


def _ffn(P, nc, T, S, marks):
    psf, psb = S["psf"], S["psb"]
    identb, ones1, wgam = S["identb"], S["ones1"], S["wgam"]
    hold = {}

    with ExitStack() as cO:
        EO = cO.enter_context
        sxg = EO(nc.sbuf_tensor("sxg", [128, NTT], F32))
        sxv = EO(nc.sbuf_tensor("sxv", [128, NTT], F32))
        sO = EO(nc.sbuf_tensor("sO", [128, NTT], F32))
        gvmax = EO(nc.sbuf_tensor("gvmax", [128, NTT], F32))
        for eng in ("sync", "vector", "scalar", "tensor", "gpsimd"):
            P.wait(eng, "CC", marks["CC_Z"])
        P.emit("gpsimd", lambda E: E.memset(gvmax[:], 0.0), inc="GP")
        P.wait("vector", "GP", P.val("GP"))

        def emit_ln(src_fn, dst_fn, tn):
            P.emit("vector", lambda E: E.reduce_sum(tn["red"][:], src_fn(), AXX))
            P.emit("vector", lambda E: E.drain(), inc="VE")
            P.wait("scalar", "VE", P.val("VE"))
            P.emit("scalar", lambda E: E.activation(
                tn["sqs"][:], src_fn(), Act.Square))
            P.emit("scalar", lambda E: E.drain(), inc="SC")
            P.wait("vector", "SC", P.val("SC"))
            P.emit("vector", lambda E: E.reduce_sum(tn["ms"][:], tn["sqs"][:], AXX))
            P.emit("vector", lambda E: E.drain())
            P.emit("vector", lambda E: E.tensor_tensor(
                tn["mu"][:], tn["red"][:], S["dinv"][:], Alu.mult))
            P.emit("vector", lambda E: E.drain())
            P.emit("vector", lambda E: E.tensor_tensor(
                tn["t1"][:], tn["mu"][:], tn["mu"][:], Alu.mult))
            P.emit("vector", lambda E: E.tensor_tensor(
                tn["ms"][:], tn["ms"][:], S["dinv"][:], Alu.mult))
            P.emit("vector", lambda E: E.drain())
            P.emit("vector", lambda E: E.tensor_tensor(
                tn["vpe"][:], tn["ms"][:], tn["t1"][:], Alu.subtract))
            P.emit("vector", lambda E: E.drain())
            P.emit("vector", lambda E: E.tensor_tensor(
                tn["vpe"][:], tn["vpe"][:], S["epsc"][:], Alu.add))
            P.emit("vector", lambda E: E.drain(), inc="VE")
            P.wait("scalar", "VE", P.val("VE"))
            P.emit("scalar", lambda E: E.activation(
                tn["y0"][:], tn["vpe"][:], Act.Ln))
            P.emit("scalar", lambda E: E.drain())
            P.emit("scalar", lambda E: E.activation(
                tn["y0"][:], tn["y0"][:], Act.Exp, scale=-0.5))
            P.emit("scalar", lambda E: E.drain(), inc="SC")
            P.wait("vector", "SC", P.val("SC"))
            P.emit("vector", lambda E: E.tensor_tensor(
                tn["t1"][:], tn["y0"][:], tn["y0"][:], Alu.mult))
            P.emit("vector", lambda E: E.drain())
            P.emit("vector", lambda E: E.tensor_tensor(
                tn["t1"][:], tn["t1"][:], tn["vpe"][:], Alu.mult))
            P.emit("vector", lambda E: E.drain())
            P.emit("vector", lambda E: E.tensor_scalar(
                tn["t1"][:], tn["t1"][:], -0.5, 1.5, op0=Alu.mult, op1=Alu.add))
            P.emit("vector", lambda E: E.drain())
            P.emit("vector", lambda E: E.tensor_tensor(
                tn["rstd"][:], tn["y0"][:], tn["t1"][:], Alu.mult))
            P.emit("vector", lambda E: E.drain())
            P.emit("vector", lambda E: E.tensor_scalar(
                dst_fn(), src_fn(), tn["mu"][:], tn["rstd"][:],
                op0=Alu.subtract, op1=Alu.mult), inc="VE")

        with ExitStack() as cA:
            EA = cA.enter_context
            XqT = EA(nc.sbuf_tensor("XqT", [128, NKD * 1024], BF16))
            lnsb = EA(nc.sbuf_tensor("lnsb", [128, 8 * NTT], F32))
            cA1 = EA(ExitStack())
            EA1 = cA1.enter_context
            zall = EA1(nc.sbuf_tensor("zall", [128, NTT * D], F32))
            sqs = EA1(nc.sbuf_tensor("sqs", [128, D], F32))
            z1 = EA1(nc.sbuf_tensor("z1", [128, D], F32))
            Xq = EA1(nc.sbuf_tensor("Xq", [128, D], BF16))
            tn = {k: EA1(nc.sbuf_tensor("ln_" + k, [128, 1], F32))
                  for k in ("red", "mu", "ms", "vpe", "y0", "t1", "rstd",
                            "gmx", "gam", "inv", "sx")}
            tn["sqs"] = sqs
            P.wait("sync", "CC", marks["CC_Z"])

            def _mk_zoffs(E):
                hold["zb"] = E.snap(E.partition_id() * TOK,
                                    min_val=0, max_val=(NCORE - 1) * TOK,
                                    guaranteed_mod_val=TOK)
                return E.nop()
            P.emit("sync", _mk_zoffs)
            z3 = T["z_out"].ap().rearrange("(r n) d -> r n d", r=NCORE)
            for r in range(NCORE):
                P.emit("sync", lambda E, r=r: E.dma_start(
                    out=T["z_my"][:, r * DSL:(r + 1) * DSL],
                    in_=z3[r][bass.ds(hold["zb"], TOK), :]),
                    inc="LD", k=16)
            P.wait("sync", "LD", P.val("LD"))
            for tt in range(NTT):
                P.dma("sync", zall[:, tt * D:(tt + 1) * D],
                      T["z_my"][tt * 128:(tt + 1) * 128, :], "LD")
            ld_all = P.val("LD")
            P.wait("gpsimd", "LD", ld_all)
            P.wait("gpsimd", "ST", P.val("ST"))
            for a, b in (("wgq_in", "wgq_out"), ("wvq_in", "wvq_out"),
                         ("woq_in", "woq_out")):
                P.emit("gpsimd", lambda E, x=T[a], y=T[b]:
                       E.collective_compute(
                           "AllGather", Alu.bypass,
                           replica_groups=[list(range(NCORE))],
                           ins=[x.ap().opt()], outs=[y.ap().opt()]),
                       inc="CC")
            marks["CC_W"] = P.val("CC")

            tsq = 0
            xqtr_done = {}
            sd_z1 = {}
            P.wait("vector", "LD", ld_all)
            for tt in range(NTT):
                if tt - 1 in sd_z1:
                    P.wait("vector", "SD", sd_z1[tt - 1])
                emit_ln(lambda tt=tt: zall[:, tt * D:(tt + 1) * D],
                        lambda: z1[:], tn)
                if DEBUG_STOP == "ffn":
                    for k_i, k_n in enumerate(("red", "ms", "mu", "vpe", "y0",
                                               "rstd", "t1", "t1")):
                        P.emit("vector", lambda E, tt=tt, k_i=k_i, k_n=k_n:
                               E.tensor_copy(lnsb[:, tt * 8 + k_i:tt * 8 + k_i + 1],
                                             tn[k_n][:]))
                    P.emit("vector", lambda E: E.nop(), inc="VE")
                ve_z1 = P.val("VE")
                P.wait("scalar", "VE", ve_z1)
                P.emit("scalar", lambda E, tt=tt: E.dma_start(
                    out=T["z1st"][tt * 128:(tt + 1) * 128, :], in_=z1[:]),
                    inc="SD", k=16)
                sd_z1[tt] = P.val("SD")
                P.emit("vector", lambda E: E.tensor_reduce(
                    tn["gmx"][:], z1[:], AXX, Alu.max,
                    apply_absolute_value=True))
                P.emit("vector", lambda E: E.drain())
                P.emit("vector", lambda E: E.tensor_tensor(
                    tn["gam"][:], tn["gmx"][:], S["epsc"][:], Alu.max))
                P.emit("vector", lambda E: E.drain())
                P.emit("vector", lambda E: E.reciprocal(
                    tn["inv"][:], tn["gam"][:]))
                P.emit("vector", lambda E: E.drain())
                P.emit("vector", lambda E: E.tensor_tensor(
                    tn["sx"][:], tn["inv"][:], S["c127"][:], Alu.mult))
                P.emit("vector", lambda E, tt=tt: E.scalar_tensor_tensor(
                    sxg[:, tt:tt + 1], tn["gam"][:], 1.0, wgam[:, 4:5],
                    op0=Alu.mult, op1=Alu.mult))
                P.emit("vector", lambda E, tt=tt: E.scalar_tensor_tensor(
                    sxv[:, tt:tt + 1], tn["gam"][:], 1.0, wgam[:, 5:6],
                    op0=Alu.mult, op1=Alu.mult))
                P.emit("vector", lambda E: E.drain())
                P.emit("vector", lambda E: E.tensor_scalar(
                    sqs[:], z1[:], tn["sx"][:], MAGIC, op0=Alu.mult,
                    op1=Alu.add))
                if tt - 1 in xqtr_done:
                    P.wait("vector", "TE", xqtr_done[tt - 1])
                P.emit("vector", lambda E: E.tensor_scalar(
                    Xq[:], sqs[:], MAGIC, None, op0=Alu.subtract), inc="VE")
                P.wait("tensor", "VE", P.val("VE"))
                for k in range(NKD):
                    P.wait("tensor", "SC", P.val("SC") - 1)
                    P.emit("tensor", lambda E, k=k, p=tsq % 2: E.transpose(
                        psb[p][:, 0:128], Xq[:, k * 128:(k + 1) * 128],
                        identb[:]), inc="TE")
                    P.wait("scalar", "TE", P.val("TE"))
                    P.emit("scalar", lambda E, k=k, tt=tt, p=tsq % 2: E.copy(
                        XqT[:, k * 1024 + tt * 128:k * 1024 + (tt + 1) * 128],
                        psb[p][:, 0:128]), inc="SC")
                    tsq += 1
                xqtr_done[tt] = P.val("TE")
            P.wait("tensor", "SC", P.val("SC"))

            cA1.close()
            wgsl = [EA(nc.sbuf_tensor(f"wgsl{i}", [128, HID], BF16))
                    for i in range(2)]
            wvsl = [EA(nc.sbuf_tensor(f"wvsl{i}", [128, HID], BF16))
                    for i in range(2)]
            gbr = [EA(nc.sbuf_tensor(f"gbr{i}", [1, 512], F32))
                   for i in range(2)]
            vbr = [EA(nc.sbuf_tensor(f"vbr{i}", [1, 512], F32))
                   for i in range(2)]
            gbj = EA(nc.sbuf_tensor("gbj", [128, 512], F32))
            vbj = EA(nc.sbuf_tensor("vbj", [128, 512], F32))
            upre = EA(nc.sbuf_tensor("upre", [128, 512], F32))
            gsil = EA(nc.sbuf_tensor("gsil", [128, 512], F32))
            vtile = EA(nc.sbuf_tensor("vtile", [128, 512], F32))
            gvst = [EA(nc.sbuf_tensor(f"gvst{i}", [128, 512], F32))
                    for i in range(2)]
            mjt = EA(nc.sbuf_tensor("mjt", [128, 1], F32))
            P.wait("sync", "CC", marks["CC_W"])
            ld_slab = {}
            mm_done = {}
            epi_done = {}
            sd_gv = {}
            for j in range(16):
                r, half = divmod(j, 2)
                if j - 2 in mm_done:
                    P.wait("sync", "TE", mm_done[j - 2])
                for k in range(NKD):
                    P.dma("sync", wgsl[j % 2][:, k * 512:(k + 1) * 512],
                          T["wgq_out"][r * D + k * 128:r * D + (k + 1) * 128,
                                       half * 512:(half + 1) * 512], "LD")
                    P.dma("sync", wvsl[j % 2][:, k * 512:(k + 1) * 512],
                          T["wvq_out"][r * D + k * 128:r * D + (k + 1) * 128,
                                       half * 512:(half + 1) * 512], "LD")
                P.dma("sync", gbr[j % 2][0:1, :],
                      T["gbias"][0:1, j * 512:(j + 1) * 512], "LD")
                P.dma("sync", vbr[j % 2][0:1, :],
                      T["vbias"][0:1, j * 512:(j + 1) * 512], "LD")
                ld_slab[j] = P.val("LD")
                if (j - 1) * NTT + NTT - 1 in epi_done:
                    P.wait("tensor", "VE", epi_done[(j - 1) * NTT + NTT - 1])
                P.wait("tensor", "LD", ld_slab[j])
                P.emit("tensor", lambda E, j=j: E.matmul(
                    psf[4][:, 0:512], ones1[0:1, :],
                    gbr[j % 2][0:1, 0:512],
                    start=True, stop=True))
                P.emit("tensor", lambda E, j=j: E.matmul(
                    psf[5][:, 0:512], ones1[0:1, :],
                    vbr[j % 2][0:1, 0:512],
                    start=True, stop=True), inc="TE")
                P.wait("vector", "TE", P.val("TE"))
                P.emit("vector", lambda E: E.tensor_copy(gbj[:], psf[4][:, 0:512]))
                P.emit("vector", lambda E: E.tensor_copy(vbj[:], psf[5][:, 0:512]),
                       inc="VE")
                for t in range(NTT):
                    gq = j * NTT + t
                    P.wait("tensor", "LD", ld_slab[j])
                    if gq - 2 in epi_done:
                        P.wait("tensor", "VE", epi_done[gq - 2])
                    for k in range(NKD):
                        P.emit("tensor", lambda E, k=k, t=t, j=j,
                               f=(k == 0), st=(k == NKD - 1): E.matmul(
                                   psf[t % 2][:, 0:512],
                                   XqT[:, k * 1024 + t * 128:k * 1024 + (t + 1) * 128],
                                   wgsl[j % 2][:, k * 512:(k + 1) * 512],
                                   start=f, stop=st),
                               inc=("TE" if k == NKD - 1 else None))
                        P.emit("tensor", lambda E, k=k, t=t, j=j,
                               f=(k == 0), st=(k == NKD - 1): E.matmul(
                                   psf[2 + t % 2][:, 0:512],
                                   XqT[:, k * 1024 + t * 128:k * 1024 + (t + 1) * 128],
                                   wvsl[j % 2][:, k * 512:(k + 1) * 512],
                                   start=f, stop=st),
                               inc=("TE" if k == NKD - 1 else None))
                    te_gv = P.val("TE")
                    P.wait("vector", "TE", te_gv)
                    P.emit("vector", lambda E, t=t: E.scalar_tensor_tensor(
                        upre[:], psf[t % 2][:, 0:512], sxg[:, t:t + 1], gbj[:],
                        op0=Alu.mult, op1=Alu.add), inc="VE")
                    P.wait("scalar", "VE", P.val("VE"))
                    P.emit("scalar", lambda E: E.activation(
                        gsil[:], upre[:], Act.Silu))
                    P.emit("scalar", lambda E: E.drain(), inc="SC")
                    P.emit("vector", lambda E, t=t: E.scalar_tensor_tensor(
                        vtile[:], psf[2 + t % 2][:, 0:512], sxv[:, t:t + 1],
                        vbj[:], op0=Alu.mult, op1=Alu.add))
                    P.wait("vector", "SC", P.val("SC"))
                    if gq - 2 in sd_gv:
                        P.wait("vector", "SD", sd_gv[gq - 2])
                    P.emit("vector", lambda E, sl=gq % 2: E.tensor_tensor(
                        gvst[sl][:], gsil[:], vtile[:], Alu.mult))
                    P.emit("vector", lambda E, sl=gq % 2: E.tensor_reduce(
                        mjt[:], gvst[sl][:], AXX, Alu.max,
                        apply_absolute_value=True))
                    P.emit("vector", lambda E: E.drain())
                    P.emit("vector", lambda E, t=t: E.tensor_tensor(
                        gvmax[:, t:t + 1], gvmax[:, t:t + 1], mjt[:], Alu.max),
                        inc="VE")
                    epi_done[gq] = P.val("VE")
                    P.wait("scalar", "VE", epi_done[gq])
                    P.emit("scalar", lambda E, t=t, j=j, sl=gq % 2: E.dma_start(
                        out=T["gv_dram"][t, j, :, :], in_=gvst[sl][:]),
                        inc="SD", k=16)
                    sd_gv[gq] = P.val("SD")
                mm_done[j] = P.val("TE")
            P.wait("sync", "SD", P.val("SD"))
            if DEBUG_STOP == "ffn":
                P.wait("sync", "VE", P.val("VE"))
                P.dma("sync", T["lndbg"], lnsb[:], "ST")
                P.wait("sync", "ST", P.val("ST"))
            P.barrier()

        with ExitStack() as cB:
            EB = cB.enter_context
            gvrow = [EB(nc.sbuf_tensor(f"gvrow{i}", [128, HID], F32))
                     for i in range(2)]
            gvq32 = EB(nc.sbuf_tensor("gvq32", [128, HID], F32))
            GVq = [EB(nc.sbuf_tensor(f"GVq{i}", [128, HID], BF16))
                   for i in range(2)]
            gvqTst = [EB(nc.sbuf_tensor(f"gvqTst{i}", [128, HID], BF16))
                      for i in range(2)]
            g2 = {k: EB(nc.sbuf_tensor("g2_" + k, [128, 1], F32))
                  for k in ("gam", "inv", "sgv")}
            tsq = 0
            ld_gv, ve_q, tr_done, st_q = {}, {}, {}, {}
            for t in range(NTT):
                if t - 2 in ve_q:
                    P.wait("sync", "VE", ve_q[t - 2])
                for j in range(16):
                    P.dma("sync", gvrow[t % 2][:, j * 512:(j + 1) * 512],
                          T["gv_dram"][t, j, :, :], "LD")
                ld_gv[t] = P.val("LD")
                P.wait("vector", "LD", ld_gv[t])
                P.emit("vector", lambda E, t=t: E.tensor_scalar(
                    g2["gam"][:], gvmax[:, t:t + 1], EPS, None, op0=Alu.max))
                P.emit("vector", lambda E: E.drain())
                P.emit("vector", lambda E: E.reciprocal(
                    g2["inv"][:], g2["gam"][:]))
                P.emit("vector", lambda E: E.drain())
                P.emit("vector", lambda E: E.tensor_scalar(
                    g2["sgv"][:], g2["inv"][:], 127.0, None, op0=Alu.mult))
                P.emit("vector", lambda E, t=t: E.scalar_tensor_tensor(
                    sO[:, t:t + 1], g2["gam"][:], 1.0, wgam[:, 6:7],
                    op0=Alu.mult, op1=Alu.mult))
                P.emit("vector", lambda E: E.drain())
                P.emit("vector", lambda E, t=t: E.tensor_scalar(
                    gvq32[:], gvrow[t % 2][:], g2["sgv"][:], MAGIC,
                    op0=Alu.mult, op1=Alu.add))
                if t - 2 in tr_done:
                    P.wait("vector", "TE", tr_done[t - 2])
                P.emit("vector", lambda E, t=t: E.tensor_scalar(
                    GVq[t % 2][:], gvq32[:], MAGIC, None, op0=Alu.subtract),
                    inc="VE")
                ve_q[t] = P.val("VE")
                P.wait("tensor", "VE", ve_q[t])
                first = True
                evm = []
                for k in range(NKH):
                    if len(evm) >= 2:
                        eng0, v0 = evm[-2]
                        P.wait("tensor", "SC" if eng0 == "scalar" else "VE",
                               v0)
                    P.emit("tensor", lambda E, k=k, t=t, p=tsq % 2:
                           E.transpose(psb[p][:, 0:128],
                                       GVq[t % 2][:, k * 128:(k + 1) * 128],
                                       identb[:]), inc="TE")
                    eng = "scalar" if k % 2 == 0 else "vector"
                    P.wait(eng, "TE", P.val("TE"))
                    if first and t - 2 in st_q:
                        P.wait(eng, "ST", st_q[t - 2])
                        first = False
                    if eng == "scalar":
                        P.emit("scalar", lambda E, k=k, t=t, p=tsq % 2:
                               E.copy(gvqTst[t % 2][:, k * 128:(k + 1) * 128],
                                      psb[p][:, 0:128]), inc="SC")
                        evm.append(("scalar", P.val("SC")))
                    else:
                        P.emit("vector", lambda E, k=k, t=t, p=tsq % 2:
                               E.tensor_copy(
                                   gvqTst[t % 2][:, k * 128:(k + 1) * 128],
                                   psb[p][:, 0:128]), inc="VE")
                        evm.append(("vector", P.val("VE")))
                    tsq += 1
                tr_done[t] = P.val("TE")
                P.wait("sync", "SC", P.val("SC"))
                P.wait("sync", "VE", P.val("VE"))
                P.dma("sync", T["gvqT_dram"][t, :, :], gvqTst[t % 2][:], "ST")
                st_q[t] = P.val("ST")
            P.wait("sync", "ST", P.val("ST"))
            P.barrier()

        with ExitStack() as cC:
            EC = cC.enter_context
            wosl = EC(nc.sbuf_tensor("wosl", [128, NKH * 512], BF16))
            gvqrd = [EC(nc.sbuf_tensor(f"gvqrd{i}", [128, HID], BF16))
                     for i in range(2)]
            obm = EC(nc.sbuf_tensor("obm", [1, D], F32))
            obj = EC(nc.sbuf_tensor("obj", [128, 512], F32))
            Oacc = EC(nc.sbuf_tensor("Oacc", [128, NTT * D], F32))
            z1rd = EC(nc.sbuf_tensor("z1rd", [128, D], F32))
            z2b = EC(nc.sbuf_tensor("z2b", [128, D], F32))
            ofin = EC(nc.sbuf_tensor("ofin", [128, D], F32))
            sq2 = EC(nc.sbuf_tensor("sq2", [128, D], F32))
            tn2 = {k: EC(nc.sbuf_tensor("l2_" + k, [128, 1], F32))
                   for k in ("red", "mu", "ms", "vpe", "y0", "t1", "rstd")}
            tn2["sqs"] = sq2

            P.dma("sync", obm[:], T["obias"], "LD")
            P.wait("tensor", "LD", P.val("LD"))
            oq_mm = {}
            slab_mm = {}
            epi_o = {}
            for j in range(4):
                if j - 1 in slab_mm:
                    P.wait("sync", "TE", slab_mm[j - 1])
                for k in range(NKH):
                    P.dma("sync", wosl[:, k * 512:(k + 1) * 512],
                          T["woq_out"][k * 128:(k + 1) * 128,
                                       j * 512:(j + 1) * 512], "LD")
                ld_wo = P.val("LD")
                if (j - 1) * NTT + NTT - 1 in epi_o:
                    P.wait("tensor", "VE", epi_o[(j - 1) * NTT + NTT - 1])
                P.emit("tensor", lambda E, j=j: E.matmul(
                    psf[4][:, 0:512], ones1[0:1, :],
                    obm[0:1, j * 512:(j + 1) * 512],
                    start=True, stop=True), inc="TE")
                P.wait("vector", "TE", P.val("TE"))
                P.emit("vector", lambda E: E.tensor_copy(obj[:], psf[4][:, 0:512]),
                       inc="VE")
                for t in range(NTT):
                    oq = j * NTT + t
                    if oq - 2 in oq_mm:
                        P.wait("sync", "TE", oq_mm[oq - 2])
                    P.dma("sync", gvqrd[t % 2][:], T["gvqT_dram"][t, :, :], "LD")
                    P.wait("tensor", "LD", P.val("LD"))
                    if oq - 3 in epi_o:
                        P.wait("tensor", "VE", epi_o[oq - 3])
                    for k in range(NKH):
                        P.emit("tensor", lambda E, k=k, t=t, b=oq % 3,
                               f=(k == 0), st=(k == NKH - 1): E.matmul(
                                   psf[b][:, 0:512],
                                   gvqrd[t % 2][:, k * 128:(k + 1) * 128],
                                   wosl[:, k * 512:(k + 1) * 512],
                                   start=f, stop=st),
                               inc=("TE" if k == NKH - 1 else None))
                    oq_mm[oq] = P.val("TE")
                    P.wait("vector", "TE", oq_mm[oq])
                    P.emit("vector", lambda E, t=t, j=j, b=oq % 3:
                           E.scalar_tensor_tensor(
                               Oacc[:, t * D + j * 512:t * D + (j + 1) * 512],
                               psf[b][:, 0:512], sO[:, t:t + 1], obj[:],
                               op0=Alu.mult, op1=Alu.add), inc="VE")
                    epi_o[oq] = P.val("VE")
                slab_mm[j] = P.val("TE")

            st_out = {}
            for t in range(NTT):
                P.wait("sync", "VE", P.val("VE"))
                P.dma("sync", z1rd[:], T["z1st"][t * 128:(t + 1) * 128, :], "LD")
                P.wait("vector", "LD", P.val("LD"))
                if t - 1 in st_out:
                    P.wait("vector", "ST", st_out[t - 1])
                P.emit("vector", lambda E, t=t: E.tensor_tensor(
                    z2b[:], z1rd[:], Oacc[:, t * D:(t + 1) * D], Alu.add))
                emit_ln(lambda: z2b[:], lambda: ofin[:], tn2)
                P.wait("sync", "VE", P.val("VE"))
                P.dma("sync", T["out"][t * 128:(t + 1) * 128, :], ofin[:], "ST")
                st_out[t] = P.val("ST")
            P.wait("sync", "ST", P.val("ST"))
            P.wait("sync", "SD", P.val("SD"))
            if DEBUG_STOP == "ffn":
                P.dma("gpsimd", T["z1dbg"], T["z1st"].ap(), "GD")
                P.dma("gpsimd", T["gvdbg"],
                      T["gv_dram"].ap().rearrange("a b c d -> (a b c) d"), "GD")
                P.dma("gpsimd", T["gvqdbg"],
                      T["gvqT_dram"].ap().rearrange("a b c -> (a b) c"), "GD")
                P.dma("gpsimd", T["zmydbg"], T["z_my"].ap(), "GD")
                P.wait("gpsimd", "GD", P.val("GD"))


_NC_CACHE = {}


def _get_nc():
    key = DEBUG_STOP
    if key not in _NC_CACHE:
        _NC_CACHE[key] = build_nc()
    return _NC_CACHE[key]


def kernel(L0_attn_x, L0_attn_W_x, L0_attn_W_dt, L0_attn_b_dt, L0_attn_A_log,
           L0_attn_D, L0_ffn_gate_w, L0_ffn_gate_b, L0_ffn_val_w, L0_ffn_val_b,
           L0_ffn_out_w, L0_ffn_out_b):
    f32 = np.float32
    x = np.asarray(L0_attn_x, f32)
    W_x = np.ascontiguousarray(np.asarray(L0_attn_W_x, f32))
    W_dt = np.asarray(L0_attn_W_dt, f32)
    b_dt = np.asarray(L0_attn_b_dt, f32)
    A_log = np.asarray(L0_attn_A_log, f32)
    Dv = np.asarray(L0_attn_D, f32)
    WgT = np.ascontiguousarray(np.asarray(L0_ffn_gate_w, f32).T)
    WvT = np.ascontiguousarray(np.asarray(L0_ffn_val_w, f32).T)
    WoT = np.ascontiguousarray(np.asarray(L0_ffn_out_w, f32).T)
    gb = np.ascontiguousarray(np.asarray(L0_ffn_gate_b, f32).reshape(1, HID))
    vb = np.ascontiguousarray(np.asarray(L0_ffn_val_b, f32).reshape(1, HID))
    ob = np.ascontiguousarray(np.asarray(L0_ffn_out_b, f32).reshape(1, D))

    xf = np.ascontiguousarray(x.reshape(B * L, D))
    in_maps = []
    for c in range(NCORE):
        dsl = slice(DSL * c, DSL * (c + 1))
        hsl = slice(1024 * c, 1024 * (c + 1))
        in_maps.append(dict(
            x_tok=np.ascontiguousarray(xf[TOK * c:TOK * (c + 1)]),
            x_dsl=np.ascontiguousarray(x[:, :, dsl]),
            wx=W_x,
            wdtT=np.ascontiguousarray(W_dt[dsl].T),
            bdt=np.ascontiguousarray(b_dt[dsl].reshape(2, 128).T),
            alog=np.ascontiguousarray(
                A_log[dsl].reshape(2, 128, NST).transpose(1, 0, 2)
                .reshape(128, 2 * NST)),
            dvec=np.ascontiguousarray(Dv[dsl].reshape(2, 128).T),
            wgTd=np.ascontiguousarray(WgT[:, hsl]),
            wvTd=np.ascontiguousarray(WvT[:, hsl]),
            woTd=np.ascontiguousarray(WoT[hsl]),
            gbias=gb, vbias=vb, obias=ob,
        ))
    nc = _get_nc()
    res = run_bass_kernel_spmd(nc, in_maps, core_ids=list(range(NCORE)))
    if DEBUG_STOP is not None:
        return res
    o = np.concatenate([r["out"] for r in res.results], axis=0)
    return o.reshape(B, L, D)


if __name__ == "__main__":
    nc = build_nc()
    n = sum(1 for _ in nc.m.functions[0].blocks for _ in [0])
    print("built ok")
